# version 05
# baseline (speedup 1.0000x reference)
import math
from contextlib import ExitStack

import numpy as np
import ml_dtypes

import concourse.bass as bass
import concourse.mybir as mybir
from concourse.bass_utils import run_bass_kernel_spmd

F32 = mybir.dt.float32
BF16 = mybir.dt.bfloat16
I32 = mybir.dt.int32
AF = mybir.ActivationFunctionType
ALU = mybir.AluOpType

NPBF16 = ml_dtypes.bfloat16

D = 2048
INTER = 8192
NKT = 16
NTAU = 64
HD = 128
NH = 16
NKV = 4
C16 = 1.0 / (math.sqrt(2048.0) * 64.0)
C64 = 1.0 / (math.sqrt(8192.0) * 64.0)
LN_EPS = 1e-5
RMS_EPS = 1e-6
NEG_BIG = -1.0e5
ISQD = 1.0 / math.sqrt(128.0)
TC = 512


def _hadamard(n):
    h = np.array([[1.0]], dtype=np.float32)
    while h.shape[0] < n:
        h = np.block([[h, h], [h, -h]])
    return h.astype(np.float32)



ENGS = ("tensor", "scalar", "vector", "gpsimd", "sync")


class Ctx:

    def __init__(self, kind, sems):
        self.kind = kind
        self.eng = None
        self.sems = sems
        self.counts = {k: 0 for k in sems}
        self.guards = {}

    def _inc(self, instr, inc):
        if inc is None:
            return instr
        sem, n = inc
        self.counts[sem] += n
        if instr is not None:
            instr.then_inc(self.sems[sem], n)
        return instr

    def w(self, kind, sem, cnt=None):
        v = self.counts[sem] if cnt is None else cnt
        if self.kind == kind and v > 0:
            self.eng.wait_ge(self.sems[sem], v)

    def barrier(self):
        for sem, v in self.counts.items():
            if v > 0:
                self.eng.wait_ge(self.sems[sem], v)

    def guard(self, buf, sem):
        self.guards.setdefault(buf, {})[sem] = self.counts[sem]

    def await_guards(self, kind, buf):
        g = self.guards.get(buf)
        if not g:
            return
        if self.kind == kind:
            for sem, v in g.items():
                if v > 0:
                    self.eng.wait_ge(self.sems[sem], v)

    def mm(self, out, lhsT, rhs, start, stop, inc=None):
        i = self.eng.matmul(out, lhsT, rhs, start=start, stop=stop) \
            if self.kind == "tensor" else None
        return self._inc(i, inc)

    def act(self, out, in_, func, bias=0.0, scale=1.0, accum=None, inc=None):
        i = self.eng.activation(out, in_, func, bias=bias, scale=scale,
                                accum_out=accum) \
            if self.kind == "scalar" else None
        return self._inc(i, inc)

    def tt(self, out, a, b, op, inc=None):
        i = self.eng.tensor_tensor(out, a, b, op) \
            if self.kind == "vector" else None
        return self._inc(i, inc)

    def ts(self, out, a, s1, op0, s2=None, op1=ALU.bypass, inc=None):
        i = self.eng.tensor_scalar(out, a, s1, s2, op0, op1) \
            if self.kind == "vector" else None
        return self._inc(i, inc)

    def stt(self, out, in0, scalar, in1, op0, op1, inc=None):
        i = self.eng.scalar_tensor_tensor(out, in0, scalar, in1, op0, op1) \
            if self.kind == "vector" else None
        return self._inc(i, inc)

    def vcopy(self, out, in_, inc=None):
        i = self.eng.tensor_copy(out, in_) if self.kind == "vector" else None
        return self._inc(i, inc)

    def recip(self, out, in_, inc=None):
        i = self.eng.reciprocal(out, in_) if self.kind == "vector" else None
        return self._inc(i, inc)

    def memset(self, ap, v, inc=None):
        i = self.eng.memset(ap, v) if self.kind == "vector" else None
        return self._inc(i, inc)

    def dma(self, queue, out, in_, inc=None, slow=False):
        i = None
        if self.kind == queue:
            kw = {"allow_slow_non_contiguous": True} if slow else {}
            i = self.eng.dma_start(out=out, in_=in_, **kw)
        return self._inc(i, inc)


class Builder:

    def __init__(self, nc, sems):
        self.nc = nc
        self.ctxs = {k: Ctx(k, sems) for k in ENGS}

    def block(self, program):
        with self.nc.Block() as blk:
            for kind in ENGS:
                ctx = self.ctxs[kind]

                def mk(ctx):
                    def f(eng):
                        ctx.eng = eng
                        program(ctx)
                    return f
                getattr(blk, kind)(mk(ctx))



SEM_NAMES = ("pe", "act", "dve", "pre", "sv", "sto",
             "gla", "glb", "gsa", "gsb", "wla", "wlb", "wlc", "wld",
             "wfa", "wfb", "wfc", "wfd", "hla", "hlb", "rla", "rlb",
             "sta", "stb", "qla", "qlb")


def alloc_common(nc, stk):
    t = {}

    def sb(name, shape, dt):
        t[name] = stk.enter_context(nc.sbuf_tensor(name, shape, dt))
        return t[name]

    sb("H128c", [128, 128], BF16)
    sb("BD16p", [128, 128], F32)
    sb("BD64p", [128, 128], F32)
    sb("ones128f", [128, 1], F32)
    sb("ones1f", [1, 128], F32)
    sb("c64row", [1, 128], F32)
    sb("c32row", [1, 128], F32)
    sb("su_ug", [128, 16], F32)
    sb("su_dp", [128, 64], F32)
    sb("kvals", [128, 8], F32)
    sb("ones128b", [128, 1], BF16)
    return t


def alloc_psum(nc, stk, t, sfx=""):
    def ps(name, shape):
        t[name] = stk.enter_context(nc.psum_tensor(name + sfx, shape, F32))
        return t[name]
    t["psA"] = [ps("psA0", [128, 512]), ps("psA1", [128, 512])]
    t["psB"] = [ps("psB0", [128, 512]), ps("psB1", [128, 512])]
    t["psC"] = [ps("psC0", [128, 512]), ps("psC1", [128, 512])]
    t["psD"] = [ps("psD0", [128, 512]), ps("psD1", [128, 512])]
    t["ps8"] = t["psA"] + t["psB"] + t["psC"] + t["psD"]


def alloc_fold(nc, stk, t):
    def sb(name, shape, dt):
        t[name] = stk.enter_context(nc.sbuf_tensor(name, shape, dt))
        return t[name]
    sb("sbc", [128, INTER], F32)
    sb("bdt", [128, 128], BF16)
    t["win"] = [sb("win0", [128, INTER], BF16), sb("win1", [128, INTER], BF16)]
    t["wout"] = [sb("wout0", [128, INTER], BF16),
                 sb("wout1", [128, INTER], BF16)]
    sb("svbuf", [1, 512], F32)


def alloc_ffn(nc, stk, t, sfx=""):
    def sb(name, shape, dt):
        t[name] = stk.enter_context(nc.sbuf_tensor(name + sfx, shape, dt))
        return t[name]
    sb("BD16", [128, 16 * 128], BF16)
    sb("u", [128, NTAU * TC], BF16)
    t["xn"] = t["u"][:, :NKT * TC]
    sb("z", [128, NKT * TC], BF16)
    t["y"] = [sb("y0", [128, TC], BF16), sb("y1", [128, TC], BF16)]
    t["gsil"] = [sb("gsil0", [128, TC], F32), sb("gsil1", [128, TC], F32)]
    sb("stat", [1, 4 * TC], F32)
    sb("rbc", [128, TC], F32)
    sb("mrbc", [128, TC], F32)
    t["hbuf"] = [sb("hbuf0", [128, TC], F32), sb("hbuf1", [128, TC], F32)]
    t["hsq"] = [sb("hsq0", [128, TC], F32), sb("hsq1", [128, TC], F32)]
    t["wu_sl"] = [sb("wu_sl0", [128, 16 * 512], BF16),
                  sb("wu_sl1", [128, 16 * 512], BF16)]
    t["wg_sl"] = [sb("wg_sl0", [128, 16 * 512], BF16),
                  sb("wg_sl1", [128, 16 * 512], BF16)]
    t["bd_sl"] = [sb("bd_sl0", [128, 512], BF16),
                  sb("bd_sl1", [128, 512], BF16)]
    t["wf_sl"] = [sb("wf_sl0", [128, 1024], BF16),
                  sb("wf_sl1", [128, 1024], BF16)]
    t["rbuf"] = [sb("rbuf0", [128, 512], F32), sb("rbuf1", [128, 512], F32)]
    t["hnx"] = [sb("hnx0", [128, 512], F32), sb("hnx1", [128, 512], F32)]


def emit_load_consts(c, io, t):
    c.dma("gpsimd", t["H128c"][:, :], io["H128c_i"].ap(), inc=("pre", 16))
    c.dma("gpsimd", t["BD16p"][:, :], io["BD16p_i"].ap(), inc=("pre", 16))
    c.dma("gpsimd", t["BD64p"][:, :], io["BD64p_i"].ap(), inc=("pre", 16))
    c.dma("gpsimd", t["su_ug"][:, :], io["su_ug_cols"].ap(), inc=("pre", 16))
    c.dma("gpsimd", t["su_dp"][:, :], io["su_dp_cols"].ap(), inc=("pre", 16))
    c.memset(t["ones128f"][:, :], 1.0)
    c.memset(t["ones1f"][:1, :], 1.0)
    c.memset(t["c64row"][:1, :], 64.0)
    c.memset(t["kvals"][:, 0:1], LN_EPS)
    c.memset(t["kvals"][:, 1:2], RMS_EPS)
    c.memset(t["kvals"][:, 2:3], 50.0)
    c.memset(t["kvals"][:, 3:4], ISQD / 50.0)
    c.memset(t["kvals"][:, 4:5], 1.0 / 128.0)
    c.memset(t["ones128b"][:, :], 1.0)
    c.memset(t["c32row"][:1, :], 32.0, inc=("dve", 1))
    c.w("vector", "pre")


def emit_bd_tiles(c, t):
    c.w("vector", "pre")
    for k in range(16):
        c.ts(t["BD16"][:, k * 128:(k + 1) * 128], t["BD16p"][:, :],
             t["su_ug"][:, k:k + 1], ALU.mult,
             inc=("dve", 1) if k == 15 else None)


def emit_weight_fold(c, io, t):
    def fold(src_h, dst_h, sv_h, n512, n_slabs, scale_lhs, dst_ap_fn):
        for i in range(n512):
            c.w("gpsimd", "pe")
            c.dma("gpsimd", t["svbuf"][:1, :],
                  bass.AP(io[sv_h], i * 512, [[512, 1], [1, 512]]),
                  inc=("sv", 16))
            c.w("tensor", "pre")
            c.w("tensor", "sv")
            c.w("tensor", "dve")
            c.mm(t["psD"][0][:, :], t[scale_lhs][:1, :], t["svbuf"][:1, :],
                 True, True, inc=("pe", 1))
            c.w("vector", "pe")
            c.vcopy(t["sbc"][:, i * 512:(i + 1) * 512], t["psD"][0][:, :],
                    inc=("dve", 1))
        c.w("vector", "dve")
        for sl in range(n_slabs):
            bi = sl % 2
            gl = "gla" if bi == 0 else "glb"
            gs = "gsa" if bi == 0 else "gsb"
            c.await_guards("gpsimd", "win" + str(bi))
            c.dma("gpsimd", t["win"][bi][:, :n512 * 512],
                  bass.AP(io[src_h], sl * 128 * n512 * 512,
                          [[n512 * 512, 128], [1, n512 * 512]]),
                  inc=(gl, 16))
            c.w("tensor", gl)
            c.w("tensor", "dve")
            for n in range(n512):
                c.w("tensor", "dve")
                c.mm(t["psD"][n % 2][:, :], t["H128c"][:, :],
                     t["win"][bi][:, n * 512:(n + 1) * 512], True, True,
                     inc=("pe", 1))
                c.w("vector", "pe")
                c.await_guards("vector", "wout" + str(bi))
                c.tt(t["wout"][bi][:, n * 512:(n + 1) * 512],
                     t["psD"][n % 2][:, :],
                     t["sbc"][:, n * 512:(n + 1) * 512], ALU.mult,
                     inc=("dve", 1))
            c.guard("win" + str(bi), "pe")
            c.w("gpsimd", "dve")
            c.dma("gpsimd", dst_ap_fn(sl), t["wout"][bi][:, :n512 * 512],
                  inc=(gs, 16))
            c.guard("wout" + str(bi), gs)

    fold("Wu_st", "WU_hbm", "ws_up", 16, 16, "c64row",
         lambda sl: bass.AP(io["WU_hbm"], sl * 512,
                            [[8192, 128], [1048576, 16], [1, 512]]))
    fold("Wg_st", "WG_hbm", "ws_gate", 16, 16, "c64row",
         lambda sl: bass.AP(io["WG_hbm"], sl * 512,
                            [[8192, 128], [1048576, 16], [1, 512]]))
    fold("Wd_st", "WF_hbm", "ws_dn", 4, 64, "c32row",
         lambda sl: bass.AP(io["WF_hbm"], sl * 2048,
                            [[131072, 128], [1, 2048]]))
    c.w("vector", "pre")
    for k in range(64):
        c.await_guards("vector", "bdt")
        c.ts(t["bdt"][:, :], t["BD64p"][:, :], t["su_dp"][:, k:k + 1],
             ALU.mult, inc=("dve", 1))
        c.w("gpsimd", "dve")
        c.dma("gpsimd", bass.AP(io["BD64_hbm"], k * 128,
                                [[8192, 128], [1, 128]]),
              t["bdt"][:, :], inc=("gsa", 16))
        c.guard("bdt", "gsa")


def emit_ln(c, t, h_ap_fn, out_fn):
    c.w("tensor", "dve")
    for kt in range(NKT):
        bi = kt % 2
        hl = "hla" if bi == 0 else "hlb"
        c.await_guards("gpsimd", "hbuf" + str(bi))
        c.dma("gpsimd", t["hbuf"][bi][:, :], h_ap_fn(kt), inc=(hl, 16))
        c.w("scalar", hl)
        c.await_guards("scalar", "hsq" + str(bi))
        c.act(t["hsq"][bi][:, :], t["hbuf"][bi][:, :], AF.Square,
              inc=("act", 1))
        c.w("tensor", "act")
        c.w("tensor", hl)
        c.mm(t["psA"][0][:1, :], t["ones128f"][:, :], t["hbuf"][bi][:, :],
             start=(kt == 0), stop=(kt == NKT - 1))
        c.mm(t["psB"][0][:1, :], t["ones128f"][:, :], t["hsq"][bi][:, :],
             start=(kt == 0), stop=(kt == NKT - 1), inc=("pe", 1))
        c.guard("hbuf" + str(bi), "pe")
        c.guard("hsq" + str(bi), "pe")
    st = t["stat"]
    c.w("vector", "pe")
    c.ts(st[:1, 0:TC], t["psA"][0][:1, :], 1.0 / 2048.0, ALU.mult,
         inc=("dve", 1))
    c.ts(st[:1, TC:2 * TC], t["psB"][0][:1, :], 1.0 / 2048.0, ALU.mult,
         inc=("dve", 1))
    c.w("vector", "dve")
    c.tt(st[:1, 2 * TC:3 * TC], st[:1, 0:TC], st[:1, 0:TC], ALU.mult,
         inc=("dve", 1))
    c.w("vector", "dve")
    c.tt(st[:1, 2 * TC:3 * TC], st[:1, TC:2 * TC], st[:1, 2 * TC:3 * TC],
         ALU.subtract, inc=("dve", 1))
    c.w("scalar", "dve")
    c.act(st[:1, 2 * TC:3 * TC], st[:1, 2 * TC:3 * TC], AF.Sqrt,
          bias=t["kvals"][:1, 0:1], inc=("act", 1))
    c.w("vector", "act")
    c.recip(st[:1, 2 * TC:3 * TC], st[:1, 2 * TC:3 * TC], inc=("dve", 1))
    c.w("vector", "dve")
    c.tt(st[:1, 3 * TC:4 * TC], st[:1, 0:TC], st[:1, 2 * TC:3 * TC],
         ALU.mult, inc=("dve", 1))
    c.w("tensor", "dve")
    c.mm(t["psA"][0][:, :], t["ones1f"][:1, :], st[:1, 2 * TC:3 * TC],
         True, True)
    c.mm(t["psB"][0][:, :], t["ones1f"][:1, :], st[:1, 3 * TC:4 * TC],
         True, True, inc=("pe", 1))
    c.w("vector", "pe")
    c.vcopy(t["rbc"][:, :], t["psA"][0][:, :], inc=("dve", 1))
    c.vcopy(t["mrbc"][:, :], t["psB"][0][:, :], inc=("dve", 1))
    c.w("vector", "pe")
    for kt in range(NKT):
        bi = kt % 2
        hl = "hla" if bi == 0 else "hlb"
        c.await_guards("gpsimd", "hbuf" + str(bi))
        c.dma("gpsimd", t["hbuf"][bi][:, :], h_ap_fn(kt), inc=(hl, 16))
        c.w("vector", hl)
        c.w("vector", "dve")
        c.await_guards("vector", "hsq" + str(bi))
        c.tt(t["hsq"][bi][:, :], t["hbuf"][bi][:, :], t["rbc"][:, :],
             ALU.mult, inc=("dve", 1))
        c.w("vector", "dve")
        c.tt(out_fn(kt), t["hsq"][bi][:, :], t["mrbc"][:, :], ALU.subtract,
             inc=("dve", 1))
        c.guard("hbuf" + str(bi), "dve")
        c.guard("hsq" + str(bi), "dve")


def emit_ffn_chunk(c, io, t, T, row_b, h_in, h_out, dbg=None):

    def h_ap(kt):
        return bass.AP(io[h_in], kt * 128 * T + row_b, [[T, 128], [1, TC]])

    def dump(name, src_ap):
        if dbg is None or name not in dbg:
            return
        c.w("gpsimd", "dve")
        c.w("gpsimd", "act")
        c.w("gpsimd", "pe")
        c.dma("gpsimd", dbg[name].ap(), src_ap, inc=("sto", 16))

    c.w("vector", "pe")
    emit_ln(c, t, h_ap, lambda kt: t["xn"][:, kt * TC:(kt + 1) * TC])
    dump("xn", t["xn"][:, :])

    c.w("tensor", "pre")
    for kt in range(NKT):
        pb = kt % 2
        c.w("tensor", "act")
        c.w("tensor", "dve")
        c.mm(t["psA"][pb][:, :], t["BD16"][:, kt * 128:(kt + 1) * 128],
             t["xn"][:, kt * TC:(kt + 1) * TC], True, True, inc=("pe", 1))
        c.w("scalar", "pe")
        c.act(t["z"][:, kt * TC:(kt + 1) * TC], t["psA"][pb][:, :], AF.Copy,
              inc=("act", 1))
    dump("z", t["z"][:, :])

    upgate_ct = {}
    bd64_ct = {}
    silu_ct = {}
    ucopy_ct = {}
    ymul_ct = {}

    def emit_bd64(tau):
        pp = tau % 2
        qb = (tau // 4) % 2
        c.w("tensor", "dve", ymul_ct[tau])
        if tau - 2 in ucopy_ct:
            c.w("tensor", "act", ucopy_ct[tau - 2])
        c.mm(t["psC"][pp][:, :],
             t["bd_sl"][qb][:, (tau % 4) * 128:(tau % 4 + 1) * 128],
             t["y"][pp][:, :], True, True, inc=("pe", 1))
        if tau % 4 == 3:
            c.guard("bd" + str(qb), "pe")
        bd64_ct[tau] = c.counts["pe"]
        c.w("scalar", "pe", bd64_ct[tau])
        c.act(t["u"][:, tau * TC:(tau + 1) * TC], t["psC"][pp][:, :], AF.Copy,
              inc=("act", 1))
        ucopy_ct[tau] = c.counts["act"]

    for tau in range(NTAU):
        pp = tau % 2
        e4 = tau % 4
        qb = (tau // 4) % 2
        if e4 == 0:
            wl = ("wla", "wlb", "wlc", "wld")[(tau // 4) % 4]
            c.await_guards("sync", "wu" + str(qb))
            c.dma("sync", t["wu_sl"][qb][:, :],
                  bass.AP(io["WU_hbm"], (tau // 4) * 1048576,
                          [[512, 128], [65536, 16], [1, 512]]),
                  inc=(wl, 16))
            c.dma("sync", t["wg_sl"][qb][:, :],
                  bass.AP(io["WG_hbm"], (tau // 4) * 1048576,
                          [[512, 128], [65536, 16], [1, 512]]),
                  inc=(wl, 16))
            c.await_guards("sync", "bd" + str(qb))
            c.dma("sync", t["bd_sl"][qb][:, :],
                  bass.AP(io["BD64_hbm"], (tau // 4) * 512,
                          [[8192, 128], [1, 512]]),
                  inc=(wl, 16))
            c.w("tensor", wl)
        if tau - 2 in silu_ct:
            c.w("tensor", "act", silu_ct[tau - 2])
            c.w("tensor", "dve", ymul_ct[tau - 2])
        else:
            c.w("tensor", "act")
            c.w("tensor", "dve")
        for kt in range(NKT):
            zs = t["z"][:, kt * TC:(kt + 1) * TC]
            c.mm(t["psA"][pp][:, :],
                 t["wu_sl"][qb][:, kt * 512 + e4 * 128:kt * 512 + (e4 + 1) * 128],
                 zs, start=(kt == 0), stop=(kt == NKT - 1))
            c.mm(t["psB"][pp][:, :],
                 t["wg_sl"][qb][:, kt * 512 + e4 * 128:kt * 512 + (e4 + 1) * 128],
                 zs, start=(kt == 0), stop=(kt == NKT - 1),
                 inc=("pe", 1) if kt == NKT - 1 else None)
        upgate_ct[tau] = c.counts["pe"]
        if e4 == 3:
            c.guard("wu" + str(qb), "pe")
        if tau - 1 in ymul_ct:
            emit_bd64(tau - 1)
        c.w("scalar", "pe", upgate_ct[tau])
        if tau - 2 in ymul_ct:
            c.w("scalar", "dve", ymul_ct[tau - 2])
        c.act(t["gsil"][pp][:, :], t["psB"][pp][:, :], AF.Silu, inc=("act", 1))
        silu_ct[tau] = c.counts["act"]
        c.w("vector", "act", silu_ct[tau])
        if tau - 2 in bd64_ct:
            c.w("vector", "pe", bd64_ct[tau - 2])
        c.tt(t["y"][pp][:, :], t["psA"][pp][:, :], t["gsil"][pp][:, :],
             ALU.mult, inc=("dve", 1))
        ymul_ct[tau] = c.counts["dve"]
    emit_bd64(NTAU - 1)

    for half in range(2):
        for t2 in range(NTAU):
            bi = t2 % 2
            wf = ("wfa", "wfb", "wfc", "wfd")[t2 % 4]
            c.await_guards("sync", "wf" + str(bi))
            c.dma("sync", t["wf_sl"][bi][:, :],
                  bass.AP(io["WF_hbm"], t2 * 262144 + half * 1024,
                          [[2048, 128], [1, 1024]]),
                  inc=(wf, 16))
            c.w("tensor", wf)
            if t2 == 0:
                c.w("tensor", "dve")
                c.w("tensor", "act")
            us = t["u"][:, t2 * TC:(t2 + 1) * TC]
            for cc in range(8):
                c.mm(t["ps8"][cc][:, :],
                     t["wf_sl"][bi][:, cc * 128:(cc + 1) * 128], us,
                     start=(t2 == 0), stop=(t2 == NTAU - 1),
                     inc=("pe", 1) if cc == 7 else None)
            c.guard("wf" + str(bi), "pe")
        c.w("gpsimd", "pe")
        c.w("vector", "pe")
        for cc in range(8):
            co = half * 8 + cc
            bi2 = cc % 2
            rl = "rla" if bi2 == 0 else "rlb"
            st_ = "sta" if bi2 == 0 else "stb"
            c.await_guards("gpsimd", "rbuf" + str(bi2))
            c.dma("gpsimd", t["rbuf"][bi2][:, :],
                  bass.AP(io[h_in], co * 128 * T + row_b, [[T, 128], [1, TC]]),
                  inc=(rl, 16))
            c.w("vector", rl)
            c.await_guards("vector", "hnx" + str(bi2))
            c.tt(t["hnx"][bi2][:, :], t["ps8"][cc][:, :], t["rbuf"][bi2][:, :],
                 ALU.add, inc=("dve", 1))
            c.guard("rbuf" + str(bi2), "dve")
            c.w("gpsimd", "dve")
            c.dma("gpsimd",
                  bass.AP(io[h_out], co * 128 * T + row_b, [[T, 128], [1, TC]]),
                  t["hnx"][bi2][:, :], inc=(st_, 16))
            c.guard("hnx" + str(bi2), st_)



def declare_ffn_io(nc, T, debug_w=False):
    io = {}
    io["h_in"] = nc.dram_tensor("h_in", [D, T], F32, kind="ExternalInput")
    io["h_out"] = nc.dram_tensor("h_out", [D, T], F32, kind="ExternalOutput")
    for nm, shp in (("Wu_st", [16, 128, INTER]), ("Wg_st", [16, 128, INTER]),
                    ("Wd_st", [64, 128, D])):
        io[nm] = nc.dram_tensor(nm, shp, F32, kind="ExternalInput")
    for nm, n in (("ws_up", INTER), ("ws_gate", INTER), ("ws_dn", D)):
        io[nm] = nc.dram_tensor(nm, [1, n], F32, kind="ExternalInput")
    io["su_ug_cols"] = nc.dram_tensor("su_ug_cols", [128, 16], F32,
                                      kind="ExternalInput")
    io["su_dp_cols"] = nc.dram_tensor("su_dp_cols", [128, 64], F32,
                                      kind="ExternalInput")
    io["H128c_i"] = nc.dram_tensor("H128c_i", [128, 128], BF16,
                                   kind="ExternalInput")
    io["BD16p_i"] = nc.dram_tensor("BD16p_i", [128, 128], F32,
                                   kind="ExternalInput")
    io["BD64p_i"] = nc.dram_tensor("BD64p_i", [128, 128], F32,
                                   kind="ExternalInput")
    kw = {"kind": "ExternalOutput"} if debug_w else {}
    io["WU_hbm"] = nc.dram_tensor("WU_hbm", [NTAU * 16 * 128 * 128], BF16, **kw)
    io["WG_hbm"] = nc.dram_tensor("WG_hbm", [NTAU * 16 * 128 * 128], BF16, **kw)
    io["WF_hbm"] = nc.dram_tensor("WF_hbm", [16 * NTAU * 128 * 128], BF16, **kw)
    io["BD64_hbm"] = nc.dram_tensor("BD64_hbm", [128 * 8192], BF16)
    return io


def build_ffn_program(T, debug_w=False, debug_mid=False):
    nc = bass.Bass("TRN2", target_bir_lowering=False)
    stk = ExitStack()
    io = declare_ffn_io(nc, T, debug_w)
    sems = {n: stk.enter_context(nc.semaphore(n)) for n in SEM_NAMES}
    t = alloc_common(nc, stk)
    alloc_psum(nc, stk, t)
    b = Builder(nc, sems)

    fold_stk = ExitStack()
    alloc_fold(nc, fold_stk, t)

    def prog_fold(c):
        emit_load_consts(c, io, t)
        emit_weight_fold(c, io, t)

    b.block(prog_fold)
    fold_stk.close()

    alloc_ffn(nc, stk, t)

    dbg = None
    if debug_mid:
        dbg = {
            "xn": nc.dram_tensor("dbg_xn", [128, NKT * TC], BF16,
                                 kind="ExternalOutput"),
            "z": nc.dram_tensor("dbg_z", [128, NKT * TC], BF16,
                                kind="ExternalOutput"),
            "u": nc.dram_tensor("dbg_u", [128, NTAU * TC], BF16,
                                kind="ExternalOutput"),
        }

    def prog_main(c):
        c.barrier()
        emit_bd_tiles(c, t)
        for tc in range(T // TC):
            emit_ffn_chunk(c, io, t, T, tc * TC, "h_in", "h_out",
                           dbg=dbg if tc == 0 else None)
        c.w("gpsimd", "sta")
        c.w("gpsimd", "stb")
        c.w("gpsimd", "sto")

    b.block(prog_main)
    stk.close()
    return nc



def stage_ffn_inputs(ins):
    H16 = _hadamard(16)
    H64 = _hadamard(64)
    H128 = _hadamard(128)
    W_up = np.asarray(ins["L0_ffn_W_up"], dtype=np.float32)
    W_gate = np.asarray(ins["L0_ffn_W_gate"], dtype=np.float32)
    W_down = np.asarray(ins["L0_ffn_W_down"], dtype=np.float32)
    ws_ug = np.asarray(ins["L0_ffn_Wscale_ug"], dtype=np.float32)
    return {
        "Wu_st": np.ascontiguousarray(
            W_up.reshape(INTER, 128, 16).transpose(2, 1, 0)),
        "Wg_st": np.ascontiguousarray(
            W_gate.reshape(INTER, 128, 16).transpose(2, 1, 0)),
        "Wd_st": np.ascontiguousarray(
            W_down.reshape(D, 128, 64).transpose(2, 1, 0)),
        "ws_up": np.ascontiguousarray(ws_ug[:INTER].reshape(1, INTER)),
        "ws_gate": np.ascontiguousarray(ws_ug[INTER:].reshape(1, INTER)),
        "ws_dn": np.ascontiguousarray(
            np.asarray(ins["L0_ffn_Wscale_dp"], np.float32).reshape(1, D)),
        "su_ug_cols": np.ascontiguousarray(
            np.asarray(ins["L0_ffn_SU_ug"], np.float32).reshape(16, 128).T),
        "su_dp_cols": np.ascontiguousarray(
            np.asarray(ins["L0_ffn_SU_dp"], np.float32).reshape(64, 128).T),
        "H128c_i": H128.astype(NPBF16),
        "BD16p_i": np.ascontiguousarray(
            np.kron(np.eye(8, dtype=np.float32), H16) * C16),
        "BD64p_i": np.ascontiguousarray(
            np.kron(np.eye(2, dtype=np.float32), H64) * C64),
    }



ATTN_SEMS = SEM_NAMES + ("vla", "vlb", "rpa", "rpb")


def alloc_attn_common(nc, stk):
    t = {}

    def sb(name, shape, dt):
        t[name] = stk.enter_context(nc.sbuf_tensor(name, shape, dt))
        return t[name]

    sb("ones128f", [128, 1], F32)
    sb("ones1f", [1, 128], F32)
    sb("ones128b", [128, 1], BF16)
    sb("kvals", [128, 8], F32)
    sb("qn_ws", [64, 2], F32)
    sb("kn_ws", [64, 2], F32)
    sb("qn_wn", [64, 2], F32)
    sb("kn_wn", [64, 2], F32)
    sb("Ksb", [128, 4 * 8192], BF16)
    sb("q_n", [128, 16 * 2048], BF16)
    return t


def alloc_attn_prep(nc, stk, t, sfx=""):
    def sb(name, shape, dt):
        t[name] = stk.enter_context(nc.sbuf_tensor(name + sfx, shape, dt))
        return t[name]
    sb("a_sb", [128, 16 * 512], BF16)
    sb("stat", [1, 4 * TC], F32)
    sb("rbc", [128, TC], F32)
    sb("mrbc", [128, TC], F32)
    t["hbuf"] = [sb("hbuf0", [128, TC], F32), sb("hbuf1", [128, TC], F32)]
    t["hsq"] = [sb("hsq0", [128, TC], F32), sb("hsq1", [128, TC], F32)]
    t["wsl"] = [sb("wsl0", [128, 16 * 128], BF16),
                sb("wsl1", [128, 16 * 128], BF16)]
    t["wvb"] = [sb("wvb0", [128, 512], BF16), sb("wvb1", [128, 512], BF16)]
    t["cosb"] = sb("cosb", [64, 512], F32)
    t["sinb"] = sb("sinb", [64, 512], F32)
    sb("kn0a", [64, 512], F32)
    sb("kn0b", [64, 512], F32)
    t["rp"] = [sb("rp1", [64, 512], F32), sb("rp2", [64, 512], F32),
               sb("rp3", [64, 512], F32), sb("rp4", [64, 512], F32)]
    sb("rrow", [1, 512], F32)
    sb("vsb", [128, 512], BF16)


def alloc_attn_core(nc, stk, t, sfx=""):
    def sb(name, shape, dt):
        t[name] = stk.enter_context(nc.sbuf_tensor(name + sfx, shape, dt))
        return t[name]
    sb("Vsl", [128, 64 * 128], BF16)
    sb("negI", [128, 128], BF16)
    sb("bo", [128, 16], F32)
    sb("Rm", [128, 16 * 512], BF16)
    sb("Tsb", [128, 1024], F32)
    t["Psb"] = [sb("Psb0", [128, 1024], BF16), sb("Psb1", [128, 1024], BF16)]
    t["Stb"] = [stk.enter_context(nc.psum_tensor("Stb0" + sfx, [128, 1024], F32)),
                stk.enter_context(nc.psum_tensor("Stb1" + sfx, [128, 1024], F32))]
    pb0 = stk.enter_context(nc.psum_tensor("pcb0" + sfx, [128, 512], F32))
    pb1 = stk.enter_context(nc.psum_tensor("pcb1" + sfx, [128, 512], F32))
    pc0 = stk.enter_context(nc.psum_tensor("pcc0" + sfx, [128, 512], F32))
    pd0 = stk.enter_context(nc.psum_tensor("pcd0" + sfx, [128, 512], F32))
    t["psB"] = [pb0, pb1]
    t["psC"] = [pc0, pc0]
    t["psD"] = [pd0, pd0]
    sb("y_m", [128, 16 * 512], BF16)
    sb("lrow", [1, 512], F32)
    sb("rbcs", [128, 512], F32)
    t["wsl"] = [sb("wslA", [128, 16 * 128], BF16),
                sb("wslB", [128, 16 * 128], BF16)]
    t["h1t"] = [sb("h1t0", [128, 512], F32), sb("h1t1", [128, 512], F32)]
    t["h2t"] = [sb("h2t0", [128, 512], F32), sb("h2t1", [128, 512], F32)]


def emit_attn_consts(c, io, t, shared=True):
    c.dma("gpsimd", t["qn_ws"][:, :], io["qn_w"].ap(), inc=("pre", 16))
    c.dma("gpsimd", t["kn_ws"][:, :], io["kn_w"].ap(), inc=("pre", 16))
    if shared:
        c.memset(t["ones128f"][:, :], 1.0)
        c.memset(t["ones1f"][:1, :], 1.0)
        c.memset(t["ones128b"][:, :], 1.0)
        c.memset(t["kvals"][:, 0:1], LN_EPS)
        c.memset(t["kvals"][:, 1:2], RMS_EPS)
        c.memset(t["kvals"][:, 2:3], 50.0)
        c.memset(t["kvals"][:, 3:4], ISQD / 50.0)
        c.memset(t["kvals"][:, 4:5], 1.0 / 128.0, inc=("dve", 1))
    else:
        c._inc(c.eng.engine_nop() if c.kind == "vector" else None, ("dve", 1))
    c.w("vector", "pre")
    c.w("vector", "dve")
    c.ts(t["qn_wn"][:, :], t["qn_ws"][:, :], -1.0, ALU.mult)
    c.ts(t["kn_wn"][:, :], t["kn_ws"][:, :], -1.0, ALU.mult, inc=("dve", 1))


def emit_norm_rope(c, t, ps_src, w, wn, cos_ap, sin_ap, out_lo, out_hi):
    rp = t["rp"]
    c.act(t["hsq"][0][:, :], ps_src, AF.Square, inc=("act", 1))
    c.w("tensor", "act")
    c.mm(t["psB"][0][:1, :], t["ones128f"][:, :], t["hsq"][0][:, :],
         True, True, inc=("pe", 1))
    c.w("vector", "pe")
    c.ts(t["rrow"][:1, :], t["psB"][0][:1, :], 1.0 / 128.0, ALU.mult,
         inc=("dve", 1))
    c.w("scalar", "dve")
    c.act(t["rrow"][:1, :], t["rrow"][:1, :], AF.Sqrt, inc=("act", 1))
    c.w("vector", "act")
    c.ts(t["rrow"][:1, :], t["rrow"][:1, :], RMS_EPS, ALU.add, inc=("dve", 1))
    c.w("vector", "dve")
    c.recip(t["rrow"][:1, :], t["rrow"][:1, :], inc=("dve", 1))
    c.w("tensor", "dve")
    c.mm(t["psB"][1][:, :], t["ones1f"][:1, :], t["rrow"][:1, :], True, True,
         inc=("pe", 1))
    c.w("vector", "pe")
    c.vcopy(t["rbc"][:, :512], t["psB"][1][:, :], inc=("dve", 1))
    c.w("vector", "dve")
    c.tt(t["kn0a"][:, :], ps_src[0:64, :], t["rbc"][0:64, :512], ALU.mult,
         inc=("dve", 1))
    c.tt(t["kn0b"][:, :], ps_src[64:128, :], t["rbc"][64:128, :512], ALU.mult,
         inc=("dve", 1))
    c.w("vector", "dve")
    k1 = t["kn0a"][:, :]
    k2 = t["kn0b"][:, :]
    c.tt(rp[0][:, :], k1, cos_ap, ALU.mult, inc=("dve", 1))
    c.tt(rp[1][:, :], k2, sin_ap, ALU.mult, inc=("dve", 1))
    c.w("vector", "dve")
    c.ts(rp[2][:, :], rp[1][:, :], w[:, 1:2], ALU.mult, inc=("dve", 1))
    c.w("vector", "dve")
    c.stt(out_lo, rp[0][:, :], w[:, 0:1], rp[2][:, :], ALU.mult, ALU.add,
          inc=("dve", 1))
    c.tt(rp[1][:, :], k2, cos_ap, ALU.mult, inc=("dve", 1))
    c.tt(rp[3][:, :], k1, sin_ap, ALU.mult, inc=("dve", 1))
    c.w("vector", "dve")
    c.ts(rp[2][:, :], rp[1][:, :], w[:, 1:2], ALU.mult, inc=("dve", 1))
    c.w("vector", "dve")
    c.stt(out_hi, rp[3][:, :], wn[:, 0:1], rp[2][:, :], ALU.mult, ALU.add,
          inc=("dve", 1))


def emit_attn_prep(c, io, t):

    for g in range(16):
        def h_ap(kt, g=g):
            return bass.AP(io["h_full"], kt * 128 * T_FULL_B + g * 512,
                           [[T_FULL_B, 128], [1, TC]])
        emit_ln(c, t, h_ap,
                lambda kt: t["a_sb"][:, kt * 512:(kt + 1) * 512])
        rp_s = "rpa" if g % 2 == 0 else "rpb"
        c.await_guards("gpsimd", "ropeb")
        c.dma("gpsimd", t["cosb"][:, :],
              bass.AP(io["cosk_i"], g * 512, [[8192, 64], [1, 512]]),
              inc=(rp_s, 16))
        c.dma("gpsimd", t["sinb"][:, :],
              bass.AP(io["sink_i"], g * 512, [[8192, 64], [1, 512]]),
              inc=(rp_s, 16))
        for dt_ in range(4):
            wl = "qla" if dt_ % 2 == 0 else "qlb"
            c.await_guards("gpsimd", "wsl" + str(dt_ % 2))
            c.dma("gpsimd", t["wsl"][dt_ % 2][:, :],
                  bass.AP(io["Wk_T"], dt_ * 262144,
                          [[2048, 128], [1, 2048]]),
                  inc=(wl, 16))
            c.w("tensor", wl)
            c.w("tensor", "act")
            c.w("tensor", "dve")
            for kt in range(NKT):
                c.mm(t["psA"][dt_ % 2][:, :],
                     t["wsl"][dt_ % 2][:, kt * 128:(kt + 1) * 128],
                     t["a_sb"][:, kt * 512:(kt + 1) * 512],
                     start=(kt == 0), stop=(kt == NKT - 1),
                     inc=("pe", 1) if kt == NKT - 1 else None)
            c.guard("wsl" + str(dt_ % 2), "pe")
            c.w("scalar", "pe")
            c.w("scalar", "dve")
            c.w("vector", rp_s)
            emit_norm_rope(
                c, t, t["psA"][dt_ % 2][:, :], t["kn_ws"], t["kn_wn"],
                t["cosb"][:, :], t["sinb"][:, :],
                t["Ksb"][0:64, dt_ * 8192 + g * 512:dt_ * 8192 + (g + 1) * 512],
                t["Ksb"][64:128, dt_ * 8192 + g * 512:dt_ * 8192 + (g + 1) * 512])
        c.guard("ropeb", "dve")
        for kt in range(NKT):
            wv = "vla" if kt % 2 == 0 else "vlb"
            c.await_guards("gpsimd", "wvb" + str(kt % 2))
            c.dma("gpsimd", t["wvb"][kt % 2][:, :],
                  bass.AP(io["Wv_T"], kt * 128 * 512, [[512, 128], [1, 512]]),
                  inc=(wv, 16))
            c.w("tensor", wv)
            if kt == 0:
                c.w("tensor", "act")
            for ts_ in range(4):
                c.mm(t["psC"][ts_ % 2][:, :] if ts_ < 2 else t["psD"][ts_ % 2][:, :],
                     t["a_sb"][:, kt * 512 + ts_ * 128:kt * 512 + (ts_ + 1) * 128],
                     t["wvb"][kt % 2][:, :],
                     start=(kt == 0), stop=(kt == NKT - 1),
                     inc=("pe", 1) if ts_ == 3 else None)
            c.guard("wvb" + str(kt % 2), "pe")
        c.w("scalar", "pe")
        for ts_ in range(4):
            ps = t["psC"][ts_ % 2] if ts_ < 2 else t["psD"][ts_ % 2]
            c.act(t["vsb"][:, :], ps[:, :], AF.Copy, inc=("act", 1))
            c.w("gpsimd", "act")
            c.dma("gpsimd",
                  bass.AP(io["v_hbm"], (g * 4 + ts_) * 16384,
                          [[128, 128], [1048576, 4], [1, 128]]),
                  t["vsb"][:, :], inc=("sta", 16))
            c.guard("vsb", "sta")
            c.w("scalar", "sta")

    for m in range(4):
        def h_ap(kt, m=m):
            return io["h_own_ap"](kt, m)
        emit_ln(c, t, h_ap,
                lambda kt: t["a_sb"][:, kt * 512:(kt + 1) * 512])
        rp_s = "rpa" if m % 2 == 0 else "rpb"
        c.await_guards("gpsimd", "ropeb")
        c.dma("gpsimd", t["cosb"][:, :],
              bass.AP(io["cosq_i"], m * 512, [[2048, 64], [1, 512]]),
              inc=(rp_s, 16))
        c.dma("gpsimd", t["sinb"][:, :],
              bass.AP(io["sinq_i"], m * 512, [[2048, 64], [1, 512]]),
              inc=(rp_s, 16))
        c.w("vector", rp_s)
        for h in range(NH):
            wl = "qla" if h % 2 == 0 else "qlb"
            c.await_guards("gpsimd", "wsl" + str(h % 2))
            c.dma("gpsimd", t["wsl"][h % 2][:, :],
                  bass.AP(io["Wq_T"], h * 262144,
                          [[2048, 128], [1, 2048]]),
                  inc=(wl, 16))
            c.w("tensor", wl)
            c.w("tensor", "act")
            c.w("tensor", "dve")
            for kt in range(NKT):
                c.mm(t["psA"][h % 2][:, :],
                     t["wsl"][h % 2][:, kt * 128:(kt + 1) * 128],
                     t["a_sb"][:, kt * 512:(kt + 1) * 512],
                     start=(kt == 0), stop=(kt == NKT - 1),
                     inc=("pe", 1) if kt == NKT - 1 else None)
            c.guard("wsl" + str(h % 2), "pe")
            c.w("scalar", "pe")
            c.w("scalar", "dve")
            emit_norm_rope(
                c, t, t["psA"][h % 2][:, :], t["qn_ws"], t["qn_wn"],
                t["cosb"][:, :], t["sinb"][:, :],
                t["q_n"][0:64, h * 2048 + m * 512:h * 2048 + (m + 1) * 512],
                t["q_n"][64:128, h * 2048 + m * 512:h * 2048 + (m + 1) * 512])
        c.guard("ropeb", "dve")


def emit_attn_core(c, io, t):
    for m in range(4):
        nb = (4 * m + 4) * 4
        for h in range(NH):
            gg = h // 4
            if h % 4 == 0:
                c.await_guards("gpsimd", "Vsl")
                c.dma("gpsimd", t["Vsl"][:, :nb * 128],
                      bass.AP(io["v_hbm"], gg * 1048576,
                              [[128, 128], [16384, nb], [1, 128]]),
                      inc=("vla", 16))
                c.w("tensor", "vla")
            qs = t["q_n"][:, h * 2048 + m * 512:h * 2048 + (m + 1) * 512]
            c.w("tensor", "dve")
            c.w("tensor", "act")
            tanh_ct = {}
            pv_ct = {}
            for pr in range(nb // 2):
                pp = pr % 2
                if pr >= 2:
                    c.w("tensor", "act", tanh_ct[pr - 2])
                for half_ in range(2):
                    kb = 2 * pr + half_
                    trailing = kb >= nb - 16
                    stp = t["Stb"][pp][:, half_ * 512:(half_ + 1) * 512]
                    c.mm(stp,
                         t["Ksb"][:, gg * 8192 + kb * 128:
                                  gg * 8192 + (kb + 1) * 128],
                         qs, start=True, stop=not trailing,
                         inc=None if trailing else ("pe", 1))
                    if trailing:
                        i_ = kb - (nb - 16)
                        c.mm(stp, t["negI"][:, :],
                             t["Rm"][:, i_ * 512:(i_ + 1) * 512],
                             start=False, stop=True, inc=("pe", 1))
                c.w("scalar", "pe")
                c.act(t["Tsb"][:, :], t["Stb"][pp][:, :], AF.Tanh,
                      scale=t["kvals"][:, 3:4], inc=("act", 1))
                tanh_ct[pr] = c.counts["act"]
                if pr >= 2:
                    c.w("scalar", "pe", pv_ct[pr - 2])
                c.w("scalar", "act")
                c.act(t["Psb"][pp][:, :], t["Tsb"][:, :], AF.Exp,
                      scale=t["kvals"][:, 2:3], inc=("act", 1))
                c.w("tensor", "act")
                for half_ in range(2):
                    kb = 2 * pr + half_
                    pb = t["Psb"][pp][:, half_ * 512:(half_ + 1) * 512]
                    c.mm(t["psB"][0][:1, :], t["ones128b"][:, :], pb,
                         start=(kb == 0), stop=(kb == nb - 1), inc=None)
                    c.mm(t["psB"][1][:, :],
                         t["Vsl"][:, kb * 128:(kb + 1) * 128], pb,
                         start=(kb == 0), stop=(kb == nb - 1),
                         inc=("pe", 1))
                pv_ct[pr] = c.counts["pe"]
            if h % 4 == 3:
                c.guard("Vsl", "pe")
            c.w("vector", "pe")
            c.vcopy(t["lrow"][:1, :], t["psB"][0][:1, :], inc=("dve", 1))
            c.w("vector", "dve")
            c.recip(t["lrow"][:1, :], t["lrow"][:1, :], inc=("dve", 1))
            c.w("tensor", "dve")
            c.mm(t["psC"][0][:, :], t["ones1f"][:1, :], t["lrow"][:1, :],
                 True, True, inc=("pe", 1))
            c.w("vector", "pe")
            c.vcopy(t["rbcs"][:, :], t["psC"][0][:, :], inc=("dve", 1))
            c.w("vector", "dve")
            c.tt(t["y_m"][:, h * 512:(h + 1) * 512], t["psB"][1][:, :],
                 t["rbcs"][:, :], ALU.mult, inc=("dve", 1))
        for co in range(16):
            bi = co % 2
            wl = "qla" if bi == 0 else "qlb"
            st_ = "sta" if bi == 0 else "stb"
            rl = "rla" if bi == 0 else "rlb"
            c.await_guards("gpsimd", "wslo" + str(bi))
            c.dma("gpsimd", t["wsl"][bi][:, :],
                  bass.AP(io["Wo_T"], co * 262144,
                          [[2048, 128], [1, 2048]]),
                  inc=(wl, 16))
            c.await_guards("gpsimd", "h1t" + str(bi))
            c.dma("gpsimd", t["h1t"][bi][:, :], io["h_own_ap"](co, m),
                  inc=(rl, 16))
            c.w("tensor", wl)
            c.w("tensor", "dve")
            for h in range(NH):
                c.mm(t["psD"][bi][:, :],
                     t["wsl"][bi][:, h * 128:(h + 1) * 128],
                     t["y_m"][:, h * 512:(h + 1) * 512],
                     start=(h == 0), stop=(h == NH - 1),
                     inc=("pe", 1) if h == NH - 1 else None)
            c.guard("wslo" + str(bi), "pe")
            c.w("vector", "pe")
            c.w("vector", rl)
            c.await_guards("vector", "h2t" + str(bi))
            c.stt(t["h2t"][bi][:, :], t["psD"][bi][:, :], t["bo"][:, co:co + 1],
                  t["h1t"][bi][:, :], ALU.add, ALU.add, inc=("dve", 1))
            c.guard("h1t" + str(bi), "dve")
            c.w("gpsimd", "dve")
            c.dma("gpsimd",
                  bass.AP(io["h2_own"], co * 128 * 2048 + m * 512,
                          [[2048, 128], [1, TC]]),
                  t["h2t"][bi][:, :], inc=(st_, 16))
            c.guard("h2t" + str(bi), st_)


T_FULL_B = 8192


def build_attn_program():
    nc = bass.Bass("TRN2", target_bir_lowering=False)
    stk = ExitStack()
    io = {}
    io["h_full"] = nc.dram_tensor("h_full", [D, T_FULL_B], F32,
                                  kind="ExternalInput")
    io["h_own"] = nc.dram_tensor("h_own", [D, 2048], F32, kind="ExternalInput")
    io["h2_own"] = nc.dram_tensor("h2_own", [D, 2048], F32,
                                  kind="ExternalOutput")
    for nm, shp in (("Wq_T", [16, 128, 16, 128]), ("Wk_T", [4, 128, 16, 128]),
                    ("Wv_T", [D, 512]), ("Wo_T", [16, 128, 16, 128])):
        io[nm] = nc.dram_tensor(nm, shp, F32, kind="ExternalInput")
    io["bo_i"] = nc.dram_tensor("bo_i", [128, 16], F32, kind="ExternalInput")
    io["qn_w"] = nc.dram_tensor("qn_w", [64, 2], F32, kind="ExternalInput")
    io["kn_w"] = nc.dram_tensor("kn_w", [64, 2], F32, kind="ExternalInput")
    io["negI_i"] = nc.dram_tensor("negI_i", [128, 128], BF16,
                                  kind="ExternalInput")
    io["Rm_i"] = nc.dram_tensor("Rm_i", [128, 16 * 512], BF16,
                                kind="ExternalInput")
    io["cosk_i"] = nc.dram_tensor("cosk_i", [64, 8192], F32,
                                  kind="ExternalInput")
    io["sink_i"] = nc.dram_tensor("sink_i", [64, 8192], F32,
                                  kind="ExternalInput")
    io["cosq_i"] = nc.dram_tensor("cosq_i", [64, 2048], F32,
                                  kind="ExternalInput")
    io["sinq_i"] = nc.dram_tensor("sinq_i", [64, 2048], F32,
                                  kind="ExternalInput")
    io["v_hbm"] = nc.dram_tensor("v_hbm", [4 * 64 * 128 * 128], BF16)
    io["h_own_ap"] = lambda kt, m: bass.AP(
        io["h_own"], kt * 128 * 2048 + m * 512, [[2048, 128], [1, TC]])

    sems = {n: stk.enter_context(nc.semaphore(n)) for n in ATTN_SEMS}
    t = alloc_attn_common(nc, stk)
    b = Builder(nc, sems)

    prep_stk = ExitStack()
    alloc_psum(nc, prep_stk, t, sfx="P")
    alloc_attn_prep(nc, prep_stk, t)

    def prog_prep(c):
        emit_attn_consts(c, io, t)
        c.w("tensor", "pre")
        c.w("scalar", "pre")
        c.w("gpsimd", "pre")
        emit_attn_prep(c, io, t)

    b.block(prog_prep)
    prep_stk.close()
    for k in ("a_sb", "stat", "rbc", "mrbc", "hbuf", "hsq", "wsl", "wvb",
              "cosb", "sinb", "cosq", "sinq", "kn0", "rp", "rrow", "vsb"):
        t.pop(k, None)
    core_stk = ExitStack()
    alloc_attn_core(nc, core_stk, t)

    def prog_core(c):
        c.barrier()
        c.dma("gpsimd", t["negI"][:, :], io["negI_i"].ap(), inc=("pre", 16))
        c.dma("gpsimd", t["bo"][:, :], io["bo_i"].ap(), inc=("pre", 16))
        c.dma("gpsimd", t["Rm"][:, :], io["Rm_i"].ap(), inc=("pre", 16))
        c.w("tensor", "pre")
        c.w("vector", "pre")
        emit_attn_core(c, io, t)
        c.w("gpsimd", "sta")
        c.w("gpsimd", "stb")

    b.block(prog_core)
    core_stk.close()
    stk.close()
    return nc


def stage_attn_inputs(ins, j):
    Wq = np.asarray(ins["L0_attn_Wq"], np.float32)
    Wk = np.asarray(ins["L0_attn_Wk"], np.float32)
    Wv = np.asarray(ins["L0_attn_Wv"], np.float32)
    Wo = np.asarray(ins["L0_attn_Wo"], np.float32)
    inv_freq = 1.0 / 10000.0 ** (np.arange(0, HD, 2, dtype=np.float32) / HD)
    pos_full = np.arange(T_FULL_B, dtype=np.float32)
    own_tok = np.concatenate(
        [np.arange((4 * mm + j) * 512, (4 * mm + j) * 512 + 512)
         for mm in range(4)]).astype(np.float32)
    frf = pos_full[:, None] * inv_freq[None, :]
    fro = own_tok[:, None] * inv_freq[None, :]
    Rm = np.zeros((128, 16 * 512), dtype=NPBF16)
    Kk = np.arange(128)[:, None]
    tq = np.arange(512)[None, :]
    for i in range(16):
        d_ = 128 * (i - 4 * j)
        Rm[:, i * 512:(i + 1) * 512] = (Kk - tq > -d_).astype(NPBF16)
    return {
        "Wq_T": np.ascontiguousarray(
            Wq.reshape(16, 128, 16, 128).transpose(0, 3, 2, 1)),
        "Wk_T": np.ascontiguousarray(
            Wk.reshape(4, 128, 16, 128).transpose(0, 3, 2, 1)),
        "Wv_T": np.ascontiguousarray(Wv.T),
        "Wo_T": np.ascontiguousarray(
            Wo.reshape(16, 128, 16, 128).transpose(0, 3, 2, 1)),
        "bo_i": np.ascontiguousarray(
            np.asarray(ins["L0_attn_bo"], np.float32).reshape(16, 128).T),
        "qn_w": np.ascontiguousarray(
            np.asarray(ins["L0_attn_q_norm_w"], np.float32).reshape(2, 64).T),
        "kn_w": np.ascontiguousarray(
            np.asarray(ins["L0_attn_k_norm_w"], np.float32).reshape(2, 64).T),
        "negI_i": (NEG_BIG * np.eye(128, dtype=np.float32)).astype(NPBF16),
        "Rm_i": Rm,
        "cosk_i": np.ascontiguousarray(np.cos(frf).T.astype(np.float32)),
        "sink_i": np.ascontiguousarray(np.sin(frf).T.astype(np.float32)),
        "cosq_i": np.ascontiguousarray(np.cos(fro).T.astype(np.float32)),
        "sinq_i": np.ascontiguousarray(np.sin(fro).T.astype(np.float32)),
    }


def own_token_index(j):
    return np.concatenate(
        [np.arange((4 * m + j) * 512, (4 * m + j + 1) * 512) for m in range(4)])



FUSED_SEMS = SEM_NAMES + ("vla", "vlb", "rpa", "rpb", "cpy")


def alloc_attn_extra(nc, stk, t):
    def sb(name, shape, dt):
        t[name] = stk.enter_context(nc.sbuf_tensor(name, shape, dt))
        return t[name]
    sb("qn_ws", [64, 2], F32)
    sb("kn_ws", [64, 2], F32)
    sb("qn_wn", [64, 2], F32)
    sb("kn_wn", [64, 2], F32)
    sb("Ksb", [128, 4 * 8192], BF16)
    sb("q_n", [128, 16 * 2048], BF16)


def build_fused_program():
    nc = bass.Bass("TRN2", target_bir_lowering=False)
    stk = ExitStack()
    io = {}
    io["x_fm"] = nc.dram_tensor("x_fm", [D, T_FULL_B], F32,
                                kind="ExternalInput")
    io["out_fm"] = nc.dram_tensor("out_fm", [D, 2048], F32,
                                  kind="ExternalOutput")
    io["own_off"] = nc.dram_tensor("own_off", [1, 4], I32,
                                   kind="ExternalInput")
    for nm, shp in (("Wu_st", [16, 128, INTER]), ("Wg_st", [16, 128, INTER]),
                    ("Wd_st", [64, 128, D])):
        io[nm] = nc.dram_tensor(nm, shp, F32, kind="ExternalInput")
    for nm, n in (("ws_up", INTER), ("ws_gate", INTER), ("ws_dn", D)):
        io[nm] = nc.dram_tensor(nm, [1, n], F32, kind="ExternalInput")
    io["su_ug_cols"] = nc.dram_tensor("su_ug_cols", [128, 16], F32,
                                      kind="ExternalInput")
    io["su_dp_cols"] = nc.dram_tensor("su_dp_cols", [128, 64], F32,
                                      kind="ExternalInput")
    io["H128c_i"] = nc.dram_tensor("H128c_i", [128, 128], BF16,
                                   kind="ExternalInput")
    io["BD16p_i"] = nc.dram_tensor("BD16p_i", [128, 128], F32,
                                   kind="ExternalInput")
    io["BD64p_i"] = nc.dram_tensor("BD64p_i", [128, 128], F32,
                                   kind="ExternalInput")
    for nm, shp in (("Wq_T", [16, 128, 16, 128]), ("Wk_T", [4, 128, 16, 128]),
                    ("Wv_T", [D, 512]), ("Wo_T", [16, 128, 16, 128])):
        io[nm] = nc.dram_tensor(nm, shp, F32, kind="ExternalInput")
    io["bo_i"] = nc.dram_tensor("bo_i", [128, 16], F32, kind="ExternalInput")
    io["qn_w"] = nc.dram_tensor("qn_w", [64, 2], F32, kind="ExternalInput")
    io["kn_w"] = nc.dram_tensor("kn_w", [64, 2], F32, kind="ExternalInput")
    io["negI_i"] = nc.dram_tensor("negI_i", [128, 128], BF16,
                                  kind="ExternalInput")
    io["Rm_i"] = nc.dram_tensor("Rm_i", [128, 16 * 512], BF16,
                                kind="ExternalInput")
    io["cosk_i"] = nc.dram_tensor("cosk_i", [64, 8192], F32,
                                  kind="ExternalInput")
    io["sink_i"] = nc.dram_tensor("sink_i", [64, 8192], F32,
                                  kind="ExternalInput")
    io["cosq_i"] = nc.dram_tensor("cosq_i", [64, 2048], F32,
                                  kind="ExternalInput")
    io["sinq_i"] = nc.dram_tensor("sinq_i", [64, 2048], F32,
                                  kind="ExternalInput")
    io["WU_hbm"] = nc.dram_tensor("WU_hbm", [NTAU * 16 * 128 * 128], BF16)
    io["WG_hbm"] = nc.dram_tensor("WG_hbm", [NTAU * 16 * 128 * 128], BF16)
    io["WF_hbm"] = nc.dram_tensor("WF_hbm", [16 * NTAU * 128 * 128], BF16)
    io["BD64_hbm"] = nc.dram_tensor("BD64_hbm", [128 * 8192], BF16)
    io["h1_full"] = nc.dram_tensor("h1_full", [D, T_FULL_B], F32)
    io["h1_own"] = nc.dram_tensor("h1_own", [D, 2048], F32)
    io["h2_own"] = nc.dram_tensor("h2_own", [D, 2048], F32)
    io["v_hbm"] = nc.dram_tensor("v_hbm", [4 * 64 * 128 * 128], BF16)
    io["h_full"] = io["h1_full"]
    io["h_own_ap"] = lambda kt, m: bass.AP(
        io["h1_own"], kt * 128 * 2048 + m * 512, [[2048, 128], [1, TC]])

    sems = {n: stk.enter_context(nc.semaphore(n)) for n in FUSED_SEMS}
    t = alloc_common(nc, stk)
    b = Builder(nc, sems)

    ps1_stk = ExitStack()
    alloc_psum(nc, ps1_stk, t, sfx="1")
    fold_stk = ExitStack()
    alloc_fold(nc, fold_stk, t)

    def prog_fold(c):
        emit_load_consts(c, io, t)
        emit_weight_fold(c, io, t)

    b.block(prog_fold)
    fold_stk.close()

    ffn_stk = ExitStack()
    alloc_ffn(nc, ffn_stk, t)

    def prog_ffn1(c):
        c.barrier()
        emit_bd_tiles(c, t)
        for tc in range(T_FULL_B // TC):
            emit_ffn_chunk(c, io, t, T_FULL_B, tc * TC, "x_fm", "h1_full")

    b.block(prog_ffn1)
    ffn_stk.close()
    ps1_stk.close()
    for k in ("u", "xn", "z", "y", "gsil", "stat", "rbc", "mrbc", "hbuf",
              "hsq", "wu_sl", "wg_sl", "wf_sl", "rbuf", "hnx",
              "psA", "psB", "psC", "psD", "ps8"):
        t.pop(k, None)

    offs_stk = ExitStack()
    t["offs"] = offs_stk.enter_context(nc.sbuf_tensor("offs", [1, 4], I32))

    def prog_copy(c):
        c.barrier()
        c.dma("gpsimd", t["offs"][:1, :], io["own_off"].ap(), inc=("pre", 16))
        c.w("gpsimd", "pre")
        if c.kind == "gpsimd":
            with c.eng.register("roff") as roff:
                for m in range(4):
                    c.eng.reg_load(roff, t["offs"][:1, m:m + 1])
                    i = c.eng.dma_start(
                        out=bass.AP(io["h1_own"], m * 512,
                                    [[2048, 2048], [1, 512]]),
                        in_=bass.AP(io["h1_full"], roff,
                                    [[T_FULL_B, 2048], [1, 512]]))
                    i.then_inc(c.sems["cpy"], 16)
        c.counts["cpy"] += 64
        c.w("gpsimd", "cpy")

    b.block(prog_copy)

    attn_stk = ExitStack()
    alloc_attn_extra(nc, attn_stk, t)
    prep_stk = ExitStack()
    alloc_psum(nc, prep_stk, t, sfx="P")
    alloc_attn_prep(nc, prep_stk, t, sfx="AP")

    def prog_prep(c):
        c.barrier()
        emit_attn_consts(c, io, t, shared=False)
        c.w("tensor", "pre")
        c.w("scalar", "pre")
        c.w("gpsimd", "pre")
        emit_attn_prep(c, io, t)

    b.block(prog_prep)
    prep_stk.close()
    for k in ("a_sb", "stat", "rbc", "mrbc", "hbuf", "hsq", "wsl", "wvb",
              "cosb", "sinb", "kn0a", "kn0b", "rp", "rrow", "vsb",
              "psA", "psB", "psC", "psD", "ps8"):
        t.pop(k, None)

    core_stk = ExitStack()
    alloc_attn_core(nc, core_stk, t, sfx="AC")

    def prog_core(c):
        c.barrier()
        c.dma("gpsimd", t["negI"][:, :], io["negI_i"].ap(), inc=("pre", 16))
        c.dma("gpsimd", t["bo"][:, :], io["bo_i"].ap(), inc=("pre", 16))
        c.dma("gpsimd", t["Rm"][:, :], io["Rm_i"].ap(), inc=("pre", 16))
        c.w("tensor", "pre")
        c.w("vector", "pre")
        emit_attn_core(c, io, t)

    b.block(prog_core)
    core_stk.close()
    attn_stk.close()
    for k in ("Vsl", "Tsb", "Psb", "Stb", "y_m", "lrow", "rbcs", "wsl", "h1t",
              "h2t", "negI", "bo", "Rm", "Ksb", "q_n", "qn_ws", "kn_ws",
              "qn_wn", "kn_wn", "psB", "psC", "psD"):
        t.pop(k, None)

    ffn2_stk = ExitStack()
    alloc_psum(nc, ffn2_stk, t, sfx="3")
    alloc_ffn(nc, ffn2_stk, t, sfx="F3")

    def prog_ffn3(c):
        c.barrier()
        emit_bd_tiles(c, t)
        for tc in range(2048 // TC):
            emit_ffn_chunk(c, io, t, 2048, tc * TC, "h2_own", "out_fm")
        c.w("gpsimd", "sta")
        c.w("gpsimd", "stb")

    b.block(prog_ffn3)
    ffn2_stk.close()
    offs_stk.close()
    stk.close()
    return nc


def kernel(**inputs):
    x = np.asarray(inputs["L0_attn_x"], np.float32)
    nc = build_fused_program()
    staged_f = stage_ffn_inputs(inputs)
    in_maps = []
    for c_ in range(8):
        b_, j_ = c_ // 4, c_ % 4
        m = dict(staged_f)
        m.update(stage_attn_inputs(inputs, j_))
        m["x_fm"] = np.ascontiguousarray(x[b_].T)
        m["own_off"] = np.array(
            [[(4 * mm + j_) * 512 for mm in range(4)]], np.int32)
        in_maps.append(m)
    res = run_bass_kernel_spmd(nc, in_maps, core_ids=list(range(8)))
    out = np.empty((2, T_FULL_B, D), np.float32)
    for c_ in range(8):
        b_, j_ = c_ // 4, c_ % 4
        idx = own_token_index(j_)
        out[b_][idx] = res.results[c_]["out_fm"].T
    return out



def kernel_unfused(**inputs):
    x = np.asarray(inputs["L0_attn_x"], np.float32)
    staged_f = stage_ffn_inputs(inputs)

    nc_f = build_ffn_program(2048)
    in_maps = []
    xf = x.reshape(-1, D)
    for c_ in range(8):
        m = dict(staged_f)
        m["h_in"] = np.ascontiguousarray(xf[c_ * 2048:(c_ + 1) * 2048].T)
        in_maps.append(m)
    res1 = run_bass_kernel_spmd(nc_f, in_maps, core_ids=list(range(8)))
    h1 = np.empty((2, T_FULL_B, D), np.float32)
    for c_ in range(8):
        h1.reshape(-1, D)[c_ * 2048:(c_ + 1) * 2048] = res1.results[c_]["h_out"].T

    nc_a = build_attn_program()
    in_maps = []
    for c_ in range(8):
        b_, j_ = c_ // 4, c_ % 4
        m = stage_attn_inputs(inputs, j_)
        m["h_full"] = np.ascontiguousarray(h1[b_].T)
        idx = own_token_index(j_)
        m["h_own"] = np.ascontiguousarray(h1[b_][idx].T)
        in_maps.append(m)
    res2 = run_bass_kernel_spmd(nc_a, in_maps, core_ids=list(range(8)))
    h2 = np.empty((2, T_FULL_B, D), np.float32)
    for c_ in range(8):
        b_, j_ = c_ // 4, c_ % 4
        idx = own_token_index(j_)
        h2[b_][idx] = res2.results[c_]["h2_own"].T

    in_maps = []
    h2f = h2.reshape(-1, D)
    for c_ in range(8):
        m = dict(staged_f)
        m["h_in"] = np.ascontiguousarray(h2f[c_ * 2048:(c_ + 1) * 2048].T)
        in_maps.append(m)
    res3 = run_bass_kernel_spmd(nc_f, in_maps, core_ids=list(range(8)))
    out = np.empty((2, T_FULL_B, D), np.float32)
    for c_ in range(8):
        out.reshape(-1, D)[c_ * 2048:(c_ + 1) * 2048] = \
            res3.results[c_]["h_out"].T
    return out



# version 06
# speedup vs baseline: 1.0234x; 1.0234x over previous
import math
from contextlib import ExitStack

import numpy as np
import ml_dtypes

import concourse.bass as bass
import concourse.mybir as mybir
from concourse.bass_utils import run_bass_kernel_spmd

F32 = mybir.dt.float32
BF16 = mybir.dt.bfloat16
I32 = mybir.dt.int32
AF = mybir.ActivationFunctionType
ALU = mybir.AluOpType

NPBF16 = ml_dtypes.bfloat16

D = 2048
INTER = 8192
NKT = 16
NTAU = 64
HD = 128
NH = 16
NKV = 4
C16 = 1.0 / (math.sqrt(2048.0) * 64.0)
C64 = 1.0 / (math.sqrt(8192.0) * 64.0)
LN_EPS = 1e-5
RMS_EPS = 1e-6
NEG_BIG = -1.0e5
ISQD = 1.0 / math.sqrt(128.0)
TC = 512


def _hadamard(n):
    h = np.array([[1.0]], dtype=np.float32)
    while h.shape[0] < n:
        h = np.block([[h, h], [h, -h]])
    return h.astype(np.float32)



ENGS = ("tensor", "scalar", "vector", "gpsimd", "sync")


class Ctx:

    def __init__(self, kind, sems):
        self.kind = kind
        self.eng = None
        self.sems = sems
        self.counts = {k: 0 for k in sems}
        self.guards = {}

    def _inc(self, instr, inc):
        if inc is None:
            return instr
        sem, n = inc
        self.counts[sem] += n
        if instr is not None:
            instr.then_inc(self.sems[sem], n)
        return instr

    def w(self, kind, sem, cnt=None):
        v = self.counts[sem] if cnt is None else cnt
        if self.kind == kind and v > 0:
            self.eng.wait_ge(self.sems[sem], v)

    def barrier(self):
        for sem, v in self.counts.items():
            if v > 0:
                self.eng.wait_ge(self.sems[sem], v)

    def guard(self, buf, sem):
        self.guards.setdefault(buf, {})[sem] = self.counts[sem]

    def await_guards(self, kind, buf):
        g = self.guards.get(buf)
        if not g:
            return
        if self.kind == kind:
            for sem, v in g.items():
                if v > 0:
                    self.eng.wait_ge(self.sems[sem], v)

    def mm(self, out, lhsT, rhs, start, stop, inc=None):
        i = self.eng.matmul(out, lhsT, rhs, start=start, stop=stop) \
            if self.kind == "tensor" else None
        return self._inc(i, inc)

    def act(self, out, in_, func, bias=0.0, scale=1.0, accum=None, inc=None):
        i = self.eng.activation(out, in_, func, bias=bias, scale=scale,
                                accum_out=accum) \
            if self.kind == "scalar" else None
        return self._inc(i, inc)

    def tt(self, out, a, b, op, inc=None):
        i = self.eng.tensor_tensor(out, a, b, op) \
            if self.kind == "vector" else None
        return self._inc(i, inc)

    def ts(self, out, a, s1, op0, s2=None, op1=ALU.bypass, inc=None):
        i = self.eng.tensor_scalar(out, a, s1, s2, op0, op1) \
            if self.kind == "vector" else None
        return self._inc(i, inc)

    def stt(self, out, in0, scalar, in1, op0, op1, inc=None):
        i = self.eng.scalar_tensor_tensor(out, in0, scalar, in1, op0, op1) \
            if self.kind == "vector" else None
        return self._inc(i, inc)

    def vcopy(self, out, in_, inc=None):
        i = self.eng.tensor_copy(out, in_) if self.kind == "vector" else None
        return self._inc(i, inc)

    def recip(self, out, in_, inc=None):
        i = self.eng.reciprocal(out, in_) if self.kind == "vector" else None
        return self._inc(i, inc)

    def memset(self, ap, v, inc=None):
        i = self.eng.memset(ap, v) if self.kind == "vector" else None
        return self._inc(i, inc)

    def dma(self, queue, out, in_, inc=None, slow=False):
        i = None
        if self.kind == queue:
            kw = {"allow_slow_non_contiguous": True} if slow else {}
            i = self.eng.dma_start(out=out, in_=in_, **kw)
        return self._inc(i, inc)


class Builder:

    def __init__(self, nc, sems):
        self.nc = nc
        self.ctxs = {k: Ctx(k, sems) for k in ENGS}

    def block(self, program):
        with self.nc.Block() as blk:
            for kind in ENGS:
                ctx = self.ctxs[kind]

                def mk(ctx):
                    def f(eng):
                        ctx.eng = eng
                        program(ctx)
                    return f
                getattr(blk, kind)(mk(ctx))



SEM_NAMES = ("pe", "act", "dve", "pre", "sv", "sto",
             "gla", "glb", "gsa", "gsb", "wla", "wlb", "wlc", "wld",
             "wfa", "wfb", "wfc", "wfd", "hla", "hlb", "rla", "rlb",
             "sta", "stb", "qla", "qlb")


def alloc_common(nc, stk):
    t = {}

    def sb(name, shape, dt):
        t[name] = stk.enter_context(nc.sbuf_tensor(name, shape, dt))
        return t[name]

    sb("H128c", [128, 128], BF16)
    sb("BD16p", [128, 128], F32)
    sb("BD64p", [128, 128], F32)
    sb("ones128f", [128, 1], F32)
    sb("ones1f", [1, 128], F32)
    sb("c64row", [1, 128], F32)
    sb("c32row", [1, 128], F32)
    sb("su_ug", [128, 16], F32)
    sb("su_dp", [128, 64], F32)
    sb("kvals", [128, 8], F32)
    sb("ones128b", [128, 1], BF16)
    return t


def alloc_psum(nc, stk, t, sfx=""):
    def ps(name, shape):
        t[name] = stk.enter_context(nc.psum_tensor(name + sfx, shape, F32))
        return t[name]
    t["psA"] = [ps("psA0", [128, 512]), ps("psA1", [128, 512])]
    t["psB"] = [ps("psB0", [128, 512]), ps("psB1", [128, 512])]
    t["psC"] = [ps("psC0", [128, 512]), ps("psC1", [128, 512])]
    t["psD"] = [ps("psD0", [128, 512]), ps("psD1", [128, 512])]
    t["ps8"] = t["psA"] + t["psB"] + t["psC"] + t["psD"]


def alloc_fold(nc, stk, t):
    def sb(name, shape, dt):
        t[name] = stk.enter_context(nc.sbuf_tensor(name, shape, dt))
        return t[name]
    sb("sbc", [128, INTER], F32)
    sb("bdt", [128, 128], BF16)
    t["win"] = [sb("win0", [128, INTER], BF16), sb("win1", [128, INTER], BF16)]
    t["wout"] = [sb("wout0", [128, INTER], BF16),
                 sb("wout1", [128, INTER], BF16)]
    sb("svbuf", [1, 512], F32)


def alloc_ffn(nc, stk, t, sfx=""):
    def sb(name, shape, dt):
        t[name] = stk.enter_context(nc.sbuf_tensor(name + sfx, shape, dt))
        return t[name]
    sb("BD16", [128, 16 * 128], BF16)
    sb("u", [128, NTAU * TC], BF16)
    t["xn"] = t["u"][:, :NKT * TC]
    sb("z", [128, NKT * TC], BF16)
    t["y"] = [sb("y0", [128, TC], BF16), sb("y1", [128, TC], BF16)]
    t["gsil"] = [sb("gsil0", [128, TC], F32), sb("gsil1", [128, TC], F32)]
    sb("stat", [1, 4 * TC], F32)
    sb("rbc", [128, TC], F32)
    sb("mrbc", [128, TC], F32)
    t["hbuf"] = [sb("hbuf0", [128, TC], F32), sb("hbuf1", [128, TC], F32)]
    t["hsq"] = [sb("hsq0", [128, TC], F32), sb("hsq1", [128, TC], F32)]
    t["wu_sl"] = [sb("wu_sl0", [128, 16 * 512], BF16),
                  sb("wu_sl1", [128, 16 * 512], BF16)]
    t["wg_sl"] = [sb("wg_sl0", [128, 16 * 512], BF16),
                  sb("wg_sl1", [128, 16 * 512], BF16)]
    t["bd_sl"] = [sb("bd_sl0", [128, 512], BF16),
                  sb("bd_sl1", [128, 512], BF16)]
    t["wf_sl"] = [sb("wf_sl0", [128, 1024], BF16),
                  sb("wf_sl1", [128, 1024], BF16)]
    t["rbuf"] = [sb("rbuf0", [128, 512], F32), sb("rbuf1", [128, 512], F32)]
    t["hnx"] = [sb("hnx0", [128, 512], F32), sb("hnx1", [128, 512], F32)]


def emit_load_consts(c, io, t):
    c.dma("gpsimd", t["H128c"][:, :], io["H128c_i"].ap(), inc=("pre", 16))
    c.dma("gpsimd", t["BD16p"][:, :], io["BD16p_i"].ap(), inc=("pre", 16))
    c.dma("gpsimd", t["BD64p"][:, :], io["BD64p_i"].ap(), inc=("pre", 16))
    c.dma("gpsimd", t["su_ug"][:, :], io["su_ug_cols"].ap(), inc=("pre", 16))
    c.dma("gpsimd", t["su_dp"][:, :], io["su_dp_cols"].ap(), inc=("pre", 16))
    c.memset(t["ones128f"][:, :], 1.0)
    c.memset(t["ones1f"][:1, :], 1.0)
    c.memset(t["c64row"][:1, :], 64.0)
    c.memset(t["kvals"][:, 0:1], LN_EPS)
    c.memset(t["kvals"][:, 1:2], RMS_EPS)
    c.memset(t["kvals"][:, 2:3], 50.0)
    c.memset(t["kvals"][:, 3:4], ISQD / 50.0)
    c.memset(t["kvals"][:, 4:5], 1.0 / 128.0)
    c.memset(t["ones128b"][:, :], 1.0)
    c.memset(t["c32row"][:1, :], 32.0, inc=("dve", 1))
    c.w("vector", "pre")


def emit_bd_tiles(c, t):
    c.w("vector", "pre")
    for k in range(16):
        c.ts(t["BD16"][:, k * 128:(k + 1) * 128], t["BD16p"][:, :],
             t["su_ug"][:, k:k + 1], ALU.mult,
             inc=("dve", 1) if k == 15 else None)


def emit_weight_fold(c, io, t):
    def fold(src_h, dst_h, sv_h, n512, n_slabs, scale_lhs, dst_ap_fn):
        for i in range(n512):
            c.w("gpsimd", "pe")
            c.dma("gpsimd", t["svbuf"][:1, :],
                  bass.AP(io[sv_h], i * 512, [[512, 1], [1, 512]]),
                  inc=("sv", 16))
            c.w("tensor", "pre")
            c.w("tensor", "sv")
            c.w("tensor", "dve")
            c.mm(t["psD"][0][:, :], t[scale_lhs][:1, :], t["svbuf"][:1, :],
                 True, True, inc=("pe", 1))
            c.w("vector", "pe")
            c.vcopy(t["sbc"][:, i * 512:(i + 1) * 512], t["psD"][0][:, :],
                    inc=("dve", 1))
        c.w("vector", "dve")
        for sl in range(n_slabs):
            bi = sl % 2
            gl = "gla" if bi == 0 else "glb"
            gs = "gsa" if bi == 0 else "gsb"
            c.await_guards("gpsimd", "win" + str(bi))
            c.dma("gpsimd", t["win"][bi][:, :n512 * 512],
                  bass.AP(io[src_h], sl * 128 * n512 * 512,
                          [[n512 * 512, 128], [1, n512 * 512]]),
                  inc=(gl, 16))
            c.w("tensor", gl)
            tt_ct = {}
            for n in range(n512):
                if n - 2 in tt_ct:
                    c.w("tensor", "dve", tt_ct[n - 2])
                else:
                    c.w("tensor", "dve")
                c.mm(t["psD"][n % 2][:, :], t["H128c"][:, :],
                     t["win"][bi][:, n * 512:(n + 1) * 512], True, True,
                     inc=("pe", 1))
                c.w("vector", "pe")
                c.await_guards("vector", "wout" + str(bi))
                c.tt(t["wout"][bi][:, n * 512:(n + 1) * 512],
                     t["psD"][n % 2][:, :],
                     t["sbc"][:, n * 512:(n + 1) * 512], ALU.mult,
                     inc=("dve", 1))
                tt_ct[n] = c.counts["dve"]
            c.guard("win" + str(bi), "pe")
            c.w("gpsimd", "dve")
            c.dma("gpsimd", dst_ap_fn(sl), t["wout"][bi][:, :n512 * 512],
                  inc=(gs, 16))
            c.guard("wout" + str(bi), gs)

    fold("Wu_st", "WU_hbm", "ws_up", 16, 16, "c64row",
         lambda sl: bass.AP(io["WU_hbm"], sl * 512,
                            [[8192, 128], [1048576, 16], [1, 512]]))
    fold("Wg_st", "WG_hbm", "ws_gate", 16, 16, "c64row",
         lambda sl: bass.AP(io["WG_hbm"], sl * 512,
                            [[8192, 128], [1048576, 16], [1, 512]]))
    fold("Wd_st", "WF_hbm", "ws_dn", 4, 64, "c32row",
         lambda sl: bass.AP(io["WF_hbm"], sl * 2048,
                            [[131072, 128], [1, 2048]]))
    c.w("vector", "pre")
    for k in range(64):
        c.await_guards("vector", "bdt")
        c.ts(t["bdt"][:, :], t["BD64p"][:, :], t["su_dp"][:, k:k + 1],
             ALU.mult, inc=("dve", 1))
        c.w("gpsimd", "dve")
        c.dma("gpsimd", bass.AP(io["BD64_hbm"], k * 128,
                                [[8192, 128], [1, 128]]),
              t["bdt"][:, :], inc=("gsa", 16))
        c.guard("bdt", "gsa")


def emit_ln(c, t, h_ap_fn, out_fn):
    c.w("tensor", "dve")
    for kt in range(NKT):
        bi = kt % 2
        hl = "hla" if bi == 0 else "hlb"
        c.await_guards("gpsimd", "hbuf" + str(bi))
        c.dma("gpsimd", t["hbuf"][bi][:, :], h_ap_fn(kt), inc=(hl, 16))
        c.w("scalar", hl)
        c.await_guards("scalar", "hsq" + str(bi))
        c.act(t["hsq"][bi][:, :], t["hbuf"][bi][:, :], AF.Square,
              inc=("act", 1))
        c.w("tensor", "act")
        c.w("tensor", hl)
        c.mm(t["psA"][0][:1, :], t["ones128f"][:, :], t["hbuf"][bi][:, :],
             start=(kt == 0), stop=(kt == NKT - 1))
        c.mm(t["psB"][0][:1, :], t["ones128f"][:, :], t["hsq"][bi][:, :],
             start=(kt == 0), stop=(kt == NKT - 1), inc=("pe", 1))
        c.guard("hbuf" + str(bi), "pe")
        c.guard("hsq" + str(bi), "pe")
    st = t["stat"]
    c.w("vector", "pe")
    c.ts(st[:1, 0:TC], t["psA"][0][:1, :], 1.0 / 2048.0, ALU.mult,
         inc=("dve", 1))
    c.ts(st[:1, TC:2 * TC], t["psB"][0][:1, :], 1.0 / 2048.0, ALU.mult,
         inc=("dve", 1))
    c.w("vector", "dve")
    c.tt(st[:1, 2 * TC:3 * TC], st[:1, 0:TC], st[:1, 0:TC], ALU.mult,
         inc=("dve", 1))
    c.w("vector", "dve")
    c.tt(st[:1, 2 * TC:3 * TC], st[:1, TC:2 * TC], st[:1, 2 * TC:3 * TC],
         ALU.subtract, inc=("dve", 1))
    c.w("scalar", "dve")
    c.act(st[:1, 2 * TC:3 * TC], st[:1, 2 * TC:3 * TC], AF.Sqrt,
          bias=t["kvals"][:1, 0:1], inc=("act", 1))
    c.w("vector", "act")
    c.recip(st[:1, 2 * TC:3 * TC], st[:1, 2 * TC:3 * TC], inc=("dve", 1))
    c.w("vector", "dve")
    c.tt(st[:1, 3 * TC:4 * TC], st[:1, 0:TC], st[:1, 2 * TC:3 * TC],
         ALU.mult, inc=("dve", 1))
    c.w("tensor", "dve")
    c.mm(t["psA"][0][:, :], t["ones1f"][:1, :], st[:1, 2 * TC:3 * TC],
         True, True)
    c.mm(t["psB"][0][:, :], t["ones1f"][:1, :], st[:1, 3 * TC:4 * TC],
         True, True, inc=("pe", 1))
    c.w("vector", "pe")
    c.vcopy(t["rbc"][:, :], t["psA"][0][:, :], inc=("dve", 1))
    c.vcopy(t["mrbc"][:, :], t["psB"][0][:, :], inc=("dve", 1))
    c.w("vector", "pe")
    for kt in range(NKT):
        bi = kt % 2
        hl = "hla" if bi == 0 else "hlb"
        c.await_guards("gpsimd", "hbuf" + str(bi))
        c.dma("gpsimd", t["hbuf"][bi][:, :], h_ap_fn(kt), inc=(hl, 16))
        c.w("vector", hl)
        c.w("vector", "dve")
        c.await_guards("vector", "hsq" + str(bi))
        c.tt(t["hsq"][bi][:, :], t["hbuf"][bi][:, :], t["rbc"][:, :],
             ALU.mult, inc=("dve", 1))
        c.w("vector", "dve")
        c.tt(out_fn(kt), t["hsq"][bi][:, :], t["mrbc"][:, :], ALU.subtract,
             inc=("dve", 1))
        c.guard("hbuf" + str(bi), "dve")
        c.guard("hsq" + str(bi), "dve")


def emit_ffn_chunk(c, io, t, T, row_b, h_in, h_out, dbg=None):

    def h_ap(kt):
        return bass.AP(io[h_in], kt * 128 * T + row_b, [[T, 128], [1, TC]])

    def dump(name, src_ap):
        if dbg is None or name not in dbg:
            return
        c.w("gpsimd", "dve")
        c.w("gpsimd", "act")
        c.w("gpsimd", "pe")
        c.dma("gpsimd", dbg[name].ap(), src_ap, inc=("sto", 16))

    c.w("vector", "pe")
    emit_ln(c, t, h_ap, lambda kt: t["xn"][:, kt * TC:(kt + 1) * TC])
    dump("xn", t["xn"][:, :])

    c.w("tensor", "pre")
    c.w("tensor", "act")
    c.w("tensor", "dve")
    zev_ct = {}
    for kt in range(NKT):
        pb = kt % 2
        if kt - 2 in zev_ct:
            c.w("tensor", "act", zev_ct[kt - 2])
        c.mm(t["psA"][pb][:, :], t["BD16"][:, kt * 128:(kt + 1) * 128],
             t["xn"][:, kt * TC:(kt + 1) * TC], True, True, inc=("pe", 1))
        c.w("scalar", "pe")
        c.act(t["z"][:, kt * TC:(kt + 1) * TC], t["psA"][pb][:, :], AF.Copy,
              inc=("act", 1))
        zev_ct[kt] = c.counts["act"]
    dump("z", t["z"][:, :])

    upgate_ct = {}
    bd64_ct = {}
    silu_ct = {}
    ucopy_ct = {}
    ymul_ct = {}

    def emit_bd64(tau):
        pp = tau % 2
        qb = (tau // 4) % 2
        c.w("tensor", "dve", ymul_ct[tau])
        if tau - 2 in ucopy_ct:
            c.w("tensor", "act", ucopy_ct[tau - 2])
        c.mm(t["psC"][pp][:, :],
             t["bd_sl"][qb][:, (tau % 4) * 128:(tau % 4 + 1) * 128],
             t["y"][pp][:, :], True, True, inc=("pe", 1))
        if tau % 4 == 3:
            c.guard("bd" + str(qb), "pe")
        bd64_ct[tau] = c.counts["pe"]
        c.w("scalar", "pe", bd64_ct[tau])
        c.act(t["u"][:, tau * TC:(tau + 1) * TC], t["psC"][pp][:, :], AF.Copy,
              inc=("act", 1))
        ucopy_ct[tau] = c.counts["act"]

    for tau in range(NTAU):
        pp = tau % 2
        e4 = tau % 4
        qb = (tau // 4) % 2
        if e4 == 0:
            wl = ("wla", "wlb", "wlc", "wld")[(tau // 4) % 4]
            c.await_guards("sync", "wu" + str(qb))
            c.dma("sync", t["wu_sl"][qb][:, :],
                  bass.AP(io["WU_hbm"], (tau // 4) * 1048576,
                          [[512, 128], [65536, 16], [1, 512]]),
                  inc=(wl, 16))
            c.dma("sync", t["wg_sl"][qb][:, :],
                  bass.AP(io["WG_hbm"], (tau // 4) * 1048576,
                          [[512, 128], [65536, 16], [1, 512]]),
                  inc=(wl, 16))
            c.await_guards("sync", "bd" + str(qb))
            c.dma("sync", t["bd_sl"][qb][:, :],
                  bass.AP(io["BD64_hbm"], (tau // 4) * 512,
                          [[8192, 128], [1, 512]]),
                  inc=(wl, 16))
            c.w("tensor", wl)
        if tau - 2 in silu_ct:
            c.w("tensor", "act", silu_ct[tau - 2])
            c.w("tensor", "dve", ymul_ct[tau - 2])
        else:
            c.w("tensor", "act")
            c.w("tensor", "dve")
        for kt in range(NKT):
            zs = t["z"][:, kt * TC:(kt + 1) * TC]
            c.mm(t["psA"][pp][:, :],
                 t["wu_sl"][qb][:, kt * 512 + e4 * 128:kt * 512 + (e4 + 1) * 128],
                 zs, start=(kt == 0), stop=(kt == NKT - 1))
            c.mm(t["psB"][pp][:, :],
                 t["wg_sl"][qb][:, kt * 512 + e4 * 128:kt * 512 + (e4 + 1) * 128],
                 zs, start=(kt == 0), stop=(kt == NKT - 1),
                 inc=("pe", 1) if kt == NKT - 1 else None)
        upgate_ct[tau] = c.counts["pe"]
        if e4 == 3:
            c.guard("wu" + str(qb), "pe")
        if tau - 1 in ymul_ct:
            emit_bd64(tau - 1)
        c.w("scalar", "pe", upgate_ct[tau])
        if tau - 2 in ymul_ct:
            c.w("scalar", "dve", ymul_ct[tau - 2])
        c.act(t["gsil"][pp][:, :], t["psB"][pp][:, :], AF.Silu, inc=("act", 1))
        silu_ct[tau] = c.counts["act"]
        c.w("vector", "act", silu_ct[tau])
        if tau - 2 in bd64_ct:
            c.w("vector", "pe", bd64_ct[tau - 2])
        c.tt(t["y"][pp][:, :], t["psA"][pp][:, :], t["gsil"][pp][:, :],
             ALU.mult, inc=("dve", 1))
        ymul_ct[tau] = c.counts["dve"]
    emit_bd64(NTAU - 1)

    for half in range(2):
        for t2 in range(NTAU):
            bi = t2 % 2
            wf = ("wfa", "wfb", "wfc", "wfd")[t2 % 4]
            c.await_guards("sync", "wf" + str(bi))
            c.dma("sync", t["wf_sl"][bi][:, :],
                  bass.AP(io["WF_hbm"], t2 * 262144 + half * 1024,
                          [[2048, 128], [1, 1024]]),
                  inc=(wf, 16))
            c.w("tensor", wf)
            if t2 == 0:
                c.w("tensor", "dve")
                c.w("tensor", "act")
            us = t["u"][:, t2 * TC:(t2 + 1) * TC]
            for cc in range(8):
                c.mm(t["ps8"][cc][:, :],
                     t["wf_sl"][bi][:, cc * 128:(cc + 1) * 128], us,
                     start=(t2 == 0), stop=(t2 == NTAU - 1),
                     inc=("pe", 1) if cc == 7 else None)
            c.guard("wf" + str(bi), "pe")
        c.w("gpsimd", "pe")
        c.w("vector", "pe")
        for cc in range(8):
            co = half * 8 + cc
            bi2 = cc % 2
            rl = "rla" if bi2 == 0 else "rlb"
            st_ = "sta" if bi2 == 0 else "stb"
            c.await_guards("gpsimd", "rbuf" + str(bi2))
            c.dma("gpsimd", t["rbuf"][bi2][:, :],
                  bass.AP(io[h_in], co * 128 * T + row_b, [[T, 128], [1, TC]]),
                  inc=(rl, 16))
            c.w("vector", rl)
            c.await_guards("vector", "hnx" + str(bi2))
            c.tt(t["hnx"][bi2][:, :], t["ps8"][cc][:, :], t["rbuf"][bi2][:, :],
                 ALU.add, inc=("dve", 1))
            c.guard("rbuf" + str(bi2), "dve")
            c.w("gpsimd", "dve")
            c.dma("gpsimd",
                  bass.AP(io[h_out], co * 128 * T + row_b, [[T, 128], [1, TC]]),
                  t["hnx"][bi2][:, :], inc=(st_, 16))
            c.guard("hnx" + str(bi2), st_)



def declare_ffn_io(nc, T, debug_w=False):
    io = {}
    io["h_in"] = nc.dram_tensor("h_in", [D, T], F32, kind="ExternalInput")
    io["h_out"] = nc.dram_tensor("h_out", [D, T], F32, kind="ExternalOutput")
    for nm, shp in (("Wu_st", [16, 128, INTER]), ("Wg_st", [16, 128, INTER]),
                    ("Wd_st", [64, 128, D])):
        io[nm] = nc.dram_tensor(nm, shp, F32, kind="ExternalInput")
    for nm, n in (("ws_up", INTER), ("ws_gate", INTER), ("ws_dn", D)):
        io[nm] = nc.dram_tensor(nm, [1, n], F32, kind="ExternalInput")
    io["su_ug_cols"] = nc.dram_tensor("su_ug_cols", [128, 16], F32,
                                      kind="ExternalInput")
    io["su_dp_cols"] = nc.dram_tensor("su_dp_cols", [128, 64], F32,
                                      kind="ExternalInput")
    io["H128c_i"] = nc.dram_tensor("H128c_i", [128, 128], BF16,
                                   kind="ExternalInput")
    io["BD16p_i"] = nc.dram_tensor("BD16p_i", [128, 128], F32,
                                   kind="ExternalInput")
    io["BD64p_i"] = nc.dram_tensor("BD64p_i", [128, 128], F32,
                                   kind="ExternalInput")
    kw = {"kind": "ExternalOutput"} if debug_w else {}
    io["WU_hbm"] = nc.dram_tensor("WU_hbm", [NTAU * 16 * 128 * 128], BF16, **kw)
    io["WG_hbm"] = nc.dram_tensor("WG_hbm", [NTAU * 16 * 128 * 128], BF16, **kw)
    io["WF_hbm"] = nc.dram_tensor("WF_hbm", [16 * NTAU * 128 * 128], BF16, **kw)
    io["BD64_hbm"] = nc.dram_tensor("BD64_hbm", [128 * 8192], BF16)
    return io


def build_ffn_program(T, debug_w=False, debug_mid=False):
    nc = bass.Bass("TRN2", target_bir_lowering=False)
    stk = ExitStack()
    io = declare_ffn_io(nc, T, debug_w)
    sems = {n: stk.enter_context(nc.semaphore(n)) for n in SEM_NAMES}
    t = alloc_common(nc, stk)
    alloc_psum(nc, stk, t)
    b = Builder(nc, sems)

    fold_stk = ExitStack()
    alloc_fold(nc, fold_stk, t)

    def prog_fold(c):
        emit_load_consts(c, io, t)
        emit_weight_fold(c, io, t)

    b.block(prog_fold)
    fold_stk.close()

    alloc_ffn(nc, stk, t)

    dbg = None
    if debug_mid:
        dbg = {
            "xn": nc.dram_tensor("dbg_xn", [128, NKT * TC], BF16,
                                 kind="ExternalOutput"),
            "z": nc.dram_tensor("dbg_z", [128, NKT * TC], BF16,
                                kind="ExternalOutput"),
            "u": nc.dram_tensor("dbg_u", [128, NTAU * TC], BF16,
                                kind="ExternalOutput"),
        }

    def prog_main(c):
        c.barrier()
        emit_bd_tiles(c, t)
        for tc in range(T // TC):
            emit_ffn_chunk(c, io, t, T, tc * TC, "h_in", "h_out",
                           dbg=dbg if tc == 0 else None)
        c.w("gpsimd", "sta")
        c.w("gpsimd", "stb")
        c.w("gpsimd", "sto")

    b.block(prog_main)
    stk.close()
    return nc



def stage_ffn_inputs(ins):
    H16 = _hadamard(16)
    H64 = _hadamard(64)
    H128 = _hadamard(128)
    W_up = np.asarray(ins["L0_ffn_W_up"], dtype=np.float32)
    W_gate = np.asarray(ins["L0_ffn_W_gate"], dtype=np.float32)
    W_down = np.asarray(ins["L0_ffn_W_down"], dtype=np.float32)
    ws_ug = np.asarray(ins["L0_ffn_Wscale_ug"], dtype=np.float32)
    return {
        "Wu_st": np.ascontiguousarray(
            W_up.reshape(INTER, 128, 16).transpose(2, 1, 0)),
        "Wg_st": np.ascontiguousarray(
            W_gate.reshape(INTER, 128, 16).transpose(2, 1, 0)),
        "Wd_st": np.ascontiguousarray(
            W_down.reshape(D, 128, 64).transpose(2, 1, 0)),
        "ws_up": np.ascontiguousarray(ws_ug[:INTER].reshape(1, INTER)),
        "ws_gate": np.ascontiguousarray(ws_ug[INTER:].reshape(1, INTER)),
        "ws_dn": np.ascontiguousarray(
            np.asarray(ins["L0_ffn_Wscale_dp"], np.float32).reshape(1, D)),
        "su_ug_cols": np.ascontiguousarray(
            np.asarray(ins["L0_ffn_SU_ug"], np.float32).reshape(16, 128).T),
        "su_dp_cols": np.ascontiguousarray(
            np.asarray(ins["L0_ffn_SU_dp"], np.float32).reshape(64, 128).T),
        "H128c_i": H128.astype(NPBF16),
        "BD16p_i": np.ascontiguousarray(
            np.kron(np.eye(8, dtype=np.float32), H16) * C16),
        "BD64p_i": np.ascontiguousarray(
            np.kron(np.eye(2, dtype=np.float32), H64) * C64),
    }



ATTN_SEMS = SEM_NAMES + ("vla", "vlb", "rpa", "rpb")


def alloc_attn_common(nc, stk):
    t = {}

    def sb(name, shape, dt):
        t[name] = stk.enter_context(nc.sbuf_tensor(name, shape, dt))
        return t[name]

    sb("ones128f", [128, 1], F32)
    sb("ones1f", [1, 128], F32)
    sb("ones128b", [128, 1], BF16)
    sb("kvals", [128, 8], F32)
    sb("qn_ws", [64, 2], F32)
    sb("kn_ws", [64, 2], F32)
    sb("qn_wn", [64, 2], F32)
    sb("kn_wn", [64, 2], F32)
    sb("Ksb", [128, 4 * 8192], BF16)
    sb("q_n", [128, 16 * 2048], BF16)
    return t


def alloc_attn_prep(nc, stk, t, sfx=""):
    def sb(name, shape, dt):
        t[name] = stk.enter_context(nc.sbuf_tensor(name + sfx, shape, dt))
        return t[name]
    sb("a_sb", [128, 16 * 512], BF16)
    sb("stat", [1, 4 * TC], F32)
    sb("rbc", [128, TC], F32)
    sb("mrbc", [128, TC], F32)
    t["hbuf"] = [sb("hbuf0", [128, TC], F32), sb("hbuf1", [128, TC], F32)]
    t["hsq"] = [sb("hsq0", [128, TC], F32), sb("hsq1", [128, TC], F32)]
    t["wsl"] = [sb("wsl0", [128, 16 * 128], BF16),
                sb("wsl1", [128, 16 * 128], BF16)]
    t["wvb"] = [sb("wvb0", [128, 512], BF16), sb("wvb1", [128, 512], BF16)]
    t["cosb"] = sb("cosb", [64, 512], F32)
    t["sinb"] = sb("sinb", [64, 512], F32)
    sb("kn0a", [64, 512], F32)
    sb("kn0b", [64, 512], F32)
    t["rp"] = [sb("rp1", [64, 512], F32), sb("rp2", [64, 512], F32),
               sb("rp3", [64, 512], F32), sb("rp4", [64, 512], F32)]
    sb("rrow", [1, 512], F32)
    sb("vsb", [128, 512], BF16)


def alloc_attn_core(nc, stk, t, sfx=""):
    def sb(name, shape, dt):
        t[name] = stk.enter_context(nc.sbuf_tensor(name + sfx, shape, dt))
        return t[name]
    sb("Vsl", [128, 64 * 128], BF16)
    sb("negI", [128, 128], BF16)
    sb("bo", [128, 16], F32)
    sb("Rm", [128, 16 * 512], BF16)
    sb("Tsb", [128, 1024], F32)
    t["Psb"] = [sb("Psb0", [128, 1024], BF16), sb("Psb1", [128, 1024], BF16)]
    t["Stb"] = [stk.enter_context(nc.psum_tensor("Stb0" + sfx, [128, 1024], F32)),
                stk.enter_context(nc.psum_tensor("Stb1" + sfx, [128, 1024], F32))]
    pb0 = stk.enter_context(nc.psum_tensor("pcb0" + sfx, [128, 512], F32))
    pb1 = stk.enter_context(nc.psum_tensor("pcb1" + sfx, [128, 512], F32))
    pc0 = stk.enter_context(nc.psum_tensor("pcc0" + sfx, [128, 512], F32))
    pd0 = stk.enter_context(nc.psum_tensor("pcd0" + sfx, [128, 512], F32))
    t["psB"] = [pb0, pb1]
    t["psC"] = [pc0, pc0]
    t["psD"] = [pd0, pd0]
    sb("y_m", [128, 16 * 512], BF16)
    sb("lrow", [1, 512], F32)
    sb("rbcs", [128, 512], F32)
    t["wsl"] = [sb("wslA", [128, 16 * 128], BF16),
                sb("wslB", [128, 16 * 128], BF16)]
    t["h1t"] = [sb("h1t0", [128, 512], F32), sb("h1t1", [128, 512], F32)]
    t["h2t"] = [sb("h2t0", [128, 512], F32), sb("h2t1", [128, 512], F32)]


def emit_attn_consts(c, io, t, shared=True):
    c.dma("gpsimd", t["qn_ws"][:, :], io["qn_w"].ap(), inc=("pre", 16))
    c.dma("gpsimd", t["kn_ws"][:, :], io["kn_w"].ap(), inc=("pre", 16))
    if shared:
        c.memset(t["ones128f"][:, :], 1.0)
        c.memset(t["ones1f"][:1, :], 1.0)
        c.memset(t["ones128b"][:, :], 1.0)
        c.memset(t["kvals"][:, 0:1], LN_EPS)
        c.memset(t["kvals"][:, 1:2], RMS_EPS)
        c.memset(t["kvals"][:, 2:3], 50.0)
        c.memset(t["kvals"][:, 3:4], ISQD / 50.0)
        c.memset(t["kvals"][:, 4:5], 1.0 / 128.0, inc=("dve", 1))
    else:
        c._inc(c.eng.engine_nop() if c.kind == "vector" else None, ("dve", 1))
    c.w("vector", "pre")
    c.w("vector", "dve")
    c.ts(t["qn_wn"][:, :], t["qn_ws"][:, :], -1.0, ALU.mult)
    c.ts(t["kn_wn"][:, :], t["kn_ws"][:, :], -1.0, ALU.mult, inc=("dve", 1))


def emit_norm_rope(c, t, ps_src, w, wn, cos_ap, sin_ap, out_lo, out_hi):
    rp = t["rp"]
    c.act(t["hsq"][0][:, :], ps_src, AF.Square, inc=("act", 1))
    c.w("tensor", "act")
    c.mm(t["psB"][0][:1, :], t["ones128f"][:, :], t["hsq"][0][:, :],
         True, True, inc=("pe", 1))
    c.w("vector", "pe")
    c.ts(t["rrow"][:1, :], t["psB"][0][:1, :], 1.0 / 128.0, ALU.mult,
         inc=("dve", 1))
    c.w("scalar", "dve")
    c.act(t["rrow"][:1, :], t["rrow"][:1, :], AF.Sqrt, inc=("act", 1))
    c.w("vector", "act")
    c.ts(t["rrow"][:1, :], t["rrow"][:1, :], RMS_EPS, ALU.add, inc=("dve", 1))
    c.w("vector", "dve")
    c.recip(t["rrow"][:1, :], t["rrow"][:1, :], inc=("dve", 1))
    c.w("tensor", "dve")
    c.mm(t["psB"][1][:, :], t["ones1f"][:1, :], t["rrow"][:1, :], True, True,
         inc=("pe", 1))
    c.w("vector", "pe")
    c.vcopy(t["rbc"][:, :512], t["psB"][1][:, :], inc=("dve", 1))
    c.w("vector", "dve")
    c.tt(t["kn0a"][:, :], ps_src[0:64, :], t["rbc"][0:64, :512], ALU.mult,
         inc=("dve", 1))
    c.tt(t["kn0b"][:, :], ps_src[64:128, :], t["rbc"][64:128, :512], ALU.mult,
         inc=("dve", 1))
    c.w("vector", "dve")
    k1 = t["kn0a"][:, :]
    k2 = t["kn0b"][:, :]
    c.tt(rp[0][:, :], k1, cos_ap, ALU.mult, inc=("dve", 1))
    c.tt(rp[1][:, :], k2, sin_ap, ALU.mult, inc=("dve", 1))
    c.w("vector", "dve")
    c.ts(rp[2][:, :], rp[1][:, :], w[:, 1:2], ALU.mult, inc=("dve", 1))
    c.w("vector", "dve")
    c.stt(out_lo, rp[0][:, :], w[:, 0:1], rp[2][:, :], ALU.mult, ALU.add,
          inc=("dve", 1))
    c.tt(rp[1][:, :], k2, cos_ap, ALU.mult, inc=("dve", 1))
    c.tt(rp[3][:, :], k1, sin_ap, ALU.mult, inc=("dve", 1))
    c.w("vector", "dve")
    c.ts(rp[2][:, :], rp[1][:, :], w[:, 1:2], ALU.mult, inc=("dve", 1))
    c.w("vector", "dve")
    c.stt(out_hi, rp[3][:, :], wn[:, 0:1], rp[2][:, :], ALU.mult, ALU.add,
          inc=("dve", 1))


def emit_attn_prep(c, io, t):

    for g in range(16):
        def h_ap(kt, g=g):
            return bass.AP(io["h_full"], kt * 128 * T_FULL_B + g * 512,
                           [[T_FULL_B, 128], [1, TC]])
        emit_ln(c, t, h_ap,
                lambda kt: t["a_sb"][:, kt * 512:(kt + 1) * 512])
        rp_s = "rpa" if g % 2 == 0 else "rpb"
        c.await_guards("gpsimd", "ropeb")
        c.dma("gpsimd", t["cosb"][:, :],
              bass.AP(io["cosk_i"], g * 512, [[8192, 64], [1, 512]]),
              inc=(rp_s, 16))
        c.dma("gpsimd", t["sinb"][:, :],
              bass.AP(io["sink_i"], g * 512, [[8192, 64], [1, 512]]),
              inc=(rp_s, 16))
        for dt_ in range(4):
            wl = "qla" if dt_ % 2 == 0 else "qlb"
            c.await_guards("gpsimd", "wsl" + str(dt_ % 2))
            c.dma("gpsimd", t["wsl"][dt_ % 2][:, :],
                  bass.AP(io["Wk_T"], dt_ * 262144,
                          [[2048, 128], [1, 2048]]),
                  inc=(wl, 16))
            c.w("tensor", wl)
            c.w("tensor", "act")
            c.w("tensor", "dve")
            for kt in range(NKT):
                c.mm(t["psA"][dt_ % 2][:, :],
                     t["wsl"][dt_ % 2][:, kt * 128:(kt + 1) * 128],
                     t["a_sb"][:, kt * 512:(kt + 1) * 512],
                     start=(kt == 0), stop=(kt == NKT - 1),
                     inc=("pe", 1) if kt == NKT - 1 else None)
            c.guard("wsl" + str(dt_ % 2), "pe")
            c.w("scalar", "pe")
            c.w("scalar", "dve")
            c.w("vector", rp_s)
            emit_norm_rope(
                c, t, t["psA"][dt_ % 2][:, :], t["kn_ws"], t["kn_wn"],
                t["cosb"][:, :], t["sinb"][:, :],
                t["Ksb"][0:64, dt_ * 8192 + g * 512:dt_ * 8192 + (g + 1) * 512],
                t["Ksb"][64:128, dt_ * 8192 + g * 512:dt_ * 8192 + (g + 1) * 512])
        c.guard("ropeb", "dve")
        for kt in range(NKT):
            wv = "vla" if kt % 2 == 0 else "vlb"
            c.await_guards("gpsimd", "wvb" + str(kt % 2))
            c.dma("gpsimd", t["wvb"][kt % 2][:, :],
                  bass.AP(io["Wv_T"], kt * 128 * 512, [[512, 128], [1, 512]]),
                  inc=(wv, 16))
            c.w("tensor", wv)
            if kt == 0:
                c.w("tensor", "act")
            for ts_ in range(4):
                c.mm(t["psC"][ts_ % 2][:, :] if ts_ < 2 else t["psD"][ts_ % 2][:, :],
                     t["a_sb"][:, kt * 512 + ts_ * 128:kt * 512 + (ts_ + 1) * 128],
                     t["wvb"][kt % 2][:, :],
                     start=(kt == 0), stop=(kt == NKT - 1),
                     inc=("pe", 1) if ts_ == 3 else None)
            c.guard("wvb" + str(kt % 2), "pe")
        c.w("scalar", "pe")
        for ts_ in range(4):
            ps = t["psC"][ts_ % 2] if ts_ < 2 else t["psD"][ts_ % 2]
            c.act(t["vsb"][:, :], ps[:, :], AF.Copy, inc=("act", 1))
            c.w("gpsimd", "act")
            c.dma("gpsimd",
                  bass.AP(io["v_hbm"], (g * 4 + ts_) * 16384,
                          [[128, 128], [1048576, 4], [1, 128]]),
                  t["vsb"][:, :], inc=("sta", 16))
            c.guard("vsb", "sta")
            c.w("scalar", "sta")

    for m in range(4):
        def h_ap(kt, m=m):
            return io["h_own_ap"](kt, m)
        emit_ln(c, t, h_ap,
                lambda kt: t["a_sb"][:, kt * 512:(kt + 1) * 512])
        rp_s = "rpa" if m % 2 == 0 else "rpb"
        c.await_guards("gpsimd", "ropeb")
        c.dma("gpsimd", t["cosb"][:, :],
              bass.AP(io["cosq_i"], m * 512, [[2048, 64], [1, 512]]),
              inc=(rp_s, 16))
        c.dma("gpsimd", t["sinb"][:, :],
              bass.AP(io["sinq_i"], m * 512, [[2048, 64], [1, 512]]),
              inc=(rp_s, 16))
        c.w("vector", rp_s)
        for h in range(NH):
            wl = "qla" if h % 2 == 0 else "qlb"
            c.await_guards("gpsimd", "wsl" + str(h % 2))
            c.dma("gpsimd", t["wsl"][h % 2][:, :],
                  bass.AP(io["Wq_T"], h * 262144,
                          [[2048, 128], [1, 2048]]),
                  inc=(wl, 16))
            c.w("tensor", wl)
            c.w("tensor", "act")
            c.w("tensor", "dve")
            for kt in range(NKT):
                c.mm(t["psA"][h % 2][:, :],
                     t["wsl"][h % 2][:, kt * 128:(kt + 1) * 128],
                     t["a_sb"][:, kt * 512:(kt + 1) * 512],
                     start=(kt == 0), stop=(kt == NKT - 1),
                     inc=("pe", 1) if kt == NKT - 1 else None)
            c.guard("wsl" + str(h % 2), "pe")
            c.w("scalar", "pe")
            c.w("scalar", "dve")
            emit_norm_rope(
                c, t, t["psA"][h % 2][:, :], t["qn_ws"], t["qn_wn"],
                t["cosb"][:, :], t["sinb"][:, :],
                t["q_n"][0:64, h * 2048 + m * 512:h * 2048 + (m + 1) * 512],
                t["q_n"][64:128, h * 2048 + m * 512:h * 2048 + (m + 1) * 512])
        c.guard("ropeb", "dve")


def emit_attn_core(c, io, t):
    for m in range(4):
        nb = (4 * m + 4) * 4
        for h in range(NH):
            gg = h // 4
            if h % 4 == 0:
                c.await_guards("gpsimd", "Vsl")
                c.dma("gpsimd", t["Vsl"][:, :nb * 128],
                      bass.AP(io["v_hbm"], gg * 1048576,
                              [[128, 128], [16384, nb], [1, 128]]),
                      inc=("vla", 16))
                c.w("tensor", "vla")
            qs = t["q_n"][:, h * 2048 + m * 512:h * 2048 + (m + 1) * 512]
            c.w("tensor", "dve")
            c.w("tensor", "act")
            tanh_ct = {}
            pv_ct = {}
            for pr in range(nb // 2):
                pp = pr % 2
                if pr >= 2:
                    c.w("tensor", "act", tanh_ct[pr - 2])
                for half_ in range(2):
                    kb = 2 * pr + half_
                    trailing = kb >= nb - 16
                    stp = t["Stb"][pp][:, half_ * 512:(half_ + 1) * 512]
                    c.mm(stp,
                         t["Ksb"][:, gg * 8192 + kb * 128:
                                  gg * 8192 + (kb + 1) * 128],
                         qs, start=True, stop=not trailing,
                         inc=None if trailing else ("pe", 1))
                    if trailing:
                        i_ = kb - (nb - 16)
                        c.mm(stp, t["negI"][:, :],
                             t["Rm"][:, i_ * 512:(i_ + 1) * 512],
                             start=False, stop=True, inc=("pe", 1))
                c.w("scalar", "pe")
                c.act(t["Tsb"][:, :], t["Stb"][pp][:, :], AF.Tanh,
                      scale=t["kvals"][:, 3:4], inc=("act", 1))
                tanh_ct[pr] = c.counts["act"]
                if pr >= 2:
                    c.w("scalar", "pe", pv_ct[pr - 2])
                c.w("scalar", "act")
                c.act(t["Psb"][pp][:, :], t["Tsb"][:, :], AF.Exp,
                      scale=t["kvals"][:, 2:3], inc=("act", 1))
                c.w("tensor", "act")
                for half_ in range(2):
                    kb = 2 * pr + half_
                    pb = t["Psb"][pp][:, half_ * 512:(half_ + 1) * 512]
                    c.mm(t["psB"][0][:1, :], t["ones128b"][:, :], pb,
                         start=(kb == 0), stop=(kb == nb - 1), inc=None)
                    c.mm(t["psB"][1][:, :],
                         t["Vsl"][:, kb * 128:(kb + 1) * 128], pb,
                         start=(kb == 0), stop=(kb == nb - 1),
                         inc=("pe", 1))
                pv_ct[pr] = c.counts["pe"]
            if h % 4 == 3:
                c.guard("Vsl", "pe")
            c.w("vector", "pe")
            c.vcopy(t["lrow"][:1, :], t["psB"][0][:1, :], inc=("dve", 1))
            c.w("vector", "dve")
            c.recip(t["lrow"][:1, :], t["lrow"][:1, :], inc=("dve", 1))
            c.w("tensor", "dve")
            c.mm(t["psC"][0][:, :], t["ones1f"][:1, :], t["lrow"][:1, :],
                 True, True, inc=("pe", 1))
            c.w("vector", "pe")
            c.vcopy(t["rbcs"][:, :], t["psC"][0][:, :], inc=("dve", 1))
            c.w("vector", "dve")
            c.tt(t["y_m"][:, h * 512:(h + 1) * 512], t["psB"][1][:, :],
                 t["rbcs"][:, :], ALU.mult, inc=("dve", 1))
        for co in range(16):
            bi = co % 2
            wl = "qla" if bi == 0 else "qlb"
            st_ = "sta" if bi == 0 else "stb"
            rl = "rla" if bi == 0 else "rlb"
            c.await_guards("gpsimd", "wslo" + str(bi))
            c.dma("gpsimd", t["wsl"][bi][:, :],
                  bass.AP(io["Wo_T"], co * 262144,
                          [[2048, 128], [1, 2048]]),
                  inc=(wl, 16))
            c.await_guards("gpsimd", "h1t" + str(bi))
            c.dma("gpsimd", t["h1t"][bi][:, :], io["h_own_ap"](co, m),
                  inc=(rl, 16))
            c.w("tensor", wl)
            c.w("tensor", "dve")
            for h in range(NH):
                c.mm(t["psD"][bi][:, :],
                     t["wsl"][bi][:, h * 128:(h + 1) * 128],
                     t["y_m"][:, h * 512:(h + 1) * 512],
                     start=(h == 0), stop=(h == NH - 1),
                     inc=("pe", 1) if h == NH - 1 else None)
            c.guard("wslo" + str(bi), "pe")
            c.w("vector", "pe")
            c.w("vector", rl)
            c.await_guards("vector", "h2t" + str(bi))
            c.stt(t["h2t"][bi][:, :], t["psD"][bi][:, :], t["bo"][:, co:co + 1],
                  t["h1t"][bi][:, :], ALU.add, ALU.add, inc=("dve", 1))
            c.guard("h1t" + str(bi), "dve")
            c.w("gpsimd", "dve")
            c.dma("gpsimd",
                  bass.AP(io["h2_own"], co * 128 * 2048 + m * 512,
                          [[2048, 128], [1, TC]]),
                  t["h2t"][bi][:, :], inc=(st_, 16))
            c.guard("h2t" + str(bi), st_)


T_FULL_B = 8192


def build_attn_program():
    nc = bass.Bass("TRN2", target_bir_lowering=False)
    stk = ExitStack()
    io = {}
    io["h_full"] = nc.dram_tensor("h_full", [D, T_FULL_B], F32,
                                  kind="ExternalInput")
    io["h_own"] = nc.dram_tensor("h_own", [D, 2048], F32, kind="ExternalInput")
    io["h2_own"] = nc.dram_tensor("h2_own", [D, 2048], F32,
                                  kind="ExternalOutput")
    for nm, shp in (("Wq_T", [16, 128, 16, 128]), ("Wk_T", [4, 128, 16, 128]),
                    ("Wv_T", [D, 512]), ("Wo_T", [16, 128, 16, 128])):
        io[nm] = nc.dram_tensor(nm, shp, F32, kind="ExternalInput")
    io["bo_i"] = nc.dram_tensor("bo_i", [128, 16], F32, kind="ExternalInput")
    io["qn_w"] = nc.dram_tensor("qn_w", [64, 2], F32, kind="ExternalInput")
    io["kn_w"] = nc.dram_tensor("kn_w", [64, 2], F32, kind="ExternalInput")
    io["negI_i"] = nc.dram_tensor("negI_i", [128, 128], BF16,
                                  kind="ExternalInput")
    io["Rm_i"] = nc.dram_tensor("Rm_i", [128, 16 * 512], BF16,
                                kind="ExternalInput")
    io["cosk_i"] = nc.dram_tensor("cosk_i", [64, 8192], F32,
                                  kind="ExternalInput")
    io["sink_i"] = nc.dram_tensor("sink_i", [64, 8192], F32,
                                  kind="ExternalInput")
    io["cosq_i"] = nc.dram_tensor("cosq_i", [64, 2048], F32,
                                  kind="ExternalInput")
    io["sinq_i"] = nc.dram_tensor("sinq_i", [64, 2048], F32,
                                  kind="ExternalInput")
    io["v_hbm"] = nc.dram_tensor("v_hbm", [4 * 64 * 128 * 128], BF16)
    io["h_own_ap"] = lambda kt, m: bass.AP(
        io["h_own"], kt * 128 * 2048 + m * 512, [[2048, 128], [1, TC]])

    sems = {n: stk.enter_context(nc.semaphore(n)) for n in ATTN_SEMS}
    t = alloc_attn_common(nc, stk)
    b = Builder(nc, sems)

    prep_stk = ExitStack()
    alloc_psum(nc, prep_stk, t, sfx="P")
    alloc_attn_prep(nc, prep_stk, t)

    def prog_prep(c):
        emit_attn_consts(c, io, t)
        c.w("tensor", "pre")
        c.w("scalar", "pre")
        c.w("gpsimd", "pre")
        emit_attn_prep(c, io, t)

    b.block(prog_prep)
    prep_stk.close()
    for k in ("a_sb", "stat", "rbc", "mrbc", "hbuf", "hsq", "wsl", "wvb",
              "cosb", "sinb", "cosq", "sinq", "kn0", "rp", "rrow", "vsb"):
        t.pop(k, None)
    core_stk = ExitStack()
    alloc_attn_core(nc, core_stk, t)

    def prog_core(c):
        c.barrier()
        c.dma("gpsimd", t["negI"][:, :], io["negI_i"].ap(), inc=("pre", 16))
        c.dma("gpsimd", t["bo"][:, :], io["bo_i"].ap(), inc=("pre", 16))
        c.dma("gpsimd", t["Rm"][:, :], io["Rm_i"].ap(), inc=("pre", 16))
        c.w("tensor", "pre")
        c.w("vector", "pre")
        emit_attn_core(c, io, t)
        c.w("gpsimd", "sta")
        c.w("gpsimd", "stb")

    b.block(prog_core)
    core_stk.close()
    stk.close()
    return nc


def stage_attn_inputs(ins, j):
    Wq = np.asarray(ins["L0_attn_Wq"], np.float32)
    Wk = np.asarray(ins["L0_attn_Wk"], np.float32)
    Wv = np.asarray(ins["L0_attn_Wv"], np.float32)
    Wo = np.asarray(ins["L0_attn_Wo"], np.float32)
    inv_freq = 1.0 / 10000.0 ** (np.arange(0, HD, 2, dtype=np.float32) / HD)
    pos_full = np.arange(T_FULL_B, dtype=np.float32)
    own_tok = np.concatenate(
        [np.arange((4 * mm + j) * 512, (4 * mm + j) * 512 + 512)
         for mm in range(4)]).astype(np.float32)
    frf = pos_full[:, None] * inv_freq[None, :]
    fro = own_tok[:, None] * inv_freq[None, :]
    Rm = np.zeros((128, 16 * 512), dtype=NPBF16)
    Kk = np.arange(128)[:, None]
    tq = np.arange(512)[None, :]
    for i in range(16):
        d_ = 128 * (i - 4 * j)
        Rm[:, i * 512:(i + 1) * 512] = (Kk - tq > -d_).astype(NPBF16)
    return {
        "Wq_T": np.ascontiguousarray(
            Wq.reshape(16, 128, 16, 128).transpose(0, 3, 2, 1)),
        "Wk_T": np.ascontiguousarray(
            Wk.reshape(4, 128, 16, 128).transpose(0, 3, 2, 1)),
        "Wv_T": np.ascontiguousarray(Wv.T),
        "Wo_T": np.ascontiguousarray(
            Wo.reshape(16, 128, 16, 128).transpose(0, 3, 2, 1)),
        "bo_i": np.ascontiguousarray(
            np.asarray(ins["L0_attn_bo"], np.float32).reshape(16, 128).T),
        "qn_w": np.ascontiguousarray(
            np.asarray(ins["L0_attn_q_norm_w"], np.float32).reshape(2, 64).T),
        "kn_w": np.ascontiguousarray(
            np.asarray(ins["L0_attn_k_norm_w"], np.float32).reshape(2, 64).T),
        "negI_i": (NEG_BIG * np.eye(128, dtype=np.float32)).astype(NPBF16),
        "Rm_i": Rm,
        "cosk_i": np.ascontiguousarray(np.cos(frf).T.astype(np.float32)),
        "sink_i": np.ascontiguousarray(np.sin(frf).T.astype(np.float32)),
        "cosq_i": np.ascontiguousarray(np.cos(fro).T.astype(np.float32)),
        "sinq_i": np.ascontiguousarray(np.sin(fro).T.astype(np.float32)),
    }


def own_token_index(j):
    return np.concatenate(
        [np.arange((4 * m + j) * 512, (4 * m + j + 1) * 512) for m in range(4)])



FUSED_SEMS = SEM_NAMES + ("vla", "vlb", "rpa", "rpb", "cpy")


def alloc_attn_extra(nc, stk, t):
    def sb(name, shape, dt):
        t[name] = stk.enter_context(nc.sbuf_tensor(name, shape, dt))
        return t[name]
    sb("qn_ws", [64, 2], F32)
    sb("kn_ws", [64, 2], F32)
    sb("qn_wn", [64, 2], F32)
    sb("kn_wn", [64, 2], F32)
    sb("Ksb", [128, 4 * 8192], BF16)
    sb("q_n", [128, 16 * 2048], BF16)


def build_fused_program():
    nc = bass.Bass("TRN2", target_bir_lowering=False)
    stk = ExitStack()
    io = {}
    io["x_fm"] = nc.dram_tensor("x_fm", [D, T_FULL_B], F32,
                                kind="ExternalInput")
    io["out_fm"] = nc.dram_tensor("out_fm", [D, 2048], F32,
                                  kind="ExternalOutput")
    io["own_off"] = nc.dram_tensor("own_off", [1, 4], I32,
                                   kind="ExternalInput")
    for nm, shp in (("Wu_st", [16, 128, INTER]), ("Wg_st", [16, 128, INTER]),
                    ("Wd_st", [64, 128, D])):
        io[nm] = nc.dram_tensor(nm, shp, F32, kind="ExternalInput")
    for nm, n in (("ws_up", INTER), ("ws_gate", INTER), ("ws_dn", D)):
        io[nm] = nc.dram_tensor(nm, [1, n], F32, kind="ExternalInput")
    io["su_ug_cols"] = nc.dram_tensor("su_ug_cols", [128, 16], F32,
                                      kind="ExternalInput")
    io["su_dp_cols"] = nc.dram_tensor("su_dp_cols", [128, 64], F32,
                                      kind="ExternalInput")
    io["H128c_i"] = nc.dram_tensor("H128c_i", [128, 128], BF16,
                                   kind="ExternalInput")
    io["BD16p_i"] = nc.dram_tensor("BD16p_i", [128, 128], F32,
                                   kind="ExternalInput")
    io["BD64p_i"] = nc.dram_tensor("BD64p_i", [128, 128], F32,
                                   kind="ExternalInput")
    for nm, shp in (("Wq_T", [16, 128, 16, 128]), ("Wk_T", [4, 128, 16, 128]),
                    ("Wv_T", [D, 512]), ("Wo_T", [16, 128, 16, 128])):
        io[nm] = nc.dram_tensor(nm, shp, F32, kind="ExternalInput")
    io["bo_i"] = nc.dram_tensor("bo_i", [128, 16], F32, kind="ExternalInput")
    io["qn_w"] = nc.dram_tensor("qn_w", [64, 2], F32, kind="ExternalInput")
    io["kn_w"] = nc.dram_tensor("kn_w", [64, 2], F32, kind="ExternalInput")
    io["negI_i"] = nc.dram_tensor("negI_i", [128, 128], BF16,
                                  kind="ExternalInput")
    io["Rm_i"] = nc.dram_tensor("Rm_i", [128, 16 * 512], BF16,
                                kind="ExternalInput")
    io["cosk_i"] = nc.dram_tensor("cosk_i", [64, 8192], F32,
                                  kind="ExternalInput")
    io["sink_i"] = nc.dram_tensor("sink_i", [64, 8192], F32,
                                  kind="ExternalInput")
    io["cosq_i"] = nc.dram_tensor("cosq_i", [64, 2048], F32,
                                  kind="ExternalInput")
    io["sinq_i"] = nc.dram_tensor("sinq_i", [64, 2048], F32,
                                  kind="ExternalInput")
    io["WU_hbm"] = nc.dram_tensor("WU_hbm", [NTAU * 16 * 128 * 128], BF16)
    io["WG_hbm"] = nc.dram_tensor("WG_hbm", [NTAU * 16 * 128 * 128], BF16)
    io["WF_hbm"] = nc.dram_tensor("WF_hbm", [16 * NTAU * 128 * 128], BF16)
    io["BD64_hbm"] = nc.dram_tensor("BD64_hbm", [128 * 8192], BF16)
    io["h1_full"] = nc.dram_tensor("h1_full", [D, T_FULL_B], F32)
    io["h1_own"] = nc.dram_tensor("h1_own", [D, 2048], F32)
    io["h2_own"] = nc.dram_tensor("h2_own", [D, 2048], F32)
    io["v_hbm"] = nc.dram_tensor("v_hbm", [4 * 64 * 128 * 128], BF16)
    io["h_full"] = io["h1_full"]
    io["h_own_ap"] = lambda kt, m: bass.AP(
        io["h1_own"], kt * 128 * 2048 + m * 512, [[2048, 128], [1, TC]])

    sems = {n: stk.enter_context(nc.semaphore(n)) for n in FUSED_SEMS}
    t = alloc_common(nc, stk)
    b = Builder(nc, sems)

    ps1_stk = ExitStack()
    alloc_psum(nc, ps1_stk, t, sfx="1")
    fold_stk = ExitStack()
    alloc_fold(nc, fold_stk, t)

    def prog_fold(c):
        emit_load_consts(c, io, t)
        emit_weight_fold(c, io, t)

    b.block(prog_fold)
    fold_stk.close()

    ffn_stk = ExitStack()
    alloc_ffn(nc, ffn_stk, t)

    def prog_ffn1(c):
        c.barrier()
        emit_bd_tiles(c, t)
        for tc in range(T_FULL_B // TC):
            emit_ffn_chunk(c, io, t, T_FULL_B, tc * TC, "x_fm", "h1_full")

    b.block(prog_ffn1)
    ffn_stk.close()
    ps1_stk.close()
    for k in ("u", "xn", "z", "y", "gsil", "stat", "rbc", "mrbc", "hbuf",
              "hsq", "wu_sl", "wg_sl", "wf_sl", "rbuf", "hnx",
              "psA", "psB", "psC", "psD", "ps8"):
        t.pop(k, None)

    offs_stk = ExitStack()
    t["offs"] = offs_stk.enter_context(nc.sbuf_tensor("offs", [1, 4], I32))

    def prog_copy(c):
        c.barrier()
        c.dma("gpsimd", t["offs"][:1, :], io["own_off"].ap(), inc=("pre", 16))
        c.w("gpsimd", "pre")
        if c.kind == "gpsimd":
            with c.eng.register("roff") as roff:
                for m in range(4):
                    c.eng.reg_load(roff, t["offs"][:1, m:m + 1])
                    i = c.eng.dma_start(
                        out=bass.AP(io["h1_own"], m * 512,
                                    [[2048, 2048], [1, 512]]),
                        in_=bass.AP(io["h1_full"], roff,
                                    [[T_FULL_B, 2048], [1, 512]]))
                    i.then_inc(c.sems["cpy"], 16)
        c.counts["cpy"] += 64
        c.w("gpsimd", "cpy")

    b.block(prog_copy)

    attn_stk = ExitStack()
    alloc_attn_extra(nc, attn_stk, t)
    prep_stk = ExitStack()
    alloc_psum(nc, prep_stk, t, sfx="P")
    alloc_attn_prep(nc, prep_stk, t, sfx="AP")

    def prog_prep(c):
        c.barrier()
        emit_attn_consts(c, io, t, shared=False)
        c.w("tensor", "pre")
        c.w("scalar", "pre")
        c.w("gpsimd", "pre")
        emit_attn_prep(c, io, t)

    b.block(prog_prep)
    prep_stk.close()
    for k in ("a_sb", "stat", "rbc", "mrbc", "hbuf", "hsq", "wsl", "wvb",
              "cosb", "sinb", "kn0a", "kn0b", "rp", "rrow", "vsb",
              "psA", "psB", "psC", "psD", "ps8"):
        t.pop(k, None)

    core_stk = ExitStack()
    alloc_attn_core(nc, core_stk, t, sfx="AC")

    def prog_core(c):
        c.barrier()
        c.dma("gpsimd", t["negI"][:, :], io["negI_i"].ap(), inc=("pre", 16))
        c.dma("gpsimd", t["bo"][:, :], io["bo_i"].ap(), inc=("pre", 16))
        c.dma("gpsimd", t["Rm"][:, :], io["Rm_i"].ap(), inc=("pre", 16))
        c.w("tensor", "pre")
        c.w("vector", "pre")
        emit_attn_core(c, io, t)

    b.block(prog_core)
    core_stk.close()
    attn_stk.close()
    for k in ("Vsl", "Tsb", "Psb", "Stb", "y_m", "lrow", "rbcs", "wsl", "h1t",
              "h2t", "negI", "bo", "Rm", "Ksb", "q_n", "qn_ws", "kn_ws",
              "qn_wn", "kn_wn", "psB", "psC", "psD"):
        t.pop(k, None)

    ffn2_stk = ExitStack()
    alloc_psum(nc, ffn2_stk, t, sfx="3")
    alloc_ffn(nc, ffn2_stk, t, sfx="F3")

    def prog_ffn3(c):
        c.barrier()
        emit_bd_tiles(c, t)
        for tc in range(2048 // TC):
            emit_ffn_chunk(c, io, t, 2048, tc * TC, "h2_own", "out_fm")
        c.w("gpsimd", "sta")
        c.w("gpsimd", "stb")

    b.block(prog_ffn3)
    ffn2_stk.close()
    offs_stk.close()
    stk.close()
    return nc


def kernel(**inputs):
    x = np.asarray(inputs["L0_attn_x"], np.float32)
    nc = build_fused_program()
    staged_f = stage_ffn_inputs(inputs)
    in_maps = []
    for c_ in range(8):
        b_, j_ = c_ // 4, c_ % 4
        m = dict(staged_f)
        m.update(stage_attn_inputs(inputs, j_))
        m["x_fm"] = np.ascontiguousarray(x[b_].T)
        m["own_off"] = np.array(
            [[(4 * mm + j_) * 512 for mm in range(4)]], np.int32)
        in_maps.append(m)
    res = run_bass_kernel_spmd(nc, in_maps, core_ids=list(range(8)))
    out = np.empty((2, T_FULL_B, D), np.float32)
    for c_ in range(8):
        b_, j_ = c_ // 4, c_ % 4
        idx = own_token_index(j_)
        out[b_][idx] = res.results[c_]["out_fm"].T
    return out



def kernel_unfused(**inputs):
    x = np.asarray(inputs["L0_attn_x"], np.float32)
    staged_f = stage_ffn_inputs(inputs)

    nc_f = build_ffn_program(2048)
    in_maps = []
    xf = x.reshape(-1, D)
    for c_ in range(8):
        m = dict(staged_f)
        m["h_in"] = np.ascontiguousarray(xf[c_ * 2048:(c_ + 1) * 2048].T)
        in_maps.append(m)
    res1 = run_bass_kernel_spmd(nc_f, in_maps, core_ids=list(range(8)))
    h1 = np.empty((2, T_FULL_B, D), np.float32)
    for c_ in range(8):
        h1.reshape(-1, D)[c_ * 2048:(c_ + 1) * 2048] = res1.results[c_]["h_out"].T

    nc_a = build_attn_program()
    in_maps = []
    for c_ in range(8):
        b_, j_ = c_ // 4, c_ % 4
        m = stage_attn_inputs(inputs, j_)
        m["h_full"] = np.ascontiguousarray(h1[b_].T)
        idx = own_token_index(j_)
        m["h_own"] = np.ascontiguousarray(h1[b_][idx].T)
        in_maps.append(m)
    res2 = run_bass_kernel_spmd(nc_a, in_maps, core_ids=list(range(8)))
    h2 = np.empty((2, T_FULL_B, D), np.float32)
    for c_ in range(8):
        b_, j_ = c_ // 4, c_ % 4
        idx = own_token_index(j_)
        h2[b_][idx] = res2.results[c_]["h2_own"].T

    in_maps = []
    h2f = h2.reshape(-1, D)
    for c_ in range(8):
        m = dict(staged_f)
        m["h_in"] = np.ascontiguousarray(h2f[c_ * 2048:(c_ + 1) * 2048].T)
        in_maps.append(m)
    res3 = run_bass_kernel_spmd(nc_f, in_maps, core_ids=list(range(8)))
    out = np.empty((2, T_FULL_B, D), np.float32)
    for c_ in range(8):
        out.reshape(-1, D)[c_ * 2048:(c_ + 1) * 2048] = \
            res3.results[c_]["h_out"].T
    return out



# version 07
# speedup vs baseline: 1.0648x; 1.0405x over previous
import math
from contextlib import ExitStack

import numpy as np
import ml_dtypes

import concourse.bass as bass
import concourse.mybir as mybir
from concourse.bass_utils import run_bass_kernel_spmd

F32 = mybir.dt.float32
BF16 = mybir.dt.bfloat16
I32 = mybir.dt.int32
AF = mybir.ActivationFunctionType
ALU = mybir.AluOpType

NPBF16 = ml_dtypes.bfloat16

D = 2048
INTER = 8192
NKT = 16
NTAU = 64
HD = 128
NH = 16
NKV = 4
C16 = 1.0 / (math.sqrt(2048.0) * 64.0)
C64 = 1.0 / (math.sqrt(8192.0) * 64.0)
LN_EPS = 1e-5
RMS_EPS = 1e-6
NEG_BIG = -1.0e5
ISQD = 1.0 / math.sqrt(128.0)
TC = 512


def _hadamard(n):
    h = np.array([[1.0]], dtype=np.float32)
    while h.shape[0] < n:
        h = np.block([[h, h], [h, -h]])
    return h.astype(np.float32)



ENGS = ("tensor", "scalar", "vector", "gpsimd", "sync")


class Ctx:

    def __init__(self, kind, sems):
        self.kind = kind
        self.eng = None
        self.sems = sems
        self.counts = {k: 0 for k in sems}
        self.guards = {}

    def _inc(self, instr, inc):
        if inc is None:
            return instr
        sem, n = inc
        self.counts[sem] += n
        if instr is not None:
            instr.then_inc(self.sems[sem], n)
        return instr

    def w(self, kind, sem, cnt=None):
        v = self.counts[sem] if cnt is None else cnt
        if self.kind == kind and v > 0:
            self.eng.wait_ge(self.sems[sem], v)

    def barrier(self):
        for sem, v in self.counts.items():
            if v > 0:
                self.eng.wait_ge(self.sems[sem], v)

    def guard(self, buf, sem):
        self.guards.setdefault(buf, {})[sem] = self.counts[sem]

    def await_guards(self, kind, buf):
        g = self.guards.get(buf)
        if not g:
            return
        if self.kind == kind:
            for sem, v in g.items():
                if v > 0:
                    self.eng.wait_ge(self.sems[sem], v)

    def mm(self, out, lhsT, rhs, start, stop, inc=None):
        i = self.eng.matmul(out, lhsT, rhs, start=start, stop=stop) \
            if self.kind == "tensor" else None
        return self._inc(i, inc)

    def act(self, out, in_, func, bias=0.0, scale=1.0, accum=None, inc=None):
        i = self.eng.activation(out, in_, func, bias=bias, scale=scale,
                                accum_out=accum) \
            if self.kind == "scalar" else None
        return self._inc(i, inc)

    def tt(self, out, a, b, op, inc=None):
        i = self.eng.tensor_tensor(out, a, b, op) \
            if self.kind == "vector" else None
        return self._inc(i, inc)

    def ts(self, out, a, s1, op0, s2=None, op1=ALU.bypass, inc=None):
        i = self.eng.tensor_scalar(out, a, s1, s2, op0, op1) \
            if self.kind == "vector" else None
        return self._inc(i, inc)

    def stt(self, out, in0, scalar, in1, op0, op1, inc=None):
        i = self.eng.scalar_tensor_tensor(out, in0, scalar, in1, op0, op1) \
            if self.kind == "vector" else None
        return self._inc(i, inc)

    def vcopy(self, out, in_, inc=None):
        i = self.eng.tensor_copy(out, in_) if self.kind == "vector" else None
        return self._inc(i, inc)

    def recip(self, out, in_, inc=None):
        i = self.eng.reciprocal(out, in_) if self.kind == "vector" else None
        return self._inc(i, inc)

    def memset(self, ap, v, inc=None):
        i = self.eng.memset(ap, v) if self.kind == "vector" else None
        return self._inc(i, inc)

    def dma(self, queue, out, in_, inc=None, slow=False):
        i = None
        if self.kind == queue:
            kw = {"allow_slow_non_contiguous": True} if slow else {}
            i = self.eng.dma_start(out=out, in_=in_, **kw)
        return self._inc(i, inc)


class Builder:

    def __init__(self, nc, sems):
        self.nc = nc
        self.ctxs = {k: Ctx(k, sems) for k in ENGS}

    def block(self, program):
        with self.nc.Block() as blk:
            for kind in ENGS:
                ctx = self.ctxs[kind]

                def mk(ctx):
                    def f(eng):
                        ctx.eng = eng
                        program(ctx)
                    return f
                getattr(blk, kind)(mk(ctx))



SEM_NAMES = ("pe", "act", "dve", "pre", "sv", "sto",
             "gla", "glb", "gsa", "gsb", "wla", "wlb", "wlc", "wld",
             "wfa", "wfb", "wfc", "wfd", "hla", "hlb", "rla", "rlb",
             "sta", "stb", "qla", "qlb")


def alloc_common(nc, stk):
    t = {}

    def sb(name, shape, dt):
        t[name] = stk.enter_context(nc.sbuf_tensor(name, shape, dt))
        return t[name]

    sb("H128c", [128, 128], BF16)
    sb("BD16p", [128, 128], F32)
    sb("BD64p", [128, 128], F32)
    sb("ones128f", [128, 1], F32)
    sb("ones1f", [1, 128], F32)
    sb("c64row", [1, 128], F32)
    sb("c32row", [1, 128], F32)
    sb("su_ug", [128, 16], F32)
    sb("su_dp", [128, 64], F32)
    sb("kvals", [128, 8], F32)
    sb("ones128b", [128, 1], BF16)
    return t


def alloc_psum(nc, stk, t, sfx=""):
    def ps(name, shape):
        t[name] = stk.enter_context(nc.psum_tensor(name + sfx, shape, F32))
        return t[name]
    t["psA"] = [ps("psA0", [128, 512]), ps("psA1", [128, 512])]
    t["psB"] = [ps("psB0", [128, 512]), ps("psB1", [128, 512])]
    t["psC"] = [ps("psC0", [128, 512]), ps("psC1", [128, 512])]
    t["psD"] = [ps("psD0", [128, 512]), ps("psD1", [128, 512])]
    t["ps8"] = t["psA"] + t["psB"] + t["psC"] + t["psD"]


def alloc_fold(nc, stk, t):
    def sb(name, shape, dt):
        t[name] = stk.enter_context(nc.sbuf_tensor(name, shape, dt))
        return t[name]
    sb("sbc", [128, INTER], F32)
    sb("bdt", [128, 128], BF16)
    t["win"] = [sb("win0", [128, INTER], BF16), sb("win1", [128, INTER], BF16)]
    t["wout"] = [sb("wout0", [128, INTER], BF16),
                 sb("wout1", [128, INTER], BF16)]
    sb("svbuf", [1, 512], F32)


def alloc_ffn(nc, stk, t, sfx=""):
    def sb(name, shape, dt):
        t[name] = stk.enter_context(nc.sbuf_tensor(name + sfx, shape, dt))
        return t[name]
    sb("BD16", [128, 16 * 128], BF16)
    sb("u", [128, NTAU * TC], BF16)
    t["xn"] = t["u"][:, :NKT * TC]
    sb("z", [128, NKT * TC], BF16)
    t["y"] = [sb("y0", [128, TC], BF16), sb("y1", [128, TC], BF16)]
    t["gsil"] = [sb("gsil0", [128, TC], F32), sb("gsil1", [128, TC], F32)]
    sb("stat", [1, 4 * TC], F32)
    sb("rbc", [128, TC], F32)
    sb("mrbc", [128, TC], F32)
    t["hbuf"] = [sb("hbuf0", [128, TC], F32), sb("hbuf1", [128, TC], F32)]
    t["hsq"] = [sb("hsq0", [128, TC], F32), sb("hsq1", [128, TC], F32)]
    t["wu_sl"] = [sb("wu_sl0", [128, 16 * 512], BF16),
                  sb("wu_sl1", [128, 16 * 512], BF16)]
    t["wg_sl"] = [sb("wg_sl0", [128, 16 * 512], BF16),
                  sb("wg_sl1", [128, 16 * 512], BF16)]
    t["bd_sl"] = [sb("bd_sl0", [128, 512], BF16),
                  sb("bd_sl1", [128, 512], BF16)]
    t["wf_sl"] = [sb("wf_sl0", [128, 1024], BF16),
                  sb("wf_sl1", [128, 1024], BF16)]
    t["rbuf"] = [sb("rbuf0", [128, 512], F32), sb("rbuf1", [128, 512], F32)]
    t["hnx"] = [sb("hnx0", [128, 512], F32), sb("hnx1", [128, 512], F32)]


def emit_load_consts(c, io, t):
    c.dma("gpsimd", t["H128c"][:, :], io["H128c_i"].ap(), inc=("pre", 16))
    c.dma("gpsimd", t["BD16p"][:, :], io["BD16p_i"].ap(), inc=("pre", 16))
    c.dma("gpsimd", t["BD64p"][:, :], io["BD64p_i"].ap(), inc=("pre", 16))
    c.dma("gpsimd", t["su_ug"][:, :], io["su_ug_cols"].ap(), inc=("pre", 16))
    c.dma("gpsimd", t["su_dp"][:, :], io["su_dp_cols"].ap(), inc=("pre", 16))
    c.memset(t["ones128f"][:, :], 1.0)
    c.memset(t["ones1f"][:1, :], 1.0)
    c.memset(t["c64row"][:1, :], 64.0)
    c.memset(t["kvals"][:, 0:1], LN_EPS)
    c.memset(t["kvals"][:, 1:2], RMS_EPS)
    c.memset(t["kvals"][:, 2:3], 50.0)
    c.memset(t["kvals"][:, 3:4], ISQD / 50.0)
    c.memset(t["kvals"][:, 4:5], 1.0 / 128.0)
    c.memset(t["ones128b"][:, :], 1.0)
    c.memset(t["c32row"][:1, :], 32.0, inc=("dve", 1))
    c.w("vector", "pre")


def emit_bd_tiles(c, t):
    c.w("vector", "pre")
    for k in range(16):
        c.ts(t["BD16"][:, k * 128:(k + 1) * 128], t["BD16p"][:, :],
             t["su_ug"][:, k:k + 1], ALU.mult,
             inc=("dve", 1) if k == 15 else None)


def emit_weight_fold(c, io, t):
    def fold(src_h, dst_h, sv_h, n512, n_slabs, scale_lhs, dst_ap_fn):
        for i in range(n512):
            c.w("gpsimd", "pe")
            c.dma("gpsimd", t["svbuf"][:1, :],
                  bass.AP(io[sv_h], i * 512, [[512, 1], [1, 512]]),
                  inc=("sv", 16))
            c.w("tensor", "pre")
            c.w("tensor", "sv")
            c.w("tensor", "dve")
            c.mm(t["psD"][0][:, :], t[scale_lhs][:1, :], t["svbuf"][:1, :],
                 True, True, inc=("pe", 1))
            c.w("vector", "pe")
            c.vcopy(t["sbc"][:, i * 512:(i + 1) * 512], t["psD"][0][:, :],
                    inc=("dve", 1))
        c.w("vector", "dve")
        for sl in range(n_slabs):
            bi = sl % 2
            gl = "gla" if bi == 0 else "glb"
            gs = "gsa" if bi == 0 else "gsb"
            c.await_guards("gpsimd", "win" + str(bi))
            c.dma("gpsimd", t["win"][bi][:, :n512 * 512],
                  bass.AP(io[src_h], sl * 128 * n512 * 512,
                          [[n512 * 512, 128], [1, n512 * 512]]),
                  inc=(gl, 16))
            c.w("tensor", gl)
            tt_ct = {}
            for n in range(n512):
                if n - 2 in tt_ct:
                    c.w("tensor", "dve", tt_ct[n - 2])
                else:
                    c.w("tensor", "dve")
                c.mm(t["psD"][n % 2][:, :], t["H128c"][:, :],
                     t["win"][bi][:, n * 512:(n + 1) * 512], True, True,
                     inc=("pe", 1))
                c.w("vector", "pe")
                c.await_guards("vector", "wout" + str(bi))
                c.tt(t["wout"][bi][:, n * 512:(n + 1) * 512],
                     t["psD"][n % 2][:, :],
                     t["sbc"][:, n * 512:(n + 1) * 512], ALU.mult,
                     inc=("dve", 1))
                tt_ct[n] = c.counts["dve"]
            c.guard("win" + str(bi), "pe")
            c.w("gpsimd", "dve")
            c.dma("gpsimd", dst_ap_fn(sl), t["wout"][bi][:, :n512 * 512],
                  inc=(gs, 16))
            c.guard("wout" + str(bi), gs)

    fold("Wu_st", "WU_hbm", "ws_up", 16, 16, "c64row",
         lambda sl: bass.AP(io["WU_hbm"], sl * 512,
                            [[8192, 128], [1048576, 16], [1, 512]]))
    fold("Wg_st", "WG_hbm", "ws_gate", 16, 16, "c64row",
         lambda sl: bass.AP(io["WG_hbm"], sl * 512,
                            [[8192, 128], [1048576, 16], [1, 512]]))
    fold("Wd_st", "WF_hbm", "ws_dn", 4, 64, "c32row",
         lambda sl: bass.AP(io["WF_hbm"], sl * 2048,
                            [[131072, 128], [1, 2048]]))
    c.w("vector", "pre")
    for k in range(64):
        c.await_guards("vector", "bdt")
        c.ts(t["bdt"][:, :], t["BD64p"][:, :], t["su_dp"][:, k:k + 1],
             ALU.mult, inc=("dve", 1))
        c.w("gpsimd", "dve")
        c.dma("gpsimd", bass.AP(io["BD64_hbm"], k * 128,
                                [[8192, 128], [1, 128]]),
              t["bdt"][:, :], inc=("gsa", 16))
        c.guard("bdt", "gsa")


def emit_ln(c, t, h_ap_fn, out_fn):
    c.w("tensor", "dve")
    for kt in range(NKT):
        bi = kt % 2
        hl = "hla" if bi == 0 else "hlb"
        c.await_guards("gpsimd", "hbuf" + str(bi))
        c.dma("gpsimd", t["hbuf"][bi][:, :], h_ap_fn(kt), inc=(hl, 16))
        c.w("scalar", hl)
        c.await_guards("scalar", "hsq" + str(bi))
        c.act(t["hsq"][bi][:, :], t["hbuf"][bi][:, :], AF.Square,
              inc=("act", 1))
        c.w("tensor", "act")
        c.w("tensor", hl)
        c.mm(t["psA"][0][:1, :], t["ones128f"][:, :], t["hbuf"][bi][:, :],
             start=(kt == 0), stop=(kt == NKT - 1))
        c.mm(t["psB"][0][:1, :], t["ones128f"][:, :], t["hsq"][bi][:, :],
             start=(kt == 0), stop=(kt == NKT - 1), inc=("pe", 1))
        c.guard("hbuf" + str(bi), "pe")
        c.guard("hsq" + str(bi), "pe")
    st = t["stat"]
    c.w("vector", "pe")
    c.ts(st[:1, 0:TC], t["psA"][0][:1, :], 1.0 / 2048.0, ALU.mult,
         inc=("dve", 1))
    c.ts(st[:1, TC:2 * TC], t["psB"][0][:1, :], 1.0 / 2048.0, ALU.mult,
         inc=("dve", 1))
    c.w("vector", "dve")
    c.tt(st[:1, 2 * TC:3 * TC], st[:1, 0:TC], st[:1, 0:TC], ALU.mult,
         inc=("dve", 1))
    c.w("vector", "dve")
    c.tt(st[:1, 2 * TC:3 * TC], st[:1, TC:2 * TC], st[:1, 2 * TC:3 * TC],
         ALU.subtract, inc=("dve", 1))
    c.w("scalar", "dve")
    c.act(st[:1, 2 * TC:3 * TC], st[:1, 2 * TC:3 * TC], AF.Sqrt,
          bias=t["kvals"][:1, 0:1], inc=("act", 1))
    c.w("vector", "act")
    c.recip(st[:1, 2 * TC:3 * TC], st[:1, 2 * TC:3 * TC], inc=("dve", 1))
    c.w("vector", "dve")
    c.tt(st[:1, 3 * TC:4 * TC], st[:1, 0:TC], st[:1, 2 * TC:3 * TC],
         ALU.mult, inc=("dve", 1))
    c.w("tensor", "dve")
    c.mm(t["psA"][0][:, :], t["ones1f"][:1, :], st[:1, 2 * TC:3 * TC],
         True, True)
    c.mm(t["psB"][0][:, :], t["ones1f"][:1, :], st[:1, 3 * TC:4 * TC],
         True, True, inc=("pe", 1))
    c.w("vector", "pe")
    c.vcopy(t["rbc"][:, :], t["psA"][0][:, :], inc=("dve", 1))
    c.vcopy(t["mrbc"][:, :], t["psB"][0][:, :], inc=("dve", 1))
    c.w("vector", "pe")
    for kt in range(NKT):
        bi = kt % 2
        hl = "hla" if bi == 0 else "hlb"
        c.await_guards("gpsimd", "hbuf" + str(bi))
        c.dma("gpsimd", t["hbuf"][bi][:, :], h_ap_fn(kt), inc=(hl, 16))
        c.w("vector", hl)
        c.w("vector", "dve")
        c.await_guards("vector", "hsq" + str(bi))
        c.tt(t["hsq"][bi][:, :], t["hbuf"][bi][:, :], t["rbc"][:, :],
             ALU.mult, inc=("dve", 1))
        c.w("vector", "dve")
        c.tt(out_fn(kt), t["hsq"][bi][:, :], t["mrbc"][:, :], ALU.subtract,
             inc=("dve", 1))
        c.guard("hbuf" + str(bi), "dve")
        c.guard("hsq" + str(bi), "dve")


def emit_ffn_chunk(c, io, t, T, row_b, h_in, h_out, dbg=None):

    def h_ap(kt):
        return bass.AP(io[h_in], kt * 128 * T + row_b, [[T, 128], [1, TC]])

    def dump(name, src_ap):
        if dbg is None or name not in dbg:
            return
        c.w("gpsimd", "dve")
        c.w("gpsimd", "act")
        c.w("gpsimd", "pe")
        c.dma("gpsimd", dbg[name].ap(), src_ap, inc=("sto", 16))

    c.w("vector", "pe")
    emit_ln(c, t, h_ap, lambda kt: t["xn"][:, kt * TC:(kt + 1) * TC])
    dump("xn", t["xn"][:, :])

    c.w("tensor", "pre")
    c.w("tensor", "act")
    c.w("tensor", "dve")
    zev_ct = {}
    for kt in range(NKT):
        pb = kt % 2
        if kt - 2 in zev_ct:
            c.w("tensor", "act", zev_ct[kt - 2])
        c.mm(t["psA"][pb][:, :], t["BD16"][:, kt * 128:(kt + 1) * 128],
             t["xn"][:, kt * TC:(kt + 1) * TC], True, True, inc=("pe", 1))
        c.w("scalar", "pe")
        c.act(t["z"][:, kt * TC:(kt + 1) * TC], t["psA"][pb][:, :], AF.Copy,
              inc=("act", 1))
        zev_ct[kt] = c.counts["act"]
    dump("z", t["z"][:, :])

    upgate_ct = {}
    bd64_ct = {}
    silu_ct = {}
    ucopy_ct = {}
    ymul_ct = {}

    def emit_bd64(tau):
        pp = tau % 2
        qb = (tau // 4) % 2
        c.w("tensor", "dve", ymul_ct[tau])
        if tau - 2 in ucopy_ct:
            c.w("tensor", "act", ucopy_ct[tau - 2])
        c.mm(t["psC"][pp][:, :],
             t["bd_sl"][qb][:, (tau % 4) * 128:(tau % 4 + 1) * 128],
             t["y"][pp][:, :], True, True, inc=("pe", 1))
        if tau % 4 == 3:
            c.guard("bd" + str(qb), "pe")
        bd64_ct[tau] = c.counts["pe"]
        c.w("scalar", "pe", bd64_ct[tau])
        c.act(t["u"][:, tau * TC:(tau + 1) * TC], t["psC"][pp][:, :], AF.Copy,
              inc=("act", 1))
        ucopy_ct[tau] = c.counts["act"]

    for tau in range(NTAU):
        pp = tau % 2
        e4 = tau % 4
        qb = (tau // 4) % 2
        if e4 == 0:
            wl = ("wla", "wlb", "wlc", "wld")[(tau // 4) % 4]
            c.await_guards("sync", "wu" + str(qb))
            c.dma("sync", t["wu_sl"][qb][:, :],
                  bass.AP(io["WU_hbm"], (tau // 4) * 1048576,
                          [[512, 128], [65536, 16], [1, 512]]),
                  inc=(wl, 16))
            c.dma("sync", t["wg_sl"][qb][:, :],
                  bass.AP(io["WG_hbm"], (tau // 4) * 1048576,
                          [[512, 128], [65536, 16], [1, 512]]),
                  inc=(wl, 16))
            c.await_guards("sync", "bd" + str(qb))
            c.dma("sync", t["bd_sl"][qb][:, :],
                  bass.AP(io["BD64_hbm"], (tau // 4) * 512,
                          [[8192, 128], [1, 512]]),
                  inc=(wl, 16))
            c.w("tensor", wl)
        if tau - 2 in silu_ct:
            c.w("tensor", "act", silu_ct[tau - 2])
            c.w("tensor", "dve", ymul_ct[tau - 2])
        else:
            c.w("tensor", "act")
            c.w("tensor", "dve")
        for kt in range(NKT):
            zs = t["z"][:, kt * TC:(kt + 1) * TC]
            c.mm(t["psA"][pp][:, :],
                 t["wu_sl"][qb][:, kt * 512 + e4 * 128:kt * 512 + (e4 + 1) * 128],
                 zs, start=(kt == 0), stop=(kt == NKT - 1))
            c.mm(t["psB"][pp][:, :],
                 t["wg_sl"][qb][:, kt * 512 + e4 * 128:kt * 512 + (e4 + 1) * 128],
                 zs, start=(kt == 0), stop=(kt == NKT - 1),
                 inc=("pe", 1) if kt == NKT - 1 else None)
        upgate_ct[tau] = c.counts["pe"]
        if e4 == 3:
            c.guard("wu" + str(qb), "pe")
        if tau - 1 in ymul_ct:
            emit_bd64(tau - 1)
        c.w("scalar", "pe", upgate_ct[tau])
        if tau - 2 in ymul_ct:
            c.w("scalar", "dve", ymul_ct[tau - 2])
        c.act(t["gsil"][pp][:, :], t["psB"][pp][:, :], AF.Silu, inc=("act", 1))
        silu_ct[tau] = c.counts["act"]
        c.w("vector", "act", silu_ct[tau])
        if tau - 2 in bd64_ct:
            c.w("vector", "pe", bd64_ct[tau - 2])
        c.tt(t["y"][pp][:, :], t["psA"][pp][:, :], t["gsil"][pp][:, :],
             ALU.mult, inc=("dve", 1))
        ymul_ct[tau] = c.counts["dve"]
    emit_bd64(NTAU - 1)

    for half in range(2):
        for t2 in range(NTAU):
            bi = t2 % 2
            wf = ("wfa", "wfb", "wfc", "wfd")[t2 % 4]
            c.await_guards("sync", "wf" + str(bi))
            c.dma("sync", t["wf_sl"][bi][:, :],
                  bass.AP(io["WF_hbm"], t2 * 262144 + half * 1024,
                          [[2048, 128], [1, 1024]]),
                  inc=(wf, 16))
            c.w("tensor", wf)
            if t2 == 0:
                c.w("tensor", "dve")
                c.w("tensor", "act")
            us = t["u"][:, t2 * TC:(t2 + 1) * TC]
            for cc in range(8):
                c.mm(t["ps8"][cc][:, :],
                     t["wf_sl"][bi][:, cc * 128:(cc + 1) * 128], us,
                     start=(t2 == 0), stop=(t2 == NTAU - 1),
                     inc=("pe", 1) if cc == 7 else None)
            c.guard("wf" + str(bi), "pe")
        c.w("gpsimd", "pe")
        c.w("vector", "pe")
        for cc in range(8):
            co = half * 8 + cc
            bi2 = cc % 2
            rl = "rla" if bi2 == 0 else "rlb"
            st_ = "sta" if bi2 == 0 else "stb"
            c.await_guards("gpsimd", "rbuf" + str(bi2))
            c.dma("gpsimd", t["rbuf"][bi2][:, :],
                  bass.AP(io[h_in], co * 128 * T + row_b, [[T, 128], [1, TC]]),
                  inc=(rl, 16))
            c.w("vector", rl)
            c.await_guards("vector", "hnx" + str(bi2))
            c.tt(t["hnx"][bi2][:, :], t["ps8"][cc][:, :], t["rbuf"][bi2][:, :],
                 ALU.add, inc=("dve", 1))
            c.guard("rbuf" + str(bi2), "dve")
            c.w("gpsimd", "dve")
            c.dma("gpsimd",
                  bass.AP(io[h_out], co * 128 * T + row_b, [[T, 128], [1, TC]]),
                  t["hnx"][bi2][:, :], inc=(st_, 16))
            c.guard("hnx" + str(bi2), st_)



def declare_ffn_io(nc, T, debug_w=False):
    io = {}
    io["h_in"] = nc.dram_tensor("h_in", [D, T], F32, kind="ExternalInput")
    io["h_out"] = nc.dram_tensor("h_out", [D, T], F32, kind="ExternalOutput")
    for nm, shp in (("Wu_st", [16, 128, INTER]), ("Wg_st", [16, 128, INTER]),
                    ("Wd_st", [64, 128, D])):
        io[nm] = nc.dram_tensor(nm, shp, F32, kind="ExternalInput")
    for nm, n in (("ws_up", INTER), ("ws_gate", INTER), ("ws_dn", D)):
        io[nm] = nc.dram_tensor(nm, [1, n], F32, kind="ExternalInput")
    io["su_ug_cols"] = nc.dram_tensor("su_ug_cols", [128, 16], F32,
                                      kind="ExternalInput")
    io["su_dp_cols"] = nc.dram_tensor("su_dp_cols", [128, 64], F32,
                                      kind="ExternalInput")
    io["H128c_i"] = nc.dram_tensor("H128c_i", [128, 128], BF16,
                                   kind="ExternalInput")
    io["BD16p_i"] = nc.dram_tensor("BD16p_i", [128, 128], F32,
                                   kind="ExternalInput")
    io["BD64p_i"] = nc.dram_tensor("BD64p_i", [128, 128], F32,
                                   kind="ExternalInput")
    kw = {"kind": "ExternalOutput"} if debug_w else {}
    io["WU_hbm"] = nc.dram_tensor("WU_hbm", [NTAU * 16 * 128 * 128], BF16, **kw)
    io["WG_hbm"] = nc.dram_tensor("WG_hbm", [NTAU * 16 * 128 * 128], BF16, **kw)
    io["WF_hbm"] = nc.dram_tensor("WF_hbm", [16 * NTAU * 128 * 128], BF16, **kw)
    io["BD64_hbm"] = nc.dram_tensor("BD64_hbm", [128 * 8192], BF16)
    return io


def build_ffn_program(T, debug_w=False, debug_mid=False):
    nc = bass.Bass("TRN2", target_bir_lowering=False)
    stk = ExitStack()
    io = declare_ffn_io(nc, T, debug_w)
    sems = {n: stk.enter_context(nc.semaphore(n)) for n in SEM_NAMES}
    t = alloc_common(nc, stk)
    alloc_psum(nc, stk, t)
    b = Builder(nc, sems)

    fold_stk = ExitStack()
    alloc_fold(nc, fold_stk, t)

    def prog_fold(c):
        emit_load_consts(c, io, t)
        emit_weight_fold(c, io, t)

    b.block(prog_fold)
    fold_stk.close()

    alloc_ffn(nc, stk, t)

    dbg = None
    if debug_mid:
        dbg = {
            "xn": nc.dram_tensor("dbg_xn", [128, NKT * TC], BF16,
                                 kind="ExternalOutput"),
            "z": nc.dram_tensor("dbg_z", [128, NKT * TC], BF16,
                                kind="ExternalOutput"),
            "u": nc.dram_tensor("dbg_u", [128, NTAU * TC], BF16,
                                kind="ExternalOutput"),
        }

    def prog_main(c):
        c.barrier()
        emit_bd_tiles(c, t)
        for tc in range(T // TC):
            emit_ffn_chunk(c, io, t, T, tc * TC, "h_in", "h_out",
                           dbg=dbg if tc == 0 else None)
        c.w("gpsimd", "sta")
        c.w("gpsimd", "stb")
        c.w("gpsimd", "sto")

    b.block(prog_main)
    stk.close()
    return nc



def stage_ffn_inputs(ins):
    H16 = _hadamard(16)
    H64 = _hadamard(64)
    H128 = _hadamard(128)
    W_up = np.asarray(ins["L0_ffn_W_up"], dtype=np.float32)
    W_gate = np.asarray(ins["L0_ffn_W_gate"], dtype=np.float32)
    W_down = np.asarray(ins["L0_ffn_W_down"], dtype=np.float32)
    ws_ug = np.asarray(ins["L0_ffn_Wscale_ug"], dtype=np.float32)
    return {
        "Wu_st": np.ascontiguousarray(
            W_up.reshape(INTER, 128, 16).transpose(2, 1, 0)),
        "Wg_st": np.ascontiguousarray(
            W_gate.reshape(INTER, 128, 16).transpose(2, 1, 0)),
        "Wd_st": np.ascontiguousarray(
            W_down.reshape(D, 128, 64).transpose(2, 1, 0)),
        "ws_up": np.ascontiguousarray(ws_ug[:INTER].reshape(1, INTER)),
        "ws_gate": np.ascontiguousarray(ws_ug[INTER:].reshape(1, INTER)),
        "ws_dn": np.ascontiguousarray(
            np.asarray(ins["L0_ffn_Wscale_dp"], np.float32).reshape(1, D)),
        "su_ug_cols": np.ascontiguousarray(
            np.asarray(ins["L0_ffn_SU_ug"], np.float32).reshape(16, 128).T),
        "su_dp_cols": np.ascontiguousarray(
            np.asarray(ins["L0_ffn_SU_dp"], np.float32).reshape(64, 128).T),
        "H128c_i": H128.astype(NPBF16),
        "BD16p_i": np.ascontiguousarray(
            np.kron(np.eye(8, dtype=np.float32), H16) * C16),
        "BD64p_i": np.ascontiguousarray(
            np.kron(np.eye(2, dtype=np.float32), H64) * C64),
    }



ATTN_SEMS = SEM_NAMES + ("vla", "vlb", "rpa", "rpb")


def alloc_attn_common(nc, stk):
    t = {}

    def sb(name, shape, dt):
        t[name] = stk.enter_context(nc.sbuf_tensor(name, shape, dt))
        return t[name]

    sb("ones128f", [128, 1], F32)
    sb("ones1f", [1, 128], F32)
    sb("ones128b", [128, 1], BF16)
    sb("kvals", [128, 8], F32)
    sb("qn_ws", [64, 2], F32)
    sb("kn_ws", [64, 2], F32)
    sb("qn_wn", [64, 2], F32)
    sb("kn_wn", [64, 2], F32)
    sb("Ksb", [128, 4 * 8192], BF16)
    sb("q_n", [128, 16 * 2048], BF16)
    return t


def alloc_attn_prep(nc, stk, t, sfx=""):
    def sb(name, shape, dt):
        t[name] = stk.enter_context(nc.sbuf_tensor(name + sfx, shape, dt))
        return t[name]
    sb("a_sb", [128, 16 * 512], BF16)
    sb("stat", [1, 4 * TC], F32)
    sb("rbc", [128, TC], F32)
    sb("mrbc", [128, TC], F32)
    t["hbuf"] = [sb("hbuf0", [128, TC], F32), sb("hbuf1", [128, TC], F32)]
    t["hsq"] = [sb("hsq0", [128, TC], F32), sb("hsq1", [128, TC], F32)]
    t["wsl"] = [sb("wsl0", [128, 16 * 128], BF16),
                sb("wsl1", [128, 16 * 128], BF16)]
    t["wvb"] = [sb("wvb0", [128, 512], BF16), sb("wvb1", [128, 512], BF16)]
    t["cosb"] = sb("cosb", [64, 512], F32)
    t["sinb"] = sb("sinb", [64, 512], F32)
    sb("kn0a", [64, 512], F32)
    sb("kn0b", [64, 512], F32)
    t["rp"] = [sb("rp1", [64, 512], F32), sb("rp2", [64, 512], F32),
               sb("rp3", [64, 512], F32), sb("rp4", [64, 512], F32)]
    sb("rrow", [1, 512], F32)
    sb("vsb", [128, 512], BF16)


def alloc_attn_core(nc, stk, t, sfx=""):
    def sb(name, shape, dt):
        t[name] = stk.enter_context(nc.sbuf_tensor(name + sfx, shape, dt))
        return t[name]
    sb("Vsl", [128, 64 * 128], BF16)
    sb("negI", [128, 128], BF16)
    sb("bo", [128, 16], F32)
    sb("Rm", [128, 16 * 512], BF16)
    sb("Tsb", [128, 1024], F32)
    t["Psb"] = [sb("Psb0", [128, 1024], BF16), sb("Psb1", [128, 1024], BF16)]
    t["Stb"] = [stk.enter_context(nc.psum_tensor("Stb0" + sfx, [128, 1024], F32)),
                stk.enter_context(nc.psum_tensor("Stb1" + sfx, [128, 1024], F32))]
    pb0 = stk.enter_context(nc.psum_tensor("pcb0" + sfx, [128, 512], F32))
    pb1 = stk.enter_context(nc.psum_tensor("pcb1" + sfx, [128, 512], F32))
    pc0 = stk.enter_context(nc.psum_tensor("pcc0" + sfx, [128, 512], F32))
    pd0 = stk.enter_context(nc.psum_tensor("pcd0" + sfx, [128, 512], F32))
    t["psB"] = [pb0, pb1]
    t["psC"] = [pc0, pc0]
    t["psD"] = [pd0, pd0]
    sb("y_m", [128, 16 * 512], BF16)
    sb("lrow", [1, 512], F32)
    sb("rbcs", [128, 512], F32)
    t["wsl"] = [sb("wslA", [128, 16 * 128], BF16),
                sb("wslB", [128, 16 * 128], BF16)]
    t["h1t"] = [sb("h1t0", [128, 512], F32), sb("h1t1", [128, 512], F32)]
    t["h2t"] = [sb("h2t0", [128, 512], F32), sb("h2t1", [128, 512], F32)]


def emit_attn_consts(c, io, t, shared=True):
    c.dma("gpsimd", t["qn_ws"][:, :], io["qn_w"].ap(), inc=("pre", 16))
    c.dma("gpsimd", t["kn_ws"][:, :], io["kn_w"].ap(), inc=("pre", 16))
    if shared:
        c.memset(t["ones128f"][:, :], 1.0)
        c.memset(t["ones1f"][:1, :], 1.0)
        c.memset(t["ones128b"][:, :], 1.0)
        c.memset(t["kvals"][:, 0:1], LN_EPS)
        c.memset(t["kvals"][:, 1:2], RMS_EPS)
        c.memset(t["kvals"][:, 2:3], 50.0)
        c.memset(t["kvals"][:, 3:4], ISQD / 50.0)
        c.memset(t["kvals"][:, 4:5], 1.0 / 128.0, inc=("dve", 1))
    else:
        c._inc(c.eng.engine_nop() if c.kind == "vector" else None, ("dve", 1))
    c.w("vector", "pre")
    c.w("vector", "dve")
    c.ts(t["qn_wn"][:, :], t["qn_ws"][:, :], -1.0, ALU.mult)
    c.ts(t["kn_wn"][:, :], t["kn_ws"][:, :], -1.0, ALU.mult, inc=("dve", 1))


def emit_norm_rope(c, t, ps_src, w, wn, cos_ap, sin_ap, out_lo, out_hi):
    rp = t["rp"]
    c.act(t["hsq"][0][:, :], ps_src, AF.Square, inc=("act", 1))
    c.w("tensor", "act")
    c.mm(t["psB"][0][:1, :], t["ones128f"][:, :], t["hsq"][0][:, :],
         True, True, inc=("pe", 1))
    c.w("vector", "pe")
    c.ts(t["rrow"][:1, :], t["psB"][0][:1, :], 1.0 / 128.0, ALU.mult,
         inc=("dve", 1))
    c.w("scalar", "dve")
    c.act(t["rrow"][:1, :], t["rrow"][:1, :], AF.Sqrt, inc=("act", 1))
    c.w("vector", "act")
    c.ts(t["rrow"][:1, :], t["rrow"][:1, :], RMS_EPS, ALU.add, inc=("dve", 1))
    c.w("vector", "dve")
    c.recip(t["rrow"][:1, :], t["rrow"][:1, :], inc=("dve", 1))
    c.w("tensor", "dve")
    c.mm(t["psB"][1][:, :], t["ones1f"][:1, :], t["rrow"][:1, :], True, True,
         inc=("pe", 1))
    c.w("vector", "pe")
    c.vcopy(t["rbc"][:, :512], t["psB"][1][:, :], inc=("dve", 1))
    c.w("vector", "dve")
    c.tt(t["kn0a"][:, :], ps_src[0:64, :], t["rbc"][0:64, :512], ALU.mult,
         inc=("dve", 1))
    c.tt(t["kn0b"][:, :], ps_src[64:128, :], t["rbc"][64:128, :512], ALU.mult,
         inc=("dve", 1))
    c.w("vector", "dve")
    k1 = t["kn0a"][:, :]
    k2 = t["kn0b"][:, :]
    c.tt(rp[0][:, :], k1, cos_ap, ALU.mult, inc=("dve", 1))
    c.tt(rp[1][:, :], k2, sin_ap, ALU.mult, inc=("dve", 1))
    c.w("vector", "dve")
    c.ts(rp[2][:, :], rp[1][:, :], w[:, 1:2], ALU.mult, inc=("dve", 1))
    c.w("vector", "dve")
    c.stt(out_lo, rp[0][:, :], w[:, 0:1], rp[2][:, :], ALU.mult, ALU.add,
          inc=("dve", 1))
    c.tt(rp[1][:, :], k2, cos_ap, ALU.mult, inc=("dve", 1))
    c.tt(rp[3][:, :], k1, sin_ap, ALU.mult, inc=("dve", 1))
    c.w("vector", "dve")
    c.ts(rp[2][:, :], rp[1][:, :], w[:, 1:2], ALU.mult, inc=("dve", 1))
    c.w("vector", "dve")
    c.stt(out_hi, rp[3][:, :], wn[:, 0:1], rp[2][:, :], ALU.mult, ALU.add,
          inc=("dve", 1))


def emit_attn_prep(c, io, t):

    for g in range(16):
        def h_ap(kt, g=g):
            return bass.AP(io["h_full"], kt * 128 * T_FULL_B + g * 512,
                           [[T_FULL_B, 128], [1, TC]])
        emit_ln(c, t, h_ap,
                lambda kt: t["a_sb"][:, kt * 512:(kt + 1) * 512])
        rp_s = "rpa" if g % 2 == 0 else "rpb"
        c.await_guards("gpsimd", "ropeb")
        c.dma("gpsimd", t["cosb"][:, :],
              bass.AP(io["cosk_i"], g * 512, [[8192, 64], [1, 512]]),
              inc=(rp_s, 16))
        c.dma("gpsimd", t["sinb"][:, :],
              bass.AP(io["sink_i"], g * 512, [[8192, 64], [1, 512]]),
              inc=(rp_s, 16))
        for dt_ in range(4):
            wl = "qla" if dt_ % 2 == 0 else "qlb"
            c.await_guards("gpsimd", "wsl" + str(dt_ % 2))
            c.dma("gpsimd", t["wsl"][dt_ % 2][:, :],
                  bass.AP(io["Wk_T"], dt_ * 262144,
                          [[2048, 128], [1, 2048]]),
                  inc=(wl, 16))
            c.w("tensor", wl)
            c.w("tensor", "act")
            c.w("tensor", "dve")
            for kt in range(NKT):
                c.mm(t["psA"][dt_ % 2][:, :],
                     t["wsl"][dt_ % 2][:, kt * 128:(kt + 1) * 128],
                     t["a_sb"][:, kt * 512:(kt + 1) * 512],
                     start=(kt == 0), stop=(kt == NKT - 1),
                     inc=("pe", 1) if kt == NKT - 1 else None)
            c.guard("wsl" + str(dt_ % 2), "pe")
            c.w("scalar", "pe")
            c.w("scalar", "dve")
            c.w("vector", rp_s)
            emit_norm_rope(
                c, t, t["psA"][dt_ % 2][:, :], t["kn_ws"], t["kn_wn"],
                t["cosb"][:, :], t["sinb"][:, :],
                t["Ksb"][0:64, dt_ * 8192 + g * 512:dt_ * 8192 + (g + 1) * 512],
                t["Ksb"][64:128, dt_ * 8192 + g * 512:dt_ * 8192 + (g + 1) * 512])
        c.guard("ropeb", "dve")
        for kt in range(NKT):
            wv = "vla" if kt % 2 == 0 else "vlb"
            c.await_guards("gpsimd", "wvb" + str(kt % 2))
            c.dma("gpsimd", t["wvb"][kt % 2][:, :],
                  bass.AP(io["Wv_T"], kt * 128 * 512, [[512, 128], [1, 512]]),
                  inc=(wv, 16))
            c.w("tensor", wv)
            if kt == 0:
                c.w("tensor", "act")
            for ts_ in range(4):
                c.mm(t["psC"][ts_ % 2][:, :] if ts_ < 2 else t["psD"][ts_ % 2][:, :],
                     t["a_sb"][:, kt * 512 + ts_ * 128:kt * 512 + (ts_ + 1) * 128],
                     t["wvb"][kt % 2][:, :],
                     start=(kt == 0), stop=(kt == NKT - 1),
                     inc=("pe", 1) if ts_ == 3 else None)
            c.guard("wvb" + str(kt % 2), "pe")
        c.w("scalar", "pe")
        for ts_ in range(4):
            ps = t["psC"][ts_ % 2] if ts_ < 2 else t["psD"][ts_ % 2]
            c.act(t["vsb"][:, :], ps[:, :], AF.Copy, inc=("act", 1))
            c.w("gpsimd", "act")
            c.dma("gpsimd",
                  bass.AP(io["v_hbm"], (g * 4 + ts_) * 16384,
                          [[128, 128], [1048576, 4], [1, 128]]),
                  t["vsb"][:, :], inc=("sta", 16))
            c.guard("vsb", "sta")
            c.w("scalar", "sta")

    for m in range(4):
        def h_ap(kt, m=m):
            return io["h_own_ap"](kt, m)
        emit_ln(c, t, h_ap,
                lambda kt: t["a_sb"][:, kt * 512:(kt + 1) * 512])
        rp_s = "rpa" if m % 2 == 0 else "rpb"
        c.await_guards("gpsimd", "ropeb")
        c.dma("gpsimd", t["cosb"][:, :],
              bass.AP(io["cosq_i"], m * 512, [[2048, 64], [1, 512]]),
              inc=(rp_s, 16))
        c.dma("gpsimd", t["sinb"][:, :],
              bass.AP(io["sinq_i"], m * 512, [[2048, 64], [1, 512]]),
              inc=(rp_s, 16))
        c.w("vector", rp_s)
        for h in range(NH):
            wl = "qla" if h % 2 == 0 else "qlb"
            c.await_guards("gpsimd", "wsl" + str(h % 2))
            c.dma("gpsimd", t["wsl"][h % 2][:, :],
                  bass.AP(io["Wq_T"], h * 262144,
                          [[2048, 128], [1, 2048]]),
                  inc=(wl, 16))
            c.w("tensor", wl)
            c.w("tensor", "act")
            c.w("tensor", "dve")
            for kt in range(NKT):
                c.mm(t["psA"][h % 2][:, :],
                     t["wsl"][h % 2][:, kt * 128:(kt + 1) * 128],
                     t["a_sb"][:, kt * 512:(kt + 1) * 512],
                     start=(kt == 0), stop=(kt == NKT - 1),
                     inc=("pe", 1) if kt == NKT - 1 else None)
            c.guard("wsl" + str(h % 2), "pe")
            c.w("scalar", "pe")
            c.w("scalar", "dve")
            emit_norm_rope(
                c, t, t["psA"][h % 2][:, :], t["qn_ws"], t["qn_wn"],
                t["cosb"][:, :], t["sinb"][:, :],
                t["q_n"][0:64, h * 2048 + m * 512:h * 2048 + (m + 1) * 512],
                t["q_n"][64:128, h * 2048 + m * 512:h * 2048 + (m + 1) * 512])
        c.guard("ropeb", "dve")


def emit_attn_core(c, io, t):
    for m in range(4):
        nb = (4 * m + 4) * 4
        for h in range(NH):
            gg = h // 4
            if h % 4 == 0:
                c.await_guards("gpsimd", "Vsl")
                c.dma("gpsimd", t["Vsl"][:, :nb * 128],
                      bass.AP(io["v_hbm"], gg * 1048576,
                              [[128, 128], [16384, nb], [1, 128]]),
                      inc=("vla", 16))
                c.w("tensor", "vla")
            qs = t["q_n"][:, h * 2048 + m * 512:h * 2048 + (m + 1) * 512]
            c.w("tensor", "dve")
            c.w("tensor", "act")
            tanh_ct = {}
            exp_ct = {}
            pv_ct = {}

            def lpv(pr):
                c.w("tensor", "act", exp_ct[pr])
                for half_ in range(2):
                    kb = 2 * pr + half_
                    pb = t["Psb"][pr % 2][:, half_ * 512:(half_ + 1) * 512]
                    c.mm(t["psB"][0][:1, :], t["ones128b"][:, :], pb,
                         start=(kb == 0), stop=(kb == nb - 1), inc=None)
                    c.mm(t["psB"][1][:, :],
                         t["Vsl"][:, kb * 128:(kb + 1) * 128], pb,
                         start=(kb == 0), stop=(kb == nb - 1),
                         inc=("pe", 1))
                pv_ct[pr] = c.counts["pe"]

            for pr in range(nb // 2):
                pp = pr % 2
                if pr >= 2:
                    c.w("tensor", "act", tanh_ct[pr - 2])
                for half_ in range(2):
                    kb = 2 * pr + half_
                    trailing = kb >= nb - 16
                    stp = t["Stb"][pp][:, half_ * 512:(half_ + 1) * 512]
                    c.mm(stp,
                         t["Ksb"][:, gg * 8192 + kb * 128:
                                  gg * 8192 + (kb + 1) * 128],
                         qs, start=True, stop=not trailing,
                         inc=None if trailing else ("pe", 1))
                    if trailing:
                        i_ = kb - (nb - 16)
                        c.mm(stp, t["negI"][:, :],
                             t["Rm"][:, i_ * 512:(i_ + 1) * 512],
                             start=False, stop=True, inc=("pe", 1))
                if pr >= 1:
                    lpv(pr - 1)
                c.w("scalar", "pe")
                c.act(t["Tsb"][:, :], t["Stb"][pp][:, :], AF.Tanh,
                      scale=t["kvals"][:, 3:4], inc=("act", 1))
                tanh_ct[pr] = c.counts["act"]
                if pr >= 2:
                    c.w("scalar", "pe", pv_ct[pr - 2])
                c.w("scalar", "act")
                c.act(t["Psb"][pp][:, :], t["Tsb"][:, :], AF.Exp,
                      scale=t["kvals"][:, 2:3], inc=("act", 1))
                exp_ct[pr] = c.counts["act"]
            lpv(nb // 2 - 1)
            if h % 4 == 3:
                c.guard("Vsl", "pe")
            c.w("vector", "pe")
            c.vcopy(t["lrow"][:1, :], t["psB"][0][:1, :], inc=("dve", 1))
            c.w("vector", "dve")
            c.recip(t["lrow"][:1, :], t["lrow"][:1, :], inc=("dve", 1))
            c.w("tensor", "dve")
            c.mm(t["psC"][0][:, :], t["ones1f"][:1, :], t["lrow"][:1, :],
                 True, True, inc=("pe", 1))
            c.w("vector", "pe")
            c.vcopy(t["rbcs"][:, :], t["psC"][0][:, :], inc=("dve", 1))
            c.w("vector", "dve")
            c.tt(t["y_m"][:, h * 512:(h + 1) * 512], t["psB"][1][:, :],
                 t["rbcs"][:, :], ALU.mult, inc=("dve", 1))
        for co in range(16):
            bi = co % 2
            wl = "qla" if bi == 0 else "qlb"
            st_ = "sta" if bi == 0 else "stb"
            rl = "rla" if bi == 0 else "rlb"
            c.await_guards("gpsimd", "wslo" + str(bi))
            c.dma("gpsimd", t["wsl"][bi][:, :],
                  bass.AP(io["Wo_T"], co * 262144,
                          [[2048, 128], [1, 2048]]),
                  inc=(wl, 16))
            c.await_guards("gpsimd", "h1t" + str(bi))
            c.dma("gpsimd", t["h1t"][bi][:, :], io["h_own_ap"](co, m),
                  inc=(rl, 16))
            c.w("tensor", wl)
            c.w("tensor", "dve")
            for h in range(NH):
                c.mm(t["psD"][bi][:, :],
                     t["wsl"][bi][:, h * 128:(h + 1) * 128],
                     t["y_m"][:, h * 512:(h + 1) * 512],
                     start=(h == 0), stop=(h == NH - 1),
                     inc=("pe", 1) if h == NH - 1 else None)
            c.guard("wslo" + str(bi), "pe")
            c.w("vector", "pe")
            c.w("vector", rl)
            c.await_guards("vector", "h2t" + str(bi))
            c.stt(t["h2t"][bi][:, :], t["psD"][bi][:, :], t["bo"][:, co:co + 1],
                  t["h1t"][bi][:, :], ALU.add, ALU.add, inc=("dve", 1))
            c.guard("h1t" + str(bi), "dve")
            c.w("gpsimd", "dve")
            c.dma("gpsimd",
                  bass.AP(io["h2_own"], co * 128 * 2048 + m * 512,
                          [[2048, 128], [1, TC]]),
                  t["h2t"][bi][:, :], inc=(st_, 16))
            c.guard("h2t" + str(bi), st_)


T_FULL_B = 8192


def build_attn_program():
    nc = bass.Bass("TRN2", target_bir_lowering=False)
    stk = ExitStack()
    io = {}
    io["h_full"] = nc.dram_tensor("h_full", [D, T_FULL_B], F32,
                                  kind="ExternalInput")
    io["h_own"] = nc.dram_tensor("h_own", [D, 2048], F32, kind="ExternalInput")
    io["h2_own"] = nc.dram_tensor("h2_own", [D, 2048], F32,
                                  kind="ExternalOutput")
    for nm, shp in (("Wq_T", [16, 128, 16, 128]), ("Wk_T", [4, 128, 16, 128]),
                    ("Wv_T", [D, 512]), ("Wo_T", [16, 128, 16, 128])):
        io[nm] = nc.dram_tensor(nm, shp, F32, kind="ExternalInput")
    io["bo_i"] = nc.dram_tensor("bo_i", [128, 16], F32, kind="ExternalInput")
    io["qn_w"] = nc.dram_tensor("qn_w", [64, 2], F32, kind="ExternalInput")
    io["kn_w"] = nc.dram_tensor("kn_w", [64, 2], F32, kind="ExternalInput")
    io["negI_i"] = nc.dram_tensor("negI_i", [128, 128], BF16,
                                  kind="ExternalInput")
    io["Rm_i"] = nc.dram_tensor("Rm_i", [128, 16 * 512], BF16,
                                kind="ExternalInput")
    io["cosk_i"] = nc.dram_tensor("cosk_i", [64, 8192], F32,
                                  kind="ExternalInput")
    io["sink_i"] = nc.dram_tensor("sink_i", [64, 8192], F32,
                                  kind="ExternalInput")
    io["cosq_i"] = nc.dram_tensor("cosq_i", [64, 2048], F32,
                                  kind="ExternalInput")
    io["sinq_i"] = nc.dram_tensor("sinq_i", [64, 2048], F32,
                                  kind="ExternalInput")
    io["v_hbm"] = nc.dram_tensor("v_hbm", [4 * 64 * 128 * 128], BF16)
    io["h_own_ap"] = lambda kt, m: bass.AP(
        io["h_own"], kt * 128 * 2048 + m * 512, [[2048, 128], [1, TC]])

    sems = {n: stk.enter_context(nc.semaphore(n)) for n in ATTN_SEMS}
    t = alloc_attn_common(nc, stk)
    b = Builder(nc, sems)

    prep_stk = ExitStack()
    alloc_psum(nc, prep_stk, t, sfx="P")
    alloc_attn_prep(nc, prep_stk, t)

    def prog_prep(c):
        emit_attn_consts(c, io, t)
        c.w("tensor", "pre")
        c.w("scalar", "pre")
        c.w("gpsimd", "pre")
        emit_attn_prep(c, io, t)

    b.block(prog_prep)
    prep_stk.close()
    for k in ("a_sb", "stat", "rbc", "mrbc", "hbuf", "hsq", "wsl", "wvb",
              "cosb", "sinb", "cosq", "sinq", "kn0", "rp", "rrow", "vsb"):
        t.pop(k, None)
    core_stk = ExitStack()
    alloc_attn_core(nc, core_stk, t)

    def prog_core(c):
        c.barrier()
        c.dma("gpsimd", t["negI"][:, :], io["negI_i"].ap(), inc=("pre", 16))
        c.dma("gpsimd", t["bo"][:, :], io["bo_i"].ap(), inc=("pre", 16))
        c.dma("gpsimd", t["Rm"][:, :], io["Rm_i"].ap(), inc=("pre", 16))
        c.w("tensor", "pre")
        c.w("vector", "pre")
        emit_attn_core(c, io, t)
        c.w("gpsimd", "sta")
        c.w("gpsimd", "stb")

    b.block(prog_core)
    core_stk.close()
    stk.close()
    return nc


def stage_attn_inputs(ins, j):
    Wq = np.asarray(ins["L0_attn_Wq"], np.float32)
    Wk = np.asarray(ins["L0_attn_Wk"], np.float32)
    Wv = np.asarray(ins["L0_attn_Wv"], np.float32)
    Wo = np.asarray(ins["L0_attn_Wo"], np.float32)
    inv_freq = 1.0 / 10000.0 ** (np.arange(0, HD, 2, dtype=np.float32) / HD)
    pos_full = np.arange(T_FULL_B, dtype=np.float32)
    own_tok = np.concatenate(
        [np.arange((4 * mm + j) * 512, (4 * mm + j) * 512 + 512)
         for mm in range(4)]).astype(np.float32)
    frf = pos_full[:, None] * inv_freq[None, :]
    fro = own_tok[:, None] * inv_freq[None, :]
    Rm = np.zeros((128, 16 * 512), dtype=NPBF16)
    Kk = np.arange(128)[:, None]
    tq = np.arange(512)[None, :]
    for i in range(16):
        d_ = 128 * (i - 4 * j)
        Rm[:, i * 512:(i + 1) * 512] = (Kk - tq > -d_).astype(NPBF16)
    return {
        "Wq_T": np.ascontiguousarray(
            Wq.reshape(16, 128, 16, 128).transpose(0, 3, 2, 1)),
        "Wk_T": np.ascontiguousarray(
            Wk.reshape(4, 128, 16, 128).transpose(0, 3, 2, 1)),
        "Wv_T": np.ascontiguousarray(Wv.T),
        "Wo_T": np.ascontiguousarray(
            Wo.reshape(16, 128, 16, 128).transpose(0, 3, 2, 1)),
        "bo_i": np.ascontiguousarray(
            np.asarray(ins["L0_attn_bo"], np.float32).reshape(16, 128).T),
        "qn_w": np.ascontiguousarray(
            np.asarray(ins["L0_attn_q_norm_w"], np.float32).reshape(2, 64).T),
        "kn_w": np.ascontiguousarray(
            np.asarray(ins["L0_attn_k_norm_w"], np.float32).reshape(2, 64).T),
        "negI_i": (NEG_BIG * np.eye(128, dtype=np.float32)).astype(NPBF16),
        "Rm_i": Rm,
        "cosk_i": np.ascontiguousarray(np.cos(frf).T.astype(np.float32)),
        "sink_i": np.ascontiguousarray(np.sin(frf).T.astype(np.float32)),
        "cosq_i": np.ascontiguousarray(np.cos(fro).T.astype(np.float32)),
        "sinq_i": np.ascontiguousarray(np.sin(fro).T.astype(np.float32)),
    }


def own_token_index(j):
    return np.concatenate(
        [np.arange((4 * m + j) * 512, (4 * m + j + 1) * 512) for m in range(4)])



FUSED_SEMS = SEM_NAMES + ("vla", "vlb", "rpa", "rpb", "cpy")


def alloc_attn_extra(nc, stk, t):
    def sb(name, shape, dt):
        t[name] = stk.enter_context(nc.sbuf_tensor(name, shape, dt))
        return t[name]
    sb("qn_ws", [64, 2], F32)
    sb("kn_ws", [64, 2], F32)
    sb("qn_wn", [64, 2], F32)
    sb("kn_wn", [64, 2], F32)
    sb("Ksb", [128, 4 * 8192], BF16)
    sb("q_n", [128, 16 * 2048], BF16)


def build_fused_program():
    nc = bass.Bass("TRN2", target_bir_lowering=False)
    stk = ExitStack()
    io = {}
    io["x_fm"] = nc.dram_tensor("x_fm", [D, T_FULL_B], F32,
                                kind="ExternalInput")
    io["out_fm"] = nc.dram_tensor("out_fm", [D, 2048], F32,
                                  kind="ExternalOutput")
    io["own_off"] = nc.dram_tensor("own_off", [1, 4], I32,
                                   kind="ExternalInput")
    for nm, shp in (("Wu_st", [16, 128, INTER]), ("Wg_st", [16, 128, INTER]),
                    ("Wd_st", [64, 128, D])):
        io[nm] = nc.dram_tensor(nm, shp, F32, kind="ExternalInput")
    for nm, n in (("ws_up", INTER), ("ws_gate", INTER), ("ws_dn", D)):
        io[nm] = nc.dram_tensor(nm, [1, n], F32, kind="ExternalInput")
    io["su_ug_cols"] = nc.dram_tensor("su_ug_cols", [128, 16], F32,
                                      kind="ExternalInput")
    io["su_dp_cols"] = nc.dram_tensor("su_dp_cols", [128, 64], F32,
                                      kind="ExternalInput")
    io["H128c_i"] = nc.dram_tensor("H128c_i", [128, 128], BF16,
                                   kind="ExternalInput")
    io["BD16p_i"] = nc.dram_tensor("BD16p_i", [128, 128], F32,
                                   kind="ExternalInput")
    io["BD64p_i"] = nc.dram_tensor("BD64p_i", [128, 128], F32,
                                   kind="ExternalInput")
    for nm, shp in (("Wq_T", [16, 128, 16, 128]), ("Wk_T", [4, 128, 16, 128]),
                    ("Wv_T", [D, 512]), ("Wo_T", [16, 128, 16, 128])):
        io[nm] = nc.dram_tensor(nm, shp, F32, kind="ExternalInput")
    io["bo_i"] = nc.dram_tensor("bo_i", [128, 16], F32, kind="ExternalInput")
    io["qn_w"] = nc.dram_tensor("qn_w", [64, 2], F32, kind="ExternalInput")
    io["kn_w"] = nc.dram_tensor("kn_w", [64, 2], F32, kind="ExternalInput")
    io["negI_i"] = nc.dram_tensor("negI_i", [128, 128], BF16,
                                  kind="ExternalInput")
    io["Rm_i"] = nc.dram_tensor("Rm_i", [128, 16 * 512], BF16,
                                kind="ExternalInput")
    io["cosk_i"] = nc.dram_tensor("cosk_i", [64, 8192], F32,
                                  kind="ExternalInput")
    io["sink_i"] = nc.dram_tensor("sink_i", [64, 8192], F32,
                                  kind="ExternalInput")
    io["cosq_i"] = nc.dram_tensor("cosq_i", [64, 2048], F32,
                                  kind="ExternalInput")
    io["sinq_i"] = nc.dram_tensor("sinq_i", [64, 2048], F32,
                                  kind="ExternalInput")
    io["WU_hbm"] = nc.dram_tensor("WU_hbm", [NTAU * 16 * 128 * 128], BF16)
    io["WG_hbm"] = nc.dram_tensor("WG_hbm", [NTAU * 16 * 128 * 128], BF16)
    io["WF_hbm"] = nc.dram_tensor("WF_hbm", [16 * NTAU * 128 * 128], BF16)
    io["BD64_hbm"] = nc.dram_tensor("BD64_hbm", [128 * 8192], BF16)
    io["h1_full"] = nc.dram_tensor("h1_full", [D, T_FULL_B], F32)
    io["h1_own"] = nc.dram_tensor("h1_own", [D, 2048], F32)
    io["h2_own"] = nc.dram_tensor("h2_own", [D, 2048], F32)
    io["v_hbm"] = nc.dram_tensor("v_hbm", [4 * 64 * 128 * 128], BF16)
    io["h_full"] = io["h1_full"]
    io["h_own_ap"] = lambda kt, m: bass.AP(
        io["h1_own"], kt * 128 * 2048 + m * 512, [[2048, 128], [1, TC]])

    sems = {n: stk.enter_context(nc.semaphore(n)) for n in FUSED_SEMS}
    t = alloc_common(nc, stk)
    b = Builder(nc, sems)

    ps1_stk = ExitStack()
    alloc_psum(nc, ps1_stk, t, sfx="1")
    fold_stk = ExitStack()
    alloc_fold(nc, fold_stk, t)

    def prog_fold(c):
        emit_load_consts(c, io, t)
        emit_weight_fold(c, io, t)

    b.block(prog_fold)
    fold_stk.close()

    ffn_stk = ExitStack()
    alloc_ffn(nc, ffn_stk, t)

    def prog_ffn1(c):
        c.barrier()
        emit_bd_tiles(c, t)
        for tc in range(T_FULL_B // TC):
            emit_ffn_chunk(c, io, t, T_FULL_B, tc * TC, "x_fm", "h1_full")

    b.block(prog_ffn1)
    ffn_stk.close()
    ps1_stk.close()
    for k in ("u", "xn", "z", "y", "gsil", "stat", "rbc", "mrbc", "hbuf",
              "hsq", "wu_sl", "wg_sl", "wf_sl", "rbuf", "hnx",
              "psA", "psB", "psC", "psD", "ps8"):
        t.pop(k, None)

    offs_stk = ExitStack()
    t["offs"] = offs_stk.enter_context(nc.sbuf_tensor("offs", [1, 4], I32))

    def prog_copy(c):
        c.barrier()
        c.dma("gpsimd", t["offs"][:1, :], io["own_off"].ap(), inc=("pre", 16))
        c.w("gpsimd", "pre")
        if c.kind == "gpsimd":
            with c.eng.register("roff") as roff:
                for m in range(4):
                    c.eng.reg_load(roff, t["offs"][:1, m:m + 1])
                    i = c.eng.dma_start(
                        out=bass.AP(io["h1_own"], m * 512,
                                    [[2048, 2048], [1, 512]]),
                        in_=bass.AP(io["h1_full"], roff,
                                    [[T_FULL_B, 2048], [1, 512]]))
                    i.then_inc(c.sems["cpy"], 16)
        c.counts["cpy"] += 64
        c.w("gpsimd", "cpy")

    b.block(prog_copy)

    attn_stk = ExitStack()
    alloc_attn_extra(nc, attn_stk, t)
    prep_stk = ExitStack()
    alloc_psum(nc, prep_stk, t, sfx="P")
    alloc_attn_prep(nc, prep_stk, t, sfx="AP")

    def prog_prep(c):
        c.barrier()
        emit_attn_consts(c, io, t, shared=False)
        c.w("tensor", "pre")
        c.w("scalar", "pre")
        c.w("gpsimd", "pre")
        emit_attn_prep(c, io, t)

    b.block(prog_prep)
    prep_stk.close()
    for k in ("a_sb", "stat", "rbc", "mrbc", "hbuf", "hsq", "wsl", "wvb",
              "cosb", "sinb", "kn0a", "kn0b", "rp", "rrow", "vsb",
              "psA", "psB", "psC", "psD", "ps8"):
        t.pop(k, None)

    core_stk = ExitStack()
    alloc_attn_core(nc, core_stk, t, sfx="AC")

    def prog_core(c):
        c.barrier()
        c.dma("gpsimd", t["negI"][:, :], io["negI_i"].ap(), inc=("pre", 16))
        c.dma("gpsimd", t["bo"][:, :], io["bo_i"].ap(), inc=("pre", 16))
        c.dma("gpsimd", t["Rm"][:, :], io["Rm_i"].ap(), inc=("pre", 16))
        c.w("tensor", "pre")
        c.w("vector", "pre")
        emit_attn_core(c, io, t)

    b.block(prog_core)
    core_stk.close()
    attn_stk.close()
    for k in ("Vsl", "Tsb", "Psb", "Stb", "y_m", "lrow", "rbcs", "wsl", "h1t",
              "h2t", "negI", "bo", "Rm", "Ksb", "q_n", "qn_ws", "kn_ws",
              "qn_wn", "kn_wn", "psB", "psC", "psD"):
        t.pop(k, None)

    ffn2_stk = ExitStack()
    alloc_psum(nc, ffn2_stk, t, sfx="3")
    alloc_ffn(nc, ffn2_stk, t, sfx="F3")

    def prog_ffn3(c):
        c.barrier()
        emit_bd_tiles(c, t)
        for tc in range(2048 // TC):
            emit_ffn_chunk(c, io, t, 2048, tc * TC, "h2_own", "out_fm")
        c.w("gpsimd", "sta")
        c.w("gpsimd", "stb")

    b.block(prog_ffn3)
    ffn2_stk.close()
    offs_stk.close()
    stk.close()
    return nc


def kernel(**inputs):
    x = np.asarray(inputs["L0_attn_x"], np.float32)
    nc = build_fused_program()
    staged_f = stage_ffn_inputs(inputs)
    in_maps = []
    for c_ in range(8):
        b_, j_ = c_ // 4, c_ % 4
        m = dict(staged_f)
        m.update(stage_attn_inputs(inputs, j_))
        m["x_fm"] = np.ascontiguousarray(x[b_].T)
        m["own_off"] = np.array(
            [[(4 * mm + j_) * 512 for mm in range(4)]], np.int32)
        in_maps.append(m)
    res = run_bass_kernel_spmd(nc, in_maps, core_ids=list(range(8)))
    out = np.empty((2, T_FULL_B, D), np.float32)
    for c_ in range(8):
        b_, j_ = c_ // 4, c_ % 4
        idx = own_token_index(j_)
        out[b_][idx] = res.results[c_]["out_fm"].T
    return out



def kernel_unfused(**inputs):
    x = np.asarray(inputs["L0_attn_x"], np.float32)
    staged_f = stage_ffn_inputs(inputs)

    nc_f = build_ffn_program(2048)
    in_maps = []
    xf = x.reshape(-1, D)
    for c_ in range(8):
        m = dict(staged_f)
        m["h_in"] = np.ascontiguousarray(xf[c_ * 2048:(c_ + 1) * 2048].T)
        in_maps.append(m)
    res1 = run_bass_kernel_spmd(nc_f, in_maps, core_ids=list(range(8)))
    h1 = np.empty((2, T_FULL_B, D), np.float32)
    for c_ in range(8):
        h1.reshape(-1, D)[c_ * 2048:(c_ + 1) * 2048] = res1.results[c_]["h_out"].T

    nc_a = build_attn_program()
    in_maps = []
    for c_ in range(8):
        b_, j_ = c_ // 4, c_ % 4
        m = stage_attn_inputs(inputs, j_)
        m["h_full"] = np.ascontiguousarray(h1[b_].T)
        idx = own_token_index(j_)
        m["h_own"] = np.ascontiguousarray(h1[b_][idx].T)
        in_maps.append(m)
    res2 = run_bass_kernel_spmd(nc_a, in_maps, core_ids=list(range(8)))
    h2 = np.empty((2, T_FULL_B, D), np.float32)
    for c_ in range(8):
        b_, j_ = c_ // 4, c_ % 4
        idx = own_token_index(j_)
        h2[b_][idx] = res2.results[c_]["h2_own"].T

    in_maps = []
    h2f = h2.reshape(-1, D)
    for c_ in range(8):
        m = dict(staged_f)
        m["h_in"] = np.ascontiguousarray(h2f[c_ * 2048:(c_ + 1) * 2048].T)
        in_maps.append(m)
    res3 = run_bass_kernel_spmd(nc_f, in_maps, core_ids=list(range(8)))
    out = np.empty((2, T_FULL_B, D), np.float32)
    for c_ in range(8):
        out.reshape(-1, D)[c_ * 2048:(c_ + 1) * 2048] = \
            res3.results[c_]["h_out"].T
    return out



# version 08
# speedup vs baseline: 1.0812x; 1.0154x over previous
import math
from contextlib import ExitStack

import numpy as np
import ml_dtypes

import concourse.bass as bass
import concourse.mybir as mybir
import concourse.bass_utils as _bass_utils
from concourse.bass_utils import run_bass_kernel_spmd

if not getattr(_bass_utils.run_command, "_ldw_patched", False):
    _orig_run_command = _bass_utils.run_command

    def _run_command_ldw(argv, **kwargs):
        argv = ["--enable-ldw-opt=true" if a == "--enable-ldw-opt=false" else a
                for a in argv]
        return _orig_run_command(argv, **kwargs)

    _run_command_ldw._ldw_patched = True
    _bass_utils.run_command = _run_command_ldw

F32 = mybir.dt.float32
BF16 = mybir.dt.bfloat16
I32 = mybir.dt.int32
AF = mybir.ActivationFunctionType
ALU = mybir.AluOpType

NPBF16 = ml_dtypes.bfloat16

D = 2048
INTER = 8192
NKT = 16
NTAU = 64
HD = 128
NH = 16
NKV = 4
C16 = 1.0 / (math.sqrt(2048.0) * 64.0)
C64 = 1.0 / (math.sqrt(8192.0) * 64.0)
LN_EPS = 1e-5
RMS_EPS = 1e-6
NEG_BIG = -1.0e5
ISQD = 1.0 / math.sqrt(128.0)
TC = 512


def _hadamard(n):
    h = np.array([[1.0]], dtype=np.float32)
    while h.shape[0] < n:
        h = np.block([[h, h], [h, -h]])
    return h.astype(np.float32)



ENGS = ("tensor", "scalar", "vector", "gpsimd", "sync")


class Ctx:

    def __init__(self, kind, sems):
        self.kind = kind
        self.eng = None
        self.sems = sems
        self.counts = {k: 0 for k in sems}
        self.guards = {}

    def _inc(self, instr, inc):
        if inc is None:
            return instr
        sem, n = inc
        self.counts[sem] += n
        if instr is not None:
            instr.then_inc(self.sems[sem], n)
        return instr

    def w(self, kind, sem, cnt=None):
        v = self.counts[sem] if cnt is None else cnt
        if self.kind == kind and v > 0:
            self.eng.wait_ge(self.sems[sem], v)

    def barrier(self):
        for sem, v in self.counts.items():
            if v > 0:
                self.eng.wait_ge(self.sems[sem], v)

    def guard(self, buf, sem):
        self.guards.setdefault(buf, {})[sem] = self.counts[sem]

    def await_guards(self, kind, buf):
        g = self.guards.get(buf)
        if not g:
            return
        if self.kind == kind:
            for sem, v in g.items():
                if v > 0:
                    self.eng.wait_ge(self.sems[sem], v)

    def mm(self, out, lhsT, rhs, start, stop, inc=None):
        i = self.eng.matmul(out, lhsT, rhs, start=start, stop=stop) \
            if self.kind == "tensor" else None
        return self._inc(i, inc)

    def act(self, out, in_, func, bias=0.0, scale=1.0, accum=None, inc=None):
        i = self.eng.activation(out, in_, func, bias=bias, scale=scale,
                                accum_out=accum) \
            if self.kind == "scalar" else None
        return self._inc(i, inc)

    def tt(self, out, a, b, op, inc=None):
        i = self.eng.tensor_tensor(out, a, b, op) \
            if self.kind == "vector" else None
        return self._inc(i, inc)

    def ts(self, out, a, s1, op0, s2=None, op1=ALU.bypass, inc=None):
        i = self.eng.tensor_scalar(out, a, s1, s2, op0, op1) \
            if self.kind == "vector" else None
        return self._inc(i, inc)

    def stt(self, out, in0, scalar, in1, op0, op1, inc=None):
        i = self.eng.scalar_tensor_tensor(out, in0, scalar, in1, op0, op1) \
            if self.kind == "vector" else None
        return self._inc(i, inc)

    def vcopy(self, out, in_, inc=None):
        i = self.eng.tensor_copy(out, in_) if self.kind == "vector" else None
        return self._inc(i, inc)

    def recip(self, out, in_, inc=None):
        i = self.eng.reciprocal(out, in_) if self.kind == "vector" else None
        return self._inc(i, inc)

    def memset(self, ap, v, inc=None):
        i = self.eng.memset(ap, v) if self.kind == "vector" else None
        return self._inc(i, inc)

    def dma(self, queue, out, in_, inc=None, slow=False):
        i = None
        if self.kind == queue:
            kw = {"allow_slow_non_contiguous": True} if slow else {}
            i = self.eng.dma_start(out=out, in_=in_, **kw)
        return self._inc(i, inc)


class Builder:

    def __init__(self, nc, sems):
        self.nc = nc
        self.ctxs = {k: Ctx(k, sems) for k in ENGS}

    def block(self, program):
        with self.nc.Block() as blk:
            for kind in ENGS:
                ctx = self.ctxs[kind]

                def mk(ctx):
                    def f(eng):
                        ctx.eng = eng
                        program(ctx)
                    return f
                getattr(blk, kind)(mk(ctx))



SEM_NAMES = ("pe", "act", "dve", "pre", "sv", "sto",
             "gla", "glb", "gsa", "gsb", "wla", "wlb", "wlc", "wld",
             "wfa", "wfb", "wfc", "wfd", "hla", "hlb", "rla", "rlb",
             "sta", "stb", "qla", "qlb")


def alloc_common(nc, stk):
    t = {}

    def sb(name, shape, dt):
        t[name] = stk.enter_context(nc.sbuf_tensor(name, shape, dt))
        return t[name]

    sb("H128c", [128, 128], BF16)
    sb("BD16p", [128, 128], F32)
    sb("BD64p", [128, 128], F32)
    sb("ones128f", [128, 1], F32)
    sb("ones1f", [1, 128], F32)
    sb("c64row", [1, 128], F32)
    sb("c32row", [1, 128], F32)
    sb("su_ug", [128, 16], F32)
    sb("su_dp", [128, 64], F32)
    sb("kvals", [128, 8], F32)
    sb("ones128b", [128, 1], BF16)
    return t


def alloc_psum(nc, stk, t, sfx=""):
    def ps(name, shape):
        t[name] = stk.enter_context(nc.psum_tensor(name + sfx, shape, F32))
        return t[name]
    t["psA"] = [ps("psA0", [128, 512]), ps("psA1", [128, 512])]
    t["psB"] = [ps("psB0", [128, 512]), ps("psB1", [128, 512])]
    t["psC"] = [ps("psC0", [128, 512]), ps("psC1", [128, 512])]
    t["psD"] = [ps("psD0", [128, 512]), ps("psD1", [128, 512])]
    t["ps8"] = t["psA"] + t["psB"] + t["psC"] + t["psD"]


def alloc_fold(nc, stk, t):
    def sb(name, shape, dt):
        t[name] = stk.enter_context(nc.sbuf_tensor(name, shape, dt))
        return t[name]
    sb("sbc", [128, INTER], F32)
    sb("bdt", [128, 128], BF16)
    t["win"] = [sb("win0", [128, INTER], BF16), sb("win1", [128, INTER], BF16)]
    t["wout"] = [sb("wout0", [128, INTER], BF16),
                 sb("wout1", [128, INTER], BF16)]
    sb("svbuf", [1, 512], F32)


def alloc_ffn(nc, stk, t, sfx=""):
    def sb(name, shape, dt):
        t[name] = stk.enter_context(nc.sbuf_tensor(name + sfx, shape, dt))
        return t[name]
    sb("BD16", [128, 16 * 128], BF16)
    sb("u", [128, NTAU * TC], BF16)
    t["xn"] = t["u"][:, :NKT * TC]
    sb("z", [128, NKT * TC], BF16)
    t["y"] = [sb("y0", [128, TC], BF16), sb("y1", [128, TC], BF16)]
    t["gsil"] = [sb("gsil0", [128, TC], F32), sb("gsil1", [128, TC], F32)]
    sb("stat", [1, 4 * TC], F32)
    sb("rbc", [128, TC], F32)
    sb("mrbc", [128, TC], F32)
    t["hbuf"] = [sb("hbuf0", [128, TC], F32), sb("hbuf1", [128, TC], F32)]
    t["hsq"] = [sb("hsq0", [128, TC], F32), sb("hsq1", [128, TC], F32)]
    t["wu_sl"] = [sb("wu_sl0", [128, 16 * 512], BF16),
                  sb("wu_sl1", [128, 16 * 512], BF16)]
    t["wg_sl"] = [sb("wg_sl0", [128, 16 * 512], BF16),
                  sb("wg_sl1", [128, 16 * 512], BF16)]
    t["bd_sl"] = [sb("bd_sl0", [128, 512], BF16),
                  sb("bd_sl1", [128, 512], BF16)]
    t["wf_sl"] = [sb("wf_sl0", [128, 1024], BF16),
                  sb("wf_sl1", [128, 1024], BF16)]
    t["rbuf"] = [sb("rbuf0", [128, 512], F32), sb("rbuf1", [128, 512], F32)]
    t["hnx"] = [sb("hnx0", [128, 512], F32), sb("hnx1", [128, 512], F32)]


def emit_load_consts(c, io, t):
    c.dma("gpsimd", t["H128c"][:, :], io["H128c_i"].ap(), inc=("pre", 16))
    c.dma("gpsimd", t["BD16p"][:, :], io["BD16p_i"].ap(), inc=("pre", 16))
    c.dma("gpsimd", t["BD64p"][:, :], io["BD64p_i"].ap(), inc=("pre", 16))
    c.dma("gpsimd", t["su_ug"][:, :], io["su_ug_cols"].ap(), inc=("pre", 16))
    c.dma("gpsimd", t["su_dp"][:, :], io["su_dp_cols"].ap(), inc=("pre", 16))
    c.memset(t["ones128f"][:, :], 1.0)
    c.memset(t["ones1f"][:1, :], 1.0)
    c.memset(t["c64row"][:1, :], 64.0)
    c.memset(t["kvals"][:, 0:1], LN_EPS)
    c.memset(t["kvals"][:, 1:2], RMS_EPS)
    c.memset(t["kvals"][:, 2:3], 50.0)
    c.memset(t["kvals"][:, 3:4], ISQD / 50.0)
    c.memset(t["kvals"][:, 4:5], 1.0 / 128.0)
    c.memset(t["ones128b"][:, :], 1.0)
    c.memset(t["c32row"][:1, :], 32.0, inc=("dve", 1))
    c.w("vector", "pre")


def emit_bd_tiles(c, t):
    c.w("vector", "pre")
    for k in range(16):
        c.ts(t["BD16"][:, k * 128:(k + 1) * 128], t["BD16p"][:, :],
             t["su_ug"][:, k:k + 1], ALU.mult,
             inc=("dve", 1) if k == 15 else None)


def emit_weight_fold(c, io, t):
    def fold(src_h, dst_h, sv_h, n512, n_slabs, scale_lhs, dst_ap_fn):
        for i in range(n512):
            c.w("gpsimd", "pe")
            c.dma("gpsimd", t["svbuf"][:1, :],
                  bass.AP(io[sv_h], i * 512, [[512, 1], [1, 512]]),
                  inc=("sv", 16))
            c.w("tensor", "pre")
            c.w("tensor", "sv")
            c.w("tensor", "dve")
            c.mm(t["psD"][0][:, :], t[scale_lhs][:1, :], t["svbuf"][:1, :],
                 True, True, inc=("pe", 1))
            c.w("vector", "pe")
            c.vcopy(t["sbc"][:, i * 512:(i + 1) * 512], t["psD"][0][:, :],
                    inc=("dve", 1))
        c.w("vector", "dve")
        for sl in range(n_slabs):
            bi = sl % 2
            gl = "gla" if bi == 0 else "glb"
            gs = "gsa" if bi == 0 else "gsb"
            c.await_guards("gpsimd", "win" + str(bi))
            c.dma("gpsimd", t["win"][bi][:, :n512 * 512],
                  bass.AP(io[src_h], sl * 128 * n512 * 512,
                          [[n512 * 512, 128], [1, n512 * 512]]),
                  inc=(gl, 16))
            c.w("tensor", gl)
            tt_ct = {}
            for n in range(n512):
                if n - 2 in tt_ct:
                    c.w("tensor", "dve", tt_ct[n - 2])
                else:
                    c.w("tensor", "dve")
                c.mm(t["psD"][n % 2][:, :], t["H128c"][:, :],
                     t["win"][bi][:, n * 512:(n + 1) * 512], True, True,
                     inc=("pe", 1))
                c.w("vector", "pe")
                c.await_guards("vector", "wout" + str(bi))
                c.tt(t["wout"][bi][:, n * 512:(n + 1) * 512],
                     t["psD"][n % 2][:, :],
                     t["sbc"][:, n * 512:(n + 1) * 512], ALU.mult,
                     inc=("dve", 1))
                tt_ct[n] = c.counts["dve"]
            c.guard("win" + str(bi), "pe")
            c.w("gpsimd", "dve")
            c.dma("gpsimd", dst_ap_fn(sl), t["wout"][bi][:, :n512 * 512],
                  inc=(gs, 16))
            c.guard("wout" + str(bi), gs)

    fold("Wu_st", "WU_hbm", "ws_up", 16, 16, "c64row",
         lambda sl: bass.AP(io["WU_hbm"], sl * 512,
                            [[8192, 128], [1048576, 16], [1, 512]]))
    fold("Wg_st", "WG_hbm", "ws_gate", 16, 16, "c64row",
         lambda sl: bass.AP(io["WG_hbm"], sl * 512,
                            [[8192, 128], [1048576, 16], [1, 512]]))
    fold("Wd_st", "WF_hbm", "ws_dn", 4, 64, "c32row",
         lambda sl: bass.AP(io["WF_hbm"], sl * 2048,
                            [[131072, 128], [1, 2048]]))
    c.w("vector", "pre")
    for k in range(64):
        c.await_guards("vector", "bdt")
        c.ts(t["bdt"][:, :], t["BD64p"][:, :], t["su_dp"][:, k:k + 1],
             ALU.mult, inc=("dve", 1))
        c.w("gpsimd", "dve")
        c.dma("gpsimd", bass.AP(io["BD64_hbm"], k * 128,
                                [[8192, 128], [1, 128]]),
              t["bdt"][:, :], inc=("gsa", 16))
        c.guard("bdt", "gsa")


def emit_ln(c, t, h_ap_fn, out_fn):
    c.w("tensor", "dve")
    for kt in range(NKT):
        bi = kt % 2
        hl = "hla" if bi == 0 else "hlb"
        c.await_guards("gpsimd", "hbuf" + str(bi))
        c.dma("gpsimd", t["hbuf"][bi][:, :], h_ap_fn(kt), inc=(hl, 16))
        c.w("scalar", hl)
        c.await_guards("scalar", "hsq" + str(bi))
        c.act(t["hsq"][bi][:, :], t["hbuf"][bi][:, :], AF.Square,
              inc=("act", 1))
        c.w("tensor", "act")
        c.w("tensor", hl)
        c.mm(t["psA"][0][:1, :], t["ones128f"][:, :], t["hbuf"][bi][:, :],
             start=(kt == 0), stop=(kt == NKT - 1))
        c.mm(t["psB"][0][:1, :], t["ones128f"][:, :], t["hsq"][bi][:, :],
             start=(kt == 0), stop=(kt == NKT - 1), inc=("pe", 1))
        c.guard("hbuf" + str(bi), "pe")
        c.guard("hsq" + str(bi), "pe")
    st = t["stat"]
    c.w("vector", "pe")
    c.ts(st[:1, 0:TC], t["psA"][0][:1, :], 1.0 / 2048.0, ALU.mult,
         inc=("dve", 1))
    c.ts(st[:1, TC:2 * TC], t["psB"][0][:1, :], 1.0 / 2048.0, ALU.mult,
         inc=("dve", 1))
    c.w("vector", "dve")
    c.tt(st[:1, 2 * TC:3 * TC], st[:1, 0:TC], st[:1, 0:TC], ALU.mult,
         inc=("dve", 1))
    c.w("vector", "dve")
    c.tt(st[:1, 2 * TC:3 * TC], st[:1, TC:2 * TC], st[:1, 2 * TC:3 * TC],
         ALU.subtract, inc=("dve", 1))
    c.w("scalar", "dve")
    c.act(st[:1, 2 * TC:3 * TC], st[:1, 2 * TC:3 * TC], AF.Sqrt,
          bias=t["kvals"][:1, 0:1], inc=("act", 1))
    c.w("vector", "act")
    c.recip(st[:1, 2 * TC:3 * TC], st[:1, 2 * TC:3 * TC], inc=("dve", 1))
    c.w("vector", "dve")
    c.tt(st[:1, 3 * TC:4 * TC], st[:1, 0:TC], st[:1, 2 * TC:3 * TC],
         ALU.mult, inc=("dve", 1))
    c.w("tensor", "dve")
    c.mm(t["psA"][0][:, :], t["ones1f"][:1, :], st[:1, 2 * TC:3 * TC],
         True, True)
    c.mm(t["psB"][0][:, :], t["ones1f"][:1, :], st[:1, 3 * TC:4 * TC],
         True, True, inc=("pe", 1))
    c.w("vector", "pe")
    c.vcopy(t["rbc"][:, :], t["psA"][0][:, :], inc=("dve", 1))
    c.vcopy(t["mrbc"][:, :], t["psB"][0][:, :], inc=("dve", 1))
    c.w("vector", "pe")
    for kt in range(NKT):
        bi = kt % 2
        hl = "hla" if bi == 0 else "hlb"
        c.await_guards("gpsimd", "hbuf" + str(bi))
        c.dma("gpsimd", t["hbuf"][bi][:, :], h_ap_fn(kt), inc=(hl, 16))
        c.w("vector", hl)
        c.w("vector", "dve")
        c.await_guards("vector", "hsq" + str(bi))
        c.tt(t["hsq"][bi][:, :], t["hbuf"][bi][:, :], t["rbc"][:, :],
             ALU.mult, inc=("dve", 1))
        c.w("vector", "dve")
        c.tt(out_fn(kt), t["hsq"][bi][:, :], t["mrbc"][:, :], ALU.subtract,
             inc=("dve", 1))
        c.guard("hbuf" + str(bi), "dve")
        c.guard("hsq" + str(bi), "dve")


def emit_ffn_chunk(c, io, t, T, row_b, h_in, h_out, dbg=None):

    def h_ap(kt):
        return bass.AP(io[h_in], kt * 128 * T + row_b, [[T, 128], [1, TC]])

    def dump(name, src_ap):
        if dbg is None or name not in dbg:
            return
        c.w("gpsimd", "dve")
        c.w("gpsimd", "act")
        c.w("gpsimd", "pe")
        c.dma("gpsimd", dbg[name].ap(), src_ap, inc=("sto", 16))

    c.w("vector", "pe")
    emit_ln(c, t, h_ap, lambda kt: t["xn"][:, kt * TC:(kt + 1) * TC])
    dump("xn", t["xn"][:, :])

    c.w("tensor", "pre")
    c.w("tensor", "act")
    c.w("tensor", "dve")
    zev_ct = {}
    for kt in range(NKT):
        pb = kt % 2
        if kt - 2 in zev_ct:
            c.w("tensor", "act", zev_ct[kt - 2])
        c.mm(t["psA"][pb][:, :], t["BD16"][:, kt * 128:(kt + 1) * 128],
             t["xn"][:, kt * TC:(kt + 1) * TC], True, True, inc=("pe", 1))
        c.w("scalar", "pe")
        c.act(t["z"][:, kt * TC:(kt + 1) * TC], t["psA"][pb][:, :], AF.Copy,
              inc=("act", 1))
        zev_ct[kt] = c.counts["act"]
    dump("z", t["z"][:, :])

    upgate_ct = {}
    bd64_ct = {}
    silu_ct = {}
    ucopy_ct = {}
    ymul_ct = {}

    def emit_bd64(tau):
        pp = tau % 2
        qb = (tau // 4) % 2
        c.w("tensor", "dve", ymul_ct[tau])
        if tau - 2 in ucopy_ct:
            c.w("tensor", "act", ucopy_ct[tau - 2])
        c.mm(t["psC"][pp][:, :],
             t["bd_sl"][qb][:, (tau % 4) * 128:(tau % 4 + 1) * 128],
             t["y"][pp][:, :], True, True, inc=("pe", 1))
        if tau % 4 == 3:
            c.guard("bd" + str(qb), "pe")
        bd64_ct[tau] = c.counts["pe"]
        c.w("scalar", "pe", bd64_ct[tau])
        c.act(t["u"][:, tau * TC:(tau + 1) * TC], t["psC"][pp][:, :], AF.Copy,
              inc=("act", 1))
        ucopy_ct[tau] = c.counts["act"]

    for tau in range(NTAU):
        pp = tau % 2
        e4 = tau % 4
        qb = (tau // 4) % 2
        if e4 == 0:
            wl = ("wla", "wlb", "wlc", "wld")[(tau // 4) % 4]
            c.await_guards("sync", "wu" + str(qb))
            c.dma("sync", t["wu_sl"][qb][:, :],
                  bass.AP(io["WU_hbm"], (tau // 4) * 1048576,
                          [[512, 128], [65536, 16], [1, 512]]),
                  inc=(wl, 16))
            c.dma("sync", t["wg_sl"][qb][:, :],
                  bass.AP(io["WG_hbm"], (tau // 4) * 1048576,
                          [[512, 128], [65536, 16], [1, 512]]),
                  inc=(wl, 16))
            c.await_guards("sync", "bd" + str(qb))
            c.dma("sync", t["bd_sl"][qb][:, :],
                  bass.AP(io["BD64_hbm"], (tau // 4) * 512,
                          [[8192, 128], [1, 512]]),
                  inc=(wl, 16))
            c.w("tensor", wl)
        if tau - 2 in silu_ct:
            c.w("tensor", "act", silu_ct[tau - 2])
            c.w("tensor", "dve", ymul_ct[tau - 2])
        else:
            c.w("tensor", "act")
            c.w("tensor", "dve")
        for kt in range(NKT):
            zs = t["z"][:, kt * TC:(kt + 1) * TC]
            c.mm(t["psA"][pp][:, :],
                 t["wu_sl"][qb][:, kt * 512 + e4 * 128:kt * 512 + (e4 + 1) * 128],
                 zs, start=(kt == 0), stop=(kt == NKT - 1))
            c.mm(t["psB"][pp][:, :],
                 t["wg_sl"][qb][:, kt * 512 + e4 * 128:kt * 512 + (e4 + 1) * 128],
                 zs, start=(kt == 0), stop=(kt == NKT - 1),
                 inc=("pe", 1) if kt == NKT - 1 else None)
        upgate_ct[tau] = c.counts["pe"]
        if e4 == 3:
            c.guard("wu" + str(qb), "pe")
        if tau - 1 in ymul_ct:
            emit_bd64(tau - 1)
        c.w("scalar", "pe", upgate_ct[tau])
        if tau - 2 in ymul_ct:
            c.w("scalar", "dve", ymul_ct[tau - 2])
        c.act(t["gsil"][pp][:, :], t["psB"][pp][:, :], AF.Silu, inc=("act", 1))
        silu_ct[tau] = c.counts["act"]
        c.w("vector", "act", silu_ct[tau])
        if tau - 2 in bd64_ct:
            c.w("vector", "pe", bd64_ct[tau - 2])
        c.tt(t["y"][pp][:, :], t["psA"][pp][:, :], t["gsil"][pp][:, :],
             ALU.mult, inc=("dve", 1))
        ymul_ct[tau] = c.counts["dve"]
    emit_bd64(NTAU - 1)

    for half in range(2):
        for t2 in range(NTAU):
            bi = t2 % 2
            wf = ("wfa", "wfb", "wfc", "wfd")[t2 % 4]
            c.await_guards("sync", "wf" + str(bi))
            c.dma("sync", t["wf_sl"][bi][:, :],
                  bass.AP(io["WF_hbm"], t2 * 262144 + half * 1024,
                          [[2048, 128], [1, 1024]]),
                  inc=(wf, 16))
            c.w("tensor", wf)
            if t2 == 0:
                c.w("tensor", "dve")
                c.w("tensor", "act")
            us = t["u"][:, t2 * TC:(t2 + 1) * TC]
            for cc in range(8):
                c.mm(t["ps8"][cc][:, :],
                     t["wf_sl"][bi][:, cc * 128:(cc + 1) * 128], us,
                     start=(t2 == 0), stop=(t2 == NTAU - 1),
                     inc=("pe", 1) if cc == 7 else None)
            c.guard("wf" + str(bi), "pe")
        c.w("gpsimd", "pe")
        c.w("vector", "pe")
        for cc in range(8):
            co = half * 8 + cc
            bi2 = cc % 2
            rl = "rla" if bi2 == 0 else "rlb"
            st_ = "sta" if bi2 == 0 else "stb"
            c.await_guards("gpsimd", "rbuf" + str(bi2))
            c.dma("gpsimd", t["rbuf"][bi2][:, :],
                  bass.AP(io[h_in], co * 128 * T + row_b, [[T, 128], [1, TC]]),
                  inc=(rl, 16))
            c.w("vector", rl)
            c.await_guards("vector", "hnx" + str(bi2))
            c.tt(t["hnx"][bi2][:, :], t["ps8"][cc][:, :], t["rbuf"][bi2][:, :],
                 ALU.add, inc=("dve", 1))
            c.guard("rbuf" + str(bi2), "dve")
            c.w("gpsimd", "dve")
            c.dma("gpsimd",
                  bass.AP(io[h_out], co * 128 * T + row_b, [[T, 128], [1, TC]]),
                  t["hnx"][bi2][:, :], inc=(st_, 16))
            c.guard("hnx" + str(bi2), st_)



def declare_ffn_io(nc, T, debug_w=False):
    io = {}
    io["h_in"] = nc.dram_tensor("h_in", [D, T], F32, kind="ExternalInput")
    io["h_out"] = nc.dram_tensor("h_out", [D, T], F32, kind="ExternalOutput")
    for nm, shp in (("Wu_st", [16, 128, INTER]), ("Wg_st", [16, 128, INTER]),
                    ("Wd_st", [64, 128, D])):
        io[nm] = nc.dram_tensor(nm, shp, F32, kind="ExternalInput")
    for nm, n in (("ws_up", INTER), ("ws_gate", INTER), ("ws_dn", D)):
        io[nm] = nc.dram_tensor(nm, [1, n], F32, kind="ExternalInput")
    io["su_ug_cols"] = nc.dram_tensor("su_ug_cols", [128, 16], F32,
                                      kind="ExternalInput")
    io["su_dp_cols"] = nc.dram_tensor("su_dp_cols", [128, 64], F32,
                                      kind="ExternalInput")
    io["H128c_i"] = nc.dram_tensor("H128c_i", [128, 128], BF16,
                                   kind="ExternalInput")
    io["BD16p_i"] = nc.dram_tensor("BD16p_i", [128, 128], F32,
                                   kind="ExternalInput")
    io["BD64p_i"] = nc.dram_tensor("BD64p_i", [128, 128], F32,
                                   kind="ExternalInput")
    kw = {"kind": "ExternalOutput"} if debug_w else {}
    io["WU_hbm"] = nc.dram_tensor("WU_hbm", [NTAU * 16 * 128 * 128], BF16, **kw)
    io["WG_hbm"] = nc.dram_tensor("WG_hbm", [NTAU * 16 * 128 * 128], BF16, **kw)
    io["WF_hbm"] = nc.dram_tensor("WF_hbm", [16 * NTAU * 128 * 128], BF16, **kw)
    io["BD64_hbm"] = nc.dram_tensor("BD64_hbm", [128 * 8192], BF16)
    return io


def build_ffn_program(T, debug_w=False, debug_mid=False):
    nc = bass.Bass("TRN2", target_bir_lowering=False)
    stk = ExitStack()
    io = declare_ffn_io(nc, T, debug_w)
    sems = {n: stk.enter_context(nc.semaphore(n)) for n in SEM_NAMES}
    t = alloc_common(nc, stk)
    alloc_psum(nc, stk, t)
    b = Builder(nc, sems)

    fold_stk = ExitStack()
    alloc_fold(nc, fold_stk, t)

    def prog_fold(c):
        emit_load_consts(c, io, t)
        emit_weight_fold(c, io, t)

    b.block(prog_fold)
    fold_stk.close()

    alloc_ffn(nc, stk, t)

    dbg = None
    if debug_mid:
        dbg = {
            "xn": nc.dram_tensor("dbg_xn", [128, NKT * TC], BF16,
                                 kind="ExternalOutput"),
            "z": nc.dram_tensor("dbg_z", [128, NKT * TC], BF16,
                                kind="ExternalOutput"),
            "u": nc.dram_tensor("dbg_u", [128, NTAU * TC], BF16,
                                kind="ExternalOutput"),
        }

    def prog_main(c):
        c.barrier()
        emit_bd_tiles(c, t)
        for tc in range(T // TC):
            emit_ffn_chunk(c, io, t, T, tc * TC, "h_in", "h_out",
                           dbg=dbg if tc == 0 else None)
        c.w("gpsimd", "sta")
        c.w("gpsimd", "stb")
        c.w("gpsimd", "sto")

    b.block(prog_main)
    stk.close()
    return nc



def stage_ffn_inputs(ins):
    H16 = _hadamard(16)
    H64 = _hadamard(64)
    H128 = _hadamard(128)
    W_up = np.asarray(ins["L0_ffn_W_up"], dtype=np.float32)
    W_gate = np.asarray(ins["L0_ffn_W_gate"], dtype=np.float32)
    W_down = np.asarray(ins["L0_ffn_W_down"], dtype=np.float32)
    ws_ug = np.asarray(ins["L0_ffn_Wscale_ug"], dtype=np.float32)
    return {
        "Wu_st": np.ascontiguousarray(
            W_up.reshape(INTER, 128, 16).transpose(2, 1, 0)),
        "Wg_st": np.ascontiguousarray(
            W_gate.reshape(INTER, 128, 16).transpose(2, 1, 0)),
        "Wd_st": np.ascontiguousarray(
            W_down.reshape(D, 128, 64).transpose(2, 1, 0)),
        "ws_up": np.ascontiguousarray(ws_ug[:INTER].reshape(1, INTER)),
        "ws_gate": np.ascontiguousarray(ws_ug[INTER:].reshape(1, INTER)),
        "ws_dn": np.ascontiguousarray(
            np.asarray(ins["L0_ffn_Wscale_dp"], np.float32).reshape(1, D)),
        "su_ug_cols": np.ascontiguousarray(
            np.asarray(ins["L0_ffn_SU_ug"], np.float32).reshape(16, 128).T),
        "su_dp_cols": np.ascontiguousarray(
            np.asarray(ins["L0_ffn_SU_dp"], np.float32).reshape(64, 128).T),
        "H128c_i": H128.astype(NPBF16),
        "BD16p_i": np.ascontiguousarray(
            np.kron(np.eye(8, dtype=np.float32), H16) * C16),
        "BD64p_i": np.ascontiguousarray(
            np.kron(np.eye(2, dtype=np.float32), H64) * C64),
    }



ATTN_SEMS = SEM_NAMES + ("vla", "vlb", "rpa", "rpb")


def alloc_attn_common(nc, stk):
    t = {}

    def sb(name, shape, dt):
        t[name] = stk.enter_context(nc.sbuf_tensor(name, shape, dt))
        return t[name]

    sb("ones128f", [128, 1], F32)
    sb("ones1f", [1, 128], F32)
    sb("ones128b", [128, 1], BF16)
    sb("kvals", [128, 8], F32)
    sb("qn_ws", [64, 2], F32)
    sb("kn_ws", [64, 2], F32)
    sb("qn_wn", [64, 2], F32)
    sb("kn_wn", [64, 2], F32)
    sb("Ksb", [128, 4 * 8192], BF16)
    sb("q_n", [128, 16 * 2048], BF16)
    return t


def alloc_attn_prep(nc, stk, t, sfx=""):
    def sb(name, shape, dt):
        t[name] = stk.enter_context(nc.sbuf_tensor(name + sfx, shape, dt))
        return t[name]
    sb("a_sb", [128, 16 * 512], BF16)
    sb("stat", [1, 4 * TC], F32)
    sb("rbc", [128, TC], F32)
    sb("mrbc", [128, TC], F32)
    t["hbuf"] = [sb("hbuf0", [128, TC], F32), sb("hbuf1", [128, TC], F32)]
    t["hsq"] = [sb("hsq0", [128, TC], F32), sb("hsq1", [128, TC], F32)]
    t["wsl"] = [sb("wsl0", [128, 16 * 128], BF16),
                sb("wsl1", [128, 16 * 128], BF16)]
    t["wvb"] = [sb("wvb0", [128, 512], BF16), sb("wvb1", [128, 512], BF16)]
    t["cosb"] = sb("cosb", [64, 512], F32)
    t["sinb"] = sb("sinb", [64, 512], F32)
    sb("kn0a", [64, 512], F32)
    sb("kn0b", [64, 512], F32)
    t["rp"] = [sb("rp1", [64, 512], F32), sb("rp2", [64, 512], F32),
               sb("rp3", [64, 512], F32), sb("rp4", [64, 512], F32)]
    sb("rrow", [1, 512], F32)
    sb("vsb", [128, 512], BF16)


def alloc_attn_core(nc, stk, t, sfx=""):
    def sb(name, shape, dt):
        t[name] = stk.enter_context(nc.sbuf_tensor(name + sfx, shape, dt))
        return t[name]
    sb("Vsl", [128, 64 * 128], BF16)
    sb("negI", [128, 128], BF16)
    sb("bo", [128, 16], F32)
    sb("Rm", [128, 16 * 512], BF16)
    sb("Tsb", [128, 1024], F32)
    t["Psb"] = [sb("Psb0", [128, 1024], BF16), sb("Psb1", [128, 1024], BF16)]
    t["Stb"] = [stk.enter_context(nc.psum_tensor("Stb0" + sfx, [128, 1024], F32)),
                stk.enter_context(nc.psum_tensor("Stb1" + sfx, [128, 1024], F32))]
    pb0 = stk.enter_context(nc.psum_tensor("pcb0" + sfx, [128, 512], F32))
    pb1 = stk.enter_context(nc.psum_tensor("pcb1" + sfx, [128, 512], F32))
    pc0 = stk.enter_context(nc.psum_tensor("pcc0" + sfx, [128, 512], F32))
    pd0 = stk.enter_context(nc.psum_tensor("pcd0" + sfx, [128, 512], F32))
    t["psB"] = [pb0, pb1]
    t["psC"] = [pc0, pc0]
    t["psD"] = [pd0, pd0]
    sb("y_m", [128, 16 * 512], BF16)
    sb("lrow", [1, 512], F32)
    sb("rbcs", [128, 512], F32)
    t["wsl"] = [sb("wslA", [128, 16 * 128], BF16),
                sb("wslB", [128, 16 * 128], BF16)]
    t["h1t"] = [sb("h1t0", [128, 512], F32), sb("h1t1", [128, 512], F32)]
    t["h2t"] = [sb("h2t0", [128, 512], F32), sb("h2t1", [128, 512], F32)]


def emit_attn_consts(c, io, t, shared=True):
    c.dma("gpsimd", t["qn_ws"][:, :], io["qn_w"].ap(), inc=("pre", 16))
    c.dma("gpsimd", t["kn_ws"][:, :], io["kn_w"].ap(), inc=("pre", 16))
    if shared:
        c.memset(t["ones128f"][:, :], 1.0)
        c.memset(t["ones1f"][:1, :], 1.0)
        c.memset(t["ones128b"][:, :], 1.0)
        c.memset(t["kvals"][:, 0:1], LN_EPS)
        c.memset(t["kvals"][:, 1:2], RMS_EPS)
        c.memset(t["kvals"][:, 2:3], 50.0)
        c.memset(t["kvals"][:, 3:4], ISQD / 50.0)
        c.memset(t["kvals"][:, 4:5], 1.0 / 128.0, inc=("dve", 1))
    else:
        c._inc(c.eng.engine_nop() if c.kind == "vector" else None, ("dve", 1))
    c.w("vector", "pre")
    c.w("vector", "dve")
    c.ts(t["qn_wn"][:, :], t["qn_ws"][:, :], -1.0, ALU.mult)
    c.ts(t["kn_wn"][:, :], t["kn_ws"][:, :], -1.0, ALU.mult, inc=("dve", 1))


def emit_norm_rope(c, t, ps_src, w, wn, cos_ap, sin_ap, out_lo, out_hi):
    rp = t["rp"]
    c.act(t["hsq"][0][:, :], ps_src, AF.Square, inc=("act", 1))
    c.w("tensor", "act")
    c.mm(t["psB"][0][:1, :], t["ones128f"][:, :], t["hsq"][0][:, :],
         True, True, inc=("pe", 1))
    c.w("vector", "pe")
    c.ts(t["rrow"][:1, :], t["psB"][0][:1, :], 1.0 / 128.0, ALU.mult,
         inc=("dve", 1))
    c.w("scalar", "dve")
    c.act(t["rrow"][:1, :], t["rrow"][:1, :], AF.Sqrt, inc=("act", 1))
    c.w("vector", "act")
    c.ts(t["rrow"][:1, :], t["rrow"][:1, :], RMS_EPS, ALU.add, inc=("dve", 1))
    c.w("vector", "dve")
    c.recip(t["rrow"][:1, :], t["rrow"][:1, :], inc=("dve", 1))
    c.w("tensor", "dve")
    c.mm(t["psB"][1][:, :], t["ones1f"][:1, :], t["rrow"][:1, :], True, True,
         inc=("pe", 1))
    c.w("vector", "pe")
    c.vcopy(t["rbc"][:, :512], t["psB"][1][:, :], inc=("dve", 1))
    c.w("vector", "dve")
    c.tt(t["kn0a"][:, :], ps_src[0:64, :], t["rbc"][0:64, :512], ALU.mult,
         inc=("dve", 1))
    c.tt(t["kn0b"][:, :], ps_src[64:128, :], t["rbc"][64:128, :512], ALU.mult,
         inc=("dve", 1))
    c.w("vector", "dve")
    k1 = t["kn0a"][:, :]
    k2 = t["kn0b"][:, :]
    c.tt(rp[0][:, :], k1, cos_ap, ALU.mult, inc=("dve", 1))
    c.tt(rp[1][:, :], k2, sin_ap, ALU.mult, inc=("dve", 1))
    c.w("vector", "dve")
    c.ts(rp[2][:, :], rp[1][:, :], w[:, 1:2], ALU.mult, inc=("dve", 1))
    c.w("vector", "dve")
    c.stt(out_lo, rp[0][:, :], w[:, 0:1], rp[2][:, :], ALU.mult, ALU.add,
          inc=("dve", 1))
    c.tt(rp[1][:, :], k2, cos_ap, ALU.mult, inc=("dve", 1))
    c.tt(rp[3][:, :], k1, sin_ap, ALU.mult, inc=("dve", 1))
    c.w("vector", "dve")
    c.ts(rp[2][:, :], rp[1][:, :], w[:, 1:2], ALU.mult, inc=("dve", 1))
    c.w("vector", "dve")
    c.stt(out_hi, rp[3][:, :], wn[:, 0:1], rp[2][:, :], ALU.mult, ALU.add,
          inc=("dve", 1))


def emit_attn_prep(c, io, t):

    for g in range(16):
        def h_ap(kt, g=g):
            return bass.AP(io["h_full"], kt * 128 * T_FULL_B + g * 512,
                           [[T_FULL_B, 128], [1, TC]])
        emit_ln(c, t, h_ap,
                lambda kt: t["a_sb"][:, kt * 512:(kt + 1) * 512])
        rp_s = "rpa" if g % 2 == 0 else "rpb"
        c.await_guards("gpsimd", "ropeb")
        c.dma("gpsimd", t["cosb"][:, :],
              bass.AP(io["cosk_i"], g * 512, [[8192, 64], [1, 512]]),
              inc=(rp_s, 16))
        c.dma("gpsimd", t["sinb"][:, :],
              bass.AP(io["sink_i"], g * 512, [[8192, 64], [1, 512]]),
              inc=(rp_s, 16))
        for dt_ in range(4):
            wl = "qla" if dt_ % 2 == 0 else "qlb"
            c.await_guards("gpsimd", "wsl" + str(dt_ % 2))
            c.dma("gpsimd", t["wsl"][dt_ % 2][:, :],
                  bass.AP(io["Wk_T"], dt_ * 262144,
                          [[2048, 128], [1, 2048]]),
                  inc=(wl, 16))
            c.w("tensor", wl)
            c.w("tensor", "act")
            c.w("tensor", "dve")
            for kt in range(NKT):
                c.mm(t["psA"][dt_ % 2][:, :],
                     t["wsl"][dt_ % 2][:, kt * 128:(kt + 1) * 128],
                     t["a_sb"][:, kt * 512:(kt + 1) * 512],
                     start=(kt == 0), stop=(kt == NKT - 1),
                     inc=("pe", 1) if kt == NKT - 1 else None)
            c.guard("wsl" + str(dt_ % 2), "pe")
            c.w("scalar", "pe")
            c.w("scalar", "dve")
            c.w("vector", rp_s)
            emit_norm_rope(
                c, t, t["psA"][dt_ % 2][:, :], t["kn_ws"], t["kn_wn"],
                t["cosb"][:, :], t["sinb"][:, :],
                t["Ksb"][0:64, dt_ * 8192 + g * 512:dt_ * 8192 + (g + 1) * 512],
                t["Ksb"][64:128, dt_ * 8192 + g * 512:dt_ * 8192 + (g + 1) * 512])
        c.guard("ropeb", "dve")
        for kt in range(NKT):
            wv = "vla" if kt % 2 == 0 else "vlb"
            c.await_guards("gpsimd", "wvb" + str(kt % 2))
            c.dma("gpsimd", t["wvb"][kt % 2][:, :],
                  bass.AP(io["Wv_T"], kt * 128 * 512, [[512, 128], [1, 512]]),
                  inc=(wv, 16))
            c.w("tensor", wv)
            if kt == 0:
                c.w("tensor", "act")
            for ts_ in range(4):
                c.mm(t["psC"][ts_ % 2][:, :] if ts_ < 2 else t["psD"][ts_ % 2][:, :],
                     t["a_sb"][:, kt * 512 + ts_ * 128:kt * 512 + (ts_ + 1) * 128],
                     t["wvb"][kt % 2][:, :],
                     start=(kt == 0), stop=(kt == NKT - 1),
                     inc=("pe", 1) if ts_ == 3 else None)
            c.guard("wvb" + str(kt % 2), "pe")
        c.w("scalar", "pe")
        for ts_ in range(4):
            ps = t["psC"][ts_ % 2] if ts_ < 2 else t["psD"][ts_ % 2]
            c.act(t["vsb"][:, :], ps[:, :], AF.Copy, inc=("act", 1))
            c.w("gpsimd", "act")
            c.dma("gpsimd",
                  bass.AP(io["v_hbm"], (g * 4 + ts_) * 16384,
                          [[128, 128], [1048576, 4], [1, 128]]),
                  t["vsb"][:, :], inc=("sta", 16))
            c.guard("vsb", "sta")
            c.w("scalar", "sta")

    for m in range(4):
        def h_ap(kt, m=m):
            return io["h_own_ap"](kt, m)
        emit_ln(c, t, h_ap,
                lambda kt: t["a_sb"][:, kt * 512:(kt + 1) * 512])
        rp_s = "rpa" if m % 2 == 0 else "rpb"
        c.await_guards("gpsimd", "ropeb")
        c.dma("gpsimd", t["cosb"][:, :],
              bass.AP(io["cosq_i"], m * 512, [[2048, 64], [1, 512]]),
              inc=(rp_s, 16))
        c.dma("gpsimd", t["sinb"][:, :],
              bass.AP(io["sinq_i"], m * 512, [[2048, 64], [1, 512]]),
              inc=(rp_s, 16))
        c.w("vector", rp_s)
        for h in range(NH):
            wl = "qla" if h % 2 == 0 else "qlb"
            c.await_guards("gpsimd", "wsl" + str(h % 2))
            c.dma("gpsimd", t["wsl"][h % 2][:, :],
                  bass.AP(io["Wq_T"], h * 262144,
                          [[2048, 128], [1, 2048]]),
                  inc=(wl, 16))
            c.w("tensor", wl)
            c.w("tensor", "act")
            c.w("tensor", "dve")
            for kt in range(NKT):
                c.mm(t["psA"][h % 2][:, :],
                     t["wsl"][h % 2][:, kt * 128:(kt + 1) * 128],
                     t["a_sb"][:, kt * 512:(kt + 1) * 512],
                     start=(kt == 0), stop=(kt == NKT - 1),
                     inc=("pe", 1) if kt == NKT - 1 else None)
            c.guard("wsl" + str(h % 2), "pe")
            c.w("scalar", "pe")
            c.w("scalar", "dve")
            emit_norm_rope(
                c, t, t["psA"][h % 2][:, :], t["qn_ws"], t["qn_wn"],
                t["cosb"][:, :], t["sinb"][:, :],
                t["q_n"][0:64, h * 2048 + m * 512:h * 2048 + (m + 1) * 512],
                t["q_n"][64:128, h * 2048 + m * 512:h * 2048 + (m + 1) * 512])
        c.guard("ropeb", "dve")


def emit_attn_core(c, io, t):
    for m in range(4):
        nb = (4 * m + 4) * 4
        for h in range(NH):
            gg = h // 4
            if h % 4 == 0:
                c.await_guards("gpsimd", "Vsl")
                c.dma("gpsimd", t["Vsl"][:, :nb * 128],
                      bass.AP(io["v_hbm"], gg * 1048576,
                              [[128, 128], [16384, nb], [1, 128]]),
                      inc=("vla", 16))
                c.w("tensor", "vla")
            qs = t["q_n"][:, h * 2048 + m * 512:h * 2048 + (m + 1) * 512]
            c.w("tensor", "dve")
            c.w("tensor", "act")
            tanh_ct = {}
            exp_ct = {}
            pv_ct = {}

            def lpv(pr):
                c.w("tensor", "act", exp_ct[pr])
                for half_ in range(2):
                    kb = 2 * pr + half_
                    pb = t["Psb"][pr % 2][:, half_ * 512:(half_ + 1) * 512]
                    c.mm(t["psB"][0][:1, :], t["ones128b"][:, :], pb,
                         start=(kb == 0), stop=(kb == nb - 1), inc=None)
                    c.mm(t["psB"][1][:, :],
                         t["Vsl"][:, kb * 128:(kb + 1) * 128], pb,
                         start=(kb == 0), stop=(kb == nb - 1),
                         inc=("pe", 1))
                pv_ct[pr] = c.counts["pe"]

            for pr in range(nb // 2):
                pp = pr % 2
                if pr >= 2:
                    c.w("tensor", "act", tanh_ct[pr - 2])
                for half_ in range(2):
                    kb = 2 * pr + half_
                    trailing = kb >= nb - 16
                    stp = t["Stb"][pp][:, half_ * 512:(half_ + 1) * 512]
                    c.mm(stp,
                         t["Ksb"][:, gg * 8192 + kb * 128:
                                  gg * 8192 + (kb + 1) * 128],
                         qs, start=True, stop=not trailing,
                         inc=None if trailing else ("pe", 1))
                    if trailing:
                        i_ = kb - (nb - 16)
                        c.mm(stp, t["negI"][:, :],
                             t["Rm"][:, i_ * 512:(i_ + 1) * 512],
                             start=False, stop=True, inc=("pe", 1))
                if pr >= 1:
                    lpv(pr - 1)
                c.w("scalar", "pe")
                c.act(t["Tsb"][:, :], t["Stb"][pp][:, :], AF.Tanh,
                      scale=t["kvals"][:, 3:4], inc=("act", 1))
                tanh_ct[pr] = c.counts["act"]
                if pr >= 2:
                    c.w("scalar", "pe", pv_ct[pr - 2])
                c.w("scalar", "act")
                c.act(t["Psb"][pp][:, :], t["Tsb"][:, :], AF.Exp,
                      scale=t["kvals"][:, 2:3], inc=("act", 1))
                exp_ct[pr] = c.counts["act"]
            lpv(nb // 2 - 1)
            if h % 4 == 3:
                c.guard("Vsl", "pe")
            c.w("vector", "pe")
            c.vcopy(t["lrow"][:1, :], t["psB"][0][:1, :], inc=("dve", 1))
            c.w("vector", "dve")
            c.recip(t["lrow"][:1, :], t["lrow"][:1, :], inc=("dve", 1))
            c.w("tensor", "dve")
            c.mm(t["psC"][0][:, :], t["ones1f"][:1, :], t["lrow"][:1, :],
                 True, True, inc=("pe", 1))
            c.w("vector", "pe")
            c.vcopy(t["rbcs"][:, :], t["psC"][0][:, :], inc=("dve", 1))
            c.w("vector", "dve")
            c.tt(t["y_m"][:, h * 512:(h + 1) * 512], t["psB"][1][:, :],
                 t["rbcs"][:, :], ALU.mult, inc=("dve", 1))
        for co in range(16):
            bi = co % 2
            wl = "qla" if bi == 0 else "qlb"
            st_ = "sta" if bi == 0 else "stb"
            rl = "rla" if bi == 0 else "rlb"
            c.await_guards("gpsimd", "wslo" + str(bi))
            c.dma("gpsimd", t["wsl"][bi][:, :],
                  bass.AP(io["Wo_T"], co * 262144,
                          [[2048, 128], [1, 2048]]),
                  inc=(wl, 16))
            c.await_guards("gpsimd", "h1t" + str(bi))
            c.dma("gpsimd", t["h1t"][bi][:, :], io["h_own_ap"](co, m),
                  inc=(rl, 16))
            c.w("tensor", wl)
            c.w("tensor", "dve")
            for h in range(NH):
                c.mm(t["psD"][bi][:, :],
                     t["wsl"][bi][:, h * 128:(h + 1) * 128],
                     t["y_m"][:, h * 512:(h + 1) * 512],
                     start=(h == 0), stop=(h == NH - 1),
                     inc=("pe", 1) if h == NH - 1 else None)
            c.guard("wslo" + str(bi), "pe")
            c.w("vector", "pe")
            c.w("vector", rl)
            c.await_guards("vector", "h2t" + str(bi))
            c.stt(t["h2t"][bi][:, :], t["psD"][bi][:, :], t["bo"][:, co:co + 1],
                  t["h1t"][bi][:, :], ALU.add, ALU.add, inc=("dve", 1))
            c.guard("h1t" + str(bi), "dve")
            c.w("gpsimd", "dve")
            c.dma("gpsimd",
                  bass.AP(io["h2_own"], co * 128 * 2048 + m * 512,
                          [[2048, 128], [1, TC]]),
                  t["h2t"][bi][:, :], inc=(st_, 16))
            c.guard("h2t" + str(bi), st_)


T_FULL_B = 8192


def build_attn_program():
    nc = bass.Bass("TRN2", target_bir_lowering=False)
    stk = ExitStack()
    io = {}
    io["h_full"] = nc.dram_tensor("h_full", [D, T_FULL_B], F32,
                                  kind="ExternalInput")
    io["h_own"] = nc.dram_tensor("h_own", [D, 2048], F32, kind="ExternalInput")
    io["h2_own"] = nc.dram_tensor("h2_own", [D, 2048], F32,
                                  kind="ExternalOutput")
    for nm, shp in (("Wq_T", [16, 128, 16, 128]), ("Wk_T", [4, 128, 16, 128]),
                    ("Wv_T", [D, 512]), ("Wo_T", [16, 128, 16, 128])):
        io[nm] = nc.dram_tensor(nm, shp, F32, kind="ExternalInput")
    io["bo_i"] = nc.dram_tensor("bo_i", [128, 16], F32, kind="ExternalInput")
    io["qn_w"] = nc.dram_tensor("qn_w", [64, 2], F32, kind="ExternalInput")
    io["kn_w"] = nc.dram_tensor("kn_w", [64, 2], F32, kind="ExternalInput")
    io["negI_i"] = nc.dram_tensor("negI_i", [128, 128], BF16,
                                  kind="ExternalInput")
    io["Rm_i"] = nc.dram_tensor("Rm_i", [128, 16 * 512], BF16,
                                kind="ExternalInput")
    io["cosk_i"] = nc.dram_tensor("cosk_i", [64, 8192], F32,
                                  kind="ExternalInput")
    io["sink_i"] = nc.dram_tensor("sink_i", [64, 8192], F32,
                                  kind="ExternalInput")
    io["cosq_i"] = nc.dram_tensor("cosq_i", [64, 2048], F32,
                                  kind="ExternalInput")
    io["sinq_i"] = nc.dram_tensor("sinq_i", [64, 2048], F32,
                                  kind="ExternalInput")
    io["v_hbm"] = nc.dram_tensor("v_hbm", [4 * 64 * 128 * 128], BF16)
    io["h_own_ap"] = lambda kt, m: bass.AP(
        io["h_own"], kt * 128 * 2048 + m * 512, [[2048, 128], [1, TC]])

    sems = {n: stk.enter_context(nc.semaphore(n)) for n in ATTN_SEMS}
    t = alloc_attn_common(nc, stk)
    b = Builder(nc, sems)

    prep_stk = ExitStack()
    alloc_psum(nc, prep_stk, t, sfx="P")
    alloc_attn_prep(nc, prep_stk, t)

    def prog_prep(c):
        emit_attn_consts(c, io, t)
        c.w("tensor", "pre")
        c.w("scalar", "pre")
        c.w("gpsimd", "pre")
        emit_attn_prep(c, io, t)

    b.block(prog_prep)
    prep_stk.close()
    for k in ("a_sb", "stat", "rbc", "mrbc", "hbuf", "hsq", "wsl", "wvb",
              "cosb", "sinb", "cosq", "sinq", "kn0", "rp", "rrow", "vsb"):
        t.pop(k, None)
    core_stk = ExitStack()
    alloc_attn_core(nc, core_stk, t)

    def prog_core(c):
        c.barrier()
        c.dma("gpsimd", t["negI"][:, :], io["negI_i"].ap(), inc=("pre", 16))
        c.dma("gpsimd", t["bo"][:, :], io["bo_i"].ap(), inc=("pre", 16))
        c.dma("gpsimd", t["Rm"][:, :], io["Rm_i"].ap(), inc=("pre", 16))
        c.w("tensor", "pre")
        c.w("vector", "pre")
        emit_attn_core(c, io, t)
        c.w("gpsimd", "sta")
        c.w("gpsimd", "stb")

    b.block(prog_core)
    core_stk.close()
    stk.close()
    return nc


def stage_attn_inputs(ins, j):
    Wq = np.asarray(ins["L0_attn_Wq"], np.float32)
    Wk = np.asarray(ins["L0_attn_Wk"], np.float32)
    Wv = np.asarray(ins["L0_attn_Wv"], np.float32)
    Wo = np.asarray(ins["L0_attn_Wo"], np.float32)
    inv_freq = 1.0 / 10000.0 ** (np.arange(0, HD, 2, dtype=np.float32) / HD)
    pos_full = np.arange(T_FULL_B, dtype=np.float32)
    own_tok = np.concatenate(
        [np.arange((4 * mm + j) * 512, (4 * mm + j) * 512 + 512)
         for mm in range(4)]).astype(np.float32)
    frf = pos_full[:, None] * inv_freq[None, :]
    fro = own_tok[:, None] * inv_freq[None, :]
    Rm = np.zeros((128, 16 * 512), dtype=NPBF16)
    Kk = np.arange(128)[:, None]
    tq = np.arange(512)[None, :]
    for i in range(16):
        d_ = 128 * (i - 4 * j)
        Rm[:, i * 512:(i + 1) * 512] = (Kk - tq > -d_).astype(NPBF16)
    return {
        "Wq_T": np.ascontiguousarray(
            Wq.reshape(16, 128, 16, 128).transpose(0, 3, 2, 1)),
        "Wk_T": np.ascontiguousarray(
            Wk.reshape(4, 128, 16, 128).transpose(0, 3, 2, 1)),
        "Wv_T": np.ascontiguousarray(Wv.T),
        "Wo_T": np.ascontiguousarray(
            Wo.reshape(16, 128, 16, 128).transpose(0, 3, 2, 1)),
        "bo_i": np.ascontiguousarray(
            np.asarray(ins["L0_attn_bo"], np.float32).reshape(16, 128).T),
        "qn_w": np.ascontiguousarray(
            np.asarray(ins["L0_attn_q_norm_w"], np.float32).reshape(2, 64).T),
        "kn_w": np.ascontiguousarray(
            np.asarray(ins["L0_attn_k_norm_w"], np.float32).reshape(2, 64).T),
        "negI_i": (NEG_BIG * np.eye(128, dtype=np.float32)).astype(NPBF16),
        "Rm_i": Rm,
        "cosk_i": np.ascontiguousarray(np.cos(frf).T.astype(np.float32)),
        "sink_i": np.ascontiguousarray(np.sin(frf).T.astype(np.float32)),
        "cosq_i": np.ascontiguousarray(np.cos(fro).T.astype(np.float32)),
        "sinq_i": np.ascontiguousarray(np.sin(fro).T.astype(np.float32)),
    }


def own_token_index(j):
    return np.concatenate(
        [np.arange((4 * m + j) * 512, (4 * m + j + 1) * 512) for m in range(4)])



FUSED_SEMS = SEM_NAMES + ("vla", "vlb", "rpa", "rpb", "cpy")


def alloc_attn_extra(nc, stk, t):
    def sb(name, shape, dt):
        t[name] = stk.enter_context(nc.sbuf_tensor(name, shape, dt))
        return t[name]
    sb("qn_ws", [64, 2], F32)
    sb("kn_ws", [64, 2], F32)
    sb("qn_wn", [64, 2], F32)
    sb("kn_wn", [64, 2], F32)
    sb("Ksb", [128, 4 * 8192], BF16)
    sb("q_n", [128, 16 * 2048], BF16)


def build_fused_program():
    nc = bass.Bass("TRN2", target_bir_lowering=False)
    stk = ExitStack()
    io = {}
    io["x_fm"] = nc.dram_tensor("x_fm", [D, T_FULL_B], F32,
                                kind="ExternalInput")
    io["out_fm"] = nc.dram_tensor("out_fm", [D, 2048], F32,
                                  kind="ExternalOutput")
    io["own_off"] = nc.dram_tensor("own_off", [1, 4], I32,
                                   kind="ExternalInput")
    for nm, shp in (("Wu_st", [16, 128, INTER]), ("Wg_st", [16, 128, INTER]),
                    ("Wd_st", [64, 128, D])):
        io[nm] = nc.dram_tensor(nm, shp, F32, kind="ExternalInput")
    for nm, n in (("ws_up", INTER), ("ws_gate", INTER), ("ws_dn", D)):
        io[nm] = nc.dram_tensor(nm, [1, n], F32, kind="ExternalInput")
    io["su_ug_cols"] = nc.dram_tensor("su_ug_cols", [128, 16], F32,
                                      kind="ExternalInput")
    io["su_dp_cols"] = nc.dram_tensor("su_dp_cols", [128, 64], F32,
                                      kind="ExternalInput")
    io["H128c_i"] = nc.dram_tensor("H128c_i", [128, 128], BF16,
                                   kind="ExternalInput")
    io["BD16p_i"] = nc.dram_tensor("BD16p_i", [128, 128], F32,
                                   kind="ExternalInput")
    io["BD64p_i"] = nc.dram_tensor("BD64p_i", [128, 128], F32,
                                   kind="ExternalInput")
    for nm, shp in (("Wq_T", [16, 128, 16, 128]), ("Wk_T", [4, 128, 16, 128]),
                    ("Wv_T", [D, 512]), ("Wo_T", [16, 128, 16, 128])):
        io[nm] = nc.dram_tensor(nm, shp, F32, kind="ExternalInput")
    io["bo_i"] = nc.dram_tensor("bo_i", [128, 16], F32, kind="ExternalInput")
    io["qn_w"] = nc.dram_tensor("qn_w", [64, 2], F32, kind="ExternalInput")
    io["kn_w"] = nc.dram_tensor("kn_w", [64, 2], F32, kind="ExternalInput")
    io["negI_i"] = nc.dram_tensor("negI_i", [128, 128], BF16,
                                  kind="ExternalInput")
    io["Rm_i"] = nc.dram_tensor("Rm_i", [128, 16 * 512], BF16,
                                kind="ExternalInput")
    io["cosk_i"] = nc.dram_tensor("cosk_i", [64, 8192], F32,
                                  kind="ExternalInput")
    io["sink_i"] = nc.dram_tensor("sink_i", [64, 8192], F32,
                                  kind="ExternalInput")
    io["cosq_i"] = nc.dram_tensor("cosq_i", [64, 2048], F32,
                                  kind="ExternalInput")
    io["sinq_i"] = nc.dram_tensor("sinq_i", [64, 2048], F32,
                                  kind="ExternalInput")
    io["WU_hbm"] = nc.dram_tensor("WU_hbm", [NTAU * 16 * 128 * 128], BF16)
    io["WG_hbm"] = nc.dram_tensor("WG_hbm", [NTAU * 16 * 128 * 128], BF16)
    io["WF_hbm"] = nc.dram_tensor("WF_hbm", [16 * NTAU * 128 * 128], BF16)
    io["BD64_hbm"] = nc.dram_tensor("BD64_hbm", [128 * 8192], BF16)
    io["h1_full"] = nc.dram_tensor("h1_full", [D, T_FULL_B], F32)
    io["h1_own"] = nc.dram_tensor("h1_own", [D, 2048], F32)
    io["h2_own"] = nc.dram_tensor("h2_own", [D, 2048], F32)
    io["v_hbm"] = nc.dram_tensor("v_hbm", [4 * 64 * 128 * 128], BF16)
    io["h_full"] = io["h1_full"]
    io["h_own_ap"] = lambda kt, m: bass.AP(
        io["h1_own"], kt * 128 * 2048 + m * 512, [[2048, 128], [1, TC]])

    sems = {n: stk.enter_context(nc.semaphore(n)) for n in FUSED_SEMS}
    t = alloc_common(nc, stk)
    b = Builder(nc, sems)

    ps1_stk = ExitStack()
    alloc_psum(nc, ps1_stk, t, sfx="1")
    fold_stk = ExitStack()
    alloc_fold(nc, fold_stk, t)

    def prog_fold(c):
        emit_load_consts(c, io, t)
        emit_weight_fold(c, io, t)

    b.block(prog_fold)
    fold_stk.close()

    ffn_stk = ExitStack()
    alloc_ffn(nc, ffn_stk, t)

    def prog_ffn1(c):
        c.barrier()
        emit_bd_tiles(c, t)
        for tc in range(T_FULL_B // TC):
            emit_ffn_chunk(c, io, t, T_FULL_B, tc * TC, "x_fm", "h1_full")

    b.block(prog_ffn1)
    ffn_stk.close()
    ps1_stk.close()
    for k in ("u", "xn", "z", "y", "gsil", "stat", "rbc", "mrbc", "hbuf",
              "hsq", "wu_sl", "wg_sl", "wf_sl", "rbuf", "hnx",
              "psA", "psB", "psC", "psD", "ps8"):
        t.pop(k, None)

    offs_stk = ExitStack()
    t["offs"] = offs_stk.enter_context(nc.sbuf_tensor("offs", [1, 4], I32))

    def prog_copy(c):
        c.barrier()
        c.dma("gpsimd", t["offs"][:1, :], io["own_off"].ap(), inc=("pre", 16))
        c.w("gpsimd", "pre")
        if c.kind == "gpsimd":
            with c.eng.register("roff") as roff:
                for m in range(4):
                    c.eng.reg_load(roff, t["offs"][:1, m:m + 1])
                    i = c.eng.dma_start(
                        out=bass.AP(io["h1_own"], m * 512,
                                    [[2048, 2048], [1, 512]]),
                        in_=bass.AP(io["h1_full"], roff,
                                    [[T_FULL_B, 2048], [1, 512]]))
                    i.then_inc(c.sems["cpy"], 16)
        c.counts["cpy"] += 64
        c.w("gpsimd", "cpy")

    b.block(prog_copy)

    attn_stk = ExitStack()
    alloc_attn_extra(nc, attn_stk, t)
    prep_stk = ExitStack()
    alloc_psum(nc, prep_stk, t, sfx="P")
    alloc_attn_prep(nc, prep_stk, t, sfx="AP")

    def prog_prep(c):
        c.barrier()
        emit_attn_consts(c, io, t, shared=False)
        c.w("tensor", "pre")
        c.w("scalar", "pre")
        c.w("gpsimd", "pre")
        emit_attn_prep(c, io, t)

    b.block(prog_prep)
    prep_stk.close()
    for k in ("a_sb", "stat", "rbc", "mrbc", "hbuf", "hsq", "wsl", "wvb",
              "cosb", "sinb", "kn0a", "kn0b", "rp", "rrow", "vsb",
              "psA", "psB", "psC", "psD", "ps8"):
        t.pop(k, None)

    core_stk = ExitStack()
    alloc_attn_core(nc, core_stk, t, sfx="AC")

    def prog_core(c):
        c.barrier()
        c.dma("gpsimd", t["negI"][:, :], io["negI_i"].ap(), inc=("pre", 16))
        c.dma("gpsimd", t["bo"][:, :], io["bo_i"].ap(), inc=("pre", 16))
        c.dma("gpsimd", t["Rm"][:, :], io["Rm_i"].ap(), inc=("pre", 16))
        c.w("tensor", "pre")
        c.w("vector", "pre")
        emit_attn_core(c, io, t)

    b.block(prog_core)
    core_stk.close()
    attn_stk.close()
    for k in ("Vsl", "Tsb", "Psb", "Stb", "y_m", "lrow", "rbcs", "wsl", "h1t",
              "h2t", "negI", "bo", "Rm", "Ksb", "q_n", "qn_ws", "kn_ws",
              "qn_wn", "kn_wn", "psB", "psC", "psD"):
        t.pop(k, None)

    ffn2_stk = ExitStack()
    alloc_psum(nc, ffn2_stk, t, sfx="3")
    alloc_ffn(nc, ffn2_stk, t, sfx="F3")

    def prog_ffn3(c):
        c.barrier()
        emit_bd_tiles(c, t)
        for tc in range(2048 // TC):
            emit_ffn_chunk(c, io, t, 2048, tc * TC, "h2_own", "out_fm")
        c.w("gpsimd", "sta")
        c.w("gpsimd", "stb")

    b.block(prog_ffn3)
    ffn2_stk.close()
    offs_stk.close()
    stk.close()
    return nc


def kernel(**inputs):
    x = np.asarray(inputs["L0_attn_x"], np.float32)
    nc = build_fused_program()
    staged_f = stage_ffn_inputs(inputs)
    in_maps = []
    for c_ in range(8):
        b_, j_ = c_ // 4, c_ % 4
        m = dict(staged_f)
        m.update(stage_attn_inputs(inputs, j_))
        m["x_fm"] = np.ascontiguousarray(x[b_].T)
        m["own_off"] = np.array(
            [[(4 * mm + j_) * 512 for mm in range(4)]], np.int32)
        in_maps.append(m)
    res = run_bass_kernel_spmd(nc, in_maps, core_ids=list(range(8)))
    out = np.empty((2, T_FULL_B, D), np.float32)
    for c_ in range(8):
        b_, j_ = c_ // 4, c_ % 4
        idx = own_token_index(j_)
        out[b_][idx] = res.results[c_]["out_fm"].T
    return out



def kernel_unfused(**inputs):
    x = np.asarray(inputs["L0_attn_x"], np.float32)
    staged_f = stage_ffn_inputs(inputs)

    nc_f = build_ffn_program(2048)
    in_maps = []
    xf = x.reshape(-1, D)
    for c_ in range(8):
        m = dict(staged_f)
        m["h_in"] = np.ascontiguousarray(xf[c_ * 2048:(c_ + 1) * 2048].T)
        in_maps.append(m)
    res1 = run_bass_kernel_spmd(nc_f, in_maps, core_ids=list(range(8)))
    h1 = np.empty((2, T_FULL_B, D), np.float32)
    for c_ in range(8):
        h1.reshape(-1, D)[c_ * 2048:(c_ + 1) * 2048] = res1.results[c_]["h_out"].T

    nc_a = build_attn_program()
    in_maps = []
    for c_ in range(8):
        b_, j_ = c_ // 4, c_ % 4
        m = stage_attn_inputs(inputs, j_)
        m["h_full"] = np.ascontiguousarray(h1[b_].T)
        idx = own_token_index(j_)
        m["h_own"] = np.ascontiguousarray(h1[b_][idx].T)
        in_maps.append(m)
    res2 = run_bass_kernel_spmd(nc_a, in_maps, core_ids=list(range(8)))
    h2 = np.empty((2, T_FULL_B, D), np.float32)
    for c_ in range(8):
        b_, j_ = c_ // 4, c_ % 4
        idx = own_token_index(j_)
        h2[b_][idx] = res2.results[c_]["h2_own"].T

    in_maps = []
    h2f = h2.reshape(-1, D)
    for c_ in range(8):
        m = dict(staged_f)
        m["h_in"] = np.ascontiguousarray(h2f[c_ * 2048:(c_ + 1) * 2048].T)
        in_maps.append(m)
    res3 = run_bass_kernel_spmd(nc_f, in_maps, core_ids=list(range(8)))
    out = np.empty((2, T_FULL_B, D), np.float32)
    for c_ in range(8):
        out.reshape(-1, D)[c_ * 2048:(c_ + 1) * 2048] = \
            res3.results[c_]["h_out"].T
    return out



# version 09
# speedup vs baseline: 1.1447x; 1.0587x over previous
import math
from contextlib import ExitStack

import numpy as np
import ml_dtypes

import concourse.bass as bass
import concourse.mybir as mybir
import concourse.bass_utils as _bass_utils
from concourse.bass_utils import run_bass_kernel_spmd

if not getattr(_bass_utils.run_command, "_ldw_patched", False):
    _orig_run_command = _bass_utils.run_command

    def _run_command_ldw(argv, **kwargs):
        argv = ["--enable-ldw-opt=true" if a == "--enable-ldw-opt=false" else a
                for a in argv]
        return _orig_run_command(argv, **kwargs)

    _run_command_ldw._ldw_patched = True
    _bass_utils.run_command = _run_command_ldw

F32 = mybir.dt.float32
BF16 = mybir.dt.bfloat16
I32 = mybir.dt.int32
AF = mybir.ActivationFunctionType
ALU = mybir.AluOpType

NPBF16 = ml_dtypes.bfloat16

D = 2048
INTER = 8192
NKT = 16
NTAU = 64
HD = 128
NH = 16
NKV = 4
C16 = 1.0 / (math.sqrt(2048.0) * 64.0)
C64 = 1.0 / (math.sqrt(8192.0) * 64.0)
LN_EPS = 1e-5
RMS_EPS = 1e-6
NEG_BIG = -1.0e5
ISQD = 1.0 / math.sqrt(128.0)
TC = 512


def _hadamard(n):
    h = np.array([[1.0]], dtype=np.float32)
    while h.shape[0] < n:
        h = np.block([[h, h], [h, -h]])
    return h.astype(np.float32)



ENGS = ("tensor", "scalar", "vector", "gpsimd", "sync")


class Ctx:

    def __init__(self, kind, sems):
        self.kind = kind
        self.eng = None
        self.sems = sems
        self.counts = {k: 0 for k in sems}
        self.guards = {}

    def _inc(self, instr, inc):
        if inc is None:
            return instr
        sem, n = inc
        self.counts[sem] += n
        if instr is not None:
            instr.then_inc(self.sems[sem], n)
        return instr

    def w(self, kind, sem, cnt=None):
        v = self.counts[sem] if cnt is None else cnt
        if self.kind == kind and v > 0:
            self.eng.wait_ge(self.sems[sem], v)

    def barrier(self):
        for sem, v in self.counts.items():
            if v > 0:
                self.eng.wait_ge(self.sems[sem], v)

    def guard(self, buf, sem):
        self.guards.setdefault(buf, {})[sem] = self.counts[sem]

    def await_guards(self, kind, buf):
        g = self.guards.get(buf)
        if not g:
            return
        if self.kind == kind:
            for sem, v in g.items():
                if v > 0:
                    self.eng.wait_ge(self.sems[sem], v)

    def mm(self, out, lhsT, rhs, start, stop, inc=None):
        i = self.eng.matmul(out, lhsT, rhs, start=start, stop=stop) \
            if self.kind == "tensor" else None
        return self._inc(i, inc)

    def act(self, out, in_, func, bias=0.0, scale=1.0, accum=None, inc=None):
        i = self.eng.activation(out, in_, func, bias=bias, scale=scale,
                                accum_out=accum) \
            if self.kind == "scalar" else None
        return self._inc(i, inc)

    def tt(self, out, a, b, op, inc=None):
        i = self.eng.tensor_tensor(out, a, b, op) \
            if self.kind == "vector" else None
        return self._inc(i, inc)

    def ts(self, out, a, s1, op0, s2=None, op1=ALU.bypass, inc=None):
        i = self.eng.tensor_scalar(out, a, s1, s2, op0, op1) \
            if self.kind == "vector" else None
        return self._inc(i, inc)

    def stt(self, out, in0, scalar, in1, op0, op1, inc=None):
        i = self.eng.scalar_tensor_tensor(out, in0, scalar, in1, op0, op1) \
            if self.kind == "vector" else None
        return self._inc(i, inc)

    def vcopy(self, out, in_, inc=None):
        i = self.eng.tensor_copy(out, in_) if self.kind == "vector" else None
        return self._inc(i, inc)

    def recip(self, out, in_, inc=None):
        i = self.eng.reciprocal(out, in_) if self.kind == "vector" else None
        return self._inc(i, inc)

    def memset(self, ap, v, inc=None):
        i = self.eng.memset(ap, v) if self.kind == "vector" else None
        return self._inc(i, inc)

    def dma(self, queue, out, in_, inc=None, slow=False):
        i = None
        if self.kind == queue:
            kw = {"allow_slow_non_contiguous": True} if slow else {}
            i = self.eng.dma_start(out=out, in_=in_, **kw)
        return self._inc(i, inc)


class Builder:

    def __init__(self, nc, sems):
        self.nc = nc
        self.ctxs = {k: Ctx(k, sems) for k in ENGS}

    def block(self, program):
        with self.nc.Block() as blk:
            for kind in ENGS:
                ctx = self.ctxs[kind]

                def mk(ctx):
                    def f(eng):
                        ctx.eng = eng
                        program(ctx)
                    return f
                getattr(blk, kind)(mk(ctx))



SEM_NAMES = ("pe", "act", "dve", "pre", "sv", "sto",
             "gla", "glb", "gsa", "gsb", "wla", "wlb", "wlc", "wld",
             "wfa", "wfb", "wfc", "wfd", "hla", "hlb", "rla", "rlb",
             "sta", "stb", "qla", "qlb")


def alloc_common(nc, stk):
    t = {}

    def sb(name, shape, dt):
        t[name] = stk.enter_context(nc.sbuf_tensor(name, shape, dt))
        return t[name]

    sb("H128c", [128, 128], BF16)
    sb("BD16p", [128, 128], F32)
    sb("BD64p", [128, 128], F32)
    sb("ones128f", [128, 1], F32)
    sb("ones1f", [1, 128], F32)
    sb("c64row", [1, 128], F32)
    sb("c32row", [1, 128], F32)
    sb("su_ug", [128, 16], F32)
    sb("su_dp", [128, 64], F32)
    sb("kvals", [128, 8], F32)
    sb("ones128b", [128, 1], BF16)
    return t


def alloc_psum(nc, stk, t, sfx=""):
    def ps(name, shape):
        t[name] = stk.enter_context(nc.psum_tensor(name + sfx, shape, F32))
        return t[name]
    t["psA"] = [ps("psA0", [128, 512]), ps("psA1", [128, 512])]
    t["psB"] = [ps("psB0", [128, 512]), ps("psB1", [128, 512])]
    t["psC"] = [ps("psC0", [128, 512]), ps("psC1", [128, 512])]
    t["psD"] = [ps("psD0", [128, 512]), ps("psD1", [128, 512])]
    t["ps8"] = t["psA"] + t["psB"] + t["psC"] + t["psD"]


def alloc_fold(nc, stk, t):
    def sb(name, shape, dt):
        t[name] = stk.enter_context(nc.sbuf_tensor(name, shape, dt))
        return t[name]
    sb("sbc", [128, INTER], F32)
    sb("bdt", [128, 128], BF16)
    t["win"] = [sb("win0", [128, INTER], BF16), sb("win1", [128, INTER], BF16)]
    t["wout"] = [sb("wout0", [128, INTER], BF16),
                 sb("wout1", [128, INTER], BF16)]
    sb("svbuf", [1, 512], F32)


def alloc_ffn(nc, stk, t, sfx=""):
    def sb(name, shape, dt):
        t[name] = stk.enter_context(nc.sbuf_tensor(name + sfx, shape, dt))
        return t[name]
    sb("BD16", [128, 16 * 128], BF16)
    sb("u", [128, NTAU * TC], BF16)
    t["xn"] = t["u"][:, :NKT * TC]
    sb("z", [128, NKT * TC], BF16)
    t["y"] = [sb("y0", [128, TC], BF16), sb("y1", [128, TC], BF16)]
    t["gsil"] = [sb("gsil0", [128, TC], F32), sb("gsil1", [128, TC], F32)]
    sb("stat", [1, 4 * TC], F32)
    sb("rbc", [128, TC], F32)
    sb("mrbc", [128, TC], F32)
    t["hbuf"] = [sb("hbuf0", [128, TC], F32), sb("hbuf1", [128, TC], F32)]
    t["hsq"] = [sb("hsq0", [128, TC], F32), sb("hsq1", [128, TC], F32)]
    t["wu_sl"] = [sb("wu_sl0", [128, 16 * 512], BF16),
                  sb("wu_sl1", [128, 16 * 512], BF16)]
    t["wg_sl"] = [sb("wg_sl0", [128, 16 * 512], BF16),
                  sb("wg_sl1", [128, 16 * 512], BF16)]
    t["bd_sl"] = [sb("bd_sl0", [128, 512], BF16),
                  sb("bd_sl1", [128, 512], BF16)]
    t["wf_sl"] = [sb("wf_sl0", [128, 1024], BF16),
                  sb("wf_sl1", [128, 1024], BF16)]
    t["rbuf"] = [sb("rbuf0", [128, 512], F32), sb("rbuf1", [128, 512], F32)]
    t["hnx"] = [sb("hnx0", [128, 512], F32), sb("hnx1", [128, 512], F32)]


def emit_load_consts(c, io, t):
    c.dma("gpsimd", t["H128c"][:, :], io["H128c_i"].ap(), inc=("pre", 16))
    c.dma("gpsimd", t["BD16p"][:, :], io["BD16p_i"].ap(), inc=("pre", 16))
    c.dma("gpsimd", t["BD64p"][:, :], io["BD64p_i"].ap(), inc=("pre", 16))
    c.dma("gpsimd", t["su_ug"][:, :], io["su_ug_cols"].ap(), inc=("pre", 16))
    c.dma("gpsimd", t["su_dp"][:, :], io["su_dp_cols"].ap(), inc=("pre", 16))
    c.memset(t["ones128f"][:, :], 1.0)
    c.memset(t["ones1f"][:1, :], 1.0)
    c.memset(t["c64row"][:1, :], 64.0)
    c.memset(t["kvals"][:, 0:1], LN_EPS)
    c.memset(t["kvals"][:, 1:2], RMS_EPS)
    c.memset(t["kvals"][:, 2:3], 50.0)
    c.memset(t["kvals"][:, 3:4], ISQD / 50.0)
    c.memset(t["kvals"][:, 4:5], 1.0 / 128.0)
    c.memset(t["ones128b"][:, :], 1.0)
    c.memset(t["c32row"][:1, :], 32.0, inc=("dve", 1))
    c.w("vector", "pre")


def emit_bd_tiles(c, t):
    c.w("vector", "pre")
    for k in range(16):
        c.ts(t["BD16"][:, k * 128:(k + 1) * 128], t["BD16p"][:, :],
             t["su_ug"][:, k:k + 1], ALU.mult,
             inc=("dve", 1) if k == 15 else None)


def emit_weight_fold(c, io, t):
    def fold(src_h, dst_h, sv_h, n512, n_slabs, scale_lhs, dst_ap_fn):
        for i in range(n512):
            c.w("gpsimd", "pe")
            c.dma("gpsimd", t["svbuf"][:1, :],
                  bass.AP(io[sv_h], i * 512, [[512, 1], [1, 512]]),
                  inc=("sv", 16))
            c.w("tensor", "pre")
            c.w("tensor", "sv")
            c.w("tensor", "dve")
            c.mm(t["psD"][0][:, :], t[scale_lhs][:1, :], t["svbuf"][:1, :],
                 True, True, inc=("pe", 1))
            c.w("vector", "pe")
            c.vcopy(t["sbc"][:, i * 512:(i + 1) * 512], t["psD"][0][:, :],
                    inc=("dve", 1))
        c.w("vector", "dve")
        for sl in range(n_slabs):
            bi = sl % 2
            gl = "gla" if bi == 0 else "glb"
            gs = "gsa" if bi == 0 else "gsb"
            c.await_guards("gpsimd", "win" + str(bi))
            c.dma("gpsimd", t["win"][bi][:, :n512 * 512],
                  bass.AP(io[src_h], sl * 128 * n512 * 512,
                          [[n512 * 512, 128], [1, n512 * 512]]),
                  inc=(gl, 16))
            c.w("tensor", gl)
            tt_ct = {}
            for n in range(n512):
                if n - 2 in tt_ct:
                    c.w("tensor", "dve", tt_ct[n - 2])
                else:
                    c.w("tensor", "dve")
                c.mm(t["psD"][n % 2][:, :], t["H128c"][:, :],
                     t["win"][bi][:, n * 512:(n + 1) * 512], True, True,
                     inc=("pe", 1))
                c.w("vector", "pe")
                c.await_guards("vector", "wout" + str(bi))
                c.tt(t["wout"][bi][:, n * 512:(n + 1) * 512],
                     t["psD"][n % 2][:, :],
                     t["sbc"][:, n * 512:(n + 1) * 512], ALU.mult,
                     inc=("dve", 1))
                tt_ct[n] = c.counts["dve"]
            c.guard("win" + str(bi), "pe")
            c.w("gpsimd", "dve")
            c.dma("gpsimd", dst_ap_fn(sl), t["wout"][bi][:, :n512 * 512],
                  inc=(gs, 16))
            c.guard("wout" + str(bi), gs)

    fold("Wu_st", "WU_hbm", "ws_up", 16, 16, "c64row",
         lambda sl: bass.AP(io["WU_hbm"], sl * 512,
                            [[8192, 128], [1048576, 16], [1, 512]]))
    fold("Wg_st", "WG_hbm", "ws_gate", 16, 16, "c64row",
         lambda sl: bass.AP(io["WG_hbm"], sl * 512,
                            [[8192, 128], [1048576, 16], [1, 512]]))
    fold("Wd_st", "WF_hbm", "ws_dn", 4, 64, "c32row",
         lambda sl: bass.AP(io["WF_hbm"], sl * 2048,
                            [[131072, 128], [1, 2048]]))
    c.w("vector", "pre")
    for k in range(64):
        c.await_guards("vector", "bdt")
        c.ts(t["bdt"][:, :], t["BD64p"][:, :], t["su_dp"][:, k:k + 1],
             ALU.mult, inc=("dve", 1))
        c.w("gpsimd", "dve")
        c.dma("gpsimd", bass.AP(io["BD64_hbm"], k * 128,
                                [[8192, 128], [1, 128]]),
              t["bdt"][:, :], inc=("gsa", 16))
        c.guard("bdt", "gsa")


def emit_ln(c, t, h_ap_fn, out_fn):
    c.w("tensor", "dve")
    for kt in range(NKT):
        bi = kt % 2
        hl = "hla" if bi == 0 else "hlb"
        c.await_guards("gpsimd", "hbuf" + str(bi))
        c.dma("gpsimd", t["hbuf"][bi][:, :], h_ap_fn(kt), inc=(hl, 16))
        c.w("scalar", hl)
        c.await_guards("scalar", "hsq" + str(bi))
        c.act(t["hsq"][bi][:, :], t["hbuf"][bi][:, :], AF.Square,
              inc=("act", 1))
        c.w("tensor", "act")
        c.w("tensor", hl)
        c.mm(t["psA"][0][:1, :], t["ones128f"][:, :], t["hbuf"][bi][:, :],
             start=(kt == 0), stop=(kt == NKT - 1))
        c.mm(t["psB"][0][:1, :], t["ones128f"][:, :], t["hsq"][bi][:, :],
             start=(kt == 0), stop=(kt == NKT - 1), inc=("pe", 1))
        c.guard("hbuf" + str(bi), "pe")
        c.guard("hsq" + str(bi), "pe")
    st = t["stat"]
    c.w("vector", "pe")
    c.ts(st[:1, 0:TC], t["psA"][0][:1, :], 1.0 / 2048.0, ALU.mult,
         inc=("dve", 1))
    c.ts(st[:1, TC:2 * TC], t["psB"][0][:1, :], 1.0 / 2048.0, ALU.mult,
         inc=("dve", 1))
    c.w("vector", "dve")
    c.tt(st[:1, 2 * TC:3 * TC], st[:1, 0:TC], st[:1, 0:TC], ALU.mult,
         inc=("dve", 1))
    c.w("vector", "dve")
    c.tt(st[:1, 2 * TC:3 * TC], st[:1, TC:2 * TC], st[:1, 2 * TC:3 * TC],
         ALU.subtract, inc=("dve", 1))
    c.w("scalar", "dve")
    c.act(st[:1, 2 * TC:3 * TC], st[:1, 2 * TC:3 * TC], AF.Sqrt,
          bias=t["kvals"][:1, 0:1], inc=("act", 1))
    c.w("vector", "act")
    c.recip(st[:1, 2 * TC:3 * TC], st[:1, 2 * TC:3 * TC], inc=("dve", 1))
    c.w("vector", "dve")
    c.tt(st[:1, 3 * TC:4 * TC], st[:1, 0:TC], st[:1, 2 * TC:3 * TC],
         ALU.mult, inc=("dve", 1))
    c.w("tensor", "dve")
    c.mm(t["psA"][0][:, :], t["ones1f"][:1, :], st[:1, 2 * TC:3 * TC],
         True, True)
    c.mm(t["psB"][0][:, :], t["ones1f"][:1, :], st[:1, 3 * TC:4 * TC],
         True, True, inc=("pe", 1))
    c.w("vector", "pe")
    c.vcopy(t["rbc"][:, :], t["psA"][0][:, :], inc=("dve", 1))
    c.vcopy(t["mrbc"][:, :], t["psB"][0][:, :], inc=("dve", 1))
    c.w("vector", "pe")
    for kt in range(NKT):
        bi = kt % 2
        hl = "hla" if bi == 0 else "hlb"
        c.await_guards("gpsimd", "hbuf" + str(bi))
        c.dma("gpsimd", t["hbuf"][bi][:, :], h_ap_fn(kt), inc=(hl, 16))
        c.w("vector", hl)
        c.w("vector", "dve")
        c.await_guards("vector", "hsq" + str(bi))
        c.tt(t["hsq"][bi][:, :], t["hbuf"][bi][:, :], t["rbc"][:, :],
             ALU.mult, inc=("dve", 1))
        c.w("vector", "dve")
        c.tt(out_fn(kt), t["hsq"][bi][:, :], t["mrbc"][:, :], ALU.subtract,
             inc=("dve", 1))
        c.guard("hbuf" + str(bi), "dve")
        c.guard("hsq" + str(bi), "dve")


def emit_ffn_chunk(c, io, t, T, row_b, h_in, h_out, dbg=None):

    def h_ap(kt):
        return bass.AP(io[h_in], kt * 128 * T + row_b, [[T, 128], [1, TC]])

    def dump(name, src_ap):
        if dbg is None or name not in dbg:
            return
        c.w("gpsimd", "dve")
        c.w("gpsimd", "act")
        c.w("gpsimd", "pe")
        c.dma("gpsimd", dbg[name].ap(), src_ap, inc=("sto", 16))

    c.w("vector", "pe")
    emit_ln(c, t, h_ap, lambda kt: t["xn"][:, kt * TC:(kt + 1) * TC])
    dump("xn", t["xn"][:, :])

    c.w("tensor", "pre")
    c.w("tensor", "act")
    c.w("tensor", "dve")
    zev_ct = {}
    for kt in range(NKT):
        pb = kt % 2
        if kt - 2 in zev_ct:
            c.w("tensor", "act", zev_ct[kt - 2])
        c.mm(t["psA"][pb][:, :], t["BD16"][:, kt * 128:(kt + 1) * 128],
             t["xn"][:, kt * TC:(kt + 1) * TC], True, True, inc=("pe", 1))
        c.w("scalar", "pe")
        c.act(t["z"][:, kt * TC:(kt + 1) * TC], t["psA"][pb][:, :], AF.Copy,
              inc=("act", 1))
        zev_ct[kt] = c.counts["act"]
    dump("z", t["z"][:, :])

    upgate_ct = {}
    bd64_ct = {}
    silu_ct = {}
    ucopy_ct = {}
    ymul_ct = {}

    def emit_bd64(tau):
        pp = tau % 2
        qb = (tau // 4) % 2
        c.w("tensor", "dve", ymul_ct[tau])
        if tau - 2 in ucopy_ct:
            c.w("tensor", "act", ucopy_ct[tau - 2])
        c.mm(t["psC"][pp][:, :],
             t["bd_sl"][qb][:, (tau % 4) * 128:(tau % 4 + 1) * 128],
             t["y"][pp][:, :], True, True, inc=("pe", 1))
        if tau % 4 == 3:
            c.guard("bd" + str(qb), "pe")
        bd64_ct[tau] = c.counts["pe"]
        c.w("scalar", "pe", bd64_ct[tau])
        c.act(t["u"][:, tau * TC:(tau + 1) * TC], t["psC"][pp][:, :], AF.Copy,
              inc=("act", 1))
        ucopy_ct[tau] = c.counts["act"]

    for tau in range(NTAU):
        pp = tau % 2
        e4 = tau % 4
        qb = (tau // 4) % 2
        if e4 == 0:
            wl = ("wla", "wlb", "wlc", "wld")[(tau // 4) % 4]
            c.await_guards("sync", "wu" + str(qb))
            c.dma("sync", t["wu_sl"][qb][:, :],
                  bass.AP(io["WU_hbm"], (tau // 4) * 1048576,
                          [[512, 128], [65536, 16], [1, 512]]),
                  inc=(wl, 16))
            c.dma("sync", t["wg_sl"][qb][:, :],
                  bass.AP(io["WG_hbm"], (tau // 4) * 1048576,
                          [[512, 128], [65536, 16], [1, 512]]),
                  inc=(wl, 16))
            c.await_guards("sync", "bd" + str(qb))
            c.dma("sync", t["bd_sl"][qb][:, :],
                  bass.AP(io["BD64_hbm"], (tau // 4) * 512,
                          [[8192, 128], [1, 512]]),
                  inc=(wl, 16))
            c.w("tensor", wl)
        if tau - 2 in silu_ct:
            c.w("tensor", "act", silu_ct[tau - 2])
            c.w("tensor", "dve", ymul_ct[tau - 2])
        else:
            c.w("tensor", "act")
            c.w("tensor", "dve")
        for kt in range(NKT):
            zs = t["z"][:, kt * TC:(kt + 1) * TC]
            c.mm(t["psA"][pp][:, :],
                 t["wu_sl"][qb][:, kt * 512 + e4 * 128:kt * 512 + (e4 + 1) * 128],
                 zs, start=(kt == 0), stop=(kt == NKT - 1))
            c.mm(t["psB"][pp][:, :],
                 t["wg_sl"][qb][:, kt * 512 + e4 * 128:kt * 512 + (e4 + 1) * 128],
                 zs, start=(kt == 0), stop=(kt == NKT - 1),
                 inc=("pe", 1) if kt == NKT - 1 else None)
        upgate_ct[tau] = c.counts["pe"]
        if e4 == 3:
            c.guard("wu" + str(qb), "pe")
        if tau - 1 in ymul_ct:
            emit_bd64(tau - 1)
        c.w("scalar", "pe", upgate_ct[tau])
        if tau - 2 in ymul_ct:
            c.w("scalar", "dve", ymul_ct[tau - 2])
        c.act(t["gsil"][pp][:, :], t["psB"][pp][:, :], AF.Silu, inc=("act", 1))
        silu_ct[tau] = c.counts["act"]
        c.w("vector", "act", silu_ct[tau])
        if tau - 2 in bd64_ct:
            c.w("vector", "pe", bd64_ct[tau - 2])
        c.tt(t["y"][pp][:, :], t["psA"][pp][:, :], t["gsil"][pp][:, :],
             ALU.mult, inc=("dve", 1))
        ymul_ct[tau] = c.counts["dve"]
    emit_bd64(NTAU - 1)

    for half in range(2):
        for t2 in range(NTAU):
            bi = t2 % 2
            wf = ("wfa", "wfb", "wfc", "wfd")[t2 % 4]
            c.await_guards("sync", "wf" + str(bi))
            c.dma("sync", t["wf_sl"][bi][:, :],
                  bass.AP(io["WF_hbm"], t2 * 262144 + half * 1024,
                          [[2048, 128], [1, 1024]]),
                  inc=(wf, 16))
            c.w("tensor", wf)
            if t2 == 0:
                c.w("tensor", "dve")
                c.w("tensor", "act")
            us = t["u"][:, t2 * TC:(t2 + 1) * TC]
            for cc in range(8):
                c.mm(t["ps8"][cc][:, :],
                     t["wf_sl"][bi][:, cc * 128:(cc + 1) * 128], us,
                     start=(t2 == 0), stop=(t2 == NTAU - 1),
                     inc=("pe", 1) if cc == 7 else None)
            c.guard("wf" + str(bi), "pe")
        c.w("gpsimd", "pe")
        c.w("vector", "pe")
        for cc in range(8):
            co = half * 8 + cc
            bi2 = cc % 2
            rl = "rla" if bi2 == 0 else "rlb"
            st_ = "sta" if bi2 == 0 else "stb"
            c.await_guards("gpsimd", "rbuf" + str(bi2))
            c.dma("gpsimd", t["rbuf"][bi2][:, :],
                  bass.AP(io[h_in], co * 128 * T + row_b, [[T, 128], [1, TC]]),
                  inc=(rl, 16))
            c.w("vector", rl)
            c.await_guards("vector", "hnx" + str(bi2))
            c.tt(t["hnx"][bi2][:, :], t["ps8"][cc][:, :], t["rbuf"][bi2][:, :],
                 ALU.add, inc=("dve", 1))
            c.guard("rbuf" + str(bi2), "dve")
            c.w("gpsimd", "dve")
            c.dma("gpsimd",
                  bass.AP(io[h_out], co * 128 * T + row_b, [[T, 128], [1, TC]]),
                  t["hnx"][bi2][:, :], inc=(st_, 16))
            c.guard("hnx" + str(bi2), st_)



def declare_ffn_io(nc, T, debug_w=False):
    io = {}
    io["h_in"] = nc.dram_tensor("h_in", [D, T], F32, kind="ExternalInput")
    io["h_out"] = nc.dram_tensor("h_out", [D, T], F32, kind="ExternalOutput")
    for nm, shp in (("Wu_st", [16, 128, INTER]), ("Wg_st", [16, 128, INTER]),
                    ("Wd_st", [64, 128, D])):
        io[nm] = nc.dram_tensor(nm, shp, F32, kind="ExternalInput")
    for nm, n in (("ws_up", INTER), ("ws_gate", INTER), ("ws_dn", D)):
        io[nm] = nc.dram_tensor(nm, [1, n], F32, kind="ExternalInput")
    io["su_ug_cols"] = nc.dram_tensor("su_ug_cols", [128, 16], F32,
                                      kind="ExternalInput")
    io["su_dp_cols"] = nc.dram_tensor("su_dp_cols", [128, 64], F32,
                                      kind="ExternalInput")
    io["H128c_i"] = nc.dram_tensor("H128c_i", [128, 128], BF16,
                                   kind="ExternalInput")
    io["BD16p_i"] = nc.dram_tensor("BD16p_i", [128, 128], F32,
                                   kind="ExternalInput")
    io["BD64p_i"] = nc.dram_tensor("BD64p_i", [128, 128], F32,
                                   kind="ExternalInput")
    kw = {"kind": "ExternalOutput"} if debug_w else {}
    io["WU_hbm"] = nc.dram_tensor("WU_hbm", [NTAU * 16 * 128 * 128], BF16, **kw)
    io["WG_hbm"] = nc.dram_tensor("WG_hbm", [NTAU * 16 * 128 * 128], BF16, **kw)
    io["WF_hbm"] = nc.dram_tensor("WF_hbm", [16 * NTAU * 128 * 128], BF16, **kw)
    io["BD64_hbm"] = nc.dram_tensor("BD64_hbm", [128 * 8192], BF16)
    return io


def build_ffn_program(T, debug_w=False, debug_mid=False):
    nc = bass.Bass("TRN2", target_bir_lowering=False)
    stk = ExitStack()
    io = declare_ffn_io(nc, T, debug_w)
    sems = {n: stk.enter_context(nc.semaphore(n)) for n in SEM_NAMES}
    t = alloc_common(nc, stk)
    alloc_psum(nc, stk, t)
    b = Builder(nc, sems)

    fold_stk = ExitStack()
    alloc_fold(nc, fold_stk, t)

    def prog_fold(c):
        emit_load_consts(c, io, t)
        emit_weight_fold(c, io, t)

    b.block(prog_fold)
    fold_stk.close()

    alloc_ffn(nc, stk, t)

    dbg = None
    if debug_mid:
        dbg = {
            "xn": nc.dram_tensor("dbg_xn", [128, NKT * TC], BF16,
                                 kind="ExternalOutput"),
            "z": nc.dram_tensor("dbg_z", [128, NKT * TC], BF16,
                                kind="ExternalOutput"),
            "u": nc.dram_tensor("dbg_u", [128, NTAU * TC], BF16,
                                kind="ExternalOutput"),
        }

    def prog_main(c):
        c.barrier()
        emit_bd_tiles(c, t)
        for tc in range(T // TC):
            emit_ffn_chunk(c, io, t, T, tc * TC, "h_in", "h_out",
                           dbg=dbg if tc == 0 else None)
        c.w("gpsimd", "sta")
        c.w("gpsimd", "stb")
        c.w("gpsimd", "sto")

    b.block(prog_main)
    stk.close()
    return nc



def stage_ffn_inputs(ins):
    H16 = _hadamard(16)
    H64 = _hadamard(64)
    H128 = _hadamard(128)
    W_up = np.asarray(ins["L0_ffn_W_up"], dtype=np.float32)
    W_gate = np.asarray(ins["L0_ffn_W_gate"], dtype=np.float32)
    W_down = np.asarray(ins["L0_ffn_W_down"], dtype=np.float32)
    ws_ug = np.asarray(ins["L0_ffn_Wscale_ug"], dtype=np.float32)
    return {
        "Wu_st": np.ascontiguousarray(
            W_up.reshape(INTER, 128, 16).transpose(2, 1, 0)),
        "Wg_st": np.ascontiguousarray(
            W_gate.reshape(INTER, 128, 16).transpose(2, 1, 0)),
        "Wd_st": np.ascontiguousarray(
            W_down.reshape(D, 128, 64).transpose(2, 1, 0)),
        "ws_up": np.ascontiguousarray(ws_ug[:INTER].reshape(1, INTER)),
        "ws_gate": np.ascontiguousarray(ws_ug[INTER:].reshape(1, INTER)),
        "ws_dn": np.ascontiguousarray(
            np.asarray(ins["L0_ffn_Wscale_dp"], np.float32).reshape(1, D)),
        "su_ug_cols": np.ascontiguousarray(
            np.asarray(ins["L0_ffn_SU_ug"], np.float32).reshape(16, 128).T),
        "su_dp_cols": np.ascontiguousarray(
            np.asarray(ins["L0_ffn_SU_dp"], np.float32).reshape(64, 128).T),
        "H128c_i": H128.astype(NPBF16),
        "BD16p_i": np.ascontiguousarray(
            np.kron(np.eye(8, dtype=np.float32), H16) * C16),
        "BD64p_i": np.ascontiguousarray(
            np.kron(np.eye(2, dtype=np.float32), H64) * C64),
    }



ATTN_SEMS = SEM_NAMES + ("vla", "vlb", "rpa", "rpb")


def alloc_attn_common(nc, stk):
    t = {}

    def sb(name, shape, dt):
        t[name] = stk.enter_context(nc.sbuf_tensor(name, shape, dt))
        return t[name]

    sb("ones128f", [128, 1], F32)
    sb("ones1f", [1, 128], F32)
    sb("ones128b", [128, 1], BF16)
    sb("kvals", [128, 8], F32)
    sb("qn_ws", [64, 2], F32)
    sb("kn_ws", [64, 2], F32)
    sb("qn_wn", [64, 2], F32)
    sb("kn_wn", [64, 2], F32)
    sb("Ksb", [128, 4 * 8192], BF16)
    sb("q_n", [128, 16 * 2048], BF16)
    return t


def alloc_attn_prep(nc, stk, t, sfx=""):
    def sb(name, shape, dt):
        t[name] = stk.enter_context(nc.sbuf_tensor(name + sfx, shape, dt))
        return t[name]
    sb("a_sb", [128, 16 * 512], BF16)
    sb("stat", [1, 4 * TC], F32)
    sb("rbc", [128, TC], F32)
    sb("mrbc", [128, TC], F32)
    t["hbuf"] = [sb("hbuf0", [128, TC], F32), sb("hbuf1", [128, TC], F32)]
    t["hsq"] = [sb("hsq0", [128, TC], F32), sb("hsq1", [128, TC], F32)]
    t["wsl"] = [sb("wsl0", [128, 16 * 128], BF16),
                sb("wsl1", [128, 16 * 128], BF16)]
    t["wvb"] = [sb("wvb0", [128, 512], BF16), sb("wvb1", [128, 512], BF16)]
    t["cosb"] = sb("cosb", [64, 512], F32)
    t["sinb"] = sb("sinb", [64, 512], F32)
    sb("kn0a", [64, 512], F32)
    sb("kn0b", [64, 512], F32)
    t["rp"] = [sb("rp1", [64, 512], F32), sb("rp2", [64, 512], F32),
               sb("rp3", [64, 512], F32), sb("rp4", [64, 512], F32)]
    sb("rrow", [1, 512], F32)
    sb("vsb", [128, 512], BF16)


def alloc_attn_core(nc, stk, t, sfx=""):
    def sb(name, shape, dt):
        t[name] = stk.enter_context(nc.sbuf_tensor(name + sfx, shape, dt))
        return t[name]
    sb("Vsl", [128, 64 * 128], BF16)
    sb("negI", [128, 128], BF16)
    sb("bo", [128, 16], F32)
    sb("Rm", [128, 16 * 512], BF16)
    sb("Tsb", [128, 1024], F32)
    t["Psb"] = [sb("Psb0", [128, 1024], BF16), sb("Psb1", [128, 1024], BF16)]
    t["Stb"] = [stk.enter_context(nc.psum_tensor("Stb0" + sfx, [128, 1024], F32)),
                stk.enter_context(nc.psum_tensor("Stb1" + sfx, [128, 1024], F32))]
    pb0 = stk.enter_context(nc.psum_tensor("pcb0" + sfx, [128, 512], F32))
    pb1 = stk.enter_context(nc.psum_tensor("pcb1" + sfx, [128, 512], F32))
    pc0 = stk.enter_context(nc.psum_tensor("pcc0" + sfx, [128, 512], F32))
    pd0 = stk.enter_context(nc.psum_tensor("pcd0" + sfx, [128, 512], F32))
    t["psB"] = [pb0, pb1]
    t["psC"] = [pc0, pc0]
    t["psD"] = [pd0, pd0]
    sb("y_m", [128, 16 * 512], BF16)
    sb("lrow", [1, 512], F32)
    sb("rbcs", [128, 512], F32)
    t["wsl"] = [sb("wslA", [128, 16 * 128], BF16),
                sb("wslB", [128, 16 * 128], BF16)]
    t["h1t"] = [sb("h1t0", [128, 512], F32), sb("h1t1", [128, 512], F32)]
    t["h2t"] = [sb("h2t0", [128, 512], F32), sb("h2t1", [128, 512], F32)]


def emit_attn_consts(c, io, t, shared=True):
    c.dma("gpsimd", t["qn_ws"][:, :], io["qn_w"].ap(), inc=("pre", 16))
    c.dma("gpsimd", t["kn_ws"][:, :], io["kn_w"].ap(), inc=("pre", 16))
    if shared:
        c.memset(t["ones128f"][:, :], 1.0)
        c.memset(t["ones1f"][:1, :], 1.0)
        c.memset(t["ones128b"][:, :], 1.0)
        c.memset(t["kvals"][:, 0:1], LN_EPS)
        c.memset(t["kvals"][:, 1:2], RMS_EPS)
        c.memset(t["kvals"][:, 2:3], 50.0)
        c.memset(t["kvals"][:, 3:4], ISQD / 50.0)
        c.memset(t["kvals"][:, 4:5], 1.0 / 128.0, inc=("dve", 1))
    else:
        c._inc(c.eng.engine_nop() if c.kind == "vector" else None, ("dve", 1))
    c.w("vector", "pre")
    c.w("vector", "dve")
    c.ts(t["qn_wn"][:, :], t["qn_ws"][:, :], -1.0, ALU.mult)
    c.ts(t["kn_wn"][:, :], t["kn_ws"][:, :], -1.0, ALU.mult, inc=("dve", 1))


def emit_norm_rope(c, t, ps_src, w, wn, cos_ap, sin_ap, out_lo, out_hi):
    rp = t["rp"]
    c.act(t["hsq"][0][:, :], ps_src, AF.Square, inc=("act", 1))
    c.w("tensor", "act")
    c.mm(t["psB"][0][:1, :], t["ones128f"][:, :], t["hsq"][0][:, :],
         True, True, inc=("pe", 1))
    c.w("vector", "pe")
    c.ts(t["rrow"][:1, :], t["psB"][0][:1, :], 1.0 / 128.0, ALU.mult,
         inc=("dve", 1))
    c.w("scalar", "dve")
    c.act(t["rrow"][:1, :], t["rrow"][:1, :], AF.Sqrt, inc=("act", 1))
    c.w("vector", "act")
    c.ts(t["rrow"][:1, :], t["rrow"][:1, :], RMS_EPS, ALU.add, inc=("dve", 1))
    c.w("vector", "dve")
    c.recip(t["rrow"][:1, :], t["rrow"][:1, :], inc=("dve", 1))
    c.w("tensor", "dve")
    c.mm(t["psB"][1][:, :], t["ones1f"][:1, :], t["rrow"][:1, :], True, True,
         inc=("pe", 1))
    c.w("vector", "pe")
    c.vcopy(t["rbc"][:, :512], t["psB"][1][:, :], inc=("dve", 1))
    c.w("vector", "dve")
    c.tt(t["kn0a"][:, :], ps_src[0:64, :], t["rbc"][0:64, :512], ALU.mult,
         inc=("dve", 1))
    c.tt(t["kn0b"][:, :], ps_src[64:128, :], t["rbc"][64:128, :512], ALU.mult,
         inc=("dve", 1))
    c.w("vector", "dve")
    k1 = t["kn0a"][:, :]
    k2 = t["kn0b"][:, :]
    c.tt(rp[0][:, :], k1, cos_ap, ALU.mult, inc=("dve", 1))
    c.tt(rp[1][:, :], k2, sin_ap, ALU.mult, inc=("dve", 1))
    c.w("vector", "dve")
    c.ts(rp[2][:, :], rp[1][:, :], w[:, 1:2], ALU.mult, inc=("dve", 1))
    c.w("vector", "dve")
    c.stt(out_lo, rp[0][:, :], w[:, 0:1], rp[2][:, :], ALU.mult, ALU.add,
          inc=("dve", 1))
    c.tt(rp[1][:, :], k2, cos_ap, ALU.mult, inc=("dve", 1))
    c.tt(rp[3][:, :], k1, sin_ap, ALU.mult, inc=("dve", 1))
    c.w("vector", "dve")
    c.ts(rp[2][:, :], rp[1][:, :], w[:, 1:2], ALU.mult, inc=("dve", 1))
    c.w("vector", "dve")
    c.stt(out_hi, rp[3][:, :], wn[:, 0:1], rp[2][:, :], ALU.mult, ALU.add,
          inc=("dve", 1))


def emit_attn_prep(c, io, t):

    for g in range(16):
        def h_ap(kt, g=g):
            return bass.AP(io["h_full"], kt * 128 * T_FULL_B + g * 512,
                           [[T_FULL_B, 128], [1, TC]])
        emit_ln(c, t, h_ap,
                lambda kt: t["a_sb"][:, kt * 512:(kt + 1) * 512])
        rp_s = "rpa" if g % 2 == 0 else "rpb"
        c.await_guards("gpsimd", "ropeb")
        c.dma("gpsimd", t["cosb"][:, :],
              bass.AP(io["cosk_i"], g * 512, [[8192, 64], [1, 512]]),
              inc=(rp_s, 16))
        c.dma("gpsimd", t["sinb"][:, :],
              bass.AP(io["sink_i"], g * 512, [[8192, 64], [1, 512]]),
              inc=(rp_s, 16))
        for dt_ in range(4):
            wl = "qla" if dt_ % 2 == 0 else "qlb"
            c.await_guards("gpsimd", "wsl" + str(dt_ % 2))
            c.dma("gpsimd", t["wsl"][dt_ % 2][:, :],
                  bass.AP(io["Wk_T"], dt_ * 262144,
                          [[2048, 128], [1, 2048]]),
                  inc=(wl, 16))
            c.w("tensor", wl)
            c.w("tensor", "act")
            c.w("tensor", "dve")
            for kt in range(NKT):
                c.mm(t["psA"][dt_ % 2][:, :],
                     t["wsl"][dt_ % 2][:, kt * 128:(kt + 1) * 128],
                     t["a_sb"][:, kt * 512:(kt + 1) * 512],
                     start=(kt == 0), stop=(kt == NKT - 1),
                     inc=("pe", 1) if kt == NKT - 1 else None)
            c.guard("wsl" + str(dt_ % 2), "pe")
            c.w("scalar", "pe")
            c.w("scalar", "dve")
            c.w("vector", rp_s)
            emit_norm_rope(
                c, t, t["psA"][dt_ % 2][:, :], t["kn_ws"], t["kn_wn"],
                t["cosb"][:, :], t["sinb"][:, :],
                t["Ksb"][0:64, dt_ * 8192 + g * 512:dt_ * 8192 + (g + 1) * 512],
                t["Ksb"][64:128, dt_ * 8192 + g * 512:dt_ * 8192 + (g + 1) * 512])
        c.guard("ropeb", "dve")
        for kt in range(NKT):
            wv = "vla" if kt % 2 == 0 else "vlb"
            c.await_guards("gpsimd", "wvb" + str(kt % 2))
            c.dma("gpsimd", t["wvb"][kt % 2][:, :],
                  bass.AP(io["Wv_T"], kt * 128 * 512, [[512, 128], [1, 512]]),
                  inc=(wv, 16))
            c.w("tensor", wv)
            if kt == 0:
                c.w("tensor", "act")
            for ts_ in range(4):
                c.mm(t["psC"][ts_ % 2][:, :] if ts_ < 2 else t["psD"][ts_ % 2][:, :],
                     t["a_sb"][:, kt * 512 + ts_ * 128:kt * 512 + (ts_ + 1) * 128],
                     t["wvb"][kt % 2][:, :],
                     start=(kt == 0), stop=(kt == NKT - 1),
                     inc=("pe", 1) if ts_ == 3 else None)
            c.guard("wvb" + str(kt % 2), "pe")
        c.w("scalar", "pe")
        for ts_ in range(4):
            ps = t["psC"][ts_ % 2] if ts_ < 2 else t["psD"][ts_ % 2]
            c.act(t["vsb"][:, :], ps[:, :], AF.Copy, inc=("act", 1))
            c.w("gpsimd", "act")
            c.dma("gpsimd",
                  bass.AP(io["v_hbm"], (g * 4 + ts_) * 16384,
                          [[128, 128], [1048576, 4], [1, 128]]),
                  t["vsb"][:, :], inc=("sta", 16))
            c.guard("vsb", "sta")
            c.w("scalar", "sta")

    for m in range(4):
        def h_ap(kt, m=m):
            return io["h_own_ap"](kt, m)
        emit_ln(c, t, h_ap,
                lambda kt: t["a_sb"][:, kt * 512:(kt + 1) * 512])
        rp_s = "rpa" if m % 2 == 0 else "rpb"
        c.await_guards("gpsimd", "ropeb")
        c.dma("gpsimd", t["cosb"][:, :],
              bass.AP(io["cosq_i"], m * 512, [[2048, 64], [1, 512]]),
              inc=(rp_s, 16))
        c.dma("gpsimd", t["sinb"][:, :],
              bass.AP(io["sinq_i"], m * 512, [[2048, 64], [1, 512]]),
              inc=(rp_s, 16))
        c.w("vector", rp_s)
        for h in range(NH):
            wl = "qla" if h % 2 == 0 else "qlb"
            c.await_guards("gpsimd", "wsl" + str(h % 2))
            c.dma("gpsimd", t["wsl"][h % 2][:, :],
                  bass.AP(io["Wq_T"], h * 262144,
                          [[2048, 128], [1, 2048]]),
                  inc=(wl, 16))
            c.w("tensor", wl)
            c.w("tensor", "act")
            c.w("tensor", "dve")
            for kt in range(NKT):
                c.mm(t["psA"][h % 2][:, :],
                     t["wsl"][h % 2][:, kt * 128:(kt + 1) * 128],
                     t["a_sb"][:, kt * 512:(kt + 1) * 512],
                     start=(kt == 0), stop=(kt == NKT - 1),
                     inc=("pe", 1) if kt == NKT - 1 else None)
            c.guard("wsl" + str(h % 2), "pe")
            c.w("scalar", "pe")
            c.w("scalar", "dve")
            emit_norm_rope(
                c, t, t["psA"][h % 2][:, :], t["qn_ws"], t["qn_wn"],
                t["cosb"][:, :], t["sinb"][:, :],
                t["q_n"][0:64, h * 2048 + m * 512:h * 2048 + (m + 1) * 512],
                t["q_n"][64:128, h * 2048 + m * 512:h * 2048 + (m + 1) * 512])
        c.guard("ropeb", "dve")


def emit_attn_core(c, io, t):
    for m in range(4):
        nb = (4 * m + 4) * 4
        for h in range(NH):
            gg = h // 4
            if h % 4 == 0:
                c.await_guards("gpsimd", "Vsl")
                c.dma("gpsimd", t["Vsl"][:, :nb * 128],
                      bass.AP(io["v_hbm"], gg * 1048576,
                              [[128, 128], [16384, nb], [1, 128]]),
                      inc=("vla", 16))
                c.w("tensor", "vla")
            qs = t["q_n"][:, h * 2048 + m * 512:h * 2048 + (m + 1) * 512]
            c.w("tensor", "dve")
            c.w("tensor", "act")
            tanh_ct = {}
            exp_ct = {}
            pv_ct = {}

            def lpv(pr):
                c.w("tensor", "act", exp_ct[pr])
                for half_ in range(2):
                    kb = 2 * pr + half_
                    pb = t["Psb"][pr % 2][:, half_ * 512:(half_ + 1) * 512]
                    c.mm(t["psB"][0][:1, :], t["ones128b"][:, :], pb,
                         start=(kb == 0), stop=(kb == nb - 1), inc=None)
                    c.mm(t["psB"][1][:, :],
                         t["Vsl"][:, kb * 128:(kb + 1) * 128], pb,
                         start=(kb == 0), stop=(kb == nb - 1),
                         inc=("pe", 1))
                pv_ct[pr] = c.counts["pe"]

            for pr in range(nb // 2):
                pp = pr % 2
                if pr >= 2:
                    c.w("tensor", "act", tanh_ct[pr - 2])
                for half_ in range(2):
                    kb = 2 * pr + half_
                    trailing = kb >= nb - 16
                    stp = t["Stb"][pp][:, half_ * 512:(half_ + 1) * 512]
                    c.mm(stp,
                         t["Ksb"][:, gg * 8192 + kb * 128:
                                  gg * 8192 + (kb + 1) * 128],
                         qs, start=True, stop=not trailing,
                         inc=None if trailing else ("pe", 1))
                    if trailing:
                        i_ = kb - (nb - 16)
                        c.mm(stp, t["negI"][:, :],
                             t["Rm"][:, i_ * 512:(i_ + 1) * 512],
                             start=False, stop=True, inc=("pe", 1))
                qk_ct = c.counts["pe"]
                if pr >= 1:
                    lpv(pr - 1)
                c.w("scalar", "pe", qk_ct)
                if pr >= 1:
                    c.w("scalar", "act", exp_ct[pr - 1])
                c.act(t["Tsb"][:, :], t["Stb"][pp][:, :], AF.Tanh,
                      scale=t["kvals"][:, 3:4], inc=("act", 1))
                tanh_ct[pr] = c.counts["act"]
                if pr >= 2:
                    c.w("scalar", "pe", pv_ct[pr - 2])
                c.w("scalar", "act")
                c.act(t["Psb"][pp][:, :], t["Tsb"][:, :], AF.Exp,
                      scale=t["kvals"][:, 2:3], inc=("act", 1))
                exp_ct[pr] = c.counts["act"]
            lpv(nb // 2 - 1)
            if h % 4 == 3:
                c.guard("Vsl", "pe")
            c.w("vector", "pe")
            c.vcopy(t["lrow"][:1, :], t["psB"][0][:1, :], inc=("dve", 1))
            c.w("vector", "dve")
            c.recip(t["lrow"][:1, :], t["lrow"][:1, :], inc=("dve", 1))
            c.w("tensor", "dve")
            c.mm(t["psC"][0][:, :], t["ones1f"][:1, :], t["lrow"][:1, :],
                 True, True, inc=("pe", 1))
            c.w("vector", "pe")
            c.vcopy(t["rbcs"][:, :], t["psC"][0][:, :], inc=("dve", 1))
            c.w("vector", "dve")
            c.tt(t["y_m"][:, h * 512:(h + 1) * 512], t["psB"][1][:, :],
                 t["rbcs"][:, :], ALU.mult, inc=("dve", 1))
        for co in range(16):
            bi = co % 2
            wl = "qla" if bi == 0 else "qlb"
            st_ = "sta" if bi == 0 else "stb"
            rl = "rla" if bi == 0 else "rlb"
            c.await_guards("gpsimd", "wslo" + str(bi))
            c.dma("gpsimd", t["wsl"][bi][:, :],
                  bass.AP(io["Wo_T"], co * 262144,
                          [[2048, 128], [1, 2048]]),
                  inc=(wl, 16))
            c.await_guards("gpsimd", "h1t" + str(bi))
            c.dma("gpsimd", t["h1t"][bi][:, :], io["h_own_ap"](co, m),
                  inc=(rl, 16))
            c.w("tensor", wl)
            c.w("tensor", "dve")
            for h in range(NH):
                c.mm(t["psD"][bi][:, :],
                     t["wsl"][bi][:, h * 128:(h + 1) * 128],
                     t["y_m"][:, h * 512:(h + 1) * 512],
                     start=(h == 0), stop=(h == NH - 1),
                     inc=("pe", 1) if h == NH - 1 else None)
            c.guard("wslo" + str(bi), "pe")
            c.w("vector", "pe")
            c.w("vector", rl)
            c.await_guards("vector", "h2t" + str(bi))
            c.stt(t["h2t"][bi][:, :], t["psD"][bi][:, :], t["bo"][:, co:co + 1],
                  t["h1t"][bi][:, :], ALU.add, ALU.add, inc=("dve", 1))
            c.guard("h1t" + str(bi), "dve")
            c.w("gpsimd", "dve")
            c.dma("gpsimd",
                  bass.AP(io["h2_own"], co * 128 * 2048 + m * 512,
                          [[2048, 128], [1, TC]]),
                  t["h2t"][bi][:, :], inc=(st_, 16))
            c.guard("h2t" + str(bi), st_)


T_FULL_B = 8192


def build_attn_program():
    nc = bass.Bass("TRN2", target_bir_lowering=False)
    stk = ExitStack()
    io = {}
    io["h_full"] = nc.dram_tensor("h_full", [D, T_FULL_B], F32,
                                  kind="ExternalInput")
    io["h_own"] = nc.dram_tensor("h_own", [D, 2048], F32, kind="ExternalInput")
    io["h2_own"] = nc.dram_tensor("h2_own", [D, 2048], F32,
                                  kind="ExternalOutput")
    for nm, shp in (("Wq_T", [16, 128, 16, 128]), ("Wk_T", [4, 128, 16, 128]),
                    ("Wv_T", [D, 512]), ("Wo_T", [16, 128, 16, 128])):
        io[nm] = nc.dram_tensor(nm, shp, F32, kind="ExternalInput")
    io["bo_i"] = nc.dram_tensor("bo_i", [128, 16], F32, kind="ExternalInput")
    io["qn_w"] = nc.dram_tensor("qn_w", [64, 2], F32, kind="ExternalInput")
    io["kn_w"] = nc.dram_tensor("kn_w", [64, 2], F32, kind="ExternalInput")
    io["negI_i"] = nc.dram_tensor("negI_i", [128, 128], BF16,
                                  kind="ExternalInput")
    io["Rm_i"] = nc.dram_tensor("Rm_i", [128, 16 * 512], BF16,
                                kind="ExternalInput")
    io["cosk_i"] = nc.dram_tensor("cosk_i", [64, 8192], F32,
                                  kind="ExternalInput")
    io["sink_i"] = nc.dram_tensor("sink_i", [64, 8192], F32,
                                  kind="ExternalInput")
    io["cosq_i"] = nc.dram_tensor("cosq_i", [64, 2048], F32,
                                  kind="ExternalInput")
    io["sinq_i"] = nc.dram_tensor("sinq_i", [64, 2048], F32,
                                  kind="ExternalInput")
    io["v_hbm"] = nc.dram_tensor("v_hbm", [4 * 64 * 128 * 128], BF16)
    io["h_own_ap"] = lambda kt, m: bass.AP(
        io["h_own"], kt * 128 * 2048 + m * 512, [[2048, 128], [1, TC]])

    sems = {n: stk.enter_context(nc.semaphore(n)) for n in ATTN_SEMS}
    t = alloc_attn_common(nc, stk)
    b = Builder(nc, sems)

    prep_stk = ExitStack()
    alloc_psum(nc, prep_stk, t, sfx="P")
    alloc_attn_prep(nc, prep_stk, t)

    def prog_prep(c):
        emit_attn_consts(c, io, t)
        c.w("tensor", "pre")
        c.w("scalar", "pre")
        c.w("gpsimd", "pre")
        emit_attn_prep(c, io, t)

    b.block(prog_prep)
    prep_stk.close()
    for k in ("a_sb", "stat", "rbc", "mrbc", "hbuf", "hsq", "wsl", "wvb",
              "cosb", "sinb", "cosq", "sinq", "kn0", "rp", "rrow", "vsb"):
        t.pop(k, None)
    core_stk = ExitStack()
    alloc_attn_core(nc, core_stk, t)

    def prog_core(c):
        c.barrier()
        c.dma("gpsimd", t["negI"][:, :], io["negI_i"].ap(), inc=("pre", 16))
        c.dma("gpsimd", t["bo"][:, :], io["bo_i"].ap(), inc=("pre", 16))
        c.dma("gpsimd", t["Rm"][:, :], io["Rm_i"].ap(), inc=("pre", 16))
        c.w("tensor", "pre")
        c.w("vector", "pre")
        emit_attn_core(c, io, t)
        c.w("gpsimd", "sta")
        c.w("gpsimd", "stb")

    b.block(prog_core)
    core_stk.close()
    stk.close()
    return nc


def stage_attn_inputs(ins, j):
    Wq = np.asarray(ins["L0_attn_Wq"], np.float32)
    Wk = np.asarray(ins["L0_attn_Wk"], np.float32)
    Wv = np.asarray(ins["L0_attn_Wv"], np.float32)
    Wo = np.asarray(ins["L0_attn_Wo"], np.float32)
    inv_freq = 1.0 / 10000.0 ** (np.arange(0, HD, 2, dtype=np.float32) / HD)
    pos_full = np.arange(T_FULL_B, dtype=np.float32)
    own_tok = np.concatenate(
        [np.arange((4 * mm + j) * 512, (4 * mm + j) * 512 + 512)
         for mm in range(4)]).astype(np.float32)
    frf = pos_full[:, None] * inv_freq[None, :]
    fro = own_tok[:, None] * inv_freq[None, :]
    Rm = np.zeros((128, 16 * 512), dtype=NPBF16)
    Kk = np.arange(128)[:, None]
    tq = np.arange(512)[None, :]
    for i in range(16):
        d_ = 128 * (i - 4 * j)
        Rm[:, i * 512:(i + 1) * 512] = (Kk - tq > -d_).astype(NPBF16)
    return {
        "Wq_T": np.ascontiguousarray(
            Wq.reshape(16, 128, 16, 128).transpose(0, 3, 2, 1)),
        "Wk_T": np.ascontiguousarray(
            Wk.reshape(4, 128, 16, 128).transpose(0, 3, 2, 1)),
        "Wv_T": np.ascontiguousarray(Wv.T),
        "Wo_T": np.ascontiguousarray(
            Wo.reshape(16, 128, 16, 128).transpose(0, 3, 2, 1)),
        "bo_i": np.ascontiguousarray(
            np.asarray(ins["L0_attn_bo"], np.float32).reshape(16, 128).T),
        "qn_w": np.ascontiguousarray(
            np.asarray(ins["L0_attn_q_norm_w"], np.float32).reshape(2, 64).T),
        "kn_w": np.ascontiguousarray(
            np.asarray(ins["L0_attn_k_norm_w"], np.float32).reshape(2, 64).T),
        "negI_i": (NEG_BIG * np.eye(128, dtype=np.float32)).astype(NPBF16),
        "Rm_i": Rm,
        "cosk_i": np.ascontiguousarray(np.cos(frf).T.astype(np.float32)),
        "sink_i": np.ascontiguousarray(np.sin(frf).T.astype(np.float32)),
        "cosq_i": np.ascontiguousarray(np.cos(fro).T.astype(np.float32)),
        "sinq_i": np.ascontiguousarray(np.sin(fro).T.astype(np.float32)),
    }


def own_token_index(j):
    return np.concatenate(
        [np.arange((4 * m + j) * 512, (4 * m + j + 1) * 512) for m in range(4)])



FUSED_SEMS = SEM_NAMES + ("vla", "vlb", "rpa", "rpb", "cpy")


def alloc_attn_extra(nc, stk, t):
    def sb(name, shape, dt):
        t[name] = stk.enter_context(nc.sbuf_tensor(name, shape, dt))
        return t[name]
    sb("qn_ws", [64, 2], F32)
    sb("kn_ws", [64, 2], F32)
    sb("qn_wn", [64, 2], F32)
    sb("kn_wn", [64, 2], F32)
    sb("Ksb", [128, 4 * 8192], BF16)
    sb("q_n", [128, 16 * 2048], BF16)


def build_fused_program():
    nc = bass.Bass("TRN2", target_bir_lowering=False)
    stk = ExitStack()
    io = {}
    io["x_fm"] = nc.dram_tensor("x_fm", [D, T_FULL_B], F32,
                                kind="ExternalInput")
    io["out_fm"] = nc.dram_tensor("out_fm", [D, 2048], F32,
                                  kind="ExternalOutput")
    io["own_off"] = nc.dram_tensor("own_off", [1, 4], I32,
                                   kind="ExternalInput")
    for nm, shp in (("Wu_st", [16, 128, INTER]), ("Wg_st", [16, 128, INTER]),
                    ("Wd_st", [64, 128, D])):
        io[nm] = nc.dram_tensor(nm, shp, F32, kind="ExternalInput")
    for nm, n in (("ws_up", INTER), ("ws_gate", INTER), ("ws_dn", D)):
        io[nm] = nc.dram_tensor(nm, [1, n], F32, kind="ExternalInput")
    io["su_ug_cols"] = nc.dram_tensor("su_ug_cols", [128, 16], F32,
                                      kind="ExternalInput")
    io["su_dp_cols"] = nc.dram_tensor("su_dp_cols", [128, 64], F32,
                                      kind="ExternalInput")
    io["H128c_i"] = nc.dram_tensor("H128c_i", [128, 128], BF16,
                                   kind="ExternalInput")
    io["BD16p_i"] = nc.dram_tensor("BD16p_i", [128, 128], F32,
                                   kind="ExternalInput")
    io["BD64p_i"] = nc.dram_tensor("BD64p_i", [128, 128], F32,
                                   kind="ExternalInput")
    for nm, shp in (("Wq_T", [16, 128, 16, 128]), ("Wk_T", [4, 128, 16, 128]),
                    ("Wv_T", [D, 512]), ("Wo_T", [16, 128, 16, 128])):
        io[nm] = nc.dram_tensor(nm, shp, F32, kind="ExternalInput")
    io["bo_i"] = nc.dram_tensor("bo_i", [128, 16], F32, kind="ExternalInput")
    io["qn_w"] = nc.dram_tensor("qn_w", [64, 2], F32, kind="ExternalInput")
    io["kn_w"] = nc.dram_tensor("kn_w", [64, 2], F32, kind="ExternalInput")
    io["negI_i"] = nc.dram_tensor("negI_i", [128, 128], BF16,
                                  kind="ExternalInput")
    io["Rm_i"] = nc.dram_tensor("Rm_i", [128, 16 * 512], BF16,
                                kind="ExternalInput")
    io["cosk_i"] = nc.dram_tensor("cosk_i", [64, 8192], F32,
                                  kind="ExternalInput")
    io["sink_i"] = nc.dram_tensor("sink_i", [64, 8192], F32,
                                  kind="ExternalInput")
    io["cosq_i"] = nc.dram_tensor("cosq_i", [64, 2048], F32,
                                  kind="ExternalInput")
    io["sinq_i"] = nc.dram_tensor("sinq_i", [64, 2048], F32,
                                  kind="ExternalInput")
    io["WU_hbm"] = nc.dram_tensor("WU_hbm", [NTAU * 16 * 128 * 128], BF16)
    io["WG_hbm"] = nc.dram_tensor("WG_hbm", [NTAU * 16 * 128 * 128], BF16)
    io["WF_hbm"] = nc.dram_tensor("WF_hbm", [16 * NTAU * 128 * 128], BF16)
    io["BD64_hbm"] = nc.dram_tensor("BD64_hbm", [128 * 8192], BF16)
    io["h1_full"] = nc.dram_tensor("h1_full", [D, T_FULL_B], F32)
    io["h1_own"] = nc.dram_tensor("h1_own", [D, 2048], F32)
    io["h2_own"] = nc.dram_tensor("h2_own", [D, 2048], F32)
    io["v_hbm"] = nc.dram_tensor("v_hbm", [4 * 64 * 128 * 128], BF16)
    io["h_full"] = io["h1_full"]
    io["h_own_ap"] = lambda kt, m: bass.AP(
        io["h1_own"], kt * 128 * 2048 + m * 512, [[2048, 128], [1, TC]])

    sems = {n: stk.enter_context(nc.semaphore(n)) for n in FUSED_SEMS}
    t = alloc_common(nc, stk)
    b = Builder(nc, sems)

    ps1_stk = ExitStack()
    alloc_psum(nc, ps1_stk, t, sfx="1")
    fold_stk = ExitStack()
    alloc_fold(nc, fold_stk, t)

    def prog_fold(c):
        emit_load_consts(c, io, t)
        emit_weight_fold(c, io, t)

    b.block(prog_fold)
    fold_stk.close()

    ffn_stk = ExitStack()
    alloc_ffn(nc, ffn_stk, t)

    def prog_ffn1(c):
        c.barrier()
        emit_bd_tiles(c, t)
        for tc in range(T_FULL_B // TC):
            emit_ffn_chunk(c, io, t, T_FULL_B, tc * TC, "x_fm", "h1_full")

    b.block(prog_ffn1)
    ffn_stk.close()
    ps1_stk.close()
    for k in ("u", "xn", "z", "y", "gsil", "stat", "rbc", "mrbc", "hbuf",
              "hsq", "wu_sl", "wg_sl", "wf_sl", "rbuf", "hnx",
              "psA", "psB", "psC", "psD", "ps8"):
        t.pop(k, None)

    offs_stk = ExitStack()
    t["offs"] = offs_stk.enter_context(nc.sbuf_tensor("offs", [1, 4], I32))

    def prog_copy(c):
        c.barrier()
        c.dma("gpsimd", t["offs"][:1, :], io["own_off"].ap(), inc=("pre", 16))
        c.w("gpsimd", "pre")
        if c.kind == "gpsimd":
            with c.eng.register("roff") as roff:
                for m in range(4):
                    c.eng.reg_load(roff, t["offs"][:1, m:m + 1])
                    i = c.eng.dma_start(
                        out=bass.AP(io["h1_own"], m * 512,
                                    [[2048, 2048], [1, 512]]),
                        in_=bass.AP(io["h1_full"], roff,
                                    [[T_FULL_B, 2048], [1, 512]]))
                    i.then_inc(c.sems["cpy"], 16)
        c.counts["cpy"] += 64
        c.w("gpsimd", "cpy")

    b.block(prog_copy)

    attn_stk = ExitStack()
    alloc_attn_extra(nc, attn_stk, t)
    prep_stk = ExitStack()
    alloc_psum(nc, prep_stk, t, sfx="P")
    alloc_attn_prep(nc, prep_stk, t, sfx="AP")

    def prog_prep(c):
        c.barrier()
        emit_attn_consts(c, io, t, shared=False)
        c.w("tensor", "pre")
        c.w("scalar", "pre")
        c.w("gpsimd", "pre")
        emit_attn_prep(c, io, t)

    b.block(prog_prep)
    prep_stk.close()
    for k in ("a_sb", "stat", "rbc", "mrbc", "hbuf", "hsq", "wsl", "wvb",
              "cosb", "sinb", "kn0a", "kn0b", "rp", "rrow", "vsb",
              "psA", "psB", "psC", "psD", "ps8"):
        t.pop(k, None)

    core_stk = ExitStack()
    alloc_attn_core(nc, core_stk, t, sfx="AC")

    def prog_core(c):
        c.barrier()
        c.dma("gpsimd", t["negI"][:, :], io["negI_i"].ap(), inc=("pre", 16))
        c.dma("gpsimd", t["bo"][:, :], io["bo_i"].ap(), inc=("pre", 16))
        c.dma("gpsimd", t["Rm"][:, :], io["Rm_i"].ap(), inc=("pre", 16))
        c.w("tensor", "pre")
        c.w("vector", "pre")
        emit_attn_core(c, io, t)

    b.block(prog_core)
    core_stk.close()
    attn_stk.close()
    for k in ("Vsl", "Tsb", "Psb", "Stb", "y_m", "lrow", "rbcs", "wsl", "h1t",
              "h2t", "negI", "bo", "Rm", "Ksb", "q_n", "qn_ws", "kn_ws",
              "qn_wn", "kn_wn", "psB", "psC", "psD"):
        t.pop(k, None)

    ffn2_stk = ExitStack()
    alloc_psum(nc, ffn2_stk, t, sfx="3")
    alloc_ffn(nc, ffn2_stk, t, sfx="F3")

    def prog_ffn3(c):
        c.barrier()
        emit_bd_tiles(c, t)
        for tc in range(2048 // TC):
            emit_ffn_chunk(c, io, t, 2048, tc * TC, "h2_own", "out_fm")
        c.w("gpsimd", "sta")
        c.w("gpsimd", "stb")

    b.block(prog_ffn3)
    ffn2_stk.close()
    offs_stk.close()
    stk.close()
    return nc


def kernel(**inputs):
    x = np.asarray(inputs["L0_attn_x"], np.float32)
    nc = build_fused_program()
    staged_f = stage_ffn_inputs(inputs)
    in_maps = []
    for c_ in range(8):
        b_, j_ = c_ // 4, c_ % 4
        m = dict(staged_f)
        m.update(stage_attn_inputs(inputs, j_))
        m["x_fm"] = np.ascontiguousarray(x[b_].T)
        m["own_off"] = np.array(
            [[(4 * mm + j_) * 512 for mm in range(4)]], np.int32)
        in_maps.append(m)
    res = run_bass_kernel_spmd(nc, in_maps, core_ids=list(range(8)))
    out = np.empty((2, T_FULL_B, D), np.float32)
    for c_ in range(8):
        b_, j_ = c_ // 4, c_ % 4
        idx = own_token_index(j_)
        out[b_][idx] = res.results[c_]["out_fm"].T
    return out



def kernel_unfused(**inputs):
    x = np.asarray(inputs["L0_attn_x"], np.float32)
    staged_f = stage_ffn_inputs(inputs)

    nc_f = build_ffn_program(2048)
    in_maps = []
    xf = x.reshape(-1, D)
    for c_ in range(8):
        m = dict(staged_f)
        m["h_in"] = np.ascontiguousarray(xf[c_ * 2048:(c_ + 1) * 2048].T)
        in_maps.append(m)
    res1 = run_bass_kernel_spmd(nc_f, in_maps, core_ids=list(range(8)))
    h1 = np.empty((2, T_FULL_B, D), np.float32)
    for c_ in range(8):
        h1.reshape(-1, D)[c_ * 2048:(c_ + 1) * 2048] = res1.results[c_]["h_out"].T

    nc_a = build_attn_program()
    in_maps = []
    for c_ in range(8):
        b_, j_ = c_ // 4, c_ % 4
        m = stage_attn_inputs(inputs, j_)
        m["h_full"] = np.ascontiguousarray(h1[b_].T)
        idx = own_token_index(j_)
        m["h_own"] = np.ascontiguousarray(h1[b_][idx].T)
        in_maps.append(m)
    res2 = run_bass_kernel_spmd(nc_a, in_maps, core_ids=list(range(8)))
    h2 = np.empty((2, T_FULL_B, D), np.float32)
    for c_ in range(8):
        b_, j_ = c_ // 4, c_ % 4
        idx = own_token_index(j_)
        h2[b_][idx] = res2.results[c_]["h2_own"].T

    in_maps = []
    h2f = h2.reshape(-1, D)
    for c_ in range(8):
        m = dict(staged_f)
        m["h_in"] = np.ascontiguousarray(h2f[c_ * 2048:(c_ + 1) * 2048].T)
        in_maps.append(m)
    res3 = run_bass_kernel_spmd(nc_f, in_maps, core_ids=list(range(8)))
    out = np.empty((2, T_FULL_B, D), np.float32)
    for c_ in range(8):
        out.reshape(-1, D)[c_ * 2048:(c_ + 1) * 2048] = \
            res3.results[c_]["h_out"].T
    return out



# version 10
# speedup vs baseline: 1.1978x; 1.0464x over previous
import math
from contextlib import ExitStack

import numpy as np
import ml_dtypes

import concourse.bass as bass
import concourse.mybir as mybir
import concourse.bass_utils as _bass_utils
from concourse.bass_utils import run_bass_kernel_spmd

if not getattr(_bass_utils.run_command, "_ldw_patched", False):
    _orig_run_command = _bass_utils.run_command

    def _run_command_ldw(argv, **kwargs):
        argv = ["--enable-ldw-opt=true" if a == "--enable-ldw-opt=false" else a
                for a in argv]
        return _orig_run_command(argv, **kwargs)

    _run_command_ldw._ldw_patched = True
    _bass_utils.run_command = _run_command_ldw

F32 = mybir.dt.float32
BF16 = mybir.dt.bfloat16
I32 = mybir.dt.int32
AF = mybir.ActivationFunctionType
ALU = mybir.AluOpType

NPBF16 = ml_dtypes.bfloat16

D = 2048
INTER = 8192
NKT = 16
NTAU = 64
HD = 128
NH = 16
NKV = 4
C16 = 1.0 / (math.sqrt(2048.0) * 64.0)
C64 = 1.0 / (math.sqrt(8192.0) * 64.0)
LN_EPS = 1e-5
RMS_EPS = 1e-6
NEG_BIG = -1.0e5
ISQD = 1.0 / math.sqrt(128.0)
TC = 512


def _hadamard(n):
    h = np.array([[1.0]], dtype=np.float32)
    while h.shape[0] < n:
        h = np.block([[h, h], [h, -h]])
    return h.astype(np.float32)



ENGS = ("tensor", "scalar", "vector", "gpsimd", "sync")


class Ctx:

    def __init__(self, kind, sems):
        self.kind = kind
        self.eng = None
        self.sems = sems
        self.counts = {k: 0 for k in sems}
        self.guards = {}

    def _inc(self, instr, inc):
        if inc is None:
            return instr
        sem, n = inc
        self.counts[sem] += n
        if instr is not None:
            instr.then_inc(self.sems[sem], n)
        return instr

    def w(self, kind, sem, cnt=None):
        v = self.counts[sem] if cnt is None else cnt
        if self.kind == kind and v > 0:
            self.eng.wait_ge(self.sems[sem], v)

    def barrier(self):
        for sem, v in self.counts.items():
            if v > 0:
                self.eng.wait_ge(self.sems[sem], v)

    def guard(self, buf, sem):
        self.guards.setdefault(buf, {})[sem] = self.counts[sem]

    def await_guards(self, kind, buf):
        g = self.guards.get(buf)
        if not g:
            return
        if self.kind == kind:
            for sem, v in g.items():
                if v > 0:
                    self.eng.wait_ge(self.sems[sem], v)

    def mm(self, out, lhsT, rhs, start, stop, inc=None):
        i = self.eng.matmul(out, lhsT, rhs, start=start, stop=stop) \
            if self.kind == "tensor" else None
        return self._inc(i, inc)

    def act(self, out, in_, func, bias=0.0, scale=1.0, accum=None, inc=None):
        i = self.eng.activation(out, in_, func, bias=bias, scale=scale,
                                accum_out=accum) \
            if self.kind == "scalar" else None
        return self._inc(i, inc)

    def tt(self, out, a, b, op, inc=None):
        i = self.eng.tensor_tensor(out, a, b, op) \
            if self.kind == "vector" else None
        return self._inc(i, inc)

    def ts(self, out, a, s1, op0, s2=None, op1=ALU.bypass, inc=None):
        i = self.eng.tensor_scalar(out, a, s1, s2, op0, op1) \
            if self.kind == "vector" else None
        return self._inc(i, inc)

    def stt(self, out, in0, scalar, in1, op0, op1, inc=None):
        i = self.eng.scalar_tensor_tensor(out, in0, scalar, in1, op0, op1) \
            if self.kind == "vector" else None
        return self._inc(i, inc)

    def vcopy(self, out, in_, inc=None):
        i = self.eng.tensor_copy(out, in_) if self.kind == "vector" else None
        return self._inc(i, inc)

    def recip(self, out, in_, inc=None):
        i = self.eng.reciprocal(out, in_) if self.kind == "vector" else None
        return self._inc(i, inc)

    def memset(self, ap, v, inc=None):
        i = self.eng.memset(ap, v) if self.kind == "vector" else None
        return self._inc(i, inc)

    def dma(self, queue, out, in_, inc=None, slow=False):
        i = None
        if self.kind == queue:
            kw = {"allow_slow_non_contiguous": True} if slow else {}
            i = self.eng.dma_start(out=out, in_=in_, **kw)
        return self._inc(i, inc)


class Builder:

    def __init__(self, nc, sems):
        self.nc = nc
        self.ctxs = {k: Ctx(k, sems) for k in ENGS}

    def block(self, program):
        with self.nc.Block() as blk:
            for kind in ENGS:
                ctx = self.ctxs[kind]

                def mk(ctx):
                    def f(eng):
                        ctx.eng = eng
                        program(ctx)
                    return f
                getattr(blk, kind)(mk(ctx))



SEM_NAMES = ("pe", "act", "dve", "pre", "sv", "sto",
             "gla", "glb", "gsa", "gsb", "wla", "wlb", "wlc", "wld",
             "wfa", "wfb", "wfc", "wfd", "hla", "hlb", "rla", "rlb",
             "sta", "stb", "qla", "qlb")


def alloc_common(nc, stk):
    t = {}

    def sb(name, shape, dt):
        t[name] = stk.enter_context(nc.sbuf_tensor(name, shape, dt))
        return t[name]

    sb("H128c", [128, 128], BF16)
    sb("BD16p", [128, 128], F32)
    sb("BD64p", [128, 128], F32)
    sb("ones128f", [128, 1], F32)
    sb("ones1f", [1, 128], F32)
    sb("c64row", [1, 128], F32)
    sb("c32row", [1, 128], F32)
    sb("su_ug", [128, 16], F32)
    sb("su_dp", [128, 64], F32)
    sb("kvals", [128, 8], F32)
    sb("ones128b", [128, 1], BF16)
    return t


def alloc_psum(nc, stk, t, sfx=""):
    def ps(name, shape):
        t[name] = stk.enter_context(nc.psum_tensor(name + sfx, shape, F32))
        return t[name]
    t["psA"] = [ps("psA0", [128, 512]), ps("psA1", [128, 512])]
    t["psB"] = [ps("psB0", [128, 512]), ps("psB1", [128, 512])]
    t["psC"] = [ps("psC0", [128, 512]), ps("psC1", [128, 512])]
    t["psD"] = [ps("psD0", [128, 512]), ps("psD1", [128, 512])]
    t["ps8"] = t["psA"] + t["psB"] + t["psC"] + t["psD"]


def alloc_fold(nc, stk, t):
    def sb(name, shape, dt):
        t[name] = stk.enter_context(nc.sbuf_tensor(name, shape, dt))
        return t[name]
    sb("sbc", [128, INTER], F32)
    sb("bdt", [128, 128], BF16)
    t["win"] = [sb("win0", [128, INTER], BF16), sb("win1", [128, INTER], BF16)]
    t["wout"] = [sb("wout0", [128, INTER], BF16),
                 sb("wout1", [128, INTER], BF16)]
    sb("svbuf", [1, 512], F32)


def alloc_ffn(nc, stk, t, sfx=""):
    def sb(name, shape, dt):
        t[name] = stk.enter_context(nc.sbuf_tensor(name + sfx, shape, dt))
        return t[name]
    sb("BD16", [128, 16 * 128], BF16)
    sb("u", [128, NTAU * TC], BF16)
    t["xn"] = t["u"][:, :NKT * TC]
    sb("z", [128, NKT * TC], BF16)
    t["y"] = [sb("y0", [128, TC], BF16), sb("y1", [128, TC], BF16)]
    t["gsil"] = [sb("gsil0", [128, TC], F32), sb("gsil1", [128, TC], F32)]
    sb("stat", [1, 4 * TC], F32)
    sb("rbc", [128, TC], F32)
    sb("mrbc", [128, TC], F32)
    t["hbuf"] = [sb("hbuf0", [128, TC], F32), sb("hbuf1", [128, TC], F32)]
    t["hsq"] = [sb("hsq0", [128, TC], F32), sb("hsq1", [128, TC], F32)]
    t["wu_sl"] = [sb("wu_sl0", [128, 16 * 512], BF16),
                  sb("wu_sl1", [128, 16 * 512], BF16)]
    t["wg_sl"] = [sb("wg_sl0", [128, 16 * 512], BF16),
                  sb("wg_sl1", [128, 16 * 512], BF16)]
    t["bd_sl"] = [sb("bd_sl0", [128, 512], BF16),
                  sb("bd_sl1", [128, 512], BF16)]
    t["wf_sl"] = [sb("wf_sl0", [128, 1024], BF16),
                  sb("wf_sl1", [128, 1024], BF16)]
    t["rbuf"] = [sb("rbuf0", [128, 512], F32), sb("rbuf1", [128, 512], F32)]
    t["hnx"] = [sb("hnx0", [128, 512], F32), sb("hnx1", [128, 512], F32)]


def emit_load_consts(c, io, t):
    c.dma("gpsimd", t["H128c"][:, :], io["H128c_i"].ap(), inc=("pre", 16))
    c.dma("gpsimd", t["BD16p"][:, :], io["BD16p_i"].ap(), inc=("pre", 16))
    c.dma("gpsimd", t["BD64p"][:, :], io["BD64p_i"].ap(), inc=("pre", 16))
    c.dma("gpsimd", t["su_ug"][:, :], io["su_ug_cols"].ap(), inc=("pre", 16))
    c.dma("gpsimd", t["su_dp"][:, :], io["su_dp_cols"].ap(), inc=("pre", 16))
    c.memset(t["ones128f"][:, :], 1.0)
    c.memset(t["ones1f"][:1, :], 1.0)
    c.memset(t["c64row"][:1, :], 64.0)
    c.memset(t["kvals"][:, 0:1], LN_EPS)
    c.memset(t["kvals"][:, 1:2], RMS_EPS)
    c.memset(t["kvals"][:, 2:3], 50.0)
    c.memset(t["kvals"][:, 3:4], ISQD / 50.0)
    c.memset(t["kvals"][:, 4:5], 1.0 / 128.0)
    c.memset(t["ones128b"][:, :], 1.0)
    c.memset(t["c32row"][:1, :], 32.0, inc=("dve", 1))
    c.w("vector", "pre")


def emit_bd_tiles(c, t):
    c.w("vector", "pre")
    for k in range(16):
        c.ts(t["BD16"][:, k * 128:(k + 1) * 128], t["BD16p"][:, :],
             t["su_ug"][:, k:k + 1], ALU.mult,
             inc=("dve", 1) if k == 15 else None)


def emit_weight_fold(c, io, t):
    def fold(src_h, dst_h, sv_h, n512, n_slabs, scale_lhs, dst_ap_fn):
        for i in range(n512):
            c.w("gpsimd", "pe")
            c.dma("gpsimd", t["svbuf"][:1, :],
                  bass.AP(io[sv_h], i * 512, [[512, 1], [1, 512]]),
                  inc=("sv", 16))
            c.w("tensor", "pre")
            c.w("tensor", "sv")
            c.w("tensor", "dve")
            c.mm(t["psD"][0][:, :], t[scale_lhs][:1, :], t["svbuf"][:1, :],
                 True, True, inc=("pe", 1))
            c.w("vector", "pe")
            c.vcopy(t["sbc"][:, i * 512:(i + 1) * 512], t["psD"][0][:, :],
                    inc=("dve", 1))
        c.w("vector", "dve")
        for sl in range(n_slabs):
            bi = sl % 2
            gl = "gla" if bi == 0 else "glb"
            gs = "gsa" if bi == 0 else "gsb"
            c.await_guards("gpsimd", "win" + str(bi))
            c.dma("gpsimd", t["win"][bi][:, :n512 * 512],
                  bass.AP(io[src_h], sl * 128 * n512 * 512,
                          [[n512 * 512, 128], [1, n512 * 512]]),
                  inc=(gl, 16))
            c.w("tensor", gl)
            tt_ct = {}
            for n in range(n512):
                if n - 2 in tt_ct:
                    c.w("tensor", "dve", tt_ct[n - 2])
                else:
                    c.w("tensor", "dve")
                c.mm(t["psD"][n % 2][:, :], t["H128c"][:, :],
                     t["win"][bi][:, n * 512:(n + 1) * 512], True, True,
                     inc=("pe", 1))
                c.w("vector", "pe")
                c.await_guards("vector", "wout" + str(bi))
                c.tt(t["wout"][bi][:, n * 512:(n + 1) * 512],
                     t["psD"][n % 2][:, :],
                     t["sbc"][:, n * 512:(n + 1) * 512], ALU.mult,
                     inc=("dve", 1))
                tt_ct[n] = c.counts["dve"]
            c.guard("win" + str(bi), "pe")
            c.w("gpsimd", "dve")
            c.dma("gpsimd", dst_ap_fn(sl), t["wout"][bi][:, :n512 * 512],
                  inc=(gs, 16))
            c.guard("wout" + str(bi), gs)

    fold("Wu_st", "WU_hbm", "ws_up", 16, 16, "c64row",
         lambda sl: bass.AP(io["WU_hbm"], sl * 512,
                            [[8192, 128], [1048576, 16], [1, 512]]))
    fold("Wg_st", "WG_hbm", "ws_gate", 16, 16, "c64row",
         lambda sl: bass.AP(io["WG_hbm"], sl * 512,
                            [[8192, 128], [1048576, 16], [1, 512]]))
    fold("Wd_st", "WF_hbm", "ws_dn", 4, 64, "c32row",
         lambda sl: bass.AP(io["WF_hbm"], sl * 2048,
                            [[131072, 128], [1, 2048]]))
    c.w("vector", "pre")
    for k in range(64):
        c.await_guards("vector", "bdt")
        c.ts(t["bdt"][:, :], t["BD64p"][:, :], t["su_dp"][:, k:k + 1],
             ALU.mult, inc=("dve", 1))
        c.w("gpsimd", "dve")
        c.dma("gpsimd", bass.AP(io["BD64_hbm"], k * 128,
                                [[8192, 128], [1, 128]]),
              t["bdt"][:, :], inc=("gsa", 16))
        c.guard("bdt", "gsa")


def emit_ln(c, t, h_ap_fn, out_fn):
    c.w("tensor", "dve")
    for kt in range(NKT):
        bi = kt % 2
        hl = "hla" if bi == 0 else "hlb"
        c.await_guards("gpsimd", "hbuf" + str(bi))
        c.dma("gpsimd", t["hbuf"][bi][:, :], h_ap_fn(kt), inc=(hl, 16))
        c.w("scalar", hl)
        c.await_guards("scalar", "hsq" + str(bi))
        c.act(t["hsq"][bi][:, :], t["hbuf"][bi][:, :], AF.Square,
              inc=("act", 1))
        c.w("tensor", "act")
        c.w("tensor", hl)
        c.mm(t["psA"][0][:1, :], t["ones128f"][:, :], t["hbuf"][bi][:, :],
             start=(kt == 0), stop=(kt == NKT - 1))
        c.mm(t["psB"][0][:1, :], t["ones128f"][:, :], t["hsq"][bi][:, :],
             start=(kt == 0), stop=(kt == NKT - 1), inc=("pe", 1))
        c.guard("hbuf" + str(bi), "pe")
        c.guard("hsq" + str(bi), "pe")
    st = t["stat"]
    c.w("vector", "pe")
    c.ts(st[:1, 0:TC], t["psA"][0][:1, :], 1.0 / 2048.0, ALU.mult,
         inc=("dve", 1))
    c.ts(st[:1, TC:2 * TC], t["psB"][0][:1, :], 1.0 / 2048.0, ALU.mult,
         inc=("dve", 1))
    c.w("vector", "dve")
    c.tt(st[:1, 2 * TC:3 * TC], st[:1, 0:TC], st[:1, 0:TC], ALU.mult,
         inc=("dve", 1))
    c.w("vector", "dve")
    c.tt(st[:1, 2 * TC:3 * TC], st[:1, TC:2 * TC], st[:1, 2 * TC:3 * TC],
         ALU.subtract, inc=("dve", 1))
    c.w("scalar", "dve")
    c.act(st[:1, 2 * TC:3 * TC], st[:1, 2 * TC:3 * TC], AF.Sqrt,
          bias=t["kvals"][:1, 0:1], inc=("act", 1))
    c.w("vector", "act")
    c.recip(st[:1, 2 * TC:3 * TC], st[:1, 2 * TC:3 * TC], inc=("dve", 1))
    c.w("vector", "dve")
    c.tt(st[:1, 3 * TC:4 * TC], st[:1, 0:TC], st[:1, 2 * TC:3 * TC],
         ALU.mult, inc=("dve", 1))
    c.w("tensor", "dve")
    c.mm(t["psA"][0][:, :], t["ones1f"][:1, :], st[:1, 2 * TC:3 * TC],
         True, True)
    c.mm(t["psB"][0][:, :], t["ones1f"][:1, :], st[:1, 3 * TC:4 * TC],
         True, True, inc=("pe", 1))
    c.w("vector", "pe")
    c.vcopy(t["rbc"][:, :], t["psA"][0][:, :], inc=("dve", 1))
    c.vcopy(t["mrbc"][:, :], t["psB"][0][:, :], inc=("dve", 1))
    c.w("vector", "pe")
    for kt in range(NKT):
        bi = kt % 2
        hl = "hla" if bi == 0 else "hlb"
        c.await_guards("gpsimd", "hbuf" + str(bi))
        c.dma("gpsimd", t["hbuf"][bi][:, :], h_ap_fn(kt), inc=(hl, 16))
        c.w("vector", hl)
        c.w("vector", "dve")
        c.await_guards("vector", "hsq" + str(bi))
        c.tt(t["hsq"][bi][:, :], t["hbuf"][bi][:, :], t["rbc"][:, :],
             ALU.mult, inc=("dve", 1))
        c.w("vector", "dve")
        c.tt(out_fn(kt), t["hsq"][bi][:, :], t["mrbc"][:, :], ALU.subtract,
             inc=("dve", 1))
        c.guard("hbuf" + str(bi), "dve")
        c.guard("hsq" + str(bi), "dve")


def emit_ffn_chunk(c, io, t, T, row_b, h_in, h_out, dbg=None):

    def h_ap(kt):
        return bass.AP(io[h_in], kt * 128 * T + row_b, [[T, 128], [1, TC]])

    def dump(name, src_ap):
        if dbg is None or name not in dbg:
            return
        c.w("gpsimd", "dve")
        c.w("gpsimd", "act")
        c.w("gpsimd", "pe")
        c.dma("gpsimd", dbg[name].ap(), src_ap, inc=("sto", 16))

    c.w("vector", "pe")
    emit_ln(c, t, h_ap, lambda kt: t["xn"][:, kt * TC:(kt + 1) * TC])
    dump("xn", t["xn"][:, :])

    c.w("tensor", "pre")
    c.w("tensor", "act")
    c.w("tensor", "dve")
    zev_ct = {}
    for kt in range(NKT):
        pb = kt % 2
        if kt - 2 in zev_ct:
            c.w("tensor", "act", zev_ct[kt - 2])
        c.mm(t["psA"][pb][:, :], t["BD16"][:, kt * 128:(kt + 1) * 128],
             t["xn"][:, kt * TC:(kt + 1) * TC], True, True, inc=("pe", 1))
        c.w("scalar", "pe")
        c.act(t["z"][:, kt * TC:(kt + 1) * TC], t["psA"][pb][:, :], AF.Copy,
              inc=("act", 1))
        zev_ct[kt] = c.counts["act"]
    dump("z", t["z"][:, :])

    upgate_ct = {}
    bd64_ct = {}
    silu_ct = {}
    ucopy_ct = {}
    ymul_ct = {}

    def emit_bd64(tau):
        pp = tau % 2
        qb = (tau // 4) % 2
        c.w("tensor", "dve", ymul_ct[tau])
        if tau - 2 in ucopy_ct:
            c.w("tensor", "act", ucopy_ct[tau - 2])
        c.mm(t["psC"][pp][:, :],
             t["bd_sl"][qb][:, (tau % 4) * 128:(tau % 4 + 1) * 128],
             t["y"][pp][:, :], True, True, inc=("pe", 1))
        if tau % 4 == 3:
            c.guard("bd" + str(qb), "pe")
        bd64_ct[tau] = c.counts["pe"]
        c.w("scalar", "pe", bd64_ct[tau])
        c.act(t["u"][:, tau * TC:(tau + 1) * TC], t["psC"][pp][:, :], AF.Copy,
              inc=("act", 1))
        ucopy_ct[tau] = c.counts["act"]

    for tau in range(NTAU):
        pp = tau % 2
        e4 = tau % 4
        qb = (tau // 4) % 2
        if e4 == 0:
            wl = ("wla", "wlb", "wlc", "wld")[(tau // 4) % 4]
            c.await_guards("sync", "wu" + str(qb))
            c.dma("sync", t["wu_sl"][qb][:, :],
                  bass.AP(io["WU_hbm"], (tau // 4) * 1048576,
                          [[512, 128], [65536, 16], [1, 512]]),
                  inc=(wl, 16))
            c.dma("sync", t["wg_sl"][qb][:, :],
                  bass.AP(io["WG_hbm"], (tau // 4) * 1048576,
                          [[512, 128], [65536, 16], [1, 512]]),
                  inc=(wl, 16))
            c.await_guards("sync", "bd" + str(qb))
            c.dma("sync", t["bd_sl"][qb][:, :],
                  bass.AP(io["BD64_hbm"], (tau // 4) * 512,
                          [[8192, 128], [1, 512]]),
                  inc=(wl, 16))
            c.w("tensor", wl)
        if tau - 2 in silu_ct:
            c.w("tensor", "act", silu_ct[tau - 2])
            c.w("tensor", "dve", ymul_ct[tau - 2])
        else:
            c.w("tensor", "act")
            c.w("tensor", "dve")
        for kt in range(NKT):
            zs = t["z"][:, kt * TC:(kt + 1) * TC]
            c.mm(t["psA"][pp][:, :],
                 t["wu_sl"][qb][:, kt * 512 + e4 * 128:kt * 512 + (e4 + 1) * 128],
                 zs, start=(kt == 0), stop=(kt == NKT - 1))
            c.mm(t["psB"][pp][:, :],
                 t["wg_sl"][qb][:, kt * 512 + e4 * 128:kt * 512 + (e4 + 1) * 128],
                 zs, start=(kt == 0), stop=(kt == NKT - 1),
                 inc=("pe", 1) if kt == NKT - 1 else None)
        upgate_ct[tau] = c.counts["pe"]
        if e4 == 3:
            c.guard("wu" + str(qb), "pe")
        if tau - 1 in ymul_ct:
            emit_bd64(tau - 1)
        c.w("scalar", "pe", upgate_ct[tau])
        if tau - 2 in ymul_ct:
            c.w("scalar", "dve", ymul_ct[tau - 2])
        c.act(t["gsil"][pp][:, :], t["psB"][pp][:, :], AF.Silu, inc=("act", 1))
        silu_ct[tau] = c.counts["act"]
        c.w("vector", "act", silu_ct[tau])
        if tau - 2 in bd64_ct:
            c.w("vector", "pe", bd64_ct[tau - 2])
        c.tt(t["y"][pp][:, :], t["psA"][pp][:, :], t["gsil"][pp][:, :],
             ALU.mult, inc=("dve", 1))
        ymul_ct[tau] = c.counts["dve"]
    emit_bd64(NTAU - 1)

    for half in range(2):
        for t2 in range(NTAU):
            bi = t2 % 2
            wf = ("wfa", "wfb", "wfc", "wfd")[t2 % 4]
            c.await_guards("sync", "wf" + str(bi))
            c.dma("sync", t["wf_sl"][bi][:, :],
                  bass.AP(io["WF_hbm"], t2 * 262144 + half * 1024,
                          [[2048, 128], [1, 1024]]),
                  inc=(wf, 16))
            c.w("tensor", wf)
            if t2 == 0:
                c.w("tensor", "dve")
                c.w("tensor", "act")
            us = t["u"][:, t2 * TC:(t2 + 1) * TC]
            for cc in range(8):
                c.mm(t["ps8"][cc][:, :],
                     t["wf_sl"][bi][:, cc * 128:(cc + 1) * 128], us,
                     start=(t2 == 0), stop=(t2 == NTAU - 1),
                     inc=("pe", 1) if cc == 7 else None)
            c.guard("wf" + str(bi), "pe")
        c.w("gpsimd", "pe")
        c.w("vector", "pe")
        for cc in range(8):
            co = half * 8 + cc
            bi2 = cc % 2
            rl = "rla" if bi2 == 0 else "rlb"
            st_ = "sta" if bi2 == 0 else "stb"
            c.await_guards("gpsimd", "rbuf" + str(bi2))
            c.dma("gpsimd", t["rbuf"][bi2][:, :],
                  bass.AP(io[h_in], co * 128 * T + row_b, [[T, 128], [1, TC]]),
                  inc=(rl, 16))
            c.w("vector", rl)
            c.await_guards("vector", "hnx" + str(bi2))
            c.tt(t["hnx"][bi2][:, :], t["ps8"][cc][:, :], t["rbuf"][bi2][:, :],
                 ALU.add, inc=("dve", 1))
            c.guard("rbuf" + str(bi2), "dve")
            c.w("gpsimd", "dve")
            c.dma("gpsimd",
                  bass.AP(io[h_out], co * 128 * T + row_b, [[T, 128], [1, TC]]),
                  t["hnx"][bi2][:, :], inc=(st_, 16))
            c.guard("hnx" + str(bi2), st_)



def declare_ffn_io(nc, T, debug_w=False):
    io = {}
    io["h_in"] = nc.dram_tensor("h_in", [D, T], F32, kind="ExternalInput")
    io["h_out"] = nc.dram_tensor("h_out", [D, T], F32, kind="ExternalOutput")
    for nm, shp in (("Wu_st", [16, 128, INTER]), ("Wg_st", [16, 128, INTER]),
                    ("Wd_st", [64, 128, D])):
        io[nm] = nc.dram_tensor(nm, shp, F32, kind="ExternalInput")
    for nm, n in (("ws_up", INTER), ("ws_gate", INTER), ("ws_dn", D)):
        io[nm] = nc.dram_tensor(nm, [1, n], F32, kind="ExternalInput")
    io["su_ug_cols"] = nc.dram_tensor("su_ug_cols", [128, 16], F32,
                                      kind="ExternalInput")
    io["su_dp_cols"] = nc.dram_tensor("su_dp_cols", [128, 64], F32,
                                      kind="ExternalInput")
    io["H128c_i"] = nc.dram_tensor("H128c_i", [128, 128], BF16,
                                   kind="ExternalInput")
    io["BD16p_i"] = nc.dram_tensor("BD16p_i", [128, 128], F32,
                                   kind="ExternalInput")
    io["BD64p_i"] = nc.dram_tensor("BD64p_i", [128, 128], F32,
                                   kind="ExternalInput")
    kw = {"kind": "ExternalOutput"} if debug_w else {}
    io["WU_hbm"] = nc.dram_tensor("WU_hbm", [NTAU * 16 * 128 * 128], BF16, **kw)
    io["WG_hbm"] = nc.dram_tensor("WG_hbm", [NTAU * 16 * 128 * 128], BF16, **kw)
    io["WF_hbm"] = nc.dram_tensor("WF_hbm", [16 * NTAU * 128 * 128], BF16, **kw)
    io["BD64_hbm"] = nc.dram_tensor("BD64_hbm", [128 * 8192], BF16)
    return io


def build_ffn_program(T, debug_w=False, debug_mid=False):
    nc = bass.Bass("TRN2", target_bir_lowering=False)
    stk = ExitStack()
    io = declare_ffn_io(nc, T, debug_w)
    sems = {n: stk.enter_context(nc.semaphore(n)) for n in SEM_NAMES}
    t = alloc_common(nc, stk)
    alloc_psum(nc, stk, t)
    b = Builder(nc, sems)

    fold_stk = ExitStack()
    alloc_fold(nc, fold_stk, t)

    def prog_fold(c):
        emit_load_consts(c, io, t)
        emit_weight_fold(c, io, t)

    b.block(prog_fold)
    fold_stk.close()

    alloc_ffn(nc, stk, t)

    dbg = None
    if debug_mid:
        dbg = {
            "xn": nc.dram_tensor("dbg_xn", [128, NKT * TC], BF16,
                                 kind="ExternalOutput"),
            "z": nc.dram_tensor("dbg_z", [128, NKT * TC], BF16,
                                kind="ExternalOutput"),
            "u": nc.dram_tensor("dbg_u", [128, NTAU * TC], BF16,
                                kind="ExternalOutput"),
        }

    def prog_main(c):
        c.barrier()
        emit_bd_tiles(c, t)
        for tc in range(T // TC):
            emit_ffn_chunk(c, io, t, T, tc * TC, "h_in", "h_out",
                           dbg=dbg if tc == 0 else None)
        c.w("gpsimd", "sta")
        c.w("gpsimd", "stb")
        c.w("gpsimd", "sto")

    b.block(prog_main)
    stk.close()
    return nc



def stage_ffn_inputs(ins):
    H16 = _hadamard(16)
    H64 = _hadamard(64)
    H128 = _hadamard(128)
    W_up = np.asarray(ins["L0_ffn_W_up"], dtype=np.float32)
    W_gate = np.asarray(ins["L0_ffn_W_gate"], dtype=np.float32)
    W_down = np.asarray(ins["L0_ffn_W_down"], dtype=np.float32)
    ws_ug = np.asarray(ins["L0_ffn_Wscale_ug"], dtype=np.float32)
    return {
        "Wu_st": np.ascontiguousarray(
            W_up.reshape(INTER, 128, 16).transpose(2, 1, 0)),
        "Wg_st": np.ascontiguousarray(
            W_gate.reshape(INTER, 128, 16).transpose(2, 1, 0)),
        "Wd_st": np.ascontiguousarray(
            W_down.reshape(D, 128, 64).transpose(2, 1, 0)),
        "ws_up": np.ascontiguousarray(ws_ug[:INTER].reshape(1, INTER)),
        "ws_gate": np.ascontiguousarray(ws_ug[INTER:].reshape(1, INTER)),
        "ws_dn": np.ascontiguousarray(
            np.asarray(ins["L0_ffn_Wscale_dp"], np.float32).reshape(1, D)),
        "su_ug_cols": np.ascontiguousarray(
            np.asarray(ins["L0_ffn_SU_ug"], np.float32).reshape(16, 128).T),
        "su_dp_cols": np.ascontiguousarray(
            np.asarray(ins["L0_ffn_SU_dp"], np.float32).reshape(64, 128).T),
        "H128c_i": H128.astype(NPBF16),
        "BD16p_i": np.ascontiguousarray(
            np.kron(np.eye(8, dtype=np.float32), H16) * C16),
        "BD64p_i": np.ascontiguousarray(
            np.kron(np.eye(2, dtype=np.float32), H64) * C64),
    }



ATTN_SEMS = SEM_NAMES + ("vla", "vlb", "rpa", "rpb")


def alloc_attn_common(nc, stk):
    t = {}

    def sb(name, shape, dt):
        t[name] = stk.enter_context(nc.sbuf_tensor(name, shape, dt))
        return t[name]

    sb("ones128f", [128, 1], F32)
    sb("ones1f", [1, 128], F32)
    sb("ones128b", [128, 1], BF16)
    sb("kvals", [128, 8], F32)
    sb("qn_ws", [64, 2], F32)
    sb("kn_ws", [64, 2], F32)
    sb("qn_wn", [64, 2], F32)
    sb("kn_wn", [64, 2], F32)
    sb("Ksb", [128, 4 * 8192], BF16)
    sb("q_n", [128, 16 * 2048], BF16)
    return t


def alloc_attn_prep(nc, stk, t, sfx=""):
    def sb(name, shape, dt):
        t[name] = stk.enter_context(nc.sbuf_tensor(name + sfx, shape, dt))
        return t[name]
    sb("a_sb", [128, 16 * 512], BF16)
    sb("stat", [1, 4 * TC], F32)
    sb("rbc", [128, TC], F32)
    sb("mrbc", [128, TC], F32)
    t["hbuf"] = [sb("hbuf0", [128, TC], F32), sb("hbuf1", [128, TC], F32)]
    t["hsq"] = [sb("hsq0", [128, TC], F32), sb("hsq1", [128, TC], F32)]
    t["wsl"] = [sb("wsl0", [128, 16 * 128], BF16),
                sb("wsl1", [128, 16 * 128], BF16)]
    t["wvb"] = [sb("wvb0", [128, 512], BF16), sb("wvb1", [128, 512], BF16)]
    t["cosb"] = sb("cosb", [64, 512], F32)
    t["sinb"] = sb("sinb", [64, 512], F32)
    sb("kn0a", [64, 512], F32)
    sb("kn0b", [64, 512], F32)
    t["rp"] = [sb("rp1", [64, 512], F32), sb("rp2", [64, 512], F32),
               sb("rp3", [64, 512], F32), sb("rp4", [64, 512], F32)]
    sb("rrow", [1, 512], F32)
    sb("vsb", [128, 512], BF16)


def alloc_attn_core(nc, stk, t, sfx=""):
    def sb(name, shape, dt):
        t[name] = stk.enter_context(nc.sbuf_tensor(name + sfx, shape, dt))
        return t[name]
    sb("Vsl", [128, 64 * 128], BF16)
    sb("negI", [128, 128], BF16)
    sb("bo", [128, 16], F32)
    sb("Rm", [128, 16 * 512], BF16)
    sb("Tsb", [128, 1024], F32)
    t["Psb"] = [sb("Psb0", [128, 1024], BF16), sb("Psb1", [128, 1024], BF16)]
    t["Stb"] = [stk.enter_context(nc.psum_tensor("Stb0" + sfx, [128, 1024], F32)),
                stk.enter_context(nc.psum_tensor("Stb1" + sfx, [128, 1024], F32))]
    pb0 = stk.enter_context(nc.psum_tensor("pcb0" + sfx, [128, 512], F32))
    pb1 = stk.enter_context(nc.psum_tensor("pcb1" + sfx, [128, 512], F32))
    pc0 = stk.enter_context(nc.psum_tensor("pcc0" + sfx, [128, 512], F32))
    pd0 = stk.enter_context(nc.psum_tensor("pcd0" + sfx, [128, 512], F32))
    t["psB"] = [pb0, pb1]
    t["psC"] = [pc0, pc0]
    t["psD"] = [pd0, pd0]
    sb("y_m", [128, 16 * 512], BF16)
    sb("lrow", [1, 512], F32)
    sb("rbcs", [128, 512], F32)
    t["wsl"] = [sb("wslA", [128, 16 * 128], BF16),
                sb("wslB", [128, 16 * 128], BF16)]
    t["h1t"] = [sb("h1t0", [128, 512], F32), sb("h1t1", [128, 512], F32)]
    t["h2t"] = [sb("h2t0", [128, 512], F32), sb("h2t1", [128, 512], F32)]


def emit_attn_consts(c, io, t, shared=True):
    c.dma("gpsimd", t["qn_ws"][:, :], io["qn_w"].ap(), inc=("pre", 16))
    c.dma("gpsimd", t["kn_ws"][:, :], io["kn_w"].ap(), inc=("pre", 16))
    if shared:
        c.memset(t["ones128f"][:, :], 1.0)
        c.memset(t["ones1f"][:1, :], 1.0)
        c.memset(t["ones128b"][:, :], 1.0)
        c.memset(t["kvals"][:, 0:1], LN_EPS)
        c.memset(t["kvals"][:, 1:2], RMS_EPS)
        c.memset(t["kvals"][:, 2:3], 50.0)
        c.memset(t["kvals"][:, 3:4], ISQD / 50.0)
        c.memset(t["kvals"][:, 4:5], 1.0 / 128.0, inc=("dve", 1))
    else:
        c._inc(c.eng.engine_nop() if c.kind == "vector" else None, ("dve", 1))
    c.w("vector", "pre")
    c.w("vector", "dve")
    c.ts(t["qn_wn"][:, :], t["qn_ws"][:, :], -1.0, ALU.mult)
    c.ts(t["kn_wn"][:, :], t["kn_ws"][:, :], -1.0, ALU.mult, inc=("dve", 1))


def emit_norm_rope(c, t, ps_src, w, wn, cos_ap, sin_ap, out_lo, out_hi):
    rp = t["rp"]
    c.act(t["hsq"][0][:, :], ps_src, AF.Square, inc=("act", 1))
    sq_ct = c.counts["act"]
    c.w("tensor", "act")
    c.mm(t["psB"][0][:1, :], t["ones128f"][:, :], t["hsq"][0][:, :],
         True, True, inc=("pe", 1))
    c.w("vector", "pe")
    c.ts(t["rrow"][:1, :], t["psB"][0][:1, :], 1.0 / 128.0, ALU.mult,
         inc=("dve", 1))
    c.w("scalar", "dve")
    c.act(t["rrow"][:1, :], t["rrow"][:1, :], AF.Sqrt, inc=("act", 1))
    c.w("vector", "act")
    c.ts(t["rrow"][:1, :], t["rrow"][:1, :], RMS_EPS, ALU.add, inc=("dve", 1))
    c.w("vector", "dve")
    c.recip(t["rrow"][:1, :], t["rrow"][:1, :], inc=("dve", 1))
    c.w("tensor", "dve")
    c.mm(t["psB"][1][:, :], t["ones1f"][:1, :], t["rrow"][:1, :], True, True,
         inc=("pe", 1))
    c.w("vector", "pe")
    c.vcopy(t["rbc"][:, :512], t["psB"][1][:, :], inc=("dve", 1))
    c.w("vector", "dve")
    c.tt(t["kn0a"][:, :], ps_src[0:64, :], t["rbc"][0:64, :512], ALU.mult,
         inc=("dve", 1))
    c.tt(t["kn0b"][:, :], ps_src[64:128, :], t["rbc"][64:128, :512], ALU.mult,
         inc=("dve", 1))
    kn0_ct = c.counts["dve"]
    c.w("vector", "dve")
    k1 = t["kn0a"][:, :]
    k2 = t["kn0b"][:, :]
    c.tt(rp[0][:, :], k1, cos_ap, ALU.mult, inc=("dve", 1))
    c.tt(rp[1][:, :], k2, sin_ap, ALU.mult, inc=("dve", 1))
    c.w("vector", "dve")
    c.ts(rp[2][:, :], rp[1][:, :], w[:, 1:2], ALU.mult, inc=("dve", 1))
    c.w("vector", "dve")
    c.stt(out_lo, rp[0][:, :], w[:, 0:1], rp[2][:, :], ALU.mult, ALU.add,
          inc=("dve", 1))
    c.tt(rp[1][:, :], k2, cos_ap, ALU.mult, inc=("dve", 1))
    c.tt(rp[3][:, :], k1, sin_ap, ALU.mult, inc=("dve", 1))
    c.w("vector", "dve")
    c.ts(rp[2][:, :], rp[1][:, :], w[:, 1:2], ALU.mult, inc=("dve", 1))
    c.w("vector", "dve")
    c.stt(out_hi, rp[3][:, :], wn[:, 0:1], rp[2][:, :], ALU.mult, ALU.add,
          inc=("dve", 1))
    return sq_ct, kn0_ct


def emit_attn_prep(c, io, t):

    nr_cts = {}
    ncn = 0
    for g in range(16):
        def h_ap(kt, g=g):
            return bass.AP(io["h_full"], kt * 128 * T_FULL_B + g * 512,
                           [[T_FULL_B, 128], [1, TC]])
        emit_ln(c, t, h_ap,
                lambda kt: t["a_sb"][:, kt * 512:(kt + 1) * 512])
        ln_dve_ct = c.counts["dve"]
        rp_s = "rpa" if g % 2 == 0 else "rpb"
        c.await_guards("gpsimd", "ropeb")
        c.dma("gpsimd", t["cosb"][:, :],
              bass.AP(io["cosk_i"], g * 512, [[8192, 64], [1, 512]]),
              inc=(rp_s, 16))
        c.dma("gpsimd", t["sinb"][:, :],
              bass.AP(io["sink_i"], g * 512, [[8192, 64], [1, 512]]),
              inc=(rp_s, 16))
        for dt_ in range(4):
            wl = "qla" if dt_ % 2 == 0 else "qlb"
            c.await_guards("gpsimd", "wsl" + str(dt_ % 2))
            c.dma("gpsimd", t["wsl"][dt_ % 2][:, :],
                  bass.AP(io["Wk_T"], dt_ * 262144,
                          [[2048, 128], [1, 2048]]),
                  inc=(wl, 16))
            c.w("tensor", wl)
            if ncn >= 2:
                c.w("tensor", "act", nr_cts[ncn - 2][0])
                c.w("tensor", "dve",
                    max(nr_cts[ncn - 2][1],
                        ln_dve_ct if dt_ == 0 else 0))
            else:
                c.w("tensor", "act")
                c.w("tensor", "dve")
            for kt in range(NKT):
                c.mm(t["psA"][dt_ % 2][:, :],
                     t["wsl"][dt_ % 2][:, kt * 128:(kt + 1) * 128],
                     t["a_sb"][:, kt * 512:(kt + 1) * 512],
                     start=(kt == 0), stop=(kt == NKT - 1),
                     inc=("pe", 1) if kt == NKT - 1 else None)
            c.guard("wsl" + str(dt_ % 2), "pe")
            c.w("scalar", "pe")
            c.w("vector", rp_s)
            nr_cts[ncn] = emit_norm_rope(
                c, t, t["psA"][dt_ % 2][:, :], t["kn_ws"], t["kn_wn"],
                t["cosb"][:, :], t["sinb"][:, :],
                t["Ksb"][0:64, dt_ * 8192 + g * 512:dt_ * 8192 + (g + 1) * 512],
                t["Ksb"][64:128, dt_ * 8192 + g * 512:dt_ * 8192 + (g + 1) * 512])
            ncn += 1
        c.guard("ropeb", "dve")
        for kt in range(NKT):
            wv = "vla" if kt % 2 == 0 else "vlb"
            c.await_guards("gpsimd", "wvb" + str(kt % 2))
            c.dma("gpsimd", t["wvb"][kt % 2][:, :],
                  bass.AP(io["Wv_T"], kt * 128 * 512, [[512, 128], [1, 512]]),
                  inc=(wv, 16))
            c.w("tensor", wv)
            if kt == 0:
                c.w("tensor", "act")
            for ts_ in range(4):
                c.mm(t["psC"][ts_ % 2][:, :] if ts_ < 2 else t["psD"][ts_ % 2][:, :],
                     t["a_sb"][:, kt * 512 + ts_ * 128:kt * 512 + (ts_ + 1) * 128],
                     t["wvb"][kt % 2][:, :],
                     start=(kt == 0), stop=(kt == NKT - 1),
                     inc=("pe", 1) if ts_ == 3 else None)
            c.guard("wvb" + str(kt % 2), "pe")
        c.w("scalar", "pe")
        for ts_ in range(4):
            ps = t["psC"][ts_ % 2] if ts_ < 2 else t["psD"][ts_ % 2]
            c.act(t["vsb"][:, :], ps[:, :], AF.Copy, inc=("act", 1))
            c.w("gpsimd", "act")
            c.dma("gpsimd",
                  bass.AP(io["v_hbm"], (g * 4 + ts_) * 16384,
                          [[128, 128], [1048576, 4], [1, 128]]),
                  t["vsb"][:, :], inc=("sta", 16))
            c.guard("vsb", "sta")
            c.w("scalar", "sta")

    nr_cts = {}
    ncn = 0
    for m in range(4):
        def h_ap(kt, m=m):
            return io["h_own_ap"](kt, m)
        emit_ln(c, t, h_ap,
                lambda kt: t["a_sb"][:, kt * 512:(kt + 1) * 512])
        ln_dve_ct = c.counts["dve"]
        rp_s = "rpa" if m % 2 == 0 else "rpb"
        c.await_guards("gpsimd", "ropeb")
        c.dma("gpsimd", t["cosb"][:, :],
              bass.AP(io["cosq_i"], m * 512, [[2048, 64], [1, 512]]),
              inc=(rp_s, 16))
        c.dma("gpsimd", t["sinb"][:, :],
              bass.AP(io["sinq_i"], m * 512, [[2048, 64], [1, 512]]),
              inc=(rp_s, 16))
        c.w("vector", rp_s)
        for h in range(NH):
            wl = "qla" if h % 2 == 0 else "qlb"
            c.await_guards("gpsimd", "wsl" + str(h % 2))
            c.dma("gpsimd", t["wsl"][h % 2][:, :],
                  bass.AP(io["Wq_T"], h * 262144,
                          [[2048, 128], [1, 2048]]),
                  inc=(wl, 16))
            c.w("tensor", wl)
            if ncn >= 2:
                c.w("tensor", "act", nr_cts[ncn - 2][0])
                c.w("tensor", "dve",
                    max(nr_cts[ncn - 2][1],
                        ln_dve_ct if h == 0 else 0))
            else:
                c.w("tensor", "act")
                c.w("tensor", "dve")
            for kt in range(NKT):
                c.mm(t["psA"][h % 2][:, :],
                     t["wsl"][h % 2][:, kt * 128:(kt + 1) * 128],
                     t["a_sb"][:, kt * 512:(kt + 1) * 512],
                     start=(kt == 0), stop=(kt == NKT - 1),
                     inc=("pe", 1) if kt == NKT - 1 else None)
            c.guard("wsl" + str(h % 2), "pe")
            c.w("scalar", "pe")
            nr_cts[ncn] = emit_norm_rope(
                c, t, t["psA"][h % 2][:, :], t["qn_ws"], t["qn_wn"],
                t["cosb"][:, :], t["sinb"][:, :],
                t["q_n"][0:64, h * 2048 + m * 512:h * 2048 + (m + 1) * 512],
                t["q_n"][64:128, h * 2048 + m * 512:h * 2048 + (m + 1) * 512])
            ncn += 1
        c.guard("ropeb", "dve")


def emit_attn_core(c, io, t):
    ep_dve_ct = None
    for m in range(4):
        nb = (4 * m + 4) * 4
        for h in range(NH):
            gg = h // 4
            if h % 4 == 0:
                c.await_guards("gpsimd", "Vsl")
                c.dma("gpsimd", t["Vsl"][:, :nb * 128],
                      bass.AP(io["v_hbm"], gg * 1048576,
                              [[128, 128], [16384, nb], [1, 128]]),
                      inc=("vla", 16))
                c.w("tensor", "vla")
            qs = t["q_n"][:, h * 2048 + m * 512:h * 2048 + (m + 1) * 512]
            if ep_dve_ct is None:
                c.w("tensor", "dve")
            else:
                c.w("tensor", "dve", ep_dve_ct)
            c.w("tensor", "act")
            tanh_ct = {}
            exp_ct = {}
            pv_ct = {}

            def lpv(pr):
                c.w("tensor", "act", exp_ct[pr])
                for half_ in range(2):
                    kb = 2 * pr + half_
                    pb = t["Psb"][pr % 2][:, half_ * 512:(half_ + 1) * 512]
                    c.mm(t["psB"][0][:1, :], t["ones128b"][:, :], pb,
                         start=(kb == 0), stop=(kb == nb - 1), inc=None)
                    c.mm(t["psB"][1][:, :],
                         t["Vsl"][:, kb * 128:(kb + 1) * 128], pb,
                         start=(kb == 0), stop=(kb == nb - 1),
                         inc=("pe", 1))
                pv_ct[pr] = c.counts["pe"]

            for pr in range(nb // 2):
                pp = pr % 2
                if pr >= 2:
                    c.w("tensor", "act", tanh_ct[pr - 2])
                for half_ in range(2):
                    kb = 2 * pr + half_
                    trailing = kb >= nb - 16
                    stp = t["Stb"][pp][:, half_ * 512:(half_ + 1) * 512]
                    c.mm(stp,
                         t["Ksb"][:, gg * 8192 + kb * 128:
                                  gg * 8192 + (kb + 1) * 128],
                         qs, start=True, stop=not trailing,
                         inc=None if trailing else ("pe", 1))
                    if trailing:
                        i_ = kb - (nb - 16)
                        c.mm(stp, t["negI"][:, :],
                             t["Rm"][:, i_ * 512:(i_ + 1) * 512],
                             start=False, stop=True, inc=("pe", 1))
                qk_ct = c.counts["pe"]
                if pr >= 1:
                    lpv(pr - 1)
                c.w("scalar", "pe", qk_ct)
                if pr >= 1:
                    c.w("scalar", "act", exp_ct[pr - 1])
                c.act(t["Tsb"][:, :], t["Stb"][pp][:, :], AF.Tanh,
                      scale=t["kvals"][:, 3:4], inc=("act", 1))
                tanh_ct[pr] = c.counts["act"]
                if pr >= 2:
                    c.w("scalar", "pe", pv_ct[pr - 2])
                c.w("scalar", "act")
                c.act(t["Psb"][pp][:, :], t["Tsb"][:, :], AF.Exp,
                      scale=t["kvals"][:, 2:3], inc=("act", 1))
                exp_ct[pr] = c.counts["act"]
            lpv(nb // 2 - 1)
            if h % 4 == 3:
                c.guard("Vsl", "pe")
            c.w("vector", "pe")
            c.vcopy(t["lrow"][:1, :], t["psB"][0][:1, :], inc=("dve", 1))
            c.w("vector", "dve")
            c.recip(t["lrow"][:1, :], t["lrow"][:1, :], inc=("dve", 1))
            c.w("tensor", "dve")
            c.mm(t["psC"][0][:, :], t["ones1f"][:1, :], t["lrow"][:1, :],
                 True, True, inc=("pe", 1))
            c.w("vector", "pe")
            c.vcopy(t["rbcs"][:, :], t["psC"][0][:, :], inc=("dve", 1))
            c.w("vector", "dve")
            c.tt(t["y_m"][:, h * 512:(h + 1) * 512], t["psB"][1][:, :],
                 t["rbcs"][:, :], ALU.mult, inc=("dve", 1))
            ep_dve_ct = c.counts["dve"]
        for co in range(16):
            bi = co % 2
            wl = "qla" if bi == 0 else "qlb"
            st_ = "sta" if bi == 0 else "stb"
            rl = "rla" if bi == 0 else "rlb"
            c.await_guards("gpsimd", "wslo" + str(bi))
            c.dma("gpsimd", t["wsl"][bi][:, :],
                  bass.AP(io["Wo_T"], co * 262144,
                          [[2048, 128], [1, 2048]]),
                  inc=(wl, 16))
            c.await_guards("gpsimd", "h1t" + str(bi))
            c.dma("gpsimd", t["h1t"][bi][:, :], io["h_own_ap"](co, m),
                  inc=(rl, 16))
            c.w("tensor", wl)
            c.w("tensor", "dve")
            for h in range(NH):
                c.mm(t["psD"][bi][:, :],
                     t["wsl"][bi][:, h * 128:(h + 1) * 128],
                     t["y_m"][:, h * 512:(h + 1) * 512],
                     start=(h == 0), stop=(h == NH - 1),
                     inc=("pe", 1) if h == NH - 1 else None)
            c.guard("wslo" + str(bi), "pe")
            c.w("vector", "pe")
            c.w("vector", rl)
            c.await_guards("vector", "h2t" + str(bi))
            c.stt(t["h2t"][bi][:, :], t["psD"][bi][:, :], t["bo"][:, co:co + 1],
                  t["h1t"][bi][:, :], ALU.add, ALU.add, inc=("dve", 1))
            c.guard("h1t" + str(bi), "dve")
            c.w("gpsimd", "dve")
            c.dma("gpsimd",
                  bass.AP(io["h2_own"], co * 128 * 2048 + m * 512,
                          [[2048, 128], [1, TC]]),
                  t["h2t"][bi][:, :], inc=(st_, 16))
            c.guard("h2t" + str(bi), st_)


T_FULL_B = 8192


def build_attn_program():
    nc = bass.Bass("TRN2", target_bir_lowering=False)
    stk = ExitStack()
    io = {}
    io["h_full"] = nc.dram_tensor("h_full", [D, T_FULL_B], F32,
                                  kind="ExternalInput")
    io["h_own"] = nc.dram_tensor("h_own", [D, 2048], F32, kind="ExternalInput")
    io["h2_own"] = nc.dram_tensor("h2_own", [D, 2048], F32,
                                  kind="ExternalOutput")
    for nm, shp in (("Wq_T", [16, 128, 16, 128]), ("Wk_T", [4, 128, 16, 128]),
                    ("Wv_T", [D, 512]), ("Wo_T", [16, 128, 16, 128])):
        io[nm] = nc.dram_tensor(nm, shp, F32, kind="ExternalInput")
    io["bo_i"] = nc.dram_tensor("bo_i", [128, 16], F32, kind="ExternalInput")
    io["qn_w"] = nc.dram_tensor("qn_w", [64, 2], F32, kind="ExternalInput")
    io["kn_w"] = nc.dram_tensor("kn_w", [64, 2], F32, kind="ExternalInput")
    io["negI_i"] = nc.dram_tensor("negI_i", [128, 128], BF16,
                                  kind="ExternalInput")
    io["Rm_i"] = nc.dram_tensor("Rm_i", [128, 16 * 512], BF16,
                                kind="ExternalInput")
    io["cosk_i"] = nc.dram_tensor("cosk_i", [64, 8192], F32,
                                  kind="ExternalInput")
    io["sink_i"] = nc.dram_tensor("sink_i", [64, 8192], F32,
                                  kind="ExternalInput")
    io["cosq_i"] = nc.dram_tensor("cosq_i", [64, 2048], F32,
                                  kind="ExternalInput")
    io["sinq_i"] = nc.dram_tensor("sinq_i", [64, 2048], F32,
                                  kind="ExternalInput")
    io["v_hbm"] = nc.dram_tensor("v_hbm", [4 * 64 * 128 * 128], BF16)
    io["h_own_ap"] = lambda kt, m: bass.AP(
        io["h_own"], kt * 128 * 2048 + m * 512, [[2048, 128], [1, TC]])

    sems = {n: stk.enter_context(nc.semaphore(n)) for n in ATTN_SEMS}
    t = alloc_attn_common(nc, stk)
    b = Builder(nc, sems)

    prep_stk = ExitStack()
    alloc_psum(nc, prep_stk, t, sfx="P")
    alloc_attn_prep(nc, prep_stk, t)

    def prog_prep(c):
        emit_attn_consts(c, io, t)
        c.w("tensor", "pre")
        c.w("scalar", "pre")
        c.w("gpsimd", "pre")
        emit_attn_prep(c, io, t)

    b.block(prog_prep)
    prep_stk.close()
    for k in ("a_sb", "stat", "rbc", "mrbc", "hbuf", "hsq", "wsl", "wvb",
              "cosb", "sinb", "cosq", "sinq", "kn0", "rp", "rrow", "vsb"):
        t.pop(k, None)
    core_stk = ExitStack()
    alloc_attn_core(nc, core_stk, t)

    def prog_core(c):
        c.barrier()
        c.dma("gpsimd", t["negI"][:, :], io["negI_i"].ap(), inc=("pre", 16))
        c.dma("gpsimd", t["bo"][:, :], io["bo_i"].ap(), inc=("pre", 16))
        c.dma("gpsimd", t["Rm"][:, :], io["Rm_i"].ap(), inc=("pre", 16))
        c.w("tensor", "pre")
        c.w("vector", "pre")
        emit_attn_core(c, io, t)
        c.w("gpsimd", "sta")
        c.w("gpsimd", "stb")

    b.block(prog_core)
    core_stk.close()
    stk.close()
    return nc


def stage_attn_inputs(ins, j):
    Wq = np.asarray(ins["L0_attn_Wq"], np.float32)
    Wk = np.asarray(ins["L0_attn_Wk"], np.float32)
    Wv = np.asarray(ins["L0_attn_Wv"], np.float32)
    Wo = np.asarray(ins["L0_attn_Wo"], np.float32)
    inv_freq = 1.0 / 10000.0 ** (np.arange(0, HD, 2, dtype=np.float32) / HD)
    pos_full = np.arange(T_FULL_B, dtype=np.float32)
    own_tok = np.concatenate(
        [np.arange((4 * mm + j) * 512, (4 * mm + j) * 512 + 512)
         for mm in range(4)]).astype(np.float32)
    frf = pos_full[:, None] * inv_freq[None, :]
    fro = own_tok[:, None] * inv_freq[None, :]
    Rm = np.zeros((128, 16 * 512), dtype=NPBF16)
    Kk = np.arange(128)[:, None]
    tq = np.arange(512)[None, :]
    for i in range(16):
        d_ = 128 * (i - 4 * j)
        Rm[:, i * 512:(i + 1) * 512] = (Kk - tq > -d_).astype(NPBF16)
    return {
        "Wq_T": np.ascontiguousarray(
            Wq.reshape(16, 128, 16, 128).transpose(0, 3, 2, 1)),
        "Wk_T": np.ascontiguousarray(
            Wk.reshape(4, 128, 16, 128).transpose(0, 3, 2, 1)),
        "Wv_T": np.ascontiguousarray(Wv.T),
        "Wo_T": np.ascontiguousarray(
            Wo.reshape(16, 128, 16, 128).transpose(0, 3, 2, 1)),
        "bo_i": np.ascontiguousarray(
            np.asarray(ins["L0_attn_bo"], np.float32).reshape(16, 128).T),
        "qn_w": np.ascontiguousarray(
            np.asarray(ins["L0_attn_q_norm_w"], np.float32).reshape(2, 64).T),
        "kn_w": np.ascontiguousarray(
            np.asarray(ins["L0_attn_k_norm_w"], np.float32).reshape(2, 64).T),
        "negI_i": (NEG_BIG * np.eye(128, dtype=np.float32)).astype(NPBF16),
        "Rm_i": Rm,
        "cosk_i": np.ascontiguousarray(np.cos(frf).T.astype(np.float32)),
        "sink_i": np.ascontiguousarray(np.sin(frf).T.astype(np.float32)),
        "cosq_i": np.ascontiguousarray(np.cos(fro).T.astype(np.float32)),
        "sinq_i": np.ascontiguousarray(np.sin(fro).T.astype(np.float32)),
    }


def own_token_index(j):
    return np.concatenate(
        [np.arange((4 * m + j) * 512, (4 * m + j + 1) * 512) for m in range(4)])



FUSED_SEMS = SEM_NAMES + ("vla", "vlb", "rpa", "rpb", "cpy")


def alloc_attn_extra(nc, stk, t):
    def sb(name, shape, dt):
        t[name] = stk.enter_context(nc.sbuf_tensor(name, shape, dt))
        return t[name]
    sb("qn_ws", [64, 2], F32)
    sb("kn_ws", [64, 2], F32)
    sb("qn_wn", [64, 2], F32)
    sb("kn_wn", [64, 2], F32)
    sb("Ksb", [128, 4 * 8192], BF16)
    sb("q_n", [128, 16 * 2048], BF16)


def build_fused_program():
    nc = bass.Bass("TRN2", target_bir_lowering=False)
    stk = ExitStack()
    io = {}
    io["x_fm"] = nc.dram_tensor("x_fm", [D, T_FULL_B], F32,
                                kind="ExternalInput")
    io["out_fm"] = nc.dram_tensor("out_fm", [D, 2048], F32,
                                  kind="ExternalOutput")
    io["own_off"] = nc.dram_tensor("own_off", [1, 4], I32,
                                   kind="ExternalInput")
    for nm, shp in (("Wu_st", [16, 128, INTER]), ("Wg_st", [16, 128, INTER]),
                    ("Wd_st", [64, 128, D])):
        io[nm] = nc.dram_tensor(nm, shp, F32, kind="ExternalInput")
    for nm, n in (("ws_up", INTER), ("ws_gate", INTER), ("ws_dn", D)):
        io[nm] = nc.dram_tensor(nm, [1, n], F32, kind="ExternalInput")
    io["su_ug_cols"] = nc.dram_tensor("su_ug_cols", [128, 16], F32,
                                      kind="ExternalInput")
    io["su_dp_cols"] = nc.dram_tensor("su_dp_cols", [128, 64], F32,
                                      kind="ExternalInput")
    io["H128c_i"] = nc.dram_tensor("H128c_i", [128, 128], BF16,
                                   kind="ExternalInput")
    io["BD16p_i"] = nc.dram_tensor("BD16p_i", [128, 128], F32,
                                   kind="ExternalInput")
    io["BD64p_i"] = nc.dram_tensor("BD64p_i", [128, 128], F32,
                                   kind="ExternalInput")
    for nm, shp in (("Wq_T", [16, 128, 16, 128]), ("Wk_T", [4, 128, 16, 128]),
                    ("Wv_T", [D, 512]), ("Wo_T", [16, 128, 16, 128])):
        io[nm] = nc.dram_tensor(nm, shp, F32, kind="ExternalInput")
    io["bo_i"] = nc.dram_tensor("bo_i", [128, 16], F32, kind="ExternalInput")
    io["qn_w"] = nc.dram_tensor("qn_w", [64, 2], F32, kind="ExternalInput")
    io["kn_w"] = nc.dram_tensor("kn_w", [64, 2], F32, kind="ExternalInput")
    io["negI_i"] = nc.dram_tensor("negI_i", [128, 128], BF16,
                                  kind="ExternalInput")
    io["Rm_i"] = nc.dram_tensor("Rm_i", [128, 16 * 512], BF16,
                                kind="ExternalInput")
    io["cosk_i"] = nc.dram_tensor("cosk_i", [64, 8192], F32,
                                  kind="ExternalInput")
    io["sink_i"] = nc.dram_tensor("sink_i", [64, 8192], F32,
                                  kind="ExternalInput")
    io["cosq_i"] = nc.dram_tensor("cosq_i", [64, 2048], F32,
                                  kind="ExternalInput")
    io["sinq_i"] = nc.dram_tensor("sinq_i", [64, 2048], F32,
                                  kind="ExternalInput")
    io["WU_hbm"] = nc.dram_tensor("WU_hbm", [NTAU * 16 * 128 * 128], BF16)
    io["WG_hbm"] = nc.dram_tensor("WG_hbm", [NTAU * 16 * 128 * 128], BF16)
    io["WF_hbm"] = nc.dram_tensor("WF_hbm", [16 * NTAU * 128 * 128], BF16)
    io["BD64_hbm"] = nc.dram_tensor("BD64_hbm", [128 * 8192], BF16)
    io["h1_full"] = nc.dram_tensor("h1_full", [D, T_FULL_B], F32)
    io["h1_own"] = nc.dram_tensor("h1_own", [D, 2048], F32)
    io["h2_own"] = nc.dram_tensor("h2_own", [D, 2048], F32)
    io["v_hbm"] = nc.dram_tensor("v_hbm", [4 * 64 * 128 * 128], BF16)
    io["h_full"] = io["h1_full"]
    io["h_own_ap"] = lambda kt, m: bass.AP(
        io["h1_own"], kt * 128 * 2048 + m * 512, [[2048, 128], [1, TC]])

    sems = {n: stk.enter_context(nc.semaphore(n)) for n in FUSED_SEMS}
    t = alloc_common(nc, stk)
    b = Builder(nc, sems)

    ps1_stk = ExitStack()
    alloc_psum(nc, ps1_stk, t, sfx="1")
    fold_stk = ExitStack()
    alloc_fold(nc, fold_stk, t)

    def prog_fold(c):
        emit_load_consts(c, io, t)
        emit_weight_fold(c, io, t)

    b.block(prog_fold)
    fold_stk.close()

    ffn_stk = ExitStack()
    alloc_ffn(nc, ffn_stk, t)

    def prog_ffn1(c):
        c.barrier()
        emit_bd_tiles(c, t)
        for tc in range(T_FULL_B // TC):
            emit_ffn_chunk(c, io, t, T_FULL_B, tc * TC, "x_fm", "h1_full")

    b.block(prog_ffn1)
    ffn_stk.close()
    ps1_stk.close()
    for k in ("u", "xn", "z", "y", "gsil", "stat", "rbc", "mrbc", "hbuf",
              "hsq", "wu_sl", "wg_sl", "wf_sl", "rbuf", "hnx",
              "psA", "psB", "psC", "psD", "ps8"):
        t.pop(k, None)

    offs_stk = ExitStack()
    t["offs"] = offs_stk.enter_context(nc.sbuf_tensor("offs", [1, 4], I32))

    def prog_copy(c):
        c.barrier()
        c.dma("gpsimd", t["offs"][:1, :], io["own_off"].ap(), inc=("pre", 16))
        c.w("gpsimd", "pre")
        if c.kind == "gpsimd":
            with c.eng.register("roff") as roff:
                for m in range(4):
                    c.eng.reg_load(roff, t["offs"][:1, m:m + 1])
                    i = c.eng.dma_start(
                        out=bass.AP(io["h1_own"], m * 512,
                                    [[2048, 2048], [1, 512]]),
                        in_=bass.AP(io["h1_full"], roff,
                                    [[T_FULL_B, 2048], [1, 512]]))
                    i.then_inc(c.sems["cpy"], 16)
        c.counts["cpy"] += 64
        c.w("gpsimd", "cpy")

    b.block(prog_copy)

    attn_stk = ExitStack()
    alloc_attn_extra(nc, attn_stk, t)
    prep_stk = ExitStack()
    alloc_psum(nc, prep_stk, t, sfx="P")
    alloc_attn_prep(nc, prep_stk, t, sfx="AP")

    def prog_prep(c):
        c.barrier()
        emit_attn_consts(c, io, t, shared=False)
        c.w("tensor", "pre")
        c.w("scalar", "pre")
        c.w("gpsimd", "pre")
        emit_attn_prep(c, io, t)

    b.block(prog_prep)
    prep_stk.close()
    for k in ("a_sb", "stat", "rbc", "mrbc", "hbuf", "hsq", "wsl", "wvb",
              "cosb", "sinb", "kn0a", "kn0b", "rp", "rrow", "vsb",
              "psA", "psB", "psC", "psD", "ps8"):
        t.pop(k, None)

    core_stk = ExitStack()
    alloc_attn_core(nc, core_stk, t, sfx="AC")

    def prog_core(c):
        c.barrier()
        c.dma("gpsimd", t["negI"][:, :], io["negI_i"].ap(), inc=("pre", 16))
        c.dma("gpsimd", t["bo"][:, :], io["bo_i"].ap(), inc=("pre", 16))
        c.dma("gpsimd", t["Rm"][:, :], io["Rm_i"].ap(), inc=("pre", 16))
        c.w("tensor", "pre")
        c.w("vector", "pre")
        emit_attn_core(c, io, t)

    b.block(prog_core)
    core_stk.close()
    attn_stk.close()
    for k in ("Vsl", "Tsb", "Psb", "Stb", "y_m", "lrow", "rbcs", "wsl", "h1t",
              "h2t", "negI", "bo", "Rm", "Ksb", "q_n", "qn_ws", "kn_ws",
              "qn_wn", "kn_wn", "psB", "psC", "psD"):
        t.pop(k, None)

    ffn2_stk = ExitStack()
    alloc_psum(nc, ffn2_stk, t, sfx="3")
    alloc_ffn(nc, ffn2_stk, t, sfx="F3")

    def prog_ffn3(c):
        c.barrier()
        emit_bd_tiles(c, t)
        for tc in range(2048 // TC):
            emit_ffn_chunk(c, io, t, 2048, tc * TC, "h2_own", "out_fm")
        c.w("gpsimd", "sta")
        c.w("gpsimd", "stb")

    b.block(prog_ffn3)
    ffn2_stk.close()
    offs_stk.close()
    stk.close()
    return nc


def kernel(**inputs):
    x = np.asarray(inputs["L0_attn_x"], np.float32)
    nc = build_fused_program()
    staged_f = stage_ffn_inputs(inputs)
    in_maps = []
    for c_ in range(8):
        b_, j_ = c_ // 4, c_ % 4
        m = dict(staged_f)
        m.update(stage_attn_inputs(inputs, j_))
        m["x_fm"] = np.ascontiguousarray(x[b_].T)
        m["own_off"] = np.array(
            [[(4 * mm + j_) * 512 for mm in range(4)]], np.int32)
        in_maps.append(m)
    res = run_bass_kernel_spmd(nc, in_maps, core_ids=list(range(8)))
    out = np.empty((2, T_FULL_B, D), np.float32)
    for c_ in range(8):
        b_, j_ = c_ // 4, c_ % 4
        idx = own_token_index(j_)
        out[b_][idx] = res.results[c_]["out_fm"].T
    return out



def kernel_unfused(**inputs):
    x = np.asarray(inputs["L0_attn_x"], np.float32)
    staged_f = stage_ffn_inputs(inputs)

    nc_f = build_ffn_program(2048)
    in_maps = []
    xf = x.reshape(-1, D)
    for c_ in range(8):
        m = dict(staged_f)
        m["h_in"] = np.ascontiguousarray(xf[c_ * 2048:(c_ + 1) * 2048].T)
        in_maps.append(m)
    res1 = run_bass_kernel_spmd(nc_f, in_maps, core_ids=list(range(8)))
    h1 = np.empty((2, T_FULL_B, D), np.float32)
    for c_ in range(8):
        h1.reshape(-1, D)[c_ * 2048:(c_ + 1) * 2048] = res1.results[c_]["h_out"].T

    nc_a = build_attn_program()
    in_maps = []
    for c_ in range(8):
        b_, j_ = c_ // 4, c_ % 4
        m = stage_attn_inputs(inputs, j_)
        m["h_full"] = np.ascontiguousarray(h1[b_].T)
        idx = own_token_index(j_)
        m["h_own"] = np.ascontiguousarray(h1[b_][idx].T)
        in_maps.append(m)
    res2 = run_bass_kernel_spmd(nc_a, in_maps, core_ids=list(range(8)))
    h2 = np.empty((2, T_FULL_B, D), np.float32)
    for c_ in range(8):
        b_, j_ = c_ // 4, c_ % 4
        idx = own_token_index(j_)
        h2[b_][idx] = res2.results[c_]["h2_own"].T

    in_maps = []
    h2f = h2.reshape(-1, D)
    for c_ in range(8):
        m = dict(staged_f)
        m["h_in"] = np.ascontiguousarray(h2f[c_ * 2048:(c_ + 1) * 2048].T)
        in_maps.append(m)
    res3 = run_bass_kernel_spmd(nc_f, in_maps, core_ids=list(range(8)))
    out = np.empty((2, T_FULL_B, D), np.float32)
    for c_ in range(8):
        out.reshape(-1, D)[c_ * 2048:(c_ + 1) * 2048] = \
            res3.results[c_]["h_out"].T
    return out

